# version 12
# baseline (speedup 1.0000x reference)
import math
import numpy as np
import concourse.bass as bass
import concourse.mybir as mybir
from concourse.bass_utils import run_bass_kernel_spmd

F32 = mybir.dt.float32
BF16 = mybir.dt.bfloat16
U32 = mybir.dt.uint32
I32 = mybir.dt.int32
AF = mybir.ActivationFunctionType
ALU = mybir.AluOpType
AX = mybir.AxisListType

B, T, D = 8, 4096, 1024
NCORES = 8
NKT = D // 128
NTM = T // 512
PI = math.pi

ENGINES = ("tensor", "vector", "scalar", "gpsimd", "sync")

class Op:
    __slots__ = ("engine", "kind", "build", "deps", "idx", "inc", "token", "tag", "pre_wait")
    def __init__(self, engine, kind, build, deps, idx, tag):
        self.engine = engine; self.kind = kind; self.build = build
        self.deps = deps; self.idx = idx; self.inc = None; self.token = None
        self.tag = tag; self.pre_wait = None

class Prog:
    SERIALIZE = False
    SER_ENGINE = False
    NDMA = 48
    PE_SKIP = True
    def __init__(self, nc):
        self.nc = nc
        self.ops = []
        self.last_w = {}
        self.readers = {}
        self.tag = ""

    def _record(self, engine, kind, build, reads, writes):
        i = len(self.ops)
        deps = set()
        for k in reads:
            w = self.last_w.get(k)
            if w is not None:
                deps.add(w)
        for k in writes:
            w = self.last_w.get(k)
            if w is not None:
                deps.add(w)
            rs = self.readers.get(k)
            if rs:
                deps.update(rs)
        deps.discard(i)
        if Prog.SERIALIZE and i > 0:
            deps.add(i - 1)
        if Prog.SER_ENGINE:
            for j in range(i - 1, -1, -1):
                if self.ops[j].engine == engine:
                    deps.add(j)
                    break
        if engine == "tensor" and Prog.PE_SKIP and not Prog.SERIALIZE:
            deps = {d for d in deps if self.ops[d].engine != "tensor"}
        op = Op(engine, kind, build, deps, i, self.tag)
        self.ops.append(op)
        for k in reads:
            self.readers.setdefault(k, []).append(i)
        for k in writes:
            self.last_w[k] = i
            self.readers[k] = []
        return i

    def comp(self, engine, build, reads=(), writes=()):
        return self._record(engine, "comp", build, reads, writes)

    def dma(self, engine, build, reads=(), writes=()):
        return self._record(engine, "dma", build, reads, writes)

    def coll(self, build, reads=(), writes=()):
        return self._record("gpsimd", "coll", build, reads, writes)

    def emit(self, stack):
        nc = self.nc
        need_inc = set()
        for op in self.ops:
            need_inc |= op.deps
        esem = {e: stack.enter_context(nc.semaphore(f"se_{e}")) for e in ENGINES}
        pools = {
            "sync": [stack.enter_context(nc.semaphore(f"sdy_{i}")) for i in range(24)],
            "scalar": [stack.enter_context(nc.semaphore(f"sdc_{i}")) for i in range(16)],
            "gpsimd": [stack.enter_context(nc.semaphore(f"sdg_{i}")) for i in range(24)],
        }
        csem = stack.enter_context(nc.semaphore("scoll"))
        duse = {q: [0] * len(pool) for q, pool in pools.items()}
        dnext = {q: 0 for q in pools}
        ccount = 0
        ecount = {e: 0 for e in ENGINES}
        for op in self.ops:
            if op.kind == "comp" and op.idx not in need_inc:
                continue
            if op.kind == "comp":
                ecount[op.engine] += 1
                op.token = (esem[op.engine], ecount[op.engine])
                op.inc = (esem[op.engine], 1)
            elif op.kind == "coll":
                ccount += 1
                if ccount > 1:
                    op.pre_wait = (csem, ccount - 1)
                op.token = (csem, ccount)
                op.inc = (csem, 1)
            else:
                q = op.engine
                pool = pools[q]
                si = dnext[q] % len(pool); dnext[q] += 1
                duse[q][si] += 1
                if duse[q][si] > 1:
                    op.pre_wait = (pool[si], 16 * (duse[q][si] - 1))
                op.token = (pool[si], 16 * duse[q][si])
                op.inc = (pool[si], 16)
        guaranteed = {e: {} for e in ENGINES}
        engobj = {e: getattr(nc, e) for e in ENGINES}
        for op in self.ops:
            g = guaranteed[op.engine]
            waits = {}
            if op.pre_wait is not None:
                s, v = op.pre_wait
                if g.get(s.name, 0) < v:
                    waits[s] = v
            for d in op.deps:
                tk = self.ops[d].token
                assert tk is not None, f"dep {d} has no token"
                s, v = tk
                if g.get(s.name, 0) < v:
                    if waits.get(s, 0) < v:
                        waits[s] = v
            for s, v in waits.items():
                engobj[op.engine].wait_ge(s, v)
                g[s.name] = v
            inst = op.build()
            if op.inc is not None:
                assert inst is not None, f"op {op.idx} ({op.tag}) needs inc but returned no instruction"
                inst.then_inc(*op.inc)
        return esem, pools


def keys(name, c0, c1, gran=512):
    return [(name, g) for g in range(c0 // gran, (c1 + gran - 1) // gran)]


class Ctx:
    pass


def build_program(stop_after=8, debug_dump=()):
    nc = bass.Bass("TRN2", target_bir_lowering=False, debug=False)
    import contextlib
    stack = contextlib.ExitStack()
    c = Ctx()
    c.nc = nc
    c.stack = stack
    p = Prog(nc)
    c.p = p

    def din(name, shape, dtype=F32):
        return nc.dram_tensor(name, shape, dtype, kind="ExternalInput").ap()

    c.x_T = din("x_T", [D, T])
    c.out_T = nc.dram_tensor("out_T", [D, T], F32, kind="ExternalOutput").ap()
    iw = {}
    c.iw = iw
    iw["a0_wxT"] = din("a0_wxT", [D, 2 * D])
    iw["a0_bx"] = din("a0_bx", [2 * D])
    iw["a0_A"] = din("a0_A", [D, 16])
    iw["a0_D"] = din("a0_D", [D])
    iw["a0_woT"] = din("a0_woT", [D, D])
    iw["a0_bo"] = din("a0_bo", [D])
    iw["a0_gamma"] = din("a0_gamma", [D])
    iw["a0_beta"] = din("a0_beta", [D])
    for nm in ("b0_wqT", "b0_wkT", "b0_wvT", "b0_woT"):
        iw[nm] = din(nm, [D, D])
    iw["b0_bo"] = din("b0_bo", [D])
    iw["b0_wspanT"] = din("b0_wspanT", [D, 16])
    iw["b0_bspan"] = din("b0_bspan", [16])
    iw["f0_rwT"] = din("f0_rwT", [D, 8])
    iw["f0_w1"] = din("f0_w1", [8 * D, 4096])
    iw["f0_b1"] = din("f0_b1", [8, 4096])
    iw["f0_w2"] = din("f0_w2", [8 * 4096, D])
    iw["f0_b2"] = din("f0_b2", [8, D])
    for nm in ("a1_wqT", "a1_wkT", "a1_wvT", "a1_wpT"):
        iw[nm] = din(nm, [D, D])
    iw["a1_bp"] = din("a1_bp", [D])
    iw["a1_wr"] = din("a1_wr", [16 * 64, 128])
    iw["a1_br"] = din("a1_br", [16, 128])
    for nm in ("b1_wqT", "b1_wkT", "b1_wvT"):
        iw[nm] = din(nm, [D, D])
    iw["b1_bq"] = din("b1_bq", [D])
    iw["b1_bk"] = din("b1_bk", [D])
    iw["b1_bv"] = din("b1_bv", [D])
    iw["b1_gamma"] = din("b1_gamma", [D])
    iw["b1_beta"] = din("b1_beta", [D])
    iw["f1_wgT"] = din("f1_wgT", [D, 8])
    iw["f1_w1"] = din("f1_w1", [8 * D, 4096])
    iw["f1_w2"] = din("f1_w2", [8 * 2048, D])
    iw["prevmask"] = din("prevmask", [8])

    c.X = nc.dram_tensor("Xs", [D, T], F32).ap()
    c.tT = nc.dram_tensor("tT", [D, T], F32).ap()
    c.xn32 = nc.dram_tensor("xn32", [D, T], F32).ap()
    c.ctxT = nc.dram_tensor("ctxT", [2 * D, T], BF16).ap()
    c.xtok = nc.dram_tensor("xtok", [T + 1, D], BF16).ap()
    c.yT = nc.dram_tensor("yT", [D, T], F32).ap()
    c.strips = nc.dram_tensor("strips", [16, 4608], BF16).ap()
    c.table = nc.dram_tensor("mtable", [11776], U32).ap()
    c.expout = nc.dram_tensor("expout", [11776, D], F32).ap()
    c.entE = nc.dram_tensor("entE", [8192], F32).ap()
    c.entG = nc.dram_tensor("entG", [8192], F32).ap()
    c.entS = nc.dram_tensor("entS", [8192], F32).ap()
    c.entK = nc.dram_tensor("entK", [8192], F32).ap()
    c.cc_p_in = nc.dram_tensor("cc_p_in", [32, 128], F32).ap()
    c.cc_e_in = nc.dram_tensor("cc_e_in", [32, 128], F32).ap()
    c.cc_p_out = nc.dram_tensor("cc_p_out", [NCORES * 32, 128], F32).ap()
    c.cc_e_out = nc.dram_tensor("cc_e_out", [NCORES * 32, 128], F32).ap()
    c.cc_cnt_in = nc.dram_tensor("cc_cnt_in", [1, 8], F32).ap()
    c.cc_cnt_out = nc.dram_tensor("cc_cnt_out", [NCORES, 8], F32).ap()
    c.w1b = [nc.dram_tensor("w1b0", [8 * D, 4096], BF16).ap(),
             nc.dram_tensor("w1b1", [8 * D, 4096], BF16).ap()]
    c.w2b = [nc.dram_tensor("w2b0", [8 * 4096, D], BF16).ap(),
             nc.dram_tensor("w2b1", [8 * 2048, D], BF16).ap()]
    c.moe_wconv_done = {0: False, 1: False}

    sb = lambda name, shape, dt: stack.enter_context(nc.sbuf_tensor(name, shape, dt)).ap()
    c.xnb = sb("xnb", [128, NKT * T], BF16)
    c.wst = sb("wst", [128, 16384], BF16)
    c.qkv = sb("qkv", [128, 8192], BF16)
    c.ework = sb("ework", [128, 8192], BF16)
    c.f32a = sb("f32a", [128, 4096], F32)
    c.f32b = sb("f32b", [128, 4096], F32)
    c.f32c = sb("f32c", [128, 4096], F32)
    c.cols = sb("cols", [128, 1024], F32)
    c.rows = sb("rows", [1, 1664], F32)
    c.idxs = sb("idxs", [128, 64], U32)
    c.moeS = sb("moeS", [128, 512], F32)
    c.att = sb("att", [128, 2560], BF16)
    c.i512 = sb("i512", [128, 1024], I32)
    c.i32s = sb("i32s", [128, 64], I32)
    c.consts = sb("consts", [128, 416], F32)
    c.cbf = sb("cbf", [128, 512], BF16)

    c.ps = [stack.enter_context(nc.psum_tensor(f"p{i}", [128, 512], F32)).ap() for i in range(6)]
    c.qs = [stack.enter_context(nc.psum_tensor(f"q{i}", [128, 1024], BF16)).ap() for i in range(2)]

    emit_consts(c)
    emit_xtok_zero_row(c)
    emit_init_X(c)

    blocks = [
        emit_l0_attn_a,
        lambda c: emit_moe(c, layer=0),
        emit_l0_attn_b,
        lambda c: emit_moe(c, layer=0),
        emit_l1_attn_a,
        lambda c: emit_moe(c, layer=1),
        emit_l1_attn_b,
        lambda c: emit_moe(c, layer=1),
    ]
    for bi in range(stop_after):
        c.p.tag = f"blk{bi}"
        c.blk = bi
        blocks[bi](c)

    emit_finish(c)

    ndbg = 0
    for nm in debug_dump:
        ap = getattr(c, nm)
        h, w = ap.shape
        dt = ap.dtype
        dbg = nc.dram_tensor(f"dbg_{nm}", [h, w], dt, kind="ExternalOutput").ap()
        rkeys = [(nm, i) for i in range(200)]
        nrow = (h + 127) // 128
        for r in range(nrow):
            r0, r1 = r * 128, min(h, (r + 1) * 128)
            for mcol in range(0, w, 4096):
                m1 = min(w, mcol + 4096)
                src_ = ap[r0:r1, mcol:m1]
                dst_ = dbg[r0:r1, mcol:m1]
                p.dma("gpsimd", lambda src_=src_, dst_=dst_: nc.gpsimd.dma_start(out=dst_, in_=src_),
                      reads=rkeys, writes=[("dbgout", ndbg, r, mcol)])
                ndbg += 1
    if debug_dump:
        p.comp("vector", lambda: nc.vector.engine_nop(),
               reads=[k for k in p.last_w if k[0] == "dbgout"])

    p.emit(stack)
    stack.close()
    return nc


def constcell(c, val):
    if not hasattr(c, "_cells"):
        c._cells = {}
    if val not in c._cells:
        i = 960 + len(c._cells)
        assert i < 1024, "out of const cells"
        ap = c.cols[:, i:i + 1]
        g = c.nc.gpsimd
        c.p.comp("gpsimd", lambda ap=ap, val=val: g.memset(ap, float(val)), writes=[("cols", 1)])
        c._cells[val] = ap
    return c._cells[val]


def constrow(c, val):
    return constcell(c, val)[0:1, :]


def emit_consts(c):
    nc, p = c.nc, c.p
    CON = c.consts
    c.idF = CON[:, 0:128]
    c.Lst = CON[:, 128:256]
    c.Lin = CON[:, 256:384]
    c.onescol = CON[:, 384:385]
    c.iotaP = CON[:, 385:386]
    c.zcol = CON[:, 386:387]
    c.zeros512 = CON[:, 387:899] if False else None
    c.idB = c.cbf[:, 0:128]
    c.onescolB = c.cbf[:, 128:129]
    g = nc.gpsimd
    p.comp("gpsimd", lambda: g.memset(CON[:, 0:387], 0.0), writes=[("consts", 0)])
    p.comp("gpsimd", lambda: g.affine_select(out=c.idF, in_=c.idF, compare_op=ALU.not_equal,
                                             fill=1.0, base=0, pattern=[[-1, 128]], channel_multiplier=1),
           reads=[("consts", 0)], writes=[("consts", 0)])
    p.comp("gpsimd", lambda: g.memset(c.Lst, 0.0), writes=[("consts", 0)])
    p.comp("gpsimd", lambda: g.affine_select(out=c.Lst, in_=c.Lst, compare_op=ALU.is_ge,
                                             fill=1.0, base=0, pattern=[[-1, 128]], channel_multiplier=1),
           reads=[("consts", 0)], writes=[("consts", 0)])
    p.comp("gpsimd", lambda: g.memset(c.Lin, 0.0), writes=[("consts", 0)])
    p.comp("gpsimd", lambda: g.affine_select(out=c.Lin, in_=c.Lin, compare_op=ALU.is_ge,
                                             fill=1.0, base=-1, pattern=[[-1, 128]], channel_multiplier=1),
           reads=[("consts", 0)], writes=[("consts", 0)])
    p.comp("gpsimd", lambda: g.memset(c.onescol, 1.0), writes=[("consts", 0)])
    p.comp("gpsimd", lambda: g.iota(c.iotaP, [[0, 1]], base=0, channel_multiplier=1,
                                    allow_small_or_imprecise_dtypes=True), writes=[("consts", 0)])
    p.comp("gpsimd", lambda: g.memset(c.cbf[:, 0:512], 0.0), writes=[("cbf", 0)])
    p.comp("gpsimd", lambda: g.affine_select(out=c.idB, in_=c.idB, compare_op=ALU.not_equal,
                                             fill=1.0, base=0, pattern=[[-1, 128]], channel_multiplier=1),
           reads=[("cbf", 0)], writes=[("cbf", 0)])
    p.comp("gpsimd", lambda: g.memset(c.onescolB, 1.0), writes=[("cbf", 0)])
    c.io8 = CON[:, 387:395]
    p.comp("gpsimd", lambda: g.iota(c.io8, [[1, 8]], base=0, channel_multiplier=0,
                                    allow_small_or_imprecise_dtypes=True), writes=[("consts", 0)])
    c.onesrow = c.rows[0:1, 1536:1664]
    p.comp("gpsimd", lambda: g.memset(c.onesrow, 1.0), writes=[("rows", 3)])


def emit_xtok_zero_row(c):
    nc, p = c.nc, c.p
    z = c.ework[0:1, 0:1024]
    p.comp("vector", lambda: nc.vector.memset(z, 0.0), writes=keys("ework", 0, 1024))
    p.dma("sync", lambda: nc.sync.dma_start(out=c.xtok[T:T + 1, :], in_=z),
          reads=keys("ework", 0, 1024), writes=[("xtokz",)])


def emit_init_X(c):
    nc, p = c.nc, c.p
    for kt in range(NKT):
        for m in range(NTM):
            src = c.x_T[kt * 128:(kt + 1) * 128, m * 512:(m + 1) * 512]
            dst = c.X[kt * 128:(kt + 1) * 128, m * 512:(m + 1) * 512]
            buf = c.f32a[:, m * 512:(m + 1) * 512]
            p.dma("sync", lambda src=src, buf=buf: nc.sync.dma_start(out=buf, in_=src),
                  writes=[("f32a", m)])
            p.dma("sync", lambda dst=dst, buf=buf: nc.sync.dma_start(out=dst, in_=buf),
                  reads=[("f32a", m)], writes=[("X", kt * 8 + m)])


def emit_ln(c, src, srckey, gamma=None, beta=None, dst_sbuf=True, dst32=None, dst32key=None,
            src_is_sbuf=False):
    nc, p = c.nc, c.p
    v, s, t = nc.vector, nc.scalar, nc.tensor
    for m in range(NTM):
        for kt in range(NKT):
            buf = c.f32a[:, kt * 512:(kt + 1) * 512]
            if src_is_sbuf:
                srcap = src(kt, m)
                p.comp("vector", lambda b=buf, a=srcap: v.tensor_copy(b, a),
                       reads=keys(srckey, (kt * 8 + m) * 512, (kt * 8 + m) * 512 + 512),
                       writes=[("f32a", kt)])
            else:
                srcap = src[kt * 128:(kt + 1) * 128, m * 512:(m + 1) * 512]
                p.dma("sync", lambda b=buf, a=srcap: nc.sync.dma_start(out=b, in_=a),
                      reads=[(srckey, kt * 8 + m)], writes=[("f32a", kt)])
        for kt in range(NKT):
            buf = c.f32a[:, kt * 512:(kt + 1) * 512]
            sq = c.f32c[:, kt * 512:(kt + 1) * 512]
            p.comp("scalar", lambda sq=sq, b=buf: s.activation(sq, b, AF.Square, bias=0.0, scale=1.0),
                   reads=[("f32a", kt)], writes=[("f32c", kt)])
            p.comp("tensor", lambda b=buf, kt=kt: t.matmul(c.ps[4][0:1, :], c.onescol, b,
                                                           start=(kt == 0), stop=(kt == NKT - 1)),
                   reads=[("f32a", kt), ("consts", 0)], writes=[("p4",)])
            p.comp("tensor", lambda sq=sq, kt=kt: t.matmul(c.ps[5][0:1, :], c.onescol, sq,
                                                           start=(kt == 0), stop=(kt == NKT - 1)),
                   reads=[("f32c", kt), ("consts", 0)], writes=[("p5",)])
        r0 = c.rows[0:1, 0:512]; r1 = c.rows[0:1, 512:1024]; r2 = c.rows[0:1, 1024:1536]
        p.comp("vector", lambda: v.tensor_single_scalar(r0, c.ps[4][0:1, :], 1.0 / D, ALU.mult),
               reads=[("p4",)], writes=[("rows", 0)])
        p.comp("vector", lambda: v.tensor_single_scalar(r1, c.ps[5][0:1, :], 1.0 / D, ALU.mult),
               reads=[("p5",)], writes=[("rows", 1)])
        p.comp("vector", lambda: v.scalar_tensor_tensor(r2, r0, 1.0, r0, ALU.mult, ALU.mult),
               reads=[("rows", 0)], writes=[("rows", 2)])
        p.comp("vector", lambda: v.tensor_sub(r1, r1, r2),
               reads=[("rows", 1), ("rows", 2)], writes=[("rows", 1)])
        epsr = constrow(c, 1e-5)
        p.comp("scalar", lambda epsr=epsr: s.activation(r2, r1, AF.Sqrt, bias=epsr, scale=1.0),
               reads=[("rows", 1), ("cols", 1)], writes=[("rows", 2)])
        p.comp("vector", lambda: v.reciprocal(r1, r2),
               reads=[("rows", 2)], writes=[("rows", 1)])
        p.comp("vector", lambda: v.tensor_single_scalar(r0, r0, -1.0, ALU.mult),
               reads=[("rows", 0)], writes=[("rows", 0)])
        p.comp("tensor", lambda: t.matmul(c.ps[4], c.onesrow, r0, start=True, stop=True),
               reads=[("rows", 0), ("rows", 3)], writes=[("p4",)])
        p.comp("tensor", lambda: t.matmul(c.ps[5], c.onesrow, r1, start=True, stop=True),
               reads=[("rows", 1), ("rows", 3)], writes=[("p5",)])
        for kt in range(NKT):
            buf = c.f32a[:, kt * 512:(kt + 1) * 512]
            xb = c.f32b[:, kt * 512:(kt + 1) * 512]
            p.comp("vector", lambda xb=xb, b=buf: v.tensor_add(xb, b, c.ps[4]),
                   reads=[("f32a", kt), ("p4",)], writes=[("f32b", kt)])
            p.comp("vector", lambda xb=xb: v.tensor_mul(xb, xb, c.ps[5]),
                   reads=[("f32b", kt), ("p5",)], writes=[("f32b", kt)])
            if gamma is not None:
                gcol = gamma(kt); bcol = beta(kt)
                p.comp("vector", lambda xb=xb, gcol=gcol, bcol=bcol:
                       v.tensor_scalar(xb, xb, gcol, bcol, ALU.mult, ALU.add),
                       reads=[("f32b", kt), ("cols", 0)], writes=[("f32b", kt)])
            if dst_sbuf:
                xnb = c.xnb[:, kt * T + m * 512: kt * T + (m + 1) * 512]
                p.comp("scalar", lambda xnb=xnb, xb=xb: s.copy(xnb, xb),
                       reads=[("f32b", kt)], writes=keys("xnb", (kt * 8 + m) * 512, (kt * 8 + m) * 512 + 512))
            if dst32 is not None:
                dap = dst32[kt * 128:(kt + 1) * 128, m * 512:(m + 1) * 512]
                p.dma("sync", lambda dap=dap, xb=xb: nc.sync.dma_start(out=dap, in_=xb),
                      reads=[("f32b", kt)], writes=[(dst32key, kt * 8 + m)])


def emit_proj_residual(c, w_dram, Kdim, bias_dram=None, extra_dram=None, extra_key=None,
                       dst=None, dstkey=None, ctx_dram=None, ctx_key="ctxT", scale_out=None):
    nc, p = c.nc, c.p
    v, s, t = nc.vector, nc.scalar, nc.tensor
    if dst is None:
        dst = c.X; dstkey = "X"
    if ctx_dram is None:
        ctx_dram = c.ctxT
    KT = Kdim // 128
    for kt in range(KT):
        wap = w_dram[kt * 128:(kt + 1) * 128, :]
        wsb = c.wst[:, kt * 1024:(kt + 1) * 1024]
        p.dma("gpsimd", lambda wsb=wsb, wap=wap: nc.gpsimd.dma_start(out=wsb, in_=wap),
              writes=keys("wst", kt * 1024, (kt + 1) * 1024))
    if bias_dram is not None:
        emit_bias_cols(c, bias_dram, 8, 0)
    for m in range(NTM):
        for kt in range(KT):
            cap_ = ctx_dram[kt * 128:(kt + 1) * 128, m * 512:(m + 1) * 512]
            esb = c.ework[:, kt * 512:(kt + 1) * 512]
            p.dma("sync", lambda esb=esb, cap_=cap_: nc.sync.dma_start(out=esb, in_=cap_),
                  reads=[(ctx_key, kt * 8 + m)], writes=[("ework", kt)])
        for o in range(8):
            pb = o % 4
            pskey = (f"p{pb}",)
            for kt in range(KT):
                lhs = c.wst[:, kt * 1024 + o * 128: kt * 1024 + o * 128 + 128]
                rhs = c.ework[:, kt * 512:(kt + 1) * 512]
                p.comp("tensor", lambda lhs=lhs, rhs=rhs, pb=pb, kt=kt, KT=KT:
                       t.matmul(c.ps[pb], lhs, rhs, start=(kt == 0), stop=(kt == KT - 1)),
                       reads=keys("wst", kt * 1024, (kt + 1) * 1024) + [("ework", kt)],
                       writes=[pskey])
            xtile = c.f32a[:, o * 512:(o + 1) * 512]
            Xap = dst[o * 128:(o + 1) * 128, m * 512:(m + 1) * 512]
            Xsrc = c.X[o * 128:(o + 1) * 128, m * 512:(m + 1) * 512]
            acc = c.f32c[:, o * 512:(o + 1) * 512]
            p.dma("scalar", lambda xtile=xtile, Xsrc=Xsrc: nc.scalar.dma_start(out=xtile, in_=Xsrc),
                  reads=[("X", o * 8 + m)], writes=[("f32a", o)])
            if bias_dram is not None:
                bcol = c.cols[:, o:o + 1]
                p.comp("scalar", lambda acc=acc, pb=pb, bcol=bcol:
                       s.activation(acc, c.ps[pb], AF.Identity, bias=bcol, scale=1.0),
                       reads=[pskey, ("cols", 0)], writes=[("f32c", o)])
            else:
                p.comp("scalar", lambda acc=acc, pb=pb: s.copy(acc, c.ps[pb]),
                       reads=[pskey], writes=[("f32c", o)])
            if scale_out is not None:
                p.comp("vector", lambda acc=acc: v.tensor_single_scalar(acc, acc, scale_out, ALU.mult),
                       reads=[("f32c", o)], writes=[("f32c", o)])
            p.comp("vector", lambda acc=acc, xtile=xtile: v.tensor_add(acc, acc, xtile),
                   reads=[("f32c", o), ("f32a", o)], writes=[("f32c", o)])
            if extra_dram is not None:
                eap = extra_dram[o * 128:(o + 1) * 128, m * 512:(m + 1) * 512]
                etile = c.f32b[:, o * 512:(o + 1) * 512]
                p.dma("scalar", lambda etile=etile, eap=eap: nc.scalar.dma_start(out=etile, in_=eap),
                      reads=[(extra_key, o * 8 + m)], writes=[("f32b", o)])
                p.comp("vector", lambda acc=acc, etile=etile: v.tensor_add(acc, acc, etile),
                       reads=[("f32c", o), ("f32b", o)], writes=[("f32c", o)])
            p.dma("sync", lambda Xap=Xap, acc=acc: nc.sync.dma_start(out=Xap, in_=acc),
                  reads=[("f32c", o)], writes=[(dstkey, o * 8 + m)])


def emit_bias_cols(c, bias_dram, ntiles, col0):
    nc, p = c.nc, c.p
    v, t = nc.vector, nc.tensor
    bap = bias_dram.rearrange("(n p) -> n p", p=128)
    tmp = c.f32b[0:ntiles, 0:128]
    p.dma("sync", lambda: nc.sync.dma_start(out=tmp, in_=bap), writes=[("f32b", 0)])
    p.comp("tensor", lambda: t.transpose(c.ps[5][:, 0:ntiles], tmp, c.idF[0:ntiles, 0:ntiles]),
           reads=[("f32b", 0), ("consts", 0)], writes=[("p5",)])
    p.comp("vector", lambda: v.tensor_copy(c.cols[:, col0:col0 + ntiles], c.ps[5][:, 0:ntiles]),
           reads=[("p5",)], writes=[("cols", 0)])


def emit_l0_attn_a(c):
    nc, p = c.nc, c.p
    v, s, t = nc.vector, nc.scalar, nc.tensor
    iw = c.iw
    emit_bias_cols(c, iw["a0_gamma"], 8, 8)
    emit_bias_cols(c, iw["a0_beta"], 8, 16)
    emit_bias_cols(c, iw["a0_D"], 8, 24)
    Aap = iw["a0_A"].rearrange("(kt p) s -> p kt s", p=128)
    p.dma("sync", lambda: nc.sync.dma_start(out=c.cols[:, 32:160], in_=Aap, allow_slow_non_contiguous=True),
          writes=[("cols", 0)])
    emit_ln(c, c.X, "X", dst_sbuf=False, dst32=c.tT, dst32key="tT")
    emit_ln(c, c.tT, "tT", gamma=lambda kt: c.cols[:, 8 + kt:9 + kt],
            beta=lambda kt: c.cols[:, 16 + kt:17 + kt], dst_sbuf=True, dst32=c.xn32, dst32key="xn32")
    for kt in range(NKT):
        wap = iw["a0_wxT"][kt * 128:(kt + 1) * 128, :]
        wsb = c.wst[:, kt * 2048:(kt + 1) * 2048]
        p.dma("gpsimd", lambda wsb=wsb, wap=wap: nc.gpsimd.dma_start(out=wsb, in_=wap),
              writes=keys("wst", kt * 2048, (kt + 1) * 2048))
    emit_bias_cols16(c, iw["a0_bx"], 16, 160)

    for i in range(8):
        for m in range(NTM):
            pb = m % 2
            for kt in range(NKT):
                lhs = c.wst[:, kt * 2048 + i * 128: kt * 2048 + i * 128 + 128]
                rhs = c.xnb[:, kt * T + m * 512: kt * T + (m + 1) * 512]
                p.comp("tensor", lambda lhs=lhs, rhs=rhs, pb=pb, kt=kt:
                       t.matmul(c.ps[pb], lhs, rhs, start=(kt == 0), stop=(kt == NKT - 1)),
                       reads=keys("wst", kt * 2048, (kt + 1) * 2048) +
                             keys("xnb", (kt * 8 + m) * 512, (kt * 8 + m) * 512 + 512),
                       writes=[(f"p{pb}",)])
            dtile = c.f32a[:, m * 512:(m + 1) * 512]
            u = c.f32c[:, m * 512:(m + 1) * 512]
            bxcol = c.cols[:, 160 + i:161 + i]
            p.comp("scalar", lambda dtile=dtile, pb=pb, bxcol=bxcol:
                   s.activation(dtile, c.ps[pb], AF.Identity, bias=bxcol, scale=1.0),
                   reads=[(f"p{pb}",), ("cols", 0)], writes=[("f32a", m)])
            p.comp("scalar", lambda u=u, dtile=dtile: s.activation(u, dtile, AF.Abs, bias=0.0, scale=1.0),
                   reads=[("f32a", m)], writes=[("f32c", m)])
            p.comp("scalar", lambda u=u: s.activation(u, u, AF.Exp, bias=0.0, scale=-1.0),
                   reads=[("f32c", m)], writes=[("f32c", m)])
            onec = constcell(c, 1.0)
            p.comp("scalar", lambda u=u, onec=onec: s.activation(u, u, AF.Ln, bias=onec, scale=1.0),
                   reads=[("f32c", m), ("cols", 1)], writes=[("f32c", m)])
            p.comp("vector", lambda dtile=dtile: v.tensor_single_scalar(dtile, dtile, 0.0, ALU.max),
                   reads=[("f32a", m)], writes=[("f32a", m)])
            p.comp("vector", lambda dtile=dtile, u=u: v.tensor_add(dtile, dtile, u),
                   reads=[("f32a", m), ("f32c", m)], writes=[("f32a", m)])
        for si in range(16):
            acol = c.cols[:, 32 + i * 16 + si: 33 + i * 16 + si]
            for hf in range(2):
                e = c.f32c[:, hf * 2048:(hf + 1) * 2048]
                src_ = c.f32a[:, hf * 2048:(hf + 1) * 2048]
                acc = c.f32b[:, hf * 2048:(hf + 1) * 2048]
                rk = [("f32a", mm) for mm in range(hf * 4, hf * 4 + 4)]
                ek = [("f32c", mm) for mm in range(hf * 4, hf * 4 + 4)]
                ak = [("f32b", mm) for mm in range(hf * 4, hf * 4 + 4)]
                p.comp("scalar", lambda e=e, src_=src_, acol=acol:
                       s.activation(e, src_, AF.Exp, bias=0.0, scale=acol),
                       reads=rk + [("cols", 0)], writes=ek)
                if si == 0:
                    p.comp("vector", lambda acc=acc, e=e: v.tensor_copy(acc, e),
                           reads=ek, writes=ak)
                else:
                    p.comp("vector", lambda acc=acc, e=e: v.tensor_add(acc, acc, e),
                           reads=ek + ak, writes=ak)
        for m in range(NTM):
            pb = 2 + (m % 2)
            for kt in range(NKT):
                lhs = c.wst[:, kt * 2048 + (8 + i) * 128: kt * 2048 + (8 + i) * 128 + 128]
                rhs = c.xnb[:, kt * T + m * 512: kt * T + (m + 1) * 512]
                p.comp("tensor", lambda lhs=lhs, rhs=rhs, pb=pb, kt=kt:
                       t.matmul(c.ps[pb], lhs, rhs, start=(kt == 0), stop=(kt == NKT - 1)),
                       reads=keys("wst", kt * 2048, (kt + 1) * 2048) +
                             keys("xnb", (kt * 8 + m) * 512, (kt * 8 + m) * 512 + 512),
                       writes=[(f"p{pb}",)])
            bg = c.f32c[:, m * 512:(m + 1) * 512]
            bxcol = c.cols[:, 160 + 8 + i:161 + 8 + i]
            p.comp("scalar", lambda bg=bg, pb=pb, bxcol=bxcol:
                   s.activation(bg, c.ps[pb], AF.Identity, bias=bxcol, scale=1.0),
                   reads=[(f"p{pb}",), ("cols", 0)], writes=[("f32c", m)])
            xnap = c.xn32[i * 128:(i + 1) * 128, m * 512:(m + 1) * 512]
            xt = c.f32a[:, m * 512:(m + 1) * 512]
            p.dma("scalar", lambda xt=xt, xnap=xnap: nc.scalar.dma_start(out=xt, in_=xnap),
                  reads=[("xn32", i * 8 + m)], writes=[("f32a", m)])
            p.comp("vector", lambda bg=bg, xt=xt: v.tensor_mul(bg, bg, xt),
                   reads=[("f32c", m), ("f32a", m)], writes=[("f32c", m)])
        hrow = c.f32c[:, 0:4096]
        p.comp("vector", lambda: v.tensor_tensor_scan(hrow, hrow, hrow, 0.0, ALU.add, ALU.bypass),
               reads=[("f32c", mm) for mm in range(8)], writes=[("f32c", mm) for mm in range(8)])
        p.comp("vector", lambda: v.tensor_mul(hrow, hrow, c.f32b[:, 0:4096]),
               reads=[("f32c", mm) for mm in range(8)] + [("f32b", mm) for mm in range(8)],
               writes=[("f32c", mm) for mm in range(8)])
        dcol = c.cols[:, 24 + i:25 + i]
        p.comp("vector", lambda dcol=dcol: v.scalar_tensor_tensor(hrow, c.f32a[:, 0:4096], dcol, hrow,
                                                                  ALU.mult, ALU.add),
               reads=[("f32a", mm) for mm in range(8)] + [("f32c", mm) for mm in range(8)] + [("cols", 0)],
               writes=[("f32c", mm) for mm in range(8)])
        for m in range(NTM):
            src = c.f32c[:, m * 512:(m + 1) * 512]
            eb = c.ework[:, m * 512:(m + 1) * 512]
            dap = c.ctxT[i * 128:(i + 1) * 128, m * 512:(m + 1) * 512]
            p.comp("scalar", lambda eb=eb, src=src: s.copy(eb, src),
                   reads=[("f32c", m)], writes=[("ework", m)])
            p.dma("sync", lambda dap=dap, eb=eb: nc.sync.dma_start(out=dap, in_=eb),
                  reads=[("ework", m)], writes=[("ctxT", i * 8 + m)])
    emit_proj_residual(c, c.iw["a0_woT"], D, bias_dram=c.iw["a0_bo"],
                       extra_dram=c.tT, extra_key="tT")


def emit_bias_cols16(c, bias_dram, ntiles, col0):
    nc, p = c.nc, c.p
    v, t = nc.vector, nc.tensor
    done = 0
    while done < ntiles:
        n = min(8, ntiles - done)
        bap = bias_dram.rearrange("(n p) -> n p", p=128)[done:done + n, :]
        tmp = c.f32b[0:n, 0:128]
        p.dma("sync", lambda bap=bap, tmp=tmp: nc.sync.dma_start(out=tmp, in_=bap), writes=[("f32b", 0)])
        p.comp("tensor", lambda tmp=tmp, n=n: t.transpose(c.ps[5][:, 0:n], tmp, c.idF[0:n, 0:n]),
               reads=[("f32b", 0), ("consts", 0)], writes=[("p5",)])
        p.comp("vector", lambda n=n, done=done: v.tensor_copy(c.cols[:, col0 + done:col0 + done + n],
                                                              c.ps[5][:, 0:n]),
               reads=[("p5",)], writes=[("cols", 0)])
        done += n


def mk(c0, c1):
    return keys("moeS", c0, c1, gran=32)


def emit_moe_wconv(c, layer):
    nc, p = c.nc, c.p
    w1 = c.iw[f"f{layer}_w1"]; w2 = c.iw[f"f{layer}_w2"]
    for (wsrc, wdst, rows, cols) in ((w1, c.w1b[layer], 8 * D, 4096),
                                     (w2, c.w2b[layer], (32768 if layer == 0 else 16384), D)):
        nrt = rows // 128
        for r in range(nrt):
            sbuf = c.ework[:, (r % 2) * 4096:(r % 2) * 4096 + cols]
            sap = wsrc[r * 128:(r + 1) * 128, :]
            dap = wdst[r * 128:(r + 1) * 128, :]
            ek = keys("ework", (r % 2) * 4096, (r % 2) * 4096 + cols)
            p.dma("gpsimd", lambda sbuf=sbuf, sap=sap: nc.gpsimd.dma_start(out=sbuf, in_=sap),
                  writes=ek)
            p.dma("sync", lambda dap=dap, sbuf=sbuf: nc.sync.dma_start(out=dap, in_=sbuf),
                  reads=ek, writes=[("wb", layer, r % 2)])


def emit_xtok(c):
    nc, p = c.nc, c.p
    v, t = nc.vector, nc.tensor
    for tt in range(T // 128):
        qb = tt % 2
        qk = (f"q{qb}",)
        stage = c.ework[:, (tt % 2) * 1024:(tt % 2) * 1024 + 1024]
        sk = keys("ework", (tt % 2) * 1024, (tt % 2) * 1024 + 1024)
        for kt in range(NKT):
            xs = c.xnb[:, kt * T + tt * 128: kt * T + tt * 128 + 128]
            xk = keys("xnb", kt * T + tt * 128, kt * T + tt * 128 + 128, gran=512)
            p.comp("tensor", lambda xs=xs, qb=qb, kt=kt: t.transpose(c.qs[qb][:, kt * 128:(kt + 1) * 128], xs, c.idB),
                   reads=xk + [("cbf", 0)], writes=[qk])
        p.comp("vector", lambda stage=stage, qb=qb: v.tensor_copy(stage, c.qs[qb][:, 0:1024]),
               reads=[qk], writes=sk)
        dap = c.xtok[tt * 128:(tt + 1) * 128, :]
        p.dma("sync", lambda dap=dap, stage=stage: nc.sync.dma_start(out=dap, in_=stage),
              reads=sk, writes=[("xtok", tt // 4)])


MOE_STOP = 99

def emit_moe(c, layer):
    nc, p = c.nc, c.p
    v, s, t, g = nc.vector, nc.scalar, nc.tensor, nc.gpsimd
    NMAC = 15 if layer == 0 else 23
    NSLOT = NMAC * 512
    F = 4096
    KF = 32 if layer == 0 else 16
    CAP = 5120 if layer == 0 else 10240
    for ly in (0, 1):
        if not c.moe_wconv_done[ly]:
            emit_moe_wconv(c, ly)
            c.moe_wconv_done[ly] = True
    emit_ln(c, c.X, "X", dst_sbuf=True)
    emit_xtok(c)
    rw = c.iw["f0_rwT" if layer == 0 else "f1_wgT"]
    rwsb = c.cbf[:, 256:320]
    rap = rw.rearrange("(kt p) e -> p kt e", p=128)
    p.dma("gpsimd", lambda: g.dma_start(out=rwsb, in_=rap), writes=[("cbf", 0)])

    for tt in range(32):
        pbank = 4 + tt % 2
        pk = (f"p{pbank}",)
        for kt in range(NKT):
            lhs = c.xnb[:, kt * T + tt * 128: kt * T + tt * 128 + 128]
            xk = keys("xnb", kt * T + tt * 128, kt * T + tt * 128 + 128, gran=512)
            rhs = rwsb[:, kt * 8:(kt + 1) * 8]
            p.comp("tensor", lambda lhs=lhs, rhs=rhs, kt=kt, pbank=pbank:
                   t.matmul(c.ps[pbank][:, 0:8], lhs, rhs, start=(kt == 0), stop=(kt == NKT - 1)),
                   reads=xk + [("cbf", 0)], writes=[pk])
        o = (tt % 2) * 152
        lg = c.moeS[:, 144 + o:152 + o]
        p.comp("vector", lambda lg=lg, pbank=pbank: v.tensor_copy(lg, c.ps[pbank][:, 0:8]),
               reads=[pk], writes=mk(144 + o, 152 + o))
        mw = c.moeS[:, 128 + o:136 + o]
        miu = c.idxs[:, 32 + (tt % 2) * 8:40 + (tt % 2) * 8]
        mik = ("idxs", 32 + (tt % 2) * 8)
        p.comp("vector", lambda lg=lg, mw=mw: v.max(mw, lg),
               reads=mk(144 + o, 152 + o), writes=mk(128 + o, 136 + o))
        p.comp("vector", lambda lg=lg, mw=mw, miu=miu: v.max_index(miu, mw, lg),
               reads=mk(144 + o, 152 + o) + mk(128 + o, 136 + o), writes=[mik])
        if layer == 0:
            nm1 = c.moeS[:, 159 + o:160 + o]
            p.comp("vector", lambda nm1=nm1, mw=mw: v.tensor_single_scalar(nm1, mw[:, 0:1], -1.0, ALU.mult),
                   reads=mk(128 + o, 136 + o), writes=mk(159 + o, 160 + o))
            ex = c.moeS[:, 136 + o:144 + o]; se = c.moeS[:, 240 + o:241 + o]
            p.comp("scalar", lambda ex=ex, lg=lg, nm1=nm1, se=se:
                   s.activation(ex, lg, AF.Exp, bias=nm1, scale=1.0, accum_out=se),
                   reads=mk(144 + o, 152 + o) + mk(159 + o, 160 + o), writes=mk(136 + o, 144 + o) + mk(240 + o, 241 + o))
            p1 = c.moeS[:, tt:tt + 1]
            p.comp("vector", lambda p1=p1, se=se: v.reciprocal(p1, se),
                   reads=mk(240 + o, 241 + o), writes=mk(tt, tt + 1))
            e1 = c.moeS[:, 32 + tt:33 + tt]
            p.comp("vector", lambda e1=e1, miu=miu: v.tensor_copy(e1, miu[:, 0:1]),
                   reads=[mik], writes=mk(32 + tt, 33 + tt))
            pap = c.cc_p_in[tt, :]
            eap = c.cc_e_in[tt, :]
            p.dma("sync", lambda pap=pap, p1=p1: nc.sync.dma_start(out=pap, in_=p1[:, 0]),
                  reads=mk(tt, tt + 1), writes=[("ccpin", tt // 8)])
            p.dma("sync", lambda eap=eap, e1=e1: nc.sync.dma_start(out=eap, in_=e1[:, 0]),
                  reads=mk(32 + tt, 33 + tt), writes=[("ccein", tt // 8)])
        else:
            dd = c.moeS[:, 154 + o:155 + o]; g1 = c.moeS[:, 155 + o:156 + o]; g2 = c.moeS[:, 156 + o:157 + o]
            p.comp("vector", lambda dd=dd, mw=mw: v.tensor_sub(dd, mw[:, 0:1], mw[:, 1:2]),
                   reads=mk(128 + o, 136 + o), writes=mk(154 + o, 155 + o))
            p.comp("scalar", lambda g1=g1, dd=dd: s.activation(g1, dd, AF.Sigmoid, bias=0.0, scale=1.0),
                   reads=mk(154 + o, 155 + o), writes=mk(155 + o, 156 + o))
            p.comp("scalar", lambda g2=g2, dd=dd: s.activation(g2, dd, AF.Sigmoid, bias=0.0, scale=-1.0),
                   reads=mk(154 + o, 155 + o), writes=mk(156 + o, 157 + o))
            pr = c.moeS[:, 240 + o:242 + o]; gr = c.moeS[:, 242 + o:244 + o]
            p.comp("vector", lambda pr=pr, miu=miu: v.tensor_copy(pr, miu[:, 0:2]),
                   reads=[mik], writes=mk(240 + o, 242 + o))
            p.comp("vector", lambda gr=gr, g1=g1: v.tensor_copy(gr[:, 0:1], g1), reads=mk(155 + o, 156 + o), writes=mk(242 + o, 243 + o))
            p.comp("vector", lambda gr=gr, g2=g2: v.tensor_copy(gr[:, 1:2], g2), reads=mk(156 + o, 157 + o), writes=mk(243 + o, 244 + o))
            eap = c.entE[tt * 256:(tt + 1) * 256]
            gap = c.entG[tt * 256:(tt + 1) * 256]
            p.dma("sync", lambda eap=eap, pr=pr: nc.sync.dma_start(out=eap, in_=pr),
                  reads=mk(240 + o, 242 + o), writes=[("entE", tt // 8)])
            p.dma("sync", lambda gap=gap, gr=gr: nc.sync.dma_start(out=gap, in_=gr),
                  reads=mk(242 + o, 244 + o), writes=[("entG", tt // 8)])

    if MOE_STOP < 2:
        return
    rkeys = [("rows", 0)]
    if layer == 0:
        p.coll(lambda: g.collective_compute("AllGather", ALU.bypass, [list(range(NCORES))],
                                            [c.cc_p_in[:, :].opt()], [c.cc_p_out[:, :].opt()]),
               reads=[("ccpin", i) for i in range(4)], writes=[("ccpout",)])
        p.coll(lambda: g.collective_compute("AllGather", ALU.bypass, [list(range(NCORES))],
                                            [c.cc_e_in[:, :].opt()], [c.cc_e_out[:, :].opt()]),
               reads=[("ccein", i) for i in range(4)], writes=[("cceout",)])
        PT = c.f32a[:, 0:256]; ET = c.f32a[:, 256:512]
        p.dma("sync", lambda: nc.sync.dma_start(out=PT, in_=c.cc_p_out.rearrange("(p a) j -> p (a j)", p=128)),
              reads=[("ccpout",)], writes=[("f32a", 0)])
        p.dma("sync", lambda: nc.sync.dma_start(out=ET, in_=c.cc_e_out.rearrange("(p a) j -> p (a j)", p=128)),
              reads=[("cceout",)], writes=[("f32a", 0)])
        for e in range(8):
            seg = c.f32b[:, e * 256:(e + 1) * 256]
            msk = c.f32c[:, 0:256]
            p.comp("vector", lambda msk=msk, e=e: v.tensor_single_scalar(msk, ET, float(e), ALU.is_equal),
                   reads=[("f32a", 0)], writes=[("f32c", 0)])
            p.comp("vector", lambda seg=seg: v.tensor_single_scalar(seg, PT, 1.0, ALU.add),
                   reads=[("f32a", 0)], writes=[("f32b", e // 2)])
            p.comp("vector", lambda seg=seg, msk=msk: v.tensor_mul(seg, seg, msk),
                   reads=[("f32b", e // 2), ("f32c", 0)], writes=[("f32b", e // 2)])
            p.comp("vector", lambda seg=seg: v.tensor_single_scalar(seg, seg, 1.0, ALU.subtract),
                   reads=[("f32b", e // 2)], writes=[("f32b", e // 2)])
        lo = c.rows[0:1, 0:8]; hi = c.rows[0:1, 8:16]; mid = c.rows[0:1, 16:24]
        cnt = c.rows[0:1, 24:32]; tmask = c.i32s[0:1, 8:16]
        p.comp("vector", lambda: v.memset(lo, 0.0), writes=rkeys)
        p.comp("vector", lambda: v.memset(hi, 1.0), writes=rkeys)
        thb = c.moeS[:, 224:232]
        cmpred = c.moeS[:, 248:256]
        for it in range(28):
            p.comp("vector", lambda: v.tensor_add(mid, lo, hi), reads=rkeys, writes=rkeys)
            p.comp("vector", lambda: v.tensor_single_scalar(mid, mid, 0.5, ALU.mult), reads=rkeys, writes=rkeys)
            p.comp("tensor", lambda: t.matmul(c.ps[4][:, 0:8], c.onesrow, mid, start=True, stop=True),
                   reads=rkeys + [("rows", 3)], writes=[("p4",)])
            p.comp("vector", lambda: v.tensor_copy(thb, c.ps[4][:, 0:8]),
                   reads=[("p4",)], writes=mk(224, 232))
            for e in range(8):
                seg = c.f32b[:, e * 256:(e + 1) * 256]
                cm = c.f32c[:, e * 256:(e + 1) * 256]
                p.comp("vector", lambda cm=cm, seg=seg, e=e:
                       v.tensor_scalar(cm, seg, thb[:, e:e + 1], None, ALU.is_gt),
                       reads=[("f32b", e // 2)] + mk(224, 232), writes=[("f32c", e // 2)])
                p.comp("vector", lambda cm=cm, e=e: v.reduce_sum(cmpred[:, e:e + 1], cm, AX.X),
                       reads=[("f32c", e // 2)], writes=mk(248, 256))
            p.comp("tensor", lambda: t.matmul(c.ps[4][0:1, 0:8], c.onescol, cmpred, start=True, stop=True),
                   reads=mk(248, 256) + [("consts", 0)], writes=[("p4",)])
            p.comp("vector", lambda: v.tensor_copy(cnt, c.ps[4][0:1, 0:8]),
                   reads=[("p4",)], writes=rkeys)
            p.comp("vector", lambda: v.tensor_single_scalar(tmask, cnt, float(CAP), ALU.is_gt),
                   reads=rkeys, writes=[("i32s",)])
            p.comp("vector", lambda: v.select(lo, tmask, mid, lo), reads=rkeys + [("i32s",)], writes=rkeys)
            p.comp("vector", lambda: v.select(hi, tmask, hi, mid), reads=rkeys + [("i32s",)], writes=rkeys)
        th8 = hi
        thbc = c.moeS[:, 224:232]
        p.comp("tensor", lambda: t.matmul(c.ps[4][:, 0:8], c.onesrow, th8, start=True, stop=True),
               reads=rkeys + [("rows", 3)], writes=[("p4",)])
        p.comp("vector", lambda: v.tensor_copy(thbc, c.ps[4][:, 0:8]),
               reads=[("p4",)], writes=mk(224, 232))
        carry = c.rows[0:1, 48:56]
        p.comp("vector", lambda: v.memset(carry, 0.0), writes=rkeys)
        for tt in range(32):
            o = (tt % 2) * 152
            p1 = c.moeS[:, tt:tt + 1]; e1 = c.moeS[:, 32 + tt:33 + tt]
            fin = c.moeS[:, 64 + tt:65 + tt]; rnk = c.moeS[:, 96 + tt:97 + tt]
            oh = c.moeS[:, 128 + o:136 + o]; th_t = c.moeS[:, 159 + o:160 + o]
            kp = c.moeS[:, 240 + o:241 + o]; sc8 = c.moeS[:, 136 + o:144 + o]
            p.comp("vector", lambda oh=oh, e1=e1: v.tensor_scalar(oh, c.io8, e1[:, 0:1], None, ALU.is_equal),
                   reads=[("consts", 0)] + mk(32 + tt, 33 + tt), writes=mk(128 + o, 136 + o))
            p.comp("vector", lambda oh=oh, sc8=sc8: v.tensor_mul(sc8, oh, thbc),
                   reads=mk(128 + o, 136 + o) + mk(224, 232), writes=mk(136 + o, 144 + o))
            p.comp("vector", lambda th_t=th_t, sc8=sc8: v.reduce_sum(th_t, sc8, AX.X),
                   reads=mk(136 + o, 144 + o), writes=mk(159 + o, 160 + o))
            p.comp("vector", lambda kp=kp, p1=p1, th_t=th_t: v.tensor_tensor(kp, p1, th_t, ALU.is_gt),
                   reads=mk(tt, tt + 1) + mk(159 + o, 160 + o), writes=mk(240 + o, 241 + o))
            p.comp("vector", lambda fin=fin, kp=kp, e1=e1: v.tensor_mul(fin, kp, e1),
                   reads=mk(240 + o, 241 + o) + mk(32 + tt, 33 + tt), writes=mk(64 + tt, 65 + tt))
            p.comp("vector", lambda oh=oh, fin=fin: v.tensor_scalar(oh, c.io8, fin[:, 0:1], None, ALU.is_equal),
                   reads=[("consts", 0)] + mk(64 + tt, 65 + tt), writes=mk(128 + o, 136 + o))
            pb5 = 5 if tt % 2 == 0 else 3
            p.comp("tensor", lambda oh=oh, pb5=pb5: t.matmul(c.ps[pb5][:, 0:8], c.Lst, oh, start=True, stop=False),
                   reads=mk(128 + o, 136 + o) + [("consts", 0)], writes=[(f"p{pb5}",)])
            p.comp("tensor", lambda pb5=pb5: t.matmul(c.ps[pb5][:, 0:8], c.onesrow, carry, start=False, stop=True),
                   reads=rkeys + [("rows", 3)], writes=[(f"p{pb5}",)])
            pref = c.moeS[:, 136 + o:144 + o]
            p.comp("vector", lambda pref=pref, pb5=pb5: v.tensor_copy(pref, c.ps[pb5][:, 0:8]),
                   reads=[(f"p{pb5}",)], writes=mk(136 + o, 144 + o))
            p.comp("vector", lambda pref=pref, oh=oh: v.tensor_mul(pref, pref, oh),
                   reads=mk(136 + o, 144 + o) + mk(128 + o, 136 + o), writes=mk(136 + o, 144 + o))
            p.comp("vector", lambda rnk=rnk, pref=pref: v.reduce_sum(rnk, pref, AX.X),
                   reads=mk(136 + o, 144 + o), writes=mk(96 + tt, 97 + tt))
            pb4 = 4 if tt % 2 == 0 else 2
            p.comp("tensor", lambda oh=oh, pb4=pb4: t.matmul(c.ps[pb4][0:1, 0:8], c.onescol, oh, start=True, stop=False),
                   reads=mk(128 + o, 136 + o) + [("consts", 0)], writes=[(f"p{pb4}",)])
            p.comp("tensor", lambda pb4=pb4: t.matmul(c.ps[pb4][0:1, 0:8], c.onesrow[0:1, 0:1], carry, start=False, stop=True),
                   reads=rkeys + [("rows", 3)], writes=[(f"p{pb4}",)])
            p.comp("vector", lambda pb4=pb4: v.tensor_copy(carry, c.ps[pb4][0:1, 0:8]),
                   reads=[(f"p{pb4}",)], writes=rkeys)
        cnt8 = carry
    else:
        carry = c.rows[0:1, 48:56]
        p.comp("vector", lambda: v.memset(carry, 0.0), writes=rkeys)
        for ch in range(64):
            o = (ch % 2) * 152
            ec = c.moeS[:, 244 + o:245 + o]
            eap = c.entE[ch * 128:(ch + 1) * 128]
            p.dma("sync", lambda ec=ec, eap=eap: nc.sync.dma_start(out=ec, in_=eap.rearrange("(p one) -> p one", one=1)),
                  reads=[("entE", ch // 16)], writes=mk(244 + o, 245 + o))
            oh = c.moeS[:, 128 + o:136 + o]
            p.comp("vector", lambda oh=oh, ec=ec: v.tensor_scalar(oh, c.io8, ec[:, 0:1], None, ALU.is_equal),
                   reads=[("consts", 0)] + mk(244 + o, 245 + o), writes=mk(128 + o, 136 + o))
            pb5 = 5 if ch % 2 == 0 else 3
            p.comp("tensor", lambda oh=oh, pb5=pb5: t.matmul(c.ps[pb5][:, 0:8], c.Lst, oh, start=True, stop=False),
                   reads=mk(128 + o, 136 + o) + [("consts", 0)], writes=[(f"p{pb5}",)])
            p.comp("tensor", lambda pb5=pb5: t.matmul(c.ps[pb5][:, 0:8], c.onesrow, carry, start=False, stop=True),
                   reads=rkeys + [("rows", 3)], writes=[(f"p{pb5}",)])
            pref = c.moeS[:, 136 + o:144 + o]
            p.comp("vector", lambda pref=pref, pb5=pb5: v.tensor_copy(pref, c.ps[pb5][:, 0:8]),
                   reads=[(f"p{pb5}",)], writes=mk(136 + o, 144 + o))
            p.comp("vector", lambda pref=pref, oh=oh: v.tensor_mul(pref, pref, oh),
                   reads=mk(136 + o, 144 + o) + mk(128 + o, 136 + o), writes=mk(136 + o, 144 + o))
            rnk = c.moeS[:, 245 + o:246 + o]
            p.comp("vector", lambda rnk=rnk, pref=pref: v.reduce_sum(rnk, pref, AX.X),
                   reads=mk(136 + o, 144 + o), writes=mk(245 + o, 246 + o))
            rap_ = c.entS[ch * 128:(ch + 1) * 128]
            p.dma("sync", lambda rap_=rap_, rnk=rnk: nc.sync.dma_start(out=rap_, in_=rnk[:, 0]),
                  reads=mk(245 + o, 246 + o), writes=[("entS", ch // 16)])
            pb4 = 4 if ch % 2 == 0 else 2
            p.comp("tensor", lambda oh=oh, pb4=pb4: t.matmul(c.ps[pb4][0:1, 0:8], c.onescol, oh, start=True, stop=False),
                   reads=mk(128 + o, 136 + o) + [("consts", 0)], writes=[(f"p{pb4}",)])
            p.comp("tensor", lambda pb4=pb4: t.matmul(c.ps[pb4][0:1, 0:8], c.onesrow[0:1, 0:1], carry, start=False, stop=True),
                   reads=rkeys + [("rows", 3)], writes=[(f"p{pb4}",)])
            p.comp("vector", lambda pb4=pb4: v.tensor_copy(carry, c.ps[pb4][0:1, 0:8]),
                   reads=[(f"p{pb4}",)], writes=rkeys)
        cnt8 = carry
        p.dma("sync", lambda: nc.sync.dma_start(out=c.cc_cnt_in[0:1, :], in_=cnt8),
              reads=rkeys, writes=[("cntin",)])
        p.coll(lambda: g.collective_compute("AllGather", ALU.bypass, [list(range(NCORES))],
                                            [c.cc_cnt_in[:, :].opt()], [c.cc_cnt_out[:, :].opt()]),
               reads=[("cntin",)], writes=[("cntout",)])
        cg = c.moeS[0:8, 256:264]
        pmk = c.moeS[0:8, 264:265]
        p.dma("sync", lambda: nc.sync.dma_start(out=cg, in_=c.cc_cnt_out[:, :]),
              reads=[("cntout",)], writes=mk(256, 264))
        p.dma("sync", lambda: nc.sync.dma_start(out=pmk, in_=c.iw["prevmask"].rearrange("(p one) -> p one", one=1)),
              writes=mk(264, 265))
        p.comp("tensor", lambda: t.matmul(c.ps[4][0:8, 0:1], cg, pmk, start=True, stop=True),
               reads=mk(256, 265), writes=[("p4",)])
        offc = c.moeS[0:8, 265:266]
        p.comp("vector", lambda: v.tensor_copy(offc, c.ps[4][0:8, 0:1]),
               reads=[("p4",)], writes=mk(265, 266))
        p.comp("tensor", lambda: t.transpose(c.ps[4][0:1, 0:8], offc, c.idF[0:8, 0:8]),
               reads=mk(265, 266) + [("consts", 0)], writes=[("p4",)])
        off8 = c.rows[0:1, 56:64]
        p.comp("vector", lambda: v.tensor_copy(off8, c.ps[4][0:1, 0:8]),
               reads=[("p4",)], writes=rkeys)

    nmac = c.rows[0:1, 64:72]; cin = c.rows[0:1, 72:80]; cex = c.rows[0:1, 80:88]
    base8 = c.rows[0:1, 88:96]; ztmp = c.rows[0:1, 96:104]
    p.comp("vector", lambda: v.tensor_scalar(nmac, cnt8, 1.0 / 512, -(1.0 / 512 + 0.499), ALU.mult, ALU.add),
           reads=rkeys, writes=rkeys)
    p.comp("vector", lambda: v.tensor_copy(c.i32s[0:1, 0:8], nmac), reads=rkeys, writes=[("i32s",)])
    p.comp("vector", lambda: v.tensor_copy(nmac, c.i32s[0:1, 0:8]), reads=[("i32s",)], writes=rkeys)
    p.comp("vector", lambda: v.tensor_single_scalar(nmac, nmac, 1.0, ALU.add), reads=rkeys, writes=rkeys)
    p.comp("vector", lambda: v.memset(ztmp, 0.0), writes=rkeys)
    p.comp("vector", lambda: v.tensor_tensor_scan(cin, nmac, ztmp, 0.0, ALU.add, ALU.add),
           reads=rkeys, writes=rkeys)
    p.comp("vector", lambda: v.tensor_sub(cex, cin, nmac), reads=rkeys, writes=rkeys)
    p.comp("vector", lambda: v.tensor_single_scalar(base8, cex, 512.0, ALU.mult), reads=rkeys, writes=rkeys)
    cinC = c.moeS[0:8, 266:267]
    p.comp("tensor", lambda: t.transpose(c.ps[4][0:8, 0:1], cin, c.idF[0:1, 0:1]),
           reads=rkeys + [("consts", 0)], writes=[("p4",)])
    p.comp("vector", lambda: v.tensor_copy(cinC, c.ps[4][0:8, 0:1]),
           reads=[("p4",)], writes=mk(266, 267))
    iom = c.moeS[0:8, 267:267 + NMAC]
    p.comp("gpsimd", lambda: g.iota(iom, [[1, NMAC]], base=0, channel_multiplier=0,
                                    allow_small_or_imprecise_dtypes=True), writes=mk(267, 267 + NMAC))
    cmp_ = c.moeS[0:8, 290:290 + NMAC]
    p.comp("vector", lambda: v.tensor_scalar(cmp_, iom, cinC[0:8, 0:1], None, ALU.is_ge),
           reads=mk(267, 267 + NMAC) + mk(266, 267), writes=mk(290, 290 + NMAC))
    p.comp("tensor", lambda: t.matmul(c.ps[4][0:1, 0:NMAC], c.onescol[0:8, 0:1], cmp_, start=True, stop=True),
           reads=mk(290, 290 + NMAC) + [("consts", 0)], writes=[("p4",)])
    eidr = c.rows[0:1, 104:104 + NMAC]
    p.comp("vector", lambda: v.tensor_copy(eidr, c.ps[4][0:1, 0:NMAC]), reads=[("p4",)], writes=rkeys)
    p.comp("vector", lambda: v.tensor_single_scalar(eidr, eidr, 7.0, ALU.min), reads=rkeys, writes=rkeys)
    eidsb = c.moeS[:, 160:160 + NMAC]
    p.comp("tensor", lambda: t.matmul(c.ps[4][:, 0:NMAC], c.onesrow, eidr, start=True, stop=True),
           reads=rkeys + [("rows", 3)], writes=[("p4",)])
    p.comp("vector", lambda: v.tensor_copy(eidsb, c.ps[4][:, 0:NMAC]),
           reads=[("p4",)], writes=mk(160, 160 + NMAC))
    baseb = c.moeS[:, 224:232]
    p.comp("tensor", lambda: t.matmul(c.ps[4][:, 0:8], c.onesrow, base8, start=True, stop=True),
           reads=rkeys + [("rows", 3)], writes=[("p4",)])
    p.comp("vector", lambda: v.tensor_copy(baseb, c.ps[4][:, 0:8]),
           reads=[("p4",)], writes=mk(224, 232))
    if layer == 1:
        offb = c.moeS[:, 232:240]
        p.comp("tensor", lambda: t.matmul(c.ps[4][:, 0:8], c.onesrow, off8, start=True, stop=True),
               reads=rkeys + [("rows", 3)], writes=[("p4",)])
        p.comp("vector", lambda: v.tensor_copy(offb, c.ps[4][:, 0:8]),
               reads=[("p4",)], writes=mk(232, 240))

    if MOE_STOP < 3:
        return
    p.comp("vector", lambda: v.memset(c.idxs[:, 0:46], T), writes=[("idxs",)])
    p.dma("sync", lambda: nc.sync.dma_start(out=c.table[0:5888], in_=c.idxs[:, 0:46]),
          reads=[("idxs",)], writes=[("table", 0)])
    n2 = (NSLOT - 5888) // 128
    p.dma("sync", lambda n2=n2: nc.sync.dma_start(out=c.table[5888:NSLOT], in_=c.idxs[:, 0:n2]),
          reads=[("idxs",)], writes=[("table", 1)])
    if layer == 0:
        for tt in range(32):
            fin = c.moeS[:, 64 + tt:65 + tt]; rnk = c.moeS[:, 96 + tt:97 + tt]
            oh = c.moeS[:, 128:136]
            bt = c.moeS[:, 136:144]; slot = c.moeS[:, 244:245]
            p.comp("vector", lambda oh=oh, fin=fin: v.tensor_scalar(oh, c.io8, fin[:, 0:1], None, ALU.is_equal),
                   reads=[("consts", 0)] + mk(64 + tt, 65 + tt), writes=mk(128, 136))
            p.comp("vector", lambda bt=bt, oh=oh: v.tensor_mul(bt, oh, baseb),
                   reads=mk(128, 136) + mk(224, 232), writes=mk(136, 144))
            p.comp("vector", lambda slot=slot, bt=bt: v.reduce_sum(slot, bt, AX.X),
                   reads=mk(136, 144), writes=mk(244, 245))
            p.comp("vector", lambda slot=slot, rnk=rnk: v.tensor_add(slot, slot, rnk),
                   reads=mk(244, 245) + mk(96 + tt, 97 + tt), writes=mk(244, 245))
            sap = c.entS[tt * 128:(tt + 1) * 128]
            p.dma("sync", lambda sap=sap, slot=slot: nc.sync.dma_start(out=sap, in_=slot[:, 0]),
                  reads=mk(244, 245), writes=[("entS", tt // 8)])
            sidx = c.idxs[:, 48:49]; tok = c.idxs[:, 49:50]
            p.comp("vector", lambda sidx=sidx, slot=slot: v.tensor_copy(sidx, slot),
                   reads=mk(244, 245), writes=[("idxs",)])
            p.comp("vector", lambda tok=tok, tt=tt: v.tensor_scalar(tok, c.iotaP, 1.0, float(tt * 128), ALU.mult, ALU.add),
                   reads=[("consts", 0)], writes=[("idxs",)])
            p.dma("gpsimd", lambda sidx=sidx, tok=tok:
                  g.indirect_dma_start(c.table[0:NSLOT].rearrange("(n one) -> n one", one=1),
                                       bass.IndirectOffsetOnAxis(ap=sidx, axis=0),
                                       tok, None, oob_is_err=False),
                  reads=[("idxs",)], writes=[("table", 0), ("table", 1)])
    else:
        for ch in range(64):
            o = (ch % 2) * 152
            ec = c.moeS[:, 244 + o:245 + o]; rnk = c.moeS[:, 245 + o:246 + o]; gt = c.moeS[:, 246 + o:247 + o]
            eap = c.entE[ch * 128:(ch + 1) * 128]
            gap = c.entG[ch * 128:(ch + 1) * 128]
            rap_ = c.entS[ch * 128:(ch + 1) * 128]
            p.dma("sync", lambda ec=ec, eap=eap: nc.sync.dma_start(out=ec, in_=eap.rearrange("(p one) -> p one", one=1)),
                  reads=[("entE", ch // 16)], writes=mk(244, 245))
            p.dma("sync", lambda gt=gt, gap=gap: nc.sync.dma_start(out=gt, in_=gap.rearrange("(p one) -> p one", one=1)),
                  reads=[("entG", ch // 16)], writes=mk(246, 247))
            p.dma("sync", lambda rnk=rnk, rap_=rap_: nc.sync.dma_start(out=rnk, in_=rap_.rearrange("(p one) -> p one", one=1)),
                  reads=[("entS", ch // 16)], writes=mk(245, 246))
            oh = c.moeS[:, 128 + o:136 + o]; tmp8 = c.moeS[:, 136 + o:144 + o]
            ot = c.moeS[:, 247 + o:248 + o]; bt = c.moeS[:, 248 + o:249 + o]
            kp = c.moeS[:, 249 + o:250 + o]; w = c.moeS[:, 250 + o:251 + o]; slot = c.moeS[:, 251 + o:252 + o]
            p.comp("vector", lambda oh=oh, ec=ec: v.tensor_scalar(oh, c.io8, ec[:, 0:1], None, ALU.is_equal),
                   reads=[("consts", 0)] + mk(244 + o, 245 + o), writes=mk(128 + o, 136 + o))
            p.comp("vector", lambda tmp8=tmp8, oh=oh: v.tensor_mul(tmp8, oh, c.moeS[:, 232:240]),
                   reads=mk(128 + o, 136 + o) + mk(232, 240), writes=mk(136 + o, 144 + o))
            p.comp("vector", lambda ot=ot, tmp8=tmp8: v.reduce_sum(ot, tmp8, AX.X),
                   reads=mk(136 + o, 144 + o), writes=mk(247 + o, 248 + o))
            p.comp("vector", lambda tmp8=tmp8, oh=oh: v.tensor_mul(tmp8, oh, baseb),
                   reads=mk(128 + o, 136 + o) + mk(224, 232), writes=mk(136 + o, 144 + o))
            p.comp("vector", lambda bt=bt, tmp8=tmp8: v.reduce_sum(bt, tmp8, AX.X),
                   reads=mk(136 + o, 144 + o), writes=mk(248 + o, 249 + o))
            p.comp("vector", lambda ot=ot, rnk=rnk: v.tensor_add(ot, ot, rnk),
                   reads=mk(247 + o, 248 + o) + mk(245 + o, 246 + o), writes=mk(247 + o, 248 + o))
            p.comp("vector", lambda kp=kp, ot=ot: v.tensor_single_scalar(kp, ot, float(CAP), ALU.is_lt),
                   reads=mk(247 + o, 248 + o), writes=mk(249 + o, 250 + o))
            p.comp("vector", lambda w=w, kp=kp, gt=gt: v.tensor_mul(w, kp, gt),
                   reads=mk(249 + o, 250 + o) + mk(246 + o, 247 + o), writes=mk(250 + o, 251 + o))
            kap = c.entK[ch * 128:(ch + 1) * 128]
            p.dma("sync", lambda kap=kap, w=w: nc.sync.dma_start(out=kap, in_=w[:, 0]),
                  reads=mk(250 + o, 251 + o), writes=[("entK", ch // 16)])
            p.comp("vector", lambda slot=slot, bt=bt, rnk=rnk: v.tensor_add(slot, bt, rnk),
                   reads=mk(248 + o, 249 + o) + mk(245 + o, 246 + o), writes=mk(251 + o, 252 + o))
            p.dma("sync", lambda rap_=rap_, slot=slot: nc.sync.dma_start(out=rap_, in_=slot[:, 0]),
                  reads=mk(251 + o, 252 + o), writes=[("entS", ch // 16)])
            sidx = c.idxs[:, 48 + ch % 2:49 + ch % 2]; tok = c.idxs[:, 53 + ch % 2:54 + ch % 2]
            p.comp("vector", lambda sidx=sidx, slot=slot: v.tensor_copy(sidx, slot),
                   reads=mk(251 + o, 252 + o), writes=[("idxs", 48 + ch % 2)])
            p.comp("vector", lambda tok=tok, ch=ch: v.tensor_scalar(tok, c.iotaP, 0.5, float(ch * 64) - 0.25, ALU.mult, ALU.add),
                   reads=[("consts", 0)], writes=[("idxs", 53 + ch % 2)])
            p.dma("gpsimd", lambda sidx=sidx, tok=tok:
                  g.indirect_dma_start(c.table[0:NSLOT].rearrange("(n one) -> n one", one=1),
                                       bass.IndirectOffsetOnAxis(ap=sidx, axis=0),
                                       tok, None, oob_is_err=False),
                  reads=[("idxs", 48 + ch % 2), ("idxs", 53 + ch % 2)], writes=[("table", 0), ("table", 1)])

    if MOE_STOP < 4:
        return
    emit_moe_experts(c, layer, NMAC, KF)
    if MOE_STOP < 5:
        return
    emit_moe_gatherback(c, layer)


def emit_moe_experts(c, layer, NMAC, KF):
    nc, p = c.nc, c.p
    v, s, t, g = nc.vector, nc.scalar, nc.tensor, nc.gpsimd
    w1b = c.w1b[layer]; w2b = c.w2b[layer]
    w1v = w1b
    w2rows = 4096 if layer == 0 else 2048
    w2v = w2b
    NSLOT = NMAC * 512
    b1v = c.iw["f0_b1"].rearrange("e (ft p) -> (e ft) p", p=128) if layer == 0 else None
    xk = lambda col0, n: keys("xnb", col0, col0 + n, gran=512)
    qk = lambda col0, n: keys("qkv", col0, col0 + n, gran=512)
    for m in range(NMAC):
        eidcol = c.moeS[:, 160 + m:161 + m]
        ek = mk(160 + m, 161 + m)
        tw1 = c.moeS[:, 252:253]; tw2 = c.moeS[:, 253:254]
        p.comp("vector", lambda tw1=tw1, eidcol=eidcol:
               v.scalar_tensor_tensor(tw1, eidcol, 1024.0, c.iotaP, ALU.mult, ALU.add),
               reads=ek + [("consts", 0)], writes=mk(252, 253))
        p.comp("vector", lambda tw2=tw2, eidcol=eidcol:
               v.scalar_tensor_tensor(tw2, eidcol, float(w2rows), c.iotaP, ALU.mult, ALU.add),
               reads=ek + [("consts", 0)], writes=mk(253, 254))
        for k in range(8):
            ic = c.idxs[:, k:k + 1]
            p.comp("vector", lambda ic=ic, tw1=tw1, k=k: v.tensor_single_scalar(ic, tw1, float(128 * k), ALU.add),
                   reads=mk(252, 253), writes=[("idxs", k)])
        for k in range(KF):
            ic = c.idxs[:, 8 + k:9 + k]
            p.comp("vector", lambda ic=ic, tw2=tw2, k=k: v.tensor_single_scalar(ic, tw2, float(128 * k), ALU.add),
                   reads=mk(253, 254), writes=[("idxs", 8 + k)])
        if layer == 0:
            bi = c.idxs[0:32, 52:53]
            p.comp("vector", lambda bi=bi, eidcol=eidcol:
                   v.scalar_tensor_tensor(bi, eidcol[0:32, 0:1], 32.0, c.iotaP[0:32, 0:1], ALU.mult, ALU.add),
                   reads=ek + [("consts", 0)], writes=[("idxs", 52)])
            bg = c.f32c[0:32, 0:128]
            p.dma("gpsimd", lambda bg=bg, bi=bi:
                  g.indirect_dma_start(bg, None, b1v, bass.IndirectOffsetOnAxis(ap=bi, axis=0), oob_is_err=False),
                  reads=[("idxs", 52)], writes=[("f32c", 0)])
            p.comp("tensor", lambda bg=bg: t.transpose(c.ps[4][:, 0:32], bg, c.idF[0:32, 0:32]),
                   reads=[("f32c", 0), ("consts", 0)], writes=[("p4",)])
            p.comp("vector", lambda: v.tensor_copy(c.moeS[:, 192:224], c.ps[4][:, 0:32]),
                   reads=[("p4",)], writes=mk(192, 224))
        mp = m % 2
        bbuf = c.qkv if mp == 0 else c.ework
        bbk = (lambda c0, n: qk(c0, n)) if mp == 0 else (lambda c0, n: keys("ework", c0, c0 + n))
        for st in range(4):
            q = m * 4 + st
            ti = c.idxs[:, 50 if st % 2 == 0 else 51:51 if st % 2 == 0 else 52]
            tap_ = c.table[q * 128:(q + 1) * 128]
            p.dma("sync", lambda ti=ti, tap_=tap_: nc.sync.dma_start(out=ti, in_=tap_.rearrange("(p one) -> p one", one=1)),
                  reads=[("table", 0), ("table", 1)], writes=[("idxs", 50 + st % 2)])
            grow = c.att[:, (st % 2) * 1024:(st % 2) * 1024 + 1024]
            gwk = keys("att", (st % 2) * 1024, (st % 2) * 1024 + 1024)
            p.dma("gpsimd", lambda grow=grow, ti=ti:
                  g.indirect_dma_start(grow, None, c.xtok[:, :], bass.IndirectOffsetOnAxis(ap=ti, axis=0), oob_is_err=False),
                  reads=[("idxs", 50 + st % 2), ("xtok", 0), ("xtok", 1), ("xtok", 2), ("xtok", 3),
                         ("xtok", 4), ("xtok", 5), ("xtok", 6), ("xtok", 7), ("xtokz",)],
                  writes=gwk)
            qb = st % 2
            for kt in range(NKT):
                p.comp("tensor", lambda grow=grow, kt=kt, qb=qb:
                       t.transpose(c.qs[qb][:, kt * 128:(kt + 1) * 128], grow[:, kt * 128:(kt + 1) * 128], c.idB),
                       reads=gwk + [("cbf", 0)], writes=[(f"q{qb}",)])
            for kt in range(NKT):
                dst = bbuf[:, kt * 512 + st * 128: kt * 512 + st * 128 + 128]
                p.comp("vector", lambda dst=dst, kt=kt, qb=qb: v.tensor_copy(dst, c.qs[qb][:, kt * 128:(kt + 1) * 128]),
                       reads=[(f"q{qb}",)], writes=bbk(kt * 512, 512))
        for fh in range(4):
            a0 = 16384 + (fh % 2) * 8192
            for k in range(8):
                wdst = c.xnb[:, a0 + k * 1024: a0 + (k + 1) * 1024]
                ic = c.idxs[:, k:k + 1]
                p.dma("gpsimd", lambda wdst=wdst, ic=ic, fh=fh:
                      g.indirect_dma_start(wdst, None, w1v, bass.IndirectOffsetOnAxis(ap=ic, axis=0),
                                           element_offset=fh * 1024, oob_is_err=False),
                      reads=[("idxs", k), ("wb", layer, 0), ("wb", layer, 1)], writes=xk(a0 + k * 1024, 1024))
            for f in range(8):
                fi = fh * 8 + f
                pb = f % 2
                for k in range(8):
                    lhs = c.xnb[:, a0 + k * 1024 + f * 128: a0 + k * 1024 + f * 128 + 128]
                    rhs = bbuf[:, k * 512:(k + 1) * 512]
                    p.comp("tensor", lambda lhs=lhs, rhs=rhs, pb=pb, k=k:
                           t.matmul(c.ps[pb], lhs, rhs, start=(k == 0), stop=(k == 7)),
                           reads=xk(a0 + k * 1024, 1024) + bbk(k * 512, 512), writes=[(f"p{pb}",)])
                h1 = c.xnb[:, fi * 512:(fi + 1) * 512]
                if layer == 0:
                    bcol = c.moeS[:, 192 + fi:193 + fi]
                    p.comp("scalar", lambda h1=h1, pb=pb, bcol=bcol:
                           s.activation(h1, c.ps[pb], AF.Relu, bias=bcol, scale=1.0),
                           reads=[(f"p{pb}",)] + mk(192, 224), writes=xk(fi * 512, 512))
                elif fh < 2:
                    p.comp("scalar", lambda h1=h1, pb=pb: s.activation(h1, c.ps[pb], AF.Gelu, bias=0.0, scale=1.0),
                           reads=[(f"p{pb}",)], writes=xk(fi * 512, 512))
                else:
                    gf = c.xnb[:, (fi - 16) * 512:(fi - 16 + 1) * 512]
                    p.comp("vector", lambda gf=gf, pb=pb:
                           v.scalar_tensor_tensor(gf, gf, 1.0, c.ps[pb], ALU.mult, ALU.mult),
                           reads=[(f"p{pb}",)] + xk((fi - 16) * 512, 512), writes=xk((fi - 16) * 512, 512))
        for dm in range(2):
            for k in range(KF):
                wslot = 4096 + ((k % 4) + 0) * 512 if False else 4096 + (k % 4) * 512
                wslot = 4096 + (k % 4) * 512 if (k % 4) < 4 else wslot
                w2sb = c.qkv[:, wslot:wslot + 512]
                ic = c.idxs[:, 8 + k:9 + k]
                p.dma("gpsimd", lambda w2sb=w2sb, ic=ic, dm=dm:
                      g.indirect_dma_start(w2sb, None, w2v, bass.IndirectOffsetOnAxis(ap=ic, axis=0),
                                           element_offset=dm * 512, oob_is_err=False),
                      reads=[("idxs", 8 + k), ("wb", layer, 0), ("wb", layer, 1)], writes=qk(wslot, 512))
                for st in range(4):
                    lhs = c.xnb[:, k * 512 + st * 128: k * 512 + st * 128 + 128]
                    p.comp("tensor", lambda lhs=lhs, w2sb=w2sb, st=st, k=k, KF=KF:
                           t.matmul(c.ps[st + 2], lhs, w2sb, start=(k == 0), stop=(k == KF - 1)),
                           reads=xk(k * 512, 512) + qk(wslot, 512), writes=[(f"p{st + 2}",)])
            for st in range(4):
                yt = c.f32b[:, st * 1024 + dm * 512: st * 1024 + dm * 512 + 512]
                p.comp("scalar", lambda yt=yt, st=st: s.copy(yt, c.ps[st + 2]),
                       reads=[(f"p{st + 2}",)], writes=[("f32b", (st * 1024 + dm * 512) // 512)])
        for st in range(4):
            q = m * 4 + st
            ap_ = c.expout[q * 128:(q + 1) * 128, :]
            ysb = c.f32b[:, st * 1024:(st + 1) * 1024]
            p.dma("scalar", lambda ap_=ap_, ysb=ysb: nc.scalar.dma_start(out=ap_, in_=ysb),
                  reads=[("f32b", st * 2), ("f32b", st * 2 + 1)], writes=[("expout", q // 8)])


def emit_moe_gatherback(c, layer):
    nc, p = c.nc, c.p
    v, s, t, g = nc.vector, nc.scalar, nc.tensor, nc.gpsimd
    ek = [("expout", i) for i in range(12)]
    for tm in range(NTM):
        for j in range(4):
            tt = tm * 4 + j
            yt = c.f32b[:, j * 1024:(j + 1) * 1024]
            ytk = [("f32b", j * 2), ("f32b", j * 2 + 1)]
            if layer == 0:
                si = c.idxs[:, 53:54]; fi_ = c.idxs[:, 54:55]
                sl = c.moeS[:, 244:245]
                sap = c.entS[tt * 128:(tt + 1) * 128]
                p.dma("sync", lambda sl=sl, sap=sap: nc.sync.dma_start(out=sl, in_=sap.rearrange("(p one) -> p one", one=1)),
                      reads=[("entS", tt // 8)], writes=mk(244, 245))
                p.comp("vector", lambda si=si, sl=sl: v.tensor_copy(si, sl),
                       reads=mk(244, 245), writes=[("idxs",)])
                fin = c.moeS[:, 64 + tt:65 + tt]
                p.comp("vector", lambda fi_=fi_, fin=fin: v.tensor_copy(fi_, fin),
                       reads=mk(64 + tt, 65 + tt), writes=[("idxs",)])
                p.dma("gpsimd", lambda yt=yt, si=si:
                      g.indirect_dma_start(yt, None, c.expout[:, :], bass.IndirectOffsetOnAxis(ap=si, axis=0), oob_is_err=False),
                      reads=[("idxs",)] + ek, writes=ytk)
                b2g = c.f32c[:, j * 1024:(j + 1) * 1024]
                b2k = [("f32c", j * 2), ("f32c", j * 2 + 1)]
                p.dma("gpsimd", lambda b2g=b2g, fi_=fi_:
                      g.indirect_dma_start(b2g, None, c.iw["f0_b2"][:, :], bass.IndirectOffsetOnAxis(ap=fi_, axis=0), oob_is_err=False),
                      reads=[("idxs",)], writes=b2k)
                p.comp("vector", lambda yt=yt, b2g=b2g: v.tensor_add(yt, yt, b2g),
                       reads=ytk + b2k, writes=ytk)
            else:
                si0 = c.idxs[:, 53:54]; si1 = c.idxs[:, 54:55]
                s0 = c.moeS[:, 244:245]; s1 = c.moeS[:, 245:246]
                w0 = c.moeS[:, 246:247]; w1_ = c.moeS[:, 247:248]
                base = tt * 256
                sap0 = bass.AP(c.entS.tensor, base, [[2, 128], [1, 1]])
                sap1 = bass.AP(c.entS.tensor, base + 1, [[2, 128], [1, 1]])
                kap0 = bass.AP(c.entK.tensor, base, [[2, 128], [1, 1]])
                kap1 = bass.AP(c.entK.tensor, base + 1, [[2, 128], [1, 1]])
                p.dma("sync", lambda s0=s0, sap0=sap0: nc.sync.dma_start(out=s0, in_=sap0, allow_slow_non_contiguous=True),
                      reads=[("entS", tt // 8)], writes=mk(244, 245))
                p.dma("sync", lambda s1=s1, sap1=sap1: nc.sync.dma_start(out=s1, in_=sap1, allow_slow_non_contiguous=True),
                      reads=[("entS", tt // 8)], writes=mk(245, 246))
                p.dma("sync", lambda w0=w0, kap0=kap0: nc.sync.dma_start(out=w0, in_=kap0, allow_slow_non_contiguous=True),
                      reads=[("entK", tt // 8)], writes=mk(246, 247))
                p.dma("sync", lambda w1_=w1_, kap1=kap1: nc.sync.dma_start(out=w1_, in_=kap1, allow_slow_non_contiguous=True),
                      reads=[("entK", tt // 8)], writes=mk(247, 248))
                p.comp("vector", lambda si0=si0, s0=s0: v.tensor_copy(si0, s0), reads=mk(244, 245), writes=[("idxs",)])
                p.comp("vector", lambda si1=si1, s1=s1: v.tensor_copy(si1, s1), reads=mk(245, 246), writes=[("idxs",)])
                g0 = c.f32c[:, j * 1024:(j + 1) * 1024]
                g0k = [("f32c", j * 2), ("f32c", j * 2 + 1)]
                p.dma("gpsimd", lambda g0=g0, si0=si0:
                      g.indirect_dma_start(g0, None, c.expout[:, :], bass.IndirectOffsetOnAxis(ap=si0, axis=0), oob_is_err=False),
                      reads=[("idxs",)] + ek, writes=g0k)
                g1 = c.f32a[:, (j % 2) * 1024:(j % 2) * 1024 + 1024]
                g1k = [("f32a", (j % 2) * 2), ("f32a", (j % 2) * 2 + 1)]
                p.dma("gpsimd", lambda g1=g1, si1=si1:
                      g.indirect_dma_start(g1, None, c.expout[:, :], bass.IndirectOffsetOnAxis(ap=si1, axis=0), oob_is_err=False),
                      reads=[("idxs",)] + ek, writes=g1k)
                p.comp("vector", lambda yt=yt, g0=g0, w0=w0: v.tensor_scalar(yt, g0, w0[:, 0:1], None, ALU.mult),
                       reads=g0k + mk(246, 247), writes=ytk)
                p.comp("vector", lambda yt=yt, g1=g1, w1_=w1_:
                       v.scalar_tensor_tensor(yt, g1, w1_[:, 0:1], yt, ALU.mult, ALU.add),
                       reads=g1k + mk(247, 248) + ytk, writes=ytk)
        for kt in range(NKT):
            pb = kt % 2
            for j in range(4):
                yt = c.f32b[:, j * 1024:(j + 1) * 1024]
                ytk = [("f32b", j * 2), ("f32b", j * 2 + 1)]
                p.comp("tensor", lambda yt=yt, kt=kt, j=j, pb=pb:
                       t.transpose(c.ps[pb][:, j * 128:(j + 1) * 128], yt[:, kt * 128:(kt + 1) * 128], c.idF),
                       reads=ytk + [("consts", 0)], writes=[(f"p{pb}",)])
            xt_ = c.f32a[:, 2048 + (kt % 2) * 512: 2048 + (kt % 2) * 512 + 512]
            xtk = [("f32a", 4 + (kt % 2))]
            Xap = c.X[kt * 128:(kt + 1) * 128, tm * 512:(tm + 1) * 512]
            p.dma("sync", lambda xt_=xt_, Xap=Xap: nc.sync.dma_start(out=xt_, in_=Xap),
                  reads=[("X", kt * 8 + tm)], writes=xtk)
            p.comp("vector", lambda xt_=xt_, pb=pb: v.tensor_add(xt_, xt_, c.ps[pb]),
                   reads=xtk + [(f"p{pb}",)], writes=xtk)
            p.dma("sync", lambda xt_=xt_, Xap=Xap: nc.sync.dma_start(out=Xap, in_=xt_),
                  reads=xtk, writes=[("X", kt * 8 + tm)])


def emit_l0_attn_b(c):
    nc, p = c.nc, c.p
    v, s, t, g = nc.vector, nc.scalar, nc.tensor, nc.gpsimd
    iw = c.iw
    ak = lambda c0, c1: keys("att", c0, c1)
    ew = lambda c0, c1: keys("ework", c0, c1)
    qv = lambda c0, c1: keys("qkv", c0, c1)
    R = 256.0
    emit_ln(c, c.X, "X", dst_sbuf=True)
    xm = c.cols[:, 512:520]
    for kt in range(NKT):
        xr = c.xnb[:, kt * T:(kt + 1) * T]
        p.comp("vector", lambda xr=xr, kt=kt: v.reduce_sum(xm[:, kt:kt + 1], xr, AX.X),
               reads=keys("xnb", kt * T, (kt + 1) * T), writes=[("cols", 1)])
    wsp = c.f32c[:, 0:128]
    wap = iw["b0_wspanT"].rearrange("(kt p) h -> p kt h", p=128)
    p.dma("sync", lambda wsp=wsp, wap=wap: nc.sync.dma_start(out=wsp, in_=wap), writes=[("f32c", 0)])
    for kt in range(NKT):
        p.comp("tensor", lambda kt=kt: t.matmul(c.ps[4][0:16, 0:1], wsp[:, kt * 16:(kt + 1) * 16],
                                                xm[:, kt:kt + 1], start=(kt == 0), stop=(kt == NKT - 1)),
               reads=[("f32c", 0), ("cols", 1)], writes=[("p4",)])
    bsp = c.f32c[0:16, 512:513]
    p.dma("sync", lambda: nc.sync.dma_start(out=bsp, in_=iw["b0_bspan"].rearrange("(h one) -> h one", one=1)),
          writes=[("f32c", 1)])
    zc = c.f32c[0:16, 513:514]
    p.comp("scalar", lambda: s.activation(zc, c.ps[4][0:16, 0:1], AF.Sigmoid, bias=bsp, scale=1.0 / T),
           reads=[("p4",), ("f32c", 1)], writes=[("f32c", 1)])
    p.comp("vector", lambda: v.tensor_single_scalar(zc, zc, float(T) / R, ALU.mult),
           reads=[("f32c", 1)], writes=[("f32c", 1)])
    fak = [("f32a", i) for i in range(3)]
    fbk = [("f32b", i) for i in range(3)]
    for ch in range(3):
        w0 = ch * 1536
        ib = c.f32a[0:16, 0:1536]
        p.comp("gpsimd", lambda ib=ib, w0=w0: g.iota(ib, [[1, 1536]], base=w0, channel_multiplier=0,
                                                     allow_small_or_imprecise_dtypes=True),
               writes=fak)
        mm_ = c.f32b[0:16, 0:1536]
        p.comp("vector", lambda mm_=mm_, ib=ib: v.tensor_scalar(mm_, ib, 1.0 / R, 1.0 - 4095.0 / R, ALU.mult, ALU.add),
               reads=fak, writes=fbk)
        p.comp("vector", lambda mm_=mm_: v.tensor_scalar(mm_, mm_, zc[:, 0:1], None, ALU.add),
               reads=fbk + [("f32c", 1)], writes=fbk)
        p.comp("vector", lambda mm_=mm_: v.tensor_scalar(mm_, mm_, 0.0, 1.0, ALU.max, ALU.min),
               reads=fbk, writes=fbk)
        cz = c.f32a[0:16, 1536:3072]
        czk = [("f32a", i) for i in range(3, 6)]
        p.comp("vector", lambda cz=cz, ib=ib: v.tensor_single_scalar(cz, ib, 4095.0, ALU.is_le),
               reads=fak, writes=czk)
        p.comp("vector", lambda mm_=mm_, cz=cz: v.tensor_mul(mm_, mm_, cz),
               reads=fbk + czk, writes=fbk)
        mb = c.att[0:16, 0:1536]
        p.comp("scalar", lambda mb=mb, mm_=mm_: s.copy(mb, mm_),
               reads=fbk, writes=ak(0, 1536))
        sap = c.strips[:, w0:w0 + 1536]
        p.dma("sync", lambda sap=sap, mb=mb: nc.sync.dma_start(out=sap, in_=mb),
              reads=ak(0, 1536), writes=[("strips", ch)])
    negc = c.cols[:, 520:536]
    for hp in range(8):
        for wi, nm in enumerate(("b0_wqT", "b0_wkT", "b0_wvT")):
            wap = iw[nm][:, hp * 128:(hp + 1) * 128].rearrange("(kt p) m -> p kt m", p=128)
            wsb = c.wst[:, wi * 1024:(wi + 1) * 1024]
            p.dma("gpsimd", lambda wsb=wsb, wap=wap: nc.gpsimd.dma_start(out=wsb, in_=wap),
                  writes=keys("wst", wi * 1024, (wi + 1) * 1024))
        for wi, dst, dkey in ((0, c.qkv, 0), (1, c.qkv, 4096), (2, c.ework, 0)):
            for tm in range(NTM):
                pb = tm % 2
                for kt in range(NKT):
                    lhs = c.wst[:, wi * 1024 + kt * 128:(wi * 1024) + (kt + 1) * 128]
                    rhs = c.xnb[:, kt * T + tm * 512: kt * T + (tm + 1) * 512]
                    p.comp("tensor", lambda lhs=lhs, rhs=rhs, pb=pb, kt=kt:
                           t.matmul(c.ps[pb], lhs, rhs, start=(kt == 0), stop=(kt == NKT - 1)),
                           reads=keys("wst", wi * 1024, (wi + 1) * 1024) +
                                 keys("xnb", (kt * 8 + tm) * 512, (kt * 8 + tm) * 512 + 512),
                           writes=[(f"p{pb}",)])
                osb = dst[:, dkey + tm * 512: dkey + (tm + 1) * 512]
                okey = (qv if dst is c.qkv else ew)(dkey + tm * 512, dkey + (tm + 1) * 512)
                p.comp("scalar", lambda osb=osb, pb=pb: s.copy(osb, c.ps[pb]),
                       reads=[(f"p{pb}",)], writes=okey)
        for h2 in range(2):
            for tt in range(32):
                src_ = c.ework[h2 * 64:(h2 + 1) * 64, tt * 128:(tt + 1) * 128]
                qb = tt % 2
                idsl = c.idB[h2 * 64:(h2 + 1) * 64, h2 * 64:(h2 + 1) * 64]
                p.comp("tensor", lambda src_=src_, qb=qb, tt=tt, idsl=idsl:
                       t.transpose(c.qs[qb][:, (tt % 8) * 64:(tt % 8) * 64 + 64], src_, idsl),
                       reads=ew(tt * 128, (tt + 1) * 128) + [("cbf", 0)], writes=[(f"q{qb}",)])
                v0 = 3072 + (h2 * 32 + tt) * 65
                dst_ = c.ework[:, v0:v0 + 64]
                p.comp("vector", lambda dst_=dst_, qb=qb, tt=tt:
                       v.tensor_copy(dst_, c.qs[qb][:, (tt % 8) * 64:(tt % 8) * 64 + 64]),
                       reads=[(f"q{qb}",)], writes=ew(v0, v0 + 65))
                one_ = c.ework[:, v0 + 64:v0 + 65]
                p.comp("vector", lambda one_=one_: v.memset(one_, 1.0),
                       writes=ew(v0, v0 + 65))
        for h2 in range(2):
            hh = hp * 2 + h2
            for which, src0, skey in (("q", 0, qv), ("k", 4096, qv)):
                rmax = c.rows[0:1, 128 + (0 if which == "q" else 1):129 + (0 if which == "q" else 1)]
                for tm in range(NTM):
                    sq = c.f32c[h2 * 64:(h2 + 1) * 64, 0:512]
                    sap_ = c.qkv[h2 * 64:(h2 + 1) * 64, src0 + tm * 512: src0 + (tm + 1) * 512]
                    p.comp("scalar", lambda sq=sq, sap_=sap_: s.activation(sq, sap_, AF.Square, bias=0.0, scale=1.0),
                           reads=skey(src0 + tm * 512, src0 + (tm + 1) * 512), writes=[("f32c", 0)])
                    ocol = c.onescol[h2 * 64:(h2 + 1) * 64, 0:1]
                    p.comp("tensor", lambda sq=sq, ocol=ocol: t.matmul(c.ps[4][0:1, :], ocol, sq, start=True, stop=True),
                           reads=[("f32c", 0), ("consts", 0)], writes=[("p4",)])
                    cur = c.rows[0:1, 130:131]
                    p.comp("vector", lambda cur=cur: v.reduce_max(cur, c.ps[4][0:1, :], AX.X),
                           reads=[("p4",)], writes=[("rows", 0)])
                    if tm == 0:
                        p.comp("vector", lambda rmax=rmax, cur=cur: v.tensor_copy(rmax, cur),
                               reads=[("rows", 0)], writes=[("rows", 0)])
                    else:
                        p.comp("vector", lambda rmax=rmax, cur=cur: v.tensor_max(rmax, rmax, cur),
                               reads=[("rows", 0)], writes=[("rows", 0)])
            qm = c.rows[0:1, 128:129]; km = c.rows[0:1, 129:130]; cc_ = c.rows[0:1, 131:132]
            p.comp("vector", lambda: v.tensor_mul(cc_, qm, km), reads=[("rows", 0)], writes=[("rows", 0)])
            p.comp("scalar", lambda: s.activation(cc_, cc_, AF.Sqrt, bias=0.0, scale=1.0),
                   reads=[("rows", 0)], writes=[("rows", 0)])
            p.comp("vector", lambda: v.tensor_single_scalar(cc_, cc_, -1.0 / 8.0, ALU.mult),
                   reads=[("rows", 0)], writes=[("rows", 0)])
            p.comp("tensor", lambda hh=hh: t.matmul(c.ps[4][:, 0:1], c.onesrow, cc_, start=True, stop=True),
                   reads=[("rows", 0), ("rows", 3)], writes=[("p4",)])
            p.comp("vector", lambda hh=hh: v.tensor_copy(negc[:, hh:hh + 1], c.ps[4][:, 0:1]),
                   reads=[("p4",)], writes=[("cols", 1)])
        ecnt = [0]
        for tm in range(NTM):
            t0 = tm * 512
            nst = 4 * tm + 4
            def m_load(st):
                for h2 in range(2):
                    hh = hp * 2 + h2
                    mslot = (2 * (st % 2) + h2) * 512
                    m_ = c.att[:, mslot:mslot + 512]
                    moff = hh * 4608 + 3584 - t0 + st * 128
                    map_ = bass.AP(c.strips.tensor, moff, [[1, 128], [1, 512]])
                    p.dma("sync", lambda m_=m_, map_=map_: nc.sync.dma_start(out=m_, in_=map_),
                          reads=[("strips", 0), ("strips", 1), ("strips", 2)], writes=ak(mslot, mslot + 512))
            def flush(pend):
                for (h2, pst, e_, em_, ek_, emk_) in pend:
                    v0 = 3072 + (h2 * 32 + pst) * 65
                    vt = c.ework[:, v0:v0 + 65]
                    p.comp("tensor", lambda vt=vt, em_=em_, pst=pst, nst=nst, h2=h2:
                           t.matmul(c.ps[2 + h2][0:65, :], vt, em_, start=(pst == 0), stop=(pst == nst - 1)),
                           reads=ew(v0, v0 + 65) + emk_, writes=[(f"p{2 + h2}",)])
                    p.comp("tensor", lambda e_=e_, pst=pst, nst=nst, h2=h2:
                           t.matmul(c.ps[4 + h2][0:1, :], c.onescolB, e_, start=(pst == 0), stop=(pst == nst - 1)),
                           reads=ek_ + [("cbf", 0)], writes=[(f"p{4 + h2}",)])
            pend = []
            m_load(0)
            for st in range(nst):
                if st + 1 < nst:
                    m_load(st + 1)
                for h2 in range(2):
                    pb = h2
                    s0 = st * 128
                    lhs = c.qkv[h2 * 64:(h2 + 1) * 64, 4096 + s0:4096 + s0 + 128]
                    rhs = c.qkv[h2 * 64:(h2 + 1) * 64, t0:t0 + 512]
                    p.comp("tensor", lambda lhs=lhs, rhs=rhs, pb=pb:
                           t.matmul(c.ps[pb], lhs, rhs, start=True, stop=True),
                           reads=qv(4096 + s0, 4096 + s0 + 128) + qv(t0, t0 + 512), writes=[(f"p{pb}",)])
                flush(pend)
                pend = []
                for h2 in range(2):
                    hh = hp * 2 + h2
                    pb = h2
                    sl = ecnt[0] % 3
                    ecnt[0] += 1
                    e_ = c.ework[:, sl * 512:sl * 512 + 512]
                    em_ = c.ework[:, 1536 + sl * 512:1536 + sl * 512 + 512]
                    ek_ = ew(sl * 512, sl * 512 + 512)
                    emk_ = ew(1536 + sl * 512, 1536 + sl * 512 + 512)
                    mslot = (2 * (st % 2) + h2) * 512
                    m_ = c.att[:, mslot:mslot + 512]
                    p.comp("scalar", lambda e_=e_, pb=pb, hh=hh:
                           s.activation(e_, c.ps[pb], AF.Exp, bias=negc[:, hh:hh + 1], scale=0.125),
                           reads=[(f"p{pb}",), ("cols", 1)], writes=ek_)
                    p.comp("vector", lambda em_=em_, e_=e_, m_=m_: v.tensor_mul(em_, e_, m_[:, ::-1]),
                           reads=ek_ + ak(mslot, mslot + 512), writes=emk_)
                    pend.append((h2, st, e_, em_, ek_, emk_))
            flush(pend)
            for h2 in range(2):
                hh = hp * 2 + h2
                se_ = c.rows[0:1, 132 + h2 * 512:644 + h2 * 512]
                sek = [("rows", h2), ("rows", h2 + 1)]
                p.comp("vector", lambda se_=se_, h2=h2: v.tensor_copy(se_, c.ps[4 + h2][0:1, :]),
                       reads=[(f"p{4 + h2}",)], writes=sek)
                p.comp("vector", lambda se_=se_, h2=h2:
                       v.scalar_tensor_tensor(se_, se_, 1e-8, c.ps[2 + h2][64:65, :], ALU.mult, ALU.add),
                       reads=sek + [(f"p{2 + h2}",)], writes=sek)
                p.comp("vector", lambda se_=se_: v.reciprocal(se_, se_),
                       reads=sek, writes=sek)
                cx = c.f32c[:, h2 * 512:h2 * 512 + 512]
                p.comp("vector", lambda cx=cx, h2=h2: v.tensor_copy(cx[0:64, :], c.ps[2 + h2][0:64, :]),
                       reads=[(f"p{2 + h2}",)], writes=[("f32c", h2)])
                p.comp("tensor", lambda se_=se_, h2=h2:
                       t.matmul(c.ps[2 + h2][0:64, :], c.onesrow[0:1, 0:64], se_, start=True, stop=True),
                       reads=sek + [("rows", 3), ("f32c", h2)], writes=[(f"p{2 + h2}",)])
                p.comp("vector", lambda cx=cx, h2=h2: v.tensor_mul(cx[0:64, :], cx[0:64, :], c.ps[2 + h2][0:64, :]),
                       reads=[("f32c", h2), (f"p{2 + h2}",)], writes=[("f32c", h2)])
                ob = c.att[0:64, 2048:2560] if h2 == 0 else c.att[64:128, 2048:2560]
                p.comp("scalar", lambda ob=ob, cx=cx: s.copy(ob, cx[0:64, :]),
                       reads=[("f32c", h2)], writes=ak(2048, 2560))
                cap_ = c.ctxT[hh * 64:(hh + 1) * 64, t0:t0 + 512]
                p.dma("sync", lambda cap_=cap_, ob=ob: nc.sync.dma_start(out=cap_, in_=ob),
                      reads=ak(2048, 2560), writes=[("ctxT", (hh // 2) * 8 + tm)])
    emit_proj_residual(c, c.iw["b0_woT"], D, bias_dram=c.iw["b0_bo"])


def emit_l1_attn_a(c):
    nc, p = c.nc, c.p
    v, s, t, g = nc.vector, nc.scalar, nc.tensor, nc.gpsimd
    iw = c.iw
    ak = lambda c0, c1: keys("att", c0, c1)
    ew = lambda c0, c1: keys("ework", c0, c1)
    qv = lambda c0, c1: keys("qkv", c0, c1)
    wk = lambda c0, c1: keys("wst", c0, c1)
    emit_ln(c, c.X, "X", dst_sbuf=True)
    halfpi = constcell(c, PI / 2)
    EPS2 = 1e-6 * 128.0
    for hp in range(8):
        for wi, nm in enumerate(("a1_wqT", "a1_wkT", "a1_wvT")):
            wap = iw[nm][:, hp * 128:(hp + 1) * 128].rearrange("(kt p) m -> p kt m", p=128)
            wsb = c.wst[:, wi * 1024:(wi + 1) * 1024]
            p.dma("gpsimd", lambda wsb=wsb, wap=wap: nc.gpsimd.dma_start(out=wsb, in_=wap),
                  writes=wk(wi * 1024, (wi + 1) * 1024))
        for h2 in range(2):
            hh = hp * 2 + h2
            wrap = iw["a1_wr"][hh * 64:(hh + 1) * 64, :]
            wrsb = c.wst[h2 * 64:(h2 + 1) * 64, 3072 + h2 * 128:3072 + h2 * 128 + 128]
            p.dma("gpsimd", lambda wrsb=wrsb, wrap=wrap: nc.gpsimd.dma_start(out=wrsb, in_=wrap),
                  writes=wk(3072, 3584))
        for wi, dst0 in ((0, 4096), (1, 8192), (2, 12288)):
            for tm in range(NTM):
                pb = tm % 2
                for kt in range(NKT):
                    lhs = c.wst[:, wi * 1024 + kt * 128: wi * 1024 + (kt + 1) * 128]
                    rhs = c.xnb[:, kt * T + tm * 512: kt * T + (tm + 1) * 512]
                    p.comp("tensor", lambda lhs=lhs, rhs=rhs, pb=pb, kt=kt:
                           t.matmul(c.ps[pb], lhs, rhs, start=(kt == 0), stop=(kt == NKT - 1)),
                           reads=wk(wi * 1024, (wi + 1) * 1024) +
                                 keys("xnb", (kt * 8 + tm) * 512, (kt * 8 + tm) * 512 + 512),
                           writes=[(f"p{pb}",)])
                osb = c.wst[:, dst0 + tm * 512: dst0 + (tm + 1) * 512]
                p.comp("scalar", lambda osb=osb, pb=pb: s.copy(osb, c.ps[pb]),
                       reads=[(f"p{pb}",)], writes=wk(dst0 + tm * 512, dst0 + (tm + 1) * 512))
        for h2 in range(2):
            hh = hp * 2 + h2
            b0 = h2 * 64
            wrsb = c.wst[b0:b0 + 64, 3072 + h2 * 128:3072 + h2 * 128 + 128]
            rbr = c.rows[0:1, 160:288]
            p.dma("sync", lambda rbr=rbr, hh=hh: nc.sync.dma_start(out=rbr, in_=iw["a1_br"][hh, :]),
                  writes=[("rows", 0)])
            brc = c.cols[:, 536:537]
            p.comp("tensor", lambda rbr=rbr: t.transpose(c.ps[4][0:128, 0:1], rbr, c.idF[0:1, 0:1]),
                   reads=[("rows", 0), ("consts", 0)], writes=[("p4",)])
            p.comp("vector", lambda brc=brc: v.tensor_copy(brc, c.ps[4][:, 0:1]),
                   reads=[("p4",)], writes=[("cols", 1)])
            for tm in range(NTM):
                pb = tm % 2
                lhs = wrsb
                rhs = c.wst[b0:b0 + 64, 4096 + tm * 512:4096 + (tm + 1) * 512]
                p.comp("tensor", lambda lhs=lhs, rhs=rhs, pb=pb:
                       t.matmul(c.ps[pb], lhs, rhs, start=True, stop=True),
                       reads=wk(3072, 3584) + wk(4096 + tm * 512, 4096 + (tm + 1) * 512),
                       writes=[(f"p{pb}",)])
                prj = c.f32a[:, (tm % 2) * 512:(tm % 2) * 512 + 512]
                pjk = [("f32a", tm % 2)]
                p.comp("scalar", lambda prj=prj, pb=pb, brc=brc:
                       s.activation(prj, c.ps[pb], AF.Identity, bias=brc, scale=1.0),
                       reads=[(f"p{pb}",), ("cols", 1)], writes=pjk)
                emit_sincos(c, prj, pjk, c.qkv[:, tm * 512:(tm + 1) * 512],
                            c.qkv[:, 4096 + tm * 512:4096 + (tm + 1) * 512],
                            qv(tm * 512, (tm + 1) * 512), qv(4096 + tm * 512, 4096 + (tm + 1) * 512),
                            halfpi, tm % 2)
            for ttg in range(8):
                for j in range(4):
                    tt = ttg * 4 + j
                    lhs = c.wst[b0:b0 + 64, 8192 + tt * 128:8192 + (tt + 1) * 128]
                    p.comp("tensor", lambda lhs=lhs, wrsb=wrsb, j=j:
                           t.matmul(c.ps[j][:, 0:128], lhs, wrsb, start=True, stop=False),
                           reads=wk(8192 + tt * 128, 8192 + (tt + 1) * 128) + wk(3072, 3584),
                           writes=[(f"p{j}",)])
                    p.comp("tensor", lambda rbr=rbr, j=j:
                           t.matmul(c.ps[j][:, 0:128], c.onesrow, rbr, start=False, stop=True),
                           reads=[("rows", 0), ("rows", 3)], writes=[(f"p{j}",)])
                prj = c.f32a[:, 1024 + (ttg % 2) * 512:1024 + (ttg % 2) * 512 + 512]
                pjk = [("f32a", 2 + (ttg % 2))]
                for j in range(4):
                    p.comp("scalar", lambda prj=prj, j=j: s.copy(prj[:, j * 128:(j + 1) * 128], c.ps[j][:, 0:128]),
                           reads=[(f"p{j}",)], writes=pjk)
                e3 = c.ework[:, ttg * 1024:(ttg + 1) * 1024].rearrange("p (a b) -> p a b", b=256)
                emit_sincos(c, prj, pjk, e3[:, :, 0:128], e3[:, :, 128:256],
                            ew(ttg * 1024, (ttg + 1) * 1024), ew(ttg * 1024, (ttg + 1) * 1024),
                            halfpi, 2 + ttg % 2)
            if h2 == 0:
                for tt in range(32):
                    src_ = c.wst[:, 12288 + tt * 128:12288 + (tt + 1) * 128]
                    qb = (tt // 8) % 2
                    p.comp("tensor", lambda src_=src_, qb=qb, tt=tt:
                           t.transpose(c.qs[qb][:, (tt % 8) * 128:(tt % 8) * 128 + 128], src_, c.idB),
                           reads=wk(12288 + tt * 128, 12288 + (tt + 1) * 128) + [("cbf", 0)], writes=[(f"q{qb}",)])
                    if tt % 8 == 7:
                        g0 = tt - 7
                        dst_ = c.wst[:, 12288 + g0 * 128:12288 + g0 * 128 + 1024]
                        p.comp("vector", lambda dst_=dst_, qb=qb:
                               v.tensor_copy(dst_, c.qs[qb][:, 0:1024]),
                               reads=[(f"q{qb}",)], writes=wk(12288 + g0 * 128, 12288 + g0 * 128 + 1024))
            for fp in range(2):
                for tt in range(32):
                    lhs = c.ework[:, tt * 256 + fp * 128:tt * 256 + fp * 128 + 128]
                    vt = c.wst[:, 12288 + tt * 128 + h2 * 64:12288 + tt * 128 + h2 * 64 + 64]
                    p.comp("tensor", lambda lhs=lhs, vt=vt, fp=fp, tt=tt:
                           t.matmul(c.ps[2 + fp][:, 0:64], lhs, vt, start=(tt == 0), stop=(tt == 31)),
                           reads=ew(tt * 256 + fp * 128, tt * 256 + fp * 128 + 128) +
                                 wk(12288 + tt * 128, 12288 + (tt + 1) * 128),
                           writes=[(f"p{2 + fp}",)])
                    p.comp("tensor", lambda lhs=lhs, fp=fp, tt=tt:
                           t.matmul(c.ps[4 + fp][:, 0:1], lhs, c.onescolB, start=(tt == 0), stop=(tt == 31)),
                           reads=ew(tt * 256 + fp * 128, tt * 256 + fp * 128 + 128) + [("cbf", 0)],
                           writes=[(f"p{4 + fp}",)])
            for fp in range(2):
                kv = c.att[:, 65 * fp:65 * fp + 65]
                p.comp("vector", lambda kv=kv, fp=fp: v.tensor_copy(kv[:, 0:64], c.ps[2 + fp][:, 0:64]),
                       reads=[(f"p{2 + fp}",)], writes=ak(65 * fp, 65 * fp + 65))
                p.comp("vector", lambda kv=kv, fp=fp: v.tensor_copy(kv[:, 64:65], c.ps[4 + fp][:, 0:1]),
                       reads=[(f"p{4 + fp}",)], writes=ak(65 * fp, 65 * fp + 65))
            for tm in range(NTM):
                for fp in range(2):
                    kv = c.att[:, 65 * fp:65 * fp + 65]
                    rhs = c.qkv[:, fp * 4096 + tm * 512: fp * 4096 + (tm + 1) * 512]
                    p.comp("tensor", lambda kv=kv, rhs=rhs, fp=fp:
                           t.matmul(c.ps[0][0:65, :], kv, rhs, start=(fp == 0), stop=(fp == 1)),
                           reads=ak(65 * fp, 65 * fp + 65) + qv(fp * 4096 + tm * 512, fp * 4096 + (tm + 1) * 512),
                           writes=[("p0",)])
                dn = c.rows[0:1, 644:1156]
                p.comp("vector", lambda dn=dn: v.tensor_scalar(dn, c.ps[0][64:65, :], EPS2, None, ALU.add),
                       reads=[("p0",)], writes=[("rows", 1), ("rows", 2)])
                p.comp("vector", lambda dn=dn: v.reciprocal(dn, dn),
                       reads=[("rows", 1), ("rows", 2)], writes=[("rows", 1), ("rows", 2)])
                p.comp("tensor", lambda dn=dn: t.matmul(c.ps[4][0:64, :], c.onesrow[0:1, 0:64], dn, start=True, stop=True),
                       reads=[("rows", 1), ("rows", 2), ("rows", 3)], writes=[("p4",)])
                cx = c.f32c[:, 0:512]
                p.comp("vector", lambda cx=cx: v.tensor_copy(cx[0:64, :], c.ps[0][0:64, :]),
                       reads=[("p0",)], writes=[("f32c", 0)])
                p.comp("vector", lambda cx=cx: v.tensor_mul(cx[0:64, :], cx[0:64, :], c.ps[4][0:64, :]),
                       reads=[("f32c", 0), ("p4",)], writes=[("f32c", 0)])
                ob = c.att[0:64, 1024:1536]
                p.comp("scalar", lambda ob=ob, cx=cx: s.copy(ob, cx[0:64, :]),
                       reads=[("f32c", 0)], writes=ak(1024, 1536))
                cap_ = c.ctxT[hh * 64:(hh + 1) * 64, tm * 512:(tm + 1) * 512]
                p.dma("sync", lambda cap_=cap_, ob=ob: nc.sync.dma_start(out=cap_, in_=ob),
                      reads=ak(1024, 1536), writes=[("ctxT", (hh // 2) * 8 + tm)])
    emit_proj_residual(c, c.iw["a1_wpT"], D, bias_dram=c.iw["a1_bp"])


def emit_sincos(c, prj, pjk, cosdst, sindst, cosk, sink, halfpi, scratch_id):
    nc, p = c.nc, c.p
    v, s = nc.vector, nc.scalar
    N = prj.shape[1]
    isl = (scratch_id % 4) * 128 if N <= 128 else (scratch_id % 2) * 512
    k32 = c.i512[:, isl:isl + N]
    kf = c.f32b[:, scratch_id * 512:scratch_id * 512 + N]
    kfk = [("f32b", scratch_id)]
    zz = c.f32c[:, scratch_id * 512:scratch_id * 512 + N]
    zzk = [("f32c", scratch_id)]
    aa = c.f32b[:, (scratch_id + 4) * 512:(scratch_id + 4) * 512 + N]
    aak = [("f32b", scratch_id + 4)]
    ik = [("i512", isl // 128), ("i512", isl // 128 + (1 if N > 128 else 0))]
    p.comp("vector", lambda: v.tensor_single_scalar(k32, prj, 1.0 / (2 * PI), ALU.mult),
           reads=pjk, writes=ik)
    p.comp("vector", lambda: v.tensor_copy(kf, k32), reads=ik, writes=kfk)
    p.comp("vector", lambda: v.scalar_tensor_tensor(zz, kf, 2 * PI, prj, ALU.mult, ALU.subtract),
           reads=kfk + pjk, writes=zzk)
    p.comp("scalar", lambda: s.activation(sindst, zz, AF.Sin, bias=0.0, scale=-1.0),
           reads=zzk, writes=sink)
    p.comp("scalar", lambda: s.activation(aa, zz, AF.Abs, bias=0.0, scale=1.0),
           reads=zzk, writes=aak)
    p.comp("scalar", lambda: s.activation(cosdst, aa, AF.Sin, bias=halfpi, scale=-1.0),
           reads=aak + [("cols", 1)], writes=cosk)


def emit_l1_attn_b(c):
    nc, p = c.nc, c.p
    v, s, t, g = nc.vector, nc.scalar, nc.tensor, nc.gpsimd
    iw = c.iw
    ak = lambda c0, c1: keys("att", c0, c1)
    ew = lambda c0, c1: keys("ework", c0, c1)
    wk = lambda c0, c1: keys("wst", c0, c1)
    emit_ln(c, c.X, "X", dst_sbuf=True, dst32=c.tT, dst32key="tT")
    emit_bias_cols(c, iw["b1_bq"], 8, 40)
    emit_bias_cols(c, iw["b1_bk"], 8, 48)
    emit_bias_cols(c, iw["b1_bv"], 8, 56)
    for hp in range(8):
        for wi, nm in enumerate(("b1_wqT", "b1_wkT", "b1_wvT")):
            wap = iw[nm][:, hp * 128:(hp + 1) * 128].rearrange("(kt p) m -> p kt m", p=128)
            wsb = c.wst[:, wi * 1024:(wi + 1) * 1024]
            p.dma("gpsimd", lambda wsb=wsb, wap=wap: nc.gpsimd.dma_start(out=wsb, in_=wap),
                  writes=wk(wi * 1024, (wi + 1) * 1024))
        for wi, dst0, bcol0, act in ((0, 4096, 40, AF.Relu), (1, 8192, 48, AF.Relu), (2, 12288, 56, AF.Identity)):
            for tm in range(NTM):
                pb = tm % 2
                for kt in range(NKT):
                    lhs = c.wst[:, wi * 1024 + kt * 128: wi * 1024 + (kt + 1) * 128]
                    rhs = c.xnb[:, kt * T + tm * 512: kt * T + (tm + 1) * 512]
                    p.comp("tensor", lambda lhs=lhs, rhs=rhs, pb=pb, kt=kt:
                           t.matmul(c.ps[pb], lhs, rhs, start=(kt == 0), stop=(kt == NKT - 1)),
                           reads=wk(wi * 1024, (wi + 1) * 1024) +
                                 keys("xnb", (kt * 8 + tm) * 512, (kt * 8 + tm) * 512 + 512),
                           writes=[(f"p{pb}",)])
                osb = c.wst[:, dst0 + tm * 512: dst0 + (tm + 1) * 512]
                bcol = c.cols[:, bcol0 + hp:bcol0 + hp + 1]
                p.comp("scalar", lambda osb=osb, pb=pb, bcol=bcol, act=act:
                       s.activation(osb, c.ps[pb], act, bias=bcol, scale=1.0),
                       reads=[(f"p{pb}",), ("cols", 0)], writes=wk(dst0 + tm * 512, dst0 + (tm + 1) * 512))
        for which, src0, dst0 in (("k", 8192, 0), ("v", 12288, 4096)):
            for tt in range(32):
                src_ = c.wst[:, src0 + tt * 128:src0 + (tt + 1) * 128]
                qb = (tt // 8) % 2
                p.comp("tensor", lambda src_=src_, qb=qb, tt=tt:
                       t.transpose(c.qs[qb][:, (tt % 8) * 128:(tt % 8) * 128 + 128], src_, c.idB),
                       reads=wk(src0 + tt * 128, src0 + (tt + 1) * 128) + [("cbf", 0)], writes=[(f"q{qb}",)])
                if tt % 8 == 7:
                    g0 = tt - 7
                    dst_ = c.ework[:, dst0 + g0 * 128:dst0 + g0 * 128 + 1024]
                    p.comp("vector", lambda dst_=dst_, qb=qb:
                           v.tensor_copy(dst_, c.qs[qb][:, 0:1024]),
                           reads=[(f"q{qb}",)], writes=ew(dst0 + g0 * 128, dst0 + g0 * 128 + 1024))
        for h2 in range(2):
            hh = hp * 2 + h2
            b0 = h2 * 64
            for tt in range(32):
                kt_ = c.ework[:, tt * 128 + b0:tt * 128 + b0 + 64]
                vt_ = c.ework[:, 4096 + tt * 128 + b0:4096 + tt * 128 + b0 + 64]
                p.comp("tensor", lambda kt_=kt_, vt_=vt_, tt=tt, h2=h2:
                       t.matmul(c.ps[2 + h2][0:64, 0:64], kt_, vt_, start=(tt == 0), stop=(tt == 31)),
                       reads=ew(tt * 128, (tt + 1) * 128) + ew(4096 + tt * 128, 4096 + (tt + 1) * 128),
                       writes=[(f"p{2 + h2}",)])
                p.comp("tensor", lambda kt_=kt_, tt=tt, h2=h2:
                       t.matmul(c.ps[4 + h2][0:64, 0:1], kt_, c.onescolB, start=(tt == 0), stop=(tt == 31)),
                       reads=ew(tt * 128, (tt + 1) * 128) + [("cbf", 0)], writes=[(f"p{4 + h2}",)])
        for h2 in range(2):
            b0 = h2 * 64
            kv = c.att[b0:b0 + 64, 2048 - 65 - h2 * 65:2048 - h2 * 65]
            p.comp("vector", lambda kv=kv, h2=h2: v.tensor_copy(kv[:, 0:64], c.ps[2 + h2][0:64, 0:64]),
                   reads=[(f"p{2 + h2}",)], writes=ak(1918, 2048))
            p.comp("vector", lambda kv=kv, h2=h2: v.tensor_copy(kv[:, 64:65], c.ps[4 + h2][0:64, 0:1]),
                   reads=[(f"p{4 + h2}",)], writes=ak(1918, 2048))
        for h2 in range(2):
            hh = hp * 2 + h2
            b0 = h2 * 64
            kv = c.att[b0:b0 + 64, 2048 - 65 - h2 * 65:2048 - h2 * 65]
            for tm in range(NTM):
                rhs = c.wst[b0:b0 + 64, 4096 + tm * 512:4096 + (tm + 1) * 512]
                p.comp("tensor", lambda kv=kv, rhs=rhs:
                       t.matmul(c.ps[0][0:65, :], kv, rhs, start=True, stop=True),
                       reads=ak(1918, 2048) + wk(4096 + tm * 512, 4096 + (tm + 1) * 512),
                       writes=[("p0",)])
                dn = c.rows[0:1, 644:1156]
                p.comp("vector", lambda dn=dn: v.tensor_scalar(dn, c.ps[0][64:65, :], 1e-6, None, ALU.max),
                       reads=[("p0",)], writes=[("rows", 1), ("rows", 2)])
                p.comp("vector", lambda dn=dn: v.reciprocal(dn, dn),
                       reads=[("rows", 1), ("rows", 2)], writes=[("rows", 1), ("rows", 2)])
                p.comp("tensor", lambda dn=dn: t.matmul(c.ps[4][0:64, :], c.onesrow[0:1, 0:64], dn, start=True, stop=True),
                       reads=[("rows", 1), ("rows", 2), ("rows", 3)], writes=[("p4",)])
                cx = c.f32c[:, 0:512]
                p.comp("vector", lambda cx=cx: v.tensor_copy(cx[0:64, :], c.ps[0][0:64, :]),
                       reads=[("p0",)], writes=[("f32c", 0)])
                p.comp("vector", lambda cx=cx: v.tensor_mul(cx[0:64, :], cx[0:64, :], c.ps[4][0:64, :]),
                       reads=[("f32c", 0), ("p4",)], writes=[("f32c", 0)])
                tt_ = c.f32c[:, 512:1024]
                tap_ = c.tT[hh * 64:(hh + 1) * 64, tm * 512:(tm + 1) * 512]
                p.dma("scalar", lambda tt_=tt_, tap_=tap_: nc.scalar.dma_start(out=tt_[0:64, :], in_=tap_),
                      reads=[("tT", (hh // 2) * 8 + tm)], writes=[("f32c", 1)])
                p.comp("vector", lambda cx=cx, tt_=tt_: v.tensor_add(cx[0:64, :], cx[0:64, :], tt_[0:64, :]),
                       reads=[("f32c", 0), ("f32c", 1)], writes=[("f32c", 0)])
                yap = c.yT[hh * 64:(hh + 1) * 64, tm * 512:(tm + 1) * 512]
                p.dma("sync", lambda yap=yap, cx=cx: nc.sync.dma_start(out=yap, in_=cx[0:64, :]),
                      reads=[("f32c", 0)], writes=[("yT", (hh // 2) * 8 + tm)])
    emit_bias_cols(c, iw["b1_gamma"], 8, 8)
    emit_bias_cols(c, iw["b1_beta"], 8, 16)
    emit_ln_add_to_X(c)


def emit_ln_add_to_X(c):
    nc, p = c.nc, c.p
    v, s, t = nc.vector, nc.scalar, nc.tensor
    for m in range(NTM):
        for kt in range(NKT):
            buf = c.f32a[:, kt * 512:(kt + 1) * 512]
            srcap = c.yT[kt * 128:(kt + 1) * 128, m * 512:(m + 1) * 512]
            p.dma("sync", lambda b=buf, a=srcap: nc.sync.dma_start(out=b, in_=a),
                  reads=[("yT", kt * 8 + m)], writes=[("f32a", kt)])
        for kt in range(NKT):
            buf = c.f32a[:, kt * 512:(kt + 1) * 512]
            sq = c.f32c[:, kt * 512:(kt + 1) * 512]
            p.comp("scalar", lambda sq=sq, b=buf: s.activation(sq, b, AF.Square, bias=0.0, scale=1.0),
                   reads=[("f32a", kt)], writes=[("f32c", kt)])
            p.comp("tensor", lambda b=buf, kt=kt: t.matmul(c.ps[4][0:1, :], c.onescol, b,
                                                           start=(kt == 0), stop=(kt == NKT - 1)),
                   reads=[("f32a", kt), ("consts", 0)], writes=[("p4",)])
            p.comp("tensor", lambda sq=sq, kt=kt: t.matmul(c.ps[5][0:1, :], c.onescol, sq,
                                                           start=(kt == 0), stop=(kt == NKT - 1)),
                   reads=[("f32c", kt), ("consts", 0)], writes=[("p5",)])
        r0 = c.rows[0:1, 0:512]; r1 = c.rows[0:1, 512:1024]; r2 = c.rows[0:1, 1024:1536]
        p.comp("vector", lambda: v.tensor_single_scalar(r0, c.ps[4][0:1, :], 1.0 / D, ALU.mult),
               reads=[("p4",)], writes=[("rows", 0)])
        p.comp("vector", lambda: v.tensor_single_scalar(r1, c.ps[5][0:1, :], 1.0 / D, ALU.mult),
               reads=[("p5",)], writes=[("rows", 1)])
        p.comp("vector", lambda: v.scalar_tensor_tensor(r2, r0, 1.0, r0, ALU.mult, ALU.mult),
               reads=[("rows", 0)], writes=[("rows", 2)])
        p.comp("vector", lambda: v.tensor_sub(r1, r1, r2),
               reads=[("rows", 1), ("rows", 2)], writes=[("rows", 1)])
        epsr = constrow(c, 1e-5)
        p.comp("scalar", lambda epsr=epsr: s.activation(r2, r1, AF.Sqrt, bias=epsr, scale=1.0),
               reads=[("rows", 1), ("cols", 1)], writes=[("rows", 2)])
        p.comp("vector", lambda: v.reciprocal(r1, r2),
               reads=[("rows", 2)], writes=[("rows", 1)])
        p.comp("vector", lambda: v.tensor_single_scalar(r0, r0, -1.0, ALU.mult),
               reads=[("rows", 0)], writes=[("rows", 0)])
        p.comp("tensor", lambda: t.matmul(c.ps[4], c.onesrow, r0, start=True, stop=True),
               reads=[("rows", 0), ("rows", 3)], writes=[("p4",)])
        p.comp("tensor", lambda: t.matmul(c.ps[5], c.onesrow, r1, start=True, stop=True),
               reads=[("rows", 1), ("rows", 3)], writes=[("p5",)])
        for kt in range(NKT):
            buf = c.f32a[:, kt * 512:(kt + 1) * 512]
            xb = c.f32b[:, kt * 512:(kt + 1) * 512]
            p.comp("vector", lambda xb=xb, b=buf: v.tensor_add(xb, b, c.ps[4]),
                   reads=[("f32a", kt), ("p4",)], writes=[("f32b", kt)])
            p.comp("vector", lambda xb=xb: v.tensor_mul(xb, xb, c.ps[5]),
                   reads=[("f32b", kt), ("p5",)], writes=[("f32b", kt)])
            gcol = c.cols[:, 8 + kt:9 + kt]; bcol = c.cols[:, 16 + kt:17 + kt]
            p.comp("vector", lambda xb=xb, gcol=gcol, bcol=bcol:
                   v.tensor_scalar(xb, xb, gcol, bcol, ALU.mult, ALU.add),
                   reads=[("f32b", kt), ("cols", 0)], writes=[("f32b", kt)])
            xt_ = c.f32c[:, kt * 512:(kt + 1) * 512]
            Xap = c.X[kt * 128:(kt + 1) * 128, m * 512:(m + 1) * 512]
            p.dma("scalar", lambda xt_=xt_, Xap=Xap: nc.scalar.dma_start(out=xt_, in_=Xap),
                  reads=[("X", kt * 8 + m)], writes=[("f32c", kt)])
            p.comp("vector", lambda xb=xb, xt_=xt_: v.tensor_add(xb, xb, xt_),
                   reads=[("f32b", kt), ("f32c", kt)], writes=[("f32b", kt)])
            p.dma("sync", lambda Xap=Xap, xb=xb: nc.sync.dma_start(out=Xap, in_=xb),
                  reads=[("f32b", kt)], writes=[("X", kt * 8 + m)])


def emit_finish(c):
    nc, p = c.nc, c.p
    for kt in range(NKT):
        for m in range(NTM):
            src = c.X[kt * 128:(kt + 1) * 128, m * 512:(m + 1) * 512]
            dst = c.out_T[kt * 128:(kt + 1) * 128, m * 512:(m + 1) * 512]
            buf = c.f32a[:, m * 512:(m + 1) * 512]
            p.dma("sync", lambda src=src, buf=buf: nc.sync.dma_start(out=buf, in_=src),
                  reads=[("X", kt * 8 + m)], writes=[("f32a", m)])
            p.dma("sync", lambda dst=dst, buf=buf: nc.sync.dma_start(out=dst, in_=buf),
                  reads=[("f32a", m)], writes=[("out", kt * 8 + m)])
    p.comp("vector", lambda: nc.vector.engine_nop(),
           reads=[("out", i) for i in range(NKT * 8)])


def make_inputs(inputs, core):
    tr = lambda a: np.ascontiguousarray(np.asarray(a, dtype=np.float32).T)
    d = {}
    d["x_T"] = tr(inputs["L0_attn_a_x"][core])
    d["a0_wxT"] = tr(inputs["L0_attn_a_w_xproj"])
    d["a0_bx"] = np.asarray(inputs["L0_attn_a_b_xproj"], np.float32)
    d["a0_A"] = np.asarray(inputs["L0_attn_a_A"], np.float32)
    d["a0_D"] = np.asarray(inputs["L0_attn_a_D_param"], np.float32)
    d["a0_woT"] = tr(inputs["L0_attn_a_w_out"])
    d["a0_bo"] = np.asarray(inputs["L0_attn_a_b_out"], np.float32)
    d["a0_gamma"] = np.asarray(inputs["L0_attn_a_gamma"], np.float32)
    d["a0_beta"] = np.asarray(inputs["L0_attn_a_beta"], np.float32)
    d["b0_wqT"] = tr(inputs["L0_attn_b_Wq"])
    d["b0_wkT"] = tr(inputs["L0_attn_b_Wk"])
    d["b0_wvT"] = tr(inputs["L0_attn_b_Wv"])
    d["b0_woT"] = tr(inputs["L0_attn_b_Wo"])
    d["b0_bo"] = np.asarray(inputs["L0_attn_b_bo"], np.float32)
    d["b0_wspanT"] = tr(inputs["L0_attn_b_Wspan"])
    d["b0_bspan"] = np.asarray(inputs["L0_attn_b_bspan"], np.float32)
    d["f0_rwT"] = tr(inputs["L0_ffn_router_w"])
    d["f0_w1"] = np.asarray(inputs["L0_ffn_W1"], np.float32).reshape(8 * 1024, 4096)
    d["f0_b1"] = np.asarray(inputs["L0_ffn_b1"], np.float32)
    d["f0_w2"] = np.asarray(inputs["L0_ffn_W2"], np.float32).reshape(8 * 4096, 1024)
    d["f0_b2"] = np.asarray(inputs["L0_ffn_b2"], np.float32)
    d["a1_wqT"] = tr(inputs["L1_attn_a_Wq"])
    d["a1_wkT"] = tr(inputs["L1_attn_a_Wk"])
    d["a1_wvT"] = tr(inputs["L1_attn_a_Wv"])
    d["a1_wpT"] = tr(inputs["L1_attn_a_Wp"])
    d["a1_bp"] = np.asarray(inputs["L1_attn_a_bp"], np.float32)
    d["a1_wr"] = np.asarray(inputs["L1_attn_a_Wr"], np.float32).reshape(16 * 64, 128)
    d["a1_br"] = np.asarray(inputs["L1_attn_a_br"], np.float32)
    d["b1_wqT"] = tr(inputs["L1_attn_b_Wq"])
    d["b1_wkT"] = tr(inputs["L1_attn_b_Wk"])
    d["b1_wvT"] = tr(inputs["L1_attn_b_Wv"])
    d["b1_bq"] = np.asarray(inputs["L1_attn_b_bq"], np.float32)
    d["b1_bk"] = np.asarray(inputs["L1_attn_b_bk"], np.float32)
    d["b1_bv"] = np.asarray(inputs["L1_attn_b_bv"], np.float32)
    d["b1_gamma"] = np.asarray(inputs["L1_attn_b_gamma"], np.float32)
    d["b1_beta"] = np.asarray(inputs["L1_attn_b_beta"], np.float32)
    d["f1_wgT"] = tr(inputs["L1_ffn_Wg"])
    d["f1_w1"] = np.asarray(inputs["L1_ffn_W1"], np.float32).reshape(8 * 1024, 4096)
    d["f1_w2"] = np.asarray(inputs["L1_ffn_W2"], np.float32).reshape(8 * 2048, 1024)
    pm = np.zeros(8, np.float32); pm[:core] = 1.0
    d["prevmask"] = pm
    return d


def kernel(**inputs):
    nc = build_program(stop_after=8)
    in_maps = [make_inputs(inputs, core) for core in range(NCORES)]
    res = run_bass_kernel_spmd(nc, in_maps, core_ids=list(range(NCORES)))
    out = np.stack([np.ascontiguousarray(res.results[i]["out_T"].T) for i in range(NCORES)])
    return out.astype(np.float32)


# version 14
# speedup vs baseline: 1.0519x; 1.0519x over previous
import math
import numpy as np
import concourse.bass as bass
import concourse.mybir as mybir
from concourse.bass_utils import run_bass_kernel_spmd

F32 = mybir.dt.float32
BF16 = mybir.dt.bfloat16
U32 = mybir.dt.uint32
I32 = mybir.dt.int32
AF = mybir.ActivationFunctionType
ALU = mybir.AluOpType
AX = mybir.AxisListType

B, T, D = 8, 4096, 1024
NCORES = 8
NKT = D // 128
NTM = T // 512
PI = math.pi

ENGINES = ("tensor", "vector", "scalar", "gpsimd", "sync")

class Op:
    __slots__ = ("engine", "kind", "build", "deps", "idx", "inc", "token", "tag", "pre_wait")
    def __init__(self, engine, kind, build, deps, idx, tag):
        self.engine = engine; self.kind = kind; self.build = build
        self.deps = deps; self.idx = idx; self.inc = None; self.token = None
        self.tag = tag; self.pre_wait = None

class Prog:
    SERIALIZE = False
    SER_ENGINE = False
    NDMA = 48
    PE_SKIP = True
    def __init__(self, nc):
        self.nc = nc
        self.ops = []
        self.last_w = {}
        self.readers = {}
        self.tag = ""
        self.bg = False
        self.bg_ids = set()

    def _record(self, engine, kind, build, reads, writes):
        i = len(self.ops)
        deps = set()
        for k in reads:
            w = self.last_w.get(k)
            if w is not None:
                deps.add(w)
        for k in writes:
            w = self.last_w.get(k)
            if w is not None:
                deps.add(w)
            rs = self.readers.get(k)
            if rs:
                deps.update(rs)
        deps.discard(i)
        if Prog.SERIALIZE and i > 0:
            deps.add(i - 1)
        if Prog.SER_ENGINE:
            for j in range(i - 1, -1, -1):
                if self.ops[j].engine == engine:
                    deps.add(j)
                    break
        if engine == "tensor" and Prog.PE_SKIP and not Prog.SERIALIZE:
            deps = {d for d in deps if self.ops[d].engine != "tensor"}
        op = Op(engine, kind, build, deps, i, self.tag)
        self.ops.append(op)
        if self.bg:
            self.bg_ids.add(i)
        for k in reads:
            self.readers.setdefault(k, []).append(i)
        for k in writes:
            self.last_w[k] = i
            self.readers[k] = []
        return i

    def comp(self, engine, build, reads=(), writes=()):
        return self._record(engine, "comp", build, reads, writes)

    def dma(self, engine, build, reads=(), writes=()):
        return self._record(engine, "dma", build, reads, writes)

    def coll(self, build, reads=(), writes=()):
        return self._record("gpsimd", "coll", build, reads, writes)

    def emit(self, stack):
        nc = self.nc
        need_inc = set()
        for op in self.ops:
            need_inc |= op.deps
        if self.bg_ids:
            bg = [op for op in self.ops if op.idx in self.bg_ids]
            main = [op for op in self.ops if op.idx not in self.bg_ids]
            b0 = [j for j, op in enumerate(main) if op.tag == "blk0"]
            lo, hi = b0[0], b0[-1]
            n = len(bg)
            slots = [lo + int((k + 1) * (hi - lo) / (n + 1)) for k in range(n)]
            order = []
            bi = 0
            for j, op in enumerate(main):
                order.append(op)
                while bi < n and slots[bi] <= j:
                    order.append(bg[bi]); bi += 1
            order.extend(bg[bi:])
        else:
            order = self.ops
        esem = {e: stack.enter_context(nc.semaphore(f"se_{e}")) for e in ENGINES}
        pools = {
            "sync": [stack.enter_context(nc.semaphore(f"sdy_{i}")) for i in range(24)],
            "scalar": [stack.enter_context(nc.semaphore(f"sdc_{i}")) for i in range(16)],
            "gpsimd": [stack.enter_context(nc.semaphore(f"sdg_{i}")) for i in range(24)],
        }
        csem = stack.enter_context(nc.semaphore("scoll"))
        duse = {q: [0] * len(pool) for q, pool in pools.items()}
        dnext = {q: 0 for q in pools}
        ccount = 0
        ecount = {e: 0 for e in ENGINES}
        for op in order:
            if op.kind == "comp" and op.idx not in need_inc:
                continue
            if op.kind == "comp":
                ecount[op.engine] += 1
                op.token = (esem[op.engine], ecount[op.engine])
                op.inc = (esem[op.engine], 1)
            elif op.kind == "coll":
                ccount += 1
                if ccount > 1:
                    op.pre_wait = (csem, ccount - 1)
                op.token = (csem, ccount)
                op.inc = (csem, 1)
            else:
                q = op.engine
                pool = pools[q]
                si = dnext[q] % len(pool); dnext[q] += 1
                duse[q][si] += 1
                if duse[q][si] > 1:
                    op.pre_wait = (pool[si], 16 * (duse[q][si] - 1))
                op.token = (pool[si], 16 * duse[q][si])
                op.inc = (pool[si], 16)
        guaranteed = {e: {} for e in ENGINES}
        engobj = {e: getattr(nc, e) for e in ENGINES}
        for op in order:
            g = guaranteed[op.engine]
            waits = {}
            if op.pre_wait is not None:
                s, v = op.pre_wait
                if g.get(s.name, 0) < v:
                    waits[s] = v
            for d in op.deps:
                tk = self.ops[d].token
                assert tk is not None, f"dep {d} has no token"
                s, v = tk
                if g.get(s.name, 0) < v:
                    if waits.get(s, 0) < v:
                        waits[s] = v
            for s, v in waits.items():
                engobj[op.engine].wait_ge(s, v)
                g[s.name] = v
            inst = op.build()
            if op.inc is not None:
                assert inst is not None, f"op {op.idx} ({op.tag}) needs inc but returned no instruction"
                inst.then_inc(*op.inc)
        return esem, pools


def keys(name, c0, c1, gran=512):
    return [(name, g) for g in range(c0 // gran, (c1 + gran - 1) // gran)]


class Ctx:
    pass


def build_program(stop_after=8, debug_dump=()):
    nc = bass.Bass("TRN2", target_bir_lowering=False, debug=False)
    import contextlib
    stack = contextlib.ExitStack()
    c = Ctx()
    c.nc = nc
    c.stack = stack
    p = Prog(nc)
    c.p = p

    def din(name, shape, dtype=F32):
        return nc.dram_tensor(name, shape, dtype, kind="ExternalInput").ap()

    c.x_T = din("x_T", [D, T])
    c.out_T = nc.dram_tensor("out_T", [D, T], F32, kind="ExternalOutput").ap()
    iw = {}
    c.iw = iw
    iw["a0_wxT"] = din("a0_wxT", [D, 2 * D])
    iw["a0_bx"] = din("a0_bx", [2 * D])
    iw["a0_A"] = din("a0_A", [D, 16])
    iw["a0_D"] = din("a0_D", [D])
    iw["a0_woT"] = din("a0_woT", [D, D])
    iw["a0_bo"] = din("a0_bo", [D])
    iw["a0_gamma"] = din("a0_gamma", [D])
    iw["a0_beta"] = din("a0_beta", [D])
    for nm in ("b0_wqT", "b0_wkT", "b0_wvT", "b0_woT"):
        iw[nm] = din(nm, [D, D])
    iw["b0_bo"] = din("b0_bo", [D])
    iw["b0_wspanT"] = din("b0_wspanT", [D, 16])
    iw["b0_bspan"] = din("b0_bspan", [16])
    iw["f0_rwT"] = din("f0_rwT", [D, 8])
    iw["f0_w1"] = din("f0_w1", [8 * D, 4096])
    iw["f0_b1"] = din("f0_b1", [8, 4096])
    iw["f0_w2"] = din("f0_w2", [8 * 4096, D])
    iw["f0_b2"] = din("f0_b2", [8, D])
    for nm in ("a1_wqT", "a1_wkT", "a1_wvT", "a1_wpT"):
        iw[nm] = din(nm, [D, D])
    iw["a1_bp"] = din("a1_bp", [D])
    iw["a1_wr"] = din("a1_wr", [16 * 64, 128])
    iw["a1_br"] = din("a1_br", [16, 128])
    for nm in ("b1_wqT", "b1_wkT", "b1_wvT"):
        iw[nm] = din(nm, [D, D])
    iw["b1_bq"] = din("b1_bq", [D])
    iw["b1_bk"] = din("b1_bk", [D])
    iw["b1_bv"] = din("b1_bv", [D])
    iw["b1_gamma"] = din("b1_gamma", [D])
    iw["b1_beta"] = din("b1_beta", [D])
    iw["f1_wgT"] = din("f1_wgT", [D, 8])
    iw["f1_w1"] = din("f1_w1", [8 * D, 4096])
    iw["f1_w2"] = din("f1_w2", [8 * 2048, D])
    iw["prevmask"] = din("prevmask", [8])

    c.X = nc.dram_tensor("Xs", [D, T], F32).ap()
    c.tT = nc.dram_tensor("tT", [D, T], F32).ap()
    c.xn32 = nc.dram_tensor("xn32", [D, T], F32).ap()
    c.ctxT = nc.dram_tensor("ctxT", [2 * D, T], BF16).ap()
    c.xtok = nc.dram_tensor("xtok", [T + 1, D], BF16).ap()
    c.yT = nc.dram_tensor("yT", [D, T], F32).ap()
    c.strips = nc.dram_tensor("strips", [16, 4608], BF16).ap()
    c.table = nc.dram_tensor("mtable", [11776], U32).ap()
    c.expout = nc.dram_tensor("expout", [11776, D], F32).ap()
    c.entE = nc.dram_tensor("entE", [8192], F32).ap()
    c.entG = nc.dram_tensor("entG", [8192], F32).ap()
    c.entS = nc.dram_tensor("entS", [8192], F32).ap()
    c.entK = nc.dram_tensor("entK", [8192], F32).ap()
    c.cc_p_in = nc.dram_tensor("cc_p_in", [32, 128], F32).ap()
    c.cc_e_in = nc.dram_tensor("cc_e_in", [32, 128], F32).ap()
    c.cc_p_out = nc.dram_tensor("cc_p_out", [NCORES * 32, 128], F32).ap()
    c.cc_e_out = nc.dram_tensor("cc_e_out", [NCORES * 32, 128], F32).ap()
    c.cc_cnt_in = nc.dram_tensor("cc_cnt_in", [1, 8], F32).ap()
    c.cc_cnt_out = nc.dram_tensor("cc_cnt_out", [NCORES, 8], F32).ap()
    c.w1b = [nc.dram_tensor("w1b0", [8 * D, 4096], BF16).ap(),
             nc.dram_tensor("w1b1", [8 * D, 4096], BF16).ap()]
    c.w2b = [nc.dram_tensor("w2b0", [8 * 4096, D], BF16).ap(),
             nc.dram_tensor("w2b1", [8 * 2048, D], BF16).ap()]
    c.moe_wconv_done = {0: False, 1: False}

    sb = lambda name, shape, dt: stack.enter_context(nc.sbuf_tensor(name, shape, dt)).ap()
    c.xnb = sb("xnb", [128, NKT * T], BF16)
    c.wst = sb("wst", [128, 16384], BF16)
    c.qkv = sb("qkv", [128, 8192], BF16)
    c.ework = sb("ework", [128, 8192], BF16)
    c.f32a = sb("f32a", [128, 4096], F32)
    c.f32b = sb("f32b", [128, 4096], F32)
    c.f32c = sb("f32c", [128, 4096], F32)
    c.cols = sb("cols", [128, 1024], F32)
    c.rows = sb("rows", [1, 1664], F32)
    c.idxs = sb("idxs", [128, 64], U32)
    c.moeS = sb("moeS", [128, 512], F32)
    c.att = sb("att", [128, 2560], BF16)
    c.i512 = sb("i512", [128, 1024], I32)
    c.i32s = sb("i32s", [128, 64], I32)
    c.consts = sb("consts", [128, 416], F32)
    c.cbf = sb("cbf", [128, 512], BF16)

    c.ps = [stack.enter_context(nc.psum_tensor(f"p{i}", [128, 512], F32)).ap() for i in range(6)]
    c.qs = [stack.enter_context(nc.psum_tensor(f"q{i}", [128, 1024], BF16)).ap() for i in range(2)]

    emit_consts(c)
    emit_xtok_zero_row(c)
    emit_init_X(c)

    c.wb_keys = {}
    p.bg = True
    emit_moe_wconv(c, 0)
    emit_moe_wconv(c, 1)
    p.bg = False
    c.moe_wconv_done = {0: True, 1: True}

    blocks = [
        emit_l0_attn_a,
        lambda c: emit_moe(c, layer=0),
        emit_l0_attn_b,
        lambda c: emit_moe(c, layer=0),
        emit_l1_attn_a,
        lambda c: emit_moe(c, layer=1),
        emit_l1_attn_b,
        lambda c: emit_moe(c, layer=1),
    ]
    for bi in range(stop_after):
        c.p.tag = f"blk{bi}"
        c.blk = bi
        blocks[bi](c)

    emit_finish(c)

    ndbg = 0
    for nm in debug_dump:
        ap = getattr(c, nm)
        h, w = ap.shape
        dt = ap.dtype
        dbg = nc.dram_tensor(f"dbg_{nm}", [h, w], dt, kind="ExternalOutput").ap()
        rkeys = [(nm, i) for i in range(200)]
        nrow = (h + 127) // 128
        for r in range(nrow):
            r0, r1 = r * 128, min(h, (r + 1) * 128)
            for mcol in range(0, w, 4096):
                m1 = min(w, mcol + 4096)
                src_ = ap[r0:r1, mcol:m1]
                dst_ = dbg[r0:r1, mcol:m1]
                p.dma("gpsimd", lambda src_=src_, dst_=dst_: nc.gpsimd.dma_start(out=dst_, in_=src_),
                      reads=rkeys, writes=[("dbgout", ndbg, r, mcol)])
                ndbg += 1
    if debug_dump:
        p.comp("vector", lambda: nc.vector.engine_nop(),
               reads=[k for k in p.last_w if k[0] == "dbgout"])

    p.emit(stack)
    stack.close()
    return nc


def constcell(c, val):
    if not hasattr(c, "_cells"):
        c._cells = {}
    if val not in c._cells:
        i = 960 + len(c._cells)
        assert i < 1024, "out of const cells"
        ap = c.cols[:, i:i + 1]
        g = c.nc.gpsimd
        c.p.comp("gpsimd", lambda ap=ap, val=val: g.memset(ap, float(val)), writes=[("cols", 1)])
        c._cells[val] = ap
    return c._cells[val]


def constrow(c, val):
    return constcell(c, val)[0:1, :]


def emit_consts(c):
    nc, p = c.nc, c.p
    CON = c.consts
    c.idF = CON[:, 0:128]
    c.Lst = CON[:, 128:256]
    c.Lin = CON[:, 256:384]
    c.onescol = CON[:, 384:385]
    c.iotaP = CON[:, 385:386]
    c.zcol = CON[:, 386:387]
    c.zeros512 = CON[:, 387:899] if False else None
    c.idB = c.cbf[:, 0:128]
    c.onescolB = c.cbf[:, 128:129]
    g = nc.gpsimd
    p.comp("gpsimd", lambda: g.memset(CON[:, 0:387], 0.0), writes=[("consts", 0)])
    p.comp("gpsimd", lambda: g.affine_select(out=c.idF, in_=c.idF, compare_op=ALU.not_equal,
                                             fill=1.0, base=0, pattern=[[-1, 128]], channel_multiplier=1),
           reads=[("consts", 0)], writes=[("consts", 0)])
    p.comp("gpsimd", lambda: g.memset(c.Lst, 0.0), writes=[("consts", 0)])
    p.comp("gpsimd", lambda: g.affine_select(out=c.Lst, in_=c.Lst, compare_op=ALU.is_ge,
                                             fill=1.0, base=0, pattern=[[-1, 128]], channel_multiplier=1),
           reads=[("consts", 0)], writes=[("consts", 0)])
    p.comp("gpsimd", lambda: g.memset(c.Lin, 0.0), writes=[("consts", 0)])
    p.comp("gpsimd", lambda: g.affine_select(out=c.Lin, in_=c.Lin, compare_op=ALU.is_ge,
                                             fill=1.0, base=-1, pattern=[[-1, 128]], channel_multiplier=1),
           reads=[("consts", 0)], writes=[("consts", 0)])
    p.comp("gpsimd", lambda: g.memset(c.onescol, 1.0), writes=[("consts", 0)])
    p.comp("gpsimd", lambda: g.iota(c.iotaP, [[0, 1]], base=0, channel_multiplier=1,
                                    allow_small_or_imprecise_dtypes=True), writes=[("consts", 0)])
    p.comp("gpsimd", lambda: g.memset(c.cbf[:, 0:512], 0.0), writes=[("cbf", 0)])
    p.comp("gpsimd", lambda: g.affine_select(out=c.idB, in_=c.idB, compare_op=ALU.not_equal,
                                             fill=1.0, base=0, pattern=[[-1, 128]], channel_multiplier=1),
           reads=[("cbf", 0)], writes=[("cbf", 0)])
    p.comp("gpsimd", lambda: g.memset(c.onescolB, 1.0), writes=[("cbf", 0)])
    c.io8 = CON[:, 387:395]
    p.comp("gpsimd", lambda: g.iota(c.io8, [[1, 8]], base=0, channel_multiplier=0,
                                    allow_small_or_imprecise_dtypes=True), writes=[("consts", 0)])
    c.onesrow = c.rows[0:1, 1536:1664]
    p.comp("gpsimd", lambda: g.memset(c.onesrow, 1.0), writes=[("rows", 3)])


def emit_xtok_zero_row(c):
    nc, p = c.nc, c.p
    z = c.ework[0:1, 0:1024]
    p.comp("vector", lambda: nc.vector.memset(z, 0.0), writes=keys("ework", 0, 1024))
    p.dma("sync", lambda: nc.sync.dma_start(out=c.xtok[T:T + 1, :], in_=z),
          reads=keys("ework", 0, 1024), writes=[("xtokz",)])


def emit_init_X(c):
    nc, p = c.nc, c.p
    for kt in range(NKT):
        for m in range(NTM):
            src = c.x_T[kt * 128:(kt + 1) * 128, m * 512:(m + 1) * 512]
            dst = c.X[kt * 128:(kt + 1) * 128, m * 512:(m + 1) * 512]
            buf = c.f32a[:, m * 512:(m + 1) * 512]
            p.dma("sync", lambda src=src, buf=buf: nc.sync.dma_start(out=buf, in_=src),
                  writes=[("f32a", m)])
            p.dma("sync", lambda dst=dst, buf=buf: nc.sync.dma_start(out=dst, in_=buf),
                  reads=[("f32a", m)], writes=[("X", kt * 8 + m)])


def emit_ln(c, src, srckey, gamma=None, beta=None, dst_sbuf=True, dst32=None, dst32key=None,
            src_is_sbuf=False):
    nc, p = c.nc, c.p
    v, s, t = nc.vector, nc.scalar, nc.tensor
    for m in range(NTM):
        for kt in range(NKT):
            buf = c.f32a[:, kt * 512:(kt + 1) * 512]
            if src_is_sbuf:
                srcap = src(kt, m)
                p.comp("vector", lambda b=buf, a=srcap: v.tensor_copy(b, a),
                       reads=keys(srckey, (kt * 8 + m) * 512, (kt * 8 + m) * 512 + 512),
                       writes=[("f32a", kt)])
            else:
                srcap = src[kt * 128:(kt + 1) * 128, m * 512:(m + 1) * 512]
                p.dma("sync", lambda b=buf, a=srcap: nc.sync.dma_start(out=b, in_=a),
                      reads=[(srckey, kt * 8 + m)], writes=[("f32a", kt)])
        for kt in range(NKT):
            buf = c.f32a[:, kt * 512:(kt + 1) * 512]
            sq = c.f32c[:, kt * 512:(kt + 1) * 512]
            p.comp("scalar", lambda sq=sq, b=buf: s.activation(sq, b, AF.Square, bias=0.0, scale=1.0),
                   reads=[("f32a", kt)], writes=[("f32c", kt)])
            p.comp("tensor", lambda b=buf, kt=kt: t.matmul(c.ps[4][0:1, :], c.onescol, b,
                                                           start=(kt == 0), stop=(kt == NKT - 1)),
                   reads=[("f32a", kt), ("consts", 0)], writes=[("p4",)])
            p.comp("tensor", lambda sq=sq, kt=kt: t.matmul(c.ps[5][0:1, :], c.onescol, sq,
                                                           start=(kt == 0), stop=(kt == NKT - 1)),
                   reads=[("f32c", kt), ("consts", 0)], writes=[("p5",)])
        r0 = c.rows[0:1, 0:512]; r1 = c.rows[0:1, 512:1024]; r2 = c.rows[0:1, 1024:1536]
        p.comp("vector", lambda: v.tensor_single_scalar(r0, c.ps[4][0:1, :], 1.0 / D, ALU.mult),
               reads=[("p4",)], writes=[("rows", 0)])
        p.comp("vector", lambda: v.tensor_single_scalar(r1, c.ps[5][0:1, :], 1.0 / D, ALU.mult),
               reads=[("p5",)], writes=[("rows", 1)])
        p.comp("vector", lambda: v.scalar_tensor_tensor(r2, r0, 1.0, r0, ALU.mult, ALU.mult),
               reads=[("rows", 0)], writes=[("rows", 2)])
        p.comp("vector", lambda: v.tensor_sub(r1, r1, r2),
               reads=[("rows", 1), ("rows", 2)], writes=[("rows", 1)])
        epsr = constrow(c, 1e-5)
        p.comp("scalar", lambda epsr=epsr: s.activation(r2, r1, AF.Ln, bias=epsr, scale=1.0),
               reads=[("rows", 1), ("cols", 1)], writes=[("rows", 2)])
        p.comp("scalar", lambda: s.activation(r1, r2, AF.Exp, bias=0.0, scale=-0.5),
               reads=[("rows", 2)], writes=[("rows", 1)])
        p.comp("vector", lambda: v.tensor_single_scalar(r0, r0, -1.0, ALU.mult),
               reads=[("rows", 0)], writes=[("rows", 0)])
        p.comp("tensor", lambda: t.matmul(c.ps[4], c.onesrow, r0, start=True, stop=True),
               reads=[("rows", 0), ("rows", 3)], writes=[("p4",)])
        p.comp("tensor", lambda: t.matmul(c.ps[5], c.onesrow, r1, start=True, stop=True),
               reads=[("rows", 1), ("rows", 3)], writes=[("p5",)])
        for kt in range(NKT):
            buf = c.f32a[:, kt * 512:(kt + 1) * 512]
            xb = c.f32b[:, kt * 512:(kt + 1) * 512]
            p.comp("vector", lambda xb=xb, b=buf: v.tensor_add(xb, b, c.ps[4]),
                   reads=[("f32a", kt), ("p4",)], writes=[("f32b", kt)])
            p.comp("vector", lambda xb=xb: v.tensor_mul(xb, xb, c.ps[5]),
                   reads=[("f32b", kt), ("p5",)], writes=[("f32b", kt)])
            if gamma is not None:
                gcol = gamma(kt); bcol = beta(kt)
                p.comp("vector", lambda xb=xb, gcol=gcol, bcol=bcol:
                       v.tensor_scalar(xb, xb, gcol, bcol, ALU.mult, ALU.add),
                       reads=[("f32b", kt), ("cols", 0)], writes=[("f32b", kt)])
            if dst_sbuf:
                xnb = c.xnb[:, kt * T + m * 512: kt * T + (m + 1) * 512]
                p.comp("scalar", lambda xnb=xnb, xb=xb: s.copy(xnb, xb),
                       reads=[("f32b", kt)], writes=keys("xnb", (kt * 8 + m) * 512, (kt * 8 + m) * 512 + 512))
            if dst32 is not None:
                dap = dst32[kt * 128:(kt + 1) * 128, m * 512:(m + 1) * 512]
                p.dma("sync", lambda dap=dap, xb=xb: nc.sync.dma_start(out=dap, in_=xb),
                      reads=[("f32b", kt)], writes=[(dst32key, kt * 8 + m)])


def emit_proj_residual(c, w_dram, Kdim, bias_dram=None, extra_dram=None, extra_key=None,
                       dst=None, dstkey=None, ctx_dram=None, ctx_key="ctxT", scale_out=None):
    nc, p = c.nc, c.p
    v, s, t = nc.vector, nc.scalar, nc.tensor
    if dst is None:
        dst = c.X; dstkey = "X"
    if ctx_dram is None:
        ctx_dram = c.ctxT
    KT = Kdim // 128
    for kt in range(KT):
        wap = w_dram[kt * 128:(kt + 1) * 128, :]
        wsb = c.wst[:, kt * 1024:(kt + 1) * 1024]
        p.dma("gpsimd", lambda wsb=wsb, wap=wap: nc.gpsimd.dma_start(out=wsb, in_=wap),
              writes=keys("wst", kt * 1024, (kt + 1) * 1024))
    if bias_dram is not None:
        emit_bias_cols(c, bias_dram, 8, 0)
    for m in range(NTM):
        for kt in range(KT):
            cap_ = ctx_dram[kt * 128:(kt + 1) * 128, m * 512:(m + 1) * 512]
            esb = c.ework[:, kt * 512:(kt + 1) * 512]
            p.dma("sync", lambda esb=esb, cap_=cap_: nc.sync.dma_start(out=esb, in_=cap_),
                  reads=[(ctx_key, kt * 8 + m)], writes=[("ework", kt)])
        for o in range(8):
            pb = o % 4
            pskey = (f"p{pb}",)
            for kt in range(KT):
                lhs = c.wst[:, kt * 1024 + o * 128: kt * 1024 + o * 128 + 128]
                rhs = c.ework[:, kt * 512:(kt + 1) * 512]
                p.comp("tensor", lambda lhs=lhs, rhs=rhs, pb=pb, kt=kt, KT=KT:
                       t.matmul(c.ps[pb], lhs, rhs, start=(kt == 0), stop=(kt == KT - 1)),
                       reads=keys("wst", kt * 1024, (kt + 1) * 1024) + [("ework", kt)],
                       writes=[pskey])
            xtile = c.f32a[:, o * 512:(o + 1) * 512]
            Xap = dst[o * 128:(o + 1) * 128, m * 512:(m + 1) * 512]
            Xsrc = c.X[o * 128:(o + 1) * 128, m * 512:(m + 1) * 512]
            acc = c.f32c[:, o * 512:(o + 1) * 512]
            p.dma("scalar", lambda xtile=xtile, Xsrc=Xsrc: nc.scalar.dma_start(out=xtile, in_=Xsrc),
                  reads=[("X", o * 8 + m)], writes=[("f32a", o)])
            if bias_dram is not None:
                bcol = c.cols[:, o:o + 1]
                p.comp("scalar", lambda acc=acc, pb=pb, bcol=bcol:
                       s.activation(acc, c.ps[pb], AF.Identity, bias=bcol, scale=1.0),
                       reads=[pskey, ("cols", 0)], writes=[("f32c", o)])
            else:
                p.comp("scalar", lambda acc=acc, pb=pb: s.copy(acc, c.ps[pb]),
                       reads=[pskey], writes=[("f32c", o)])
            if scale_out is not None:
                p.comp("vector", lambda acc=acc: v.tensor_single_scalar(acc, acc, scale_out, ALU.mult),
                       reads=[("f32c", o)], writes=[("f32c", o)])
            p.comp("vector", lambda acc=acc, xtile=xtile: v.tensor_add(acc, acc, xtile),
                   reads=[("f32c", o), ("f32a", o)], writes=[("f32c", o)])
            if extra_dram is not None:
                eap = extra_dram[o * 128:(o + 1) * 128, m * 512:(m + 1) * 512]
                etile = c.f32b[:, o * 512:(o + 1) * 512]
                p.dma("scalar", lambda etile=etile, eap=eap: nc.scalar.dma_start(out=etile, in_=eap),
                      reads=[(extra_key, o * 8 + m)], writes=[("f32b", o)])
                p.comp("vector", lambda acc=acc, etile=etile: v.tensor_add(acc, acc, etile),
                       reads=[("f32c", o), ("f32b", o)], writes=[("f32c", o)])
            p.dma("sync", lambda Xap=Xap, acc=acc: nc.sync.dma_start(out=Xap, in_=acc),
                  reads=[("f32c", o)], writes=[(dstkey, o * 8 + m)])


def emit_bias_cols(c, bias_dram, ntiles, col0):
    nc, p = c.nc, c.p
    v, t = nc.vector, nc.tensor
    bap = bias_dram.rearrange("(n p) -> n p", p=128)
    tmp = c.f32b[0:ntiles, 0:128]
    p.dma("sync", lambda: nc.sync.dma_start(out=tmp, in_=bap), writes=[("f32b", 0)])
    p.comp("tensor", lambda: t.transpose(c.ps[5][:, 0:ntiles], tmp, c.idF[0:ntiles, 0:ntiles]),
           reads=[("f32b", 0), ("consts", 0)], writes=[("p5",)])
    p.comp("vector", lambda: v.tensor_copy(c.cols[:, col0:col0 + ntiles], c.ps[5][:, 0:ntiles]),
           reads=[("p5",)], writes=[("cols", 0)])


def emit_l0_attn_a(c):
    nc, p = c.nc, c.p
    v, s, t = nc.vector, nc.scalar, nc.tensor
    iw = c.iw
    emit_bias_cols(c, iw["a0_gamma"], 8, 8)
    emit_bias_cols(c, iw["a0_beta"], 8, 16)
    emit_bias_cols(c, iw["a0_D"], 8, 24)
    Aap = iw["a0_A"].rearrange("(kt p) s -> p kt s", p=128)
    p.dma("sync", lambda: nc.sync.dma_start(out=c.cols[:, 32:160], in_=Aap, allow_slow_non_contiguous=True),
          writes=[("cols", 0)])
    emit_ln(c, c.X, "X", dst_sbuf=False, dst32=c.tT, dst32key="tT")
    emit_ln(c, c.tT, "tT", gamma=lambda kt: c.cols[:, 8 + kt:9 + kt],
            beta=lambda kt: c.cols[:, 16 + kt:17 + kt], dst_sbuf=True, dst32=c.xn32, dst32key="xn32")
    for kt in range(NKT):
        wap = iw["a0_wxT"][kt * 128:(kt + 1) * 128, :]
        wsb = c.wst[:, kt * 2048:(kt + 1) * 2048]
        p.dma("gpsimd", lambda wsb=wsb, wap=wap: nc.gpsimd.dma_start(out=wsb, in_=wap),
              writes=keys("wst", kt * 2048, (kt + 1) * 2048))
    emit_bias_cols16(c, iw["a0_bx"], 16, 160)

    for i in range(8):
        for m in range(NTM):
            pb = m % 2
            for kt in range(NKT):
                lhs = c.wst[:, kt * 2048 + i * 128: kt * 2048 + i * 128 + 128]
                rhs = c.xnb[:, kt * T + m * 512: kt * T + (m + 1) * 512]
                p.comp("tensor", lambda lhs=lhs, rhs=rhs, pb=pb, kt=kt:
                       t.matmul(c.ps[pb], lhs, rhs, start=(kt == 0), stop=(kt == NKT - 1)),
                       reads=keys("wst", kt * 2048, (kt + 1) * 2048) +
                             keys("xnb", (kt * 8 + m) * 512, (kt * 8 + m) * 512 + 512),
                       writes=[(f"p{pb}",)])
            dtile = c.f32a[:, m * 512:(m + 1) * 512]
            u = c.f32c[:, m * 512:(m + 1) * 512]
            bxcol = c.cols[:, 160 + i:161 + i]
            p.comp("scalar", lambda dtile=dtile, pb=pb, bxcol=bxcol:
                   s.activation(dtile, c.ps[pb], AF.Identity, bias=bxcol, scale=1.0),
                   reads=[(f"p{pb}",), ("cols", 0)], writes=[("f32a", m)])
            p.comp("scalar", lambda u=u, dtile=dtile: s.activation(u, dtile, AF.Abs, bias=0.0, scale=1.0),
                   reads=[("f32a", m)], writes=[("f32c", m)])
            p.comp("scalar", lambda u=u: s.activation(u, u, AF.Exp, bias=0.0, scale=-1.0),
                   reads=[("f32c", m)], writes=[("f32c", m)])
            onec = constcell(c, 1.0)
            p.comp("scalar", lambda u=u, onec=onec: s.activation(u, u, AF.Ln, bias=onec, scale=1.0),
                   reads=[("f32c", m), ("cols", 1)], writes=[("f32c", m)])
            p.comp("vector", lambda dtile=dtile: v.tensor_single_scalar(dtile, dtile, 0.0, ALU.max),
                   reads=[("f32a", m)], writes=[("f32a", m)])
            p.comp("vector", lambda dtile=dtile, u=u: v.tensor_add(dtile, dtile, u),
                   reads=[("f32a", m), ("f32c", m)], writes=[("f32a", m)])
        for si in range(16):
            acol = c.cols[:, 32 + i * 16 + si: 33 + i * 16 + si]
            for hf in range(2):
                e = c.f32c[:, hf * 2048:(hf + 1) * 2048]
                src_ = c.f32a[:, hf * 2048:(hf + 1) * 2048]
                acc = c.f32b[:, hf * 2048:(hf + 1) * 2048]
                rk = [("f32a", mm) for mm in range(hf * 4, hf * 4 + 4)]
                ek = [("f32c", mm) for mm in range(hf * 4, hf * 4 + 4)]
                ak = [("f32b", mm) for mm in range(hf * 4, hf * 4 + 4)]
                p.comp("scalar", lambda e=e, src_=src_, acol=acol:
                       s.activation(e, src_, AF.Exp, bias=0.0, scale=acol),
                       reads=rk + [("cols", 0)], writes=ek)
                if si == 0:
                    p.comp("vector", lambda acc=acc, e=e: v.tensor_copy(acc, e),
                           reads=ek, writes=ak)
                else:
                    p.comp("vector", lambda acc=acc, e=e: v.tensor_add(acc, acc, e),
                           reads=ek + ak, writes=ak)
        for m in range(NTM):
            pb = 2 + (m % 2)
            for kt in range(NKT):
                lhs = c.wst[:, kt * 2048 + (8 + i) * 128: kt * 2048 + (8 + i) * 128 + 128]
                rhs = c.xnb[:, kt * T + m * 512: kt * T + (m + 1) * 512]
                p.comp("tensor", lambda lhs=lhs, rhs=rhs, pb=pb, kt=kt:
                       t.matmul(c.ps[pb], lhs, rhs, start=(kt == 0), stop=(kt == NKT - 1)),
                       reads=keys("wst", kt * 2048, (kt + 1) * 2048) +
                             keys("xnb", (kt * 8 + m) * 512, (kt * 8 + m) * 512 + 512),
                       writes=[(f"p{pb}",)])
            bg = c.f32c[:, m * 512:(m + 1) * 512]
            bxcol = c.cols[:, 160 + 8 + i:161 + 8 + i]
            p.comp("scalar", lambda bg=bg, pb=pb, bxcol=bxcol:
                   s.activation(bg, c.ps[pb], AF.Identity, bias=bxcol, scale=1.0),
                   reads=[(f"p{pb}",), ("cols", 0)], writes=[("f32c", m)])
            xnap = c.xn32[i * 128:(i + 1) * 128, m * 512:(m + 1) * 512]
            xt = c.f32a[:, m * 512:(m + 1) * 512]
            p.dma("scalar", lambda xt=xt, xnap=xnap: nc.scalar.dma_start(out=xt, in_=xnap),
                  reads=[("xn32", i * 8 + m)], writes=[("f32a", m)])
            p.comp("vector", lambda bg=bg, xt=xt: v.tensor_mul(bg, bg, xt),
                   reads=[("f32c", m), ("f32a", m)], writes=[("f32c", m)])
        hrow = c.f32c[:, 0:4096]
        p.comp("vector", lambda: v.tensor_tensor_scan(hrow, hrow, hrow, 0.0, ALU.add, ALU.bypass),
               reads=[("f32c", mm) for mm in range(8)], writes=[("f32c", mm) for mm in range(8)])
        p.comp("vector", lambda: v.tensor_mul(hrow, hrow, c.f32b[:, 0:4096]),
               reads=[("f32c", mm) for mm in range(8)] + [("f32b", mm) for mm in range(8)],
               writes=[("f32c", mm) for mm in range(8)])
        dcol = c.cols[:, 24 + i:25 + i]
        p.comp("vector", lambda dcol=dcol: v.scalar_tensor_tensor(hrow, c.f32a[:, 0:4096], dcol, hrow,
                                                                  ALU.mult, ALU.add),
               reads=[("f32a", mm) for mm in range(8)] + [("f32c", mm) for mm in range(8)] + [("cols", 0)],
               writes=[("f32c", mm) for mm in range(8)])
        for m in range(NTM):
            src = c.f32c[:, m * 512:(m + 1) * 512]
            eb = c.ework[:, m * 512:(m + 1) * 512]
            dap = c.ctxT[i * 128:(i + 1) * 128, m * 512:(m + 1) * 512]
            p.comp("scalar", lambda eb=eb, src=src: s.copy(eb, src),
                   reads=[("f32c", m)], writes=[("ework", m)])
            p.dma("sync", lambda dap=dap, eb=eb: nc.sync.dma_start(out=dap, in_=eb),
                  reads=[("ework", m)], writes=[("ctxT", i * 8 + m)])
    emit_proj_residual(c, c.iw["a0_woT"], D, bias_dram=c.iw["a0_bo"],
                       extra_dram=c.tT, extra_key="tT")


def emit_bias_cols16(c, bias_dram, ntiles, col0):
    nc, p = c.nc, c.p
    v, t = nc.vector, nc.tensor
    done = 0
    while done < ntiles:
        n = min(8, ntiles - done)
        bap = bias_dram.rearrange("(n p) -> n p", p=128)[done:done + n, :]
        tmp = c.f32b[0:n, 0:128]
        p.dma("sync", lambda bap=bap, tmp=tmp: nc.sync.dma_start(out=tmp, in_=bap), writes=[("f32b", 0)])
        p.comp("tensor", lambda tmp=tmp, n=n: t.transpose(c.ps[5][:, 0:n], tmp, c.idF[0:n, 0:n]),
               reads=[("f32b", 0), ("consts", 0)], writes=[("p5",)])
        p.comp("vector", lambda n=n, done=done: v.tensor_copy(c.cols[:, col0 + done:col0 + done + n],
                                                              c.ps[5][:, 0:n]),
               reads=[("p5",)], writes=[("cols", 0)])
        done += n


def mk(c0, c1):
    return keys("moeS", c0, c1, gran=32)


def emit_moe_wconv(c, layer):
    nc, p = c.nc, c.p
    w1 = c.iw[f"f{layer}_w1"]; w2 = c.iw[f"f{layer}_w2"]
    kl = []
    for (wsrc, wdst, rows) in ((w1, c.w1b[layer], 8 * D),
                               (w2, c.w2b[layer], (32768 if layer == 0 else 16384))):
        rt = rows // 16
        for r in range(16):
            sap = wsrc[r * rt:(r + 1) * rt, :]
            dap = wdst[r * rt:(r + 1) * rt, :]
            k = ("wb", layer, len(kl))
            kl.append(k)
            p.dma("gpsimd", lambda dap=dap, sap=sap: nc.gpsimd.dma_start(out=dap, in_=sap),
                  writes=[k])
    c.wb_keys[layer] = kl


def emit_xtok(c):
    nc, p = c.nc, c.p
    v, t = nc.vector, nc.tensor
    for tt in range(T // 128):
        qb = tt % 2
        qk = (f"q{qb}",)
        stage = c.ework[:, (tt % 2) * 1024:(tt % 2) * 1024 + 1024]
        sk = keys("ework", (tt % 2) * 1024, (tt % 2) * 1024 + 1024)
        for kt in range(NKT):
            xs = c.xnb[:, kt * T + tt * 128: kt * T + tt * 128 + 128]
            xk = keys("xnb", kt * T + tt * 128, kt * T + tt * 128 + 128, gran=512)
            p.comp("tensor", lambda xs=xs, qb=qb, kt=kt: t.transpose(c.qs[qb][:, kt * 128:(kt + 1) * 128], xs, c.idB),
                   reads=xk + [("cbf", 0)], writes=[qk])
        p.comp("vector", lambda stage=stage, qb=qb: v.tensor_copy(stage, c.qs[qb][:, 0:1024]),
               reads=[qk], writes=sk)
        dap = c.xtok[tt * 128:(tt + 1) * 128, :]
        p.dma("sync", lambda dap=dap, stage=stage: nc.sync.dma_start(out=dap, in_=stage),
              reads=sk, writes=[("xtok", tt // 4)])


MOE_STOP = 99

def emit_moe(c, layer):
    nc, p = c.nc, c.p
    v, s, t, g = nc.vector, nc.scalar, nc.tensor, nc.gpsimd
    NMAC = 15 if layer == 0 else 23
    NSLOT = NMAC * 512
    F = 4096
    KF = 32 if layer == 0 else 16
    CAP = 5120 if layer == 0 else 10240
    for ly in (0, 1):
        if not c.moe_wconv_done[ly]:
            emit_moe_wconv(c, ly)
            c.moe_wconv_done[ly] = True
    emit_ln(c, c.X, "X", dst_sbuf=True)
    emit_xtok(c)
    rw = c.iw["f0_rwT" if layer == 0 else "f1_wgT"]
    rwsb = c.cbf[:, 256:320]
    rap = rw.rearrange("(kt p) e -> p kt e", p=128)
    p.dma("gpsimd", lambda: g.dma_start(out=rwsb, in_=rap), writes=[("cbf", 0)])

    for tt in range(32):
        pbank = 4 + tt % 2
        pk = (f"p{pbank}",)
        for kt in range(NKT):
            lhs = c.xnb[:, kt * T + tt * 128: kt * T + tt * 128 + 128]
            xk = keys("xnb", kt * T + tt * 128, kt * T + tt * 128 + 128, gran=512)
            rhs = rwsb[:, kt * 8:(kt + 1) * 8]
            p.comp("tensor", lambda lhs=lhs, rhs=rhs, kt=kt, pbank=pbank:
                   t.matmul(c.ps[pbank][:, 0:8], lhs, rhs, start=(kt == 0), stop=(kt == NKT - 1)),
                   reads=xk + [("cbf", 0)], writes=[pk])
        o = (tt % 2) * 152
        lg = c.moeS[:, 144 + o:152 + o]
        p.comp("vector", lambda lg=lg, pbank=pbank: v.tensor_copy(lg, c.ps[pbank][:, 0:8]),
               reads=[pk], writes=mk(144 + o, 152 + o))
        mw = c.moeS[:, 128 + o:136 + o]
        miu = c.idxs[:, 32 + (tt % 2) * 8:40 + (tt % 2) * 8]
        mik = ("idxs", 32 + (tt % 2) * 8)
        p.comp("vector", lambda lg=lg, mw=mw: v.max(mw, lg),
               reads=mk(144 + o, 152 + o), writes=mk(128 + o, 136 + o))
        p.comp("vector", lambda lg=lg, mw=mw, miu=miu: v.max_index(miu, mw, lg),
               reads=mk(144 + o, 152 + o) + mk(128 + o, 136 + o), writes=[mik])
        if layer == 0:
            nm1 = c.moeS[:, 159 + o:160 + o]
            p.comp("vector", lambda nm1=nm1, mw=mw: v.tensor_single_scalar(nm1, mw[:, 0:1], -1.0, ALU.mult),
                   reads=mk(128 + o, 136 + o), writes=mk(159 + o, 160 + o))
            ex = c.moeS[:, 136 + o:144 + o]; se = c.moeS[:, 240 + o:241 + o]
            p.comp("scalar", lambda ex=ex, lg=lg, nm1=nm1, se=se:
                   s.activation(ex, lg, AF.Exp, bias=nm1, scale=1.0, accum_out=se),
                   reads=mk(144 + o, 152 + o) + mk(159 + o, 160 + o), writes=mk(136 + o, 144 + o) + mk(240 + o, 241 + o))
            p1 = c.moeS[:, tt:tt + 1]
            p.comp("vector", lambda p1=p1, se=se: v.reciprocal(p1, se),
                   reads=mk(240 + o, 241 + o), writes=mk(tt, tt + 1))
            e1 = c.moeS[:, 32 + tt:33 + tt]
            p.comp("vector", lambda e1=e1, miu=miu: v.tensor_copy(e1, miu[:, 0:1]),
                   reads=[mik], writes=mk(32 + tt, 33 + tt))
            pap = c.cc_p_in[tt, :]
            eap = c.cc_e_in[tt, :]
            p.dma("sync", lambda pap=pap, p1=p1: nc.sync.dma_start(out=pap, in_=p1[:, 0]),
                  reads=mk(tt, tt + 1), writes=[("ccpin", tt // 8)])
            p.dma("sync", lambda eap=eap, e1=e1: nc.sync.dma_start(out=eap, in_=e1[:, 0]),
                  reads=mk(32 + tt, 33 + tt), writes=[("ccein", tt // 8)])
        else:
            dd = c.moeS[:, 154 + o:155 + o]; g1 = c.moeS[:, 155 + o:156 + o]; g2 = c.moeS[:, 156 + o:157 + o]
            p.comp("vector", lambda dd=dd, mw=mw: v.tensor_sub(dd, mw[:, 0:1], mw[:, 1:2]),
                   reads=mk(128 + o, 136 + o), writes=mk(154 + o, 155 + o))
            p.comp("scalar", lambda g1=g1, dd=dd: s.activation(g1, dd, AF.Sigmoid, bias=0.0, scale=1.0),
                   reads=mk(154 + o, 155 + o), writes=mk(155 + o, 156 + o))
            p.comp("scalar", lambda g2=g2, dd=dd: s.activation(g2, dd, AF.Sigmoid, bias=0.0, scale=-1.0),
                   reads=mk(154 + o, 155 + o), writes=mk(156 + o, 157 + o))
            pr = c.moeS[:, 240 + o:242 + o]; gr = c.moeS[:, 242 + o:244 + o]
            p.comp("vector", lambda pr=pr, miu=miu: v.tensor_copy(pr, miu[:, 0:2]),
                   reads=[mik], writes=mk(240 + o, 242 + o))
            p.comp("vector", lambda gr=gr, g1=g1: v.tensor_copy(gr[:, 0:1], g1), reads=mk(155 + o, 156 + o), writes=mk(242 + o, 243 + o))
            p.comp("vector", lambda gr=gr, g2=g2: v.tensor_copy(gr[:, 1:2], g2), reads=mk(156 + o, 157 + o), writes=mk(243 + o, 244 + o))
            eap = c.entE[tt * 256:(tt + 1) * 256]
            gap = c.entG[tt * 256:(tt + 1) * 256]
            p.dma("sync", lambda eap=eap, pr=pr: nc.sync.dma_start(out=eap, in_=pr),
                  reads=mk(240 + o, 242 + o), writes=[("entE", tt // 8)])
            p.dma("sync", lambda gap=gap, gr=gr: nc.sync.dma_start(out=gap, in_=gr),
                  reads=mk(242 + o, 244 + o), writes=[("entG", tt // 8)])

    if MOE_STOP < 2:
        return
    rkeys = [("rows", 0)]
    if layer == 0:
        p.coll(lambda: g.collective_compute("AllGather", ALU.bypass, [list(range(NCORES))],
                                            [c.cc_p_in[:, :].opt()], [c.cc_p_out[:, :].opt()]),
               reads=[("ccpin", i) for i in range(4)], writes=[("ccpout",)])
        p.coll(lambda: g.collective_compute("AllGather", ALU.bypass, [list(range(NCORES))],
                                            [c.cc_e_in[:, :].opt()], [c.cc_e_out[:, :].opt()]),
               reads=[("ccein", i) for i in range(4)], writes=[("cceout",)])
        PT = c.f32a[:, 0:256]; ET = c.f32a[:, 256:512]
        p.dma("sync", lambda: nc.sync.dma_start(out=PT, in_=c.cc_p_out.rearrange("(p a) j -> p (a j)", p=128)),
              reads=[("ccpout",)], writes=[("f32a", 0)])
        p.dma("sync", lambda: nc.sync.dma_start(out=ET, in_=c.cc_e_out.rearrange("(p a) j -> p (a j)", p=128)),
              reads=[("cceout",)], writes=[("f32a", 0)])
        for e in range(8):
            seg = c.f32b[:, e * 256:(e + 1) * 256]
            msk = c.f32c[:, 0:256]
            p.comp("vector", lambda msk=msk, e=e: v.tensor_single_scalar(msk, ET, float(e), ALU.is_equal),
                   reads=[("f32a", 0)], writes=[("f32c", 0)])
            p.comp("vector", lambda seg=seg: v.tensor_single_scalar(seg, PT, 1.0, ALU.add),
                   reads=[("f32a", 0)], writes=[("f32b", e // 2)])
            p.comp("vector", lambda seg=seg, msk=msk: v.tensor_mul(seg, seg, msk),
                   reads=[("f32b", e // 2), ("f32c", 0)], writes=[("f32b", e // 2)])
            p.comp("vector", lambda seg=seg: v.tensor_single_scalar(seg, seg, 1.0, ALU.subtract),
                   reads=[("f32b", e // 2)], writes=[("f32b", e // 2)])
        lo = c.rows[0:1, 0:8]; hi = c.rows[0:1, 8:16]; mid = c.rows[0:1, 16:24]
        cnt = c.rows[0:1, 24:32]; tmask = c.i32s[0:1, 8:16]
        p.comp("vector", lambda: v.memset(lo, 0.0), writes=rkeys)
        p.comp("vector", lambda: v.memset(hi, 1.0), writes=rkeys)
        thb = c.moeS[:, 224:232]
        cmpred = c.moeS[:, 248:256]
        for it in range(28):
            p.comp("vector", lambda: v.tensor_add(mid, lo, hi), reads=rkeys, writes=rkeys)
            p.comp("vector", lambda: v.tensor_single_scalar(mid, mid, 0.5, ALU.mult), reads=rkeys, writes=rkeys)
            p.comp("tensor", lambda: t.matmul(c.ps[4][:, 0:8], c.onesrow, mid, start=True, stop=True),
                   reads=rkeys + [("rows", 3)], writes=[("p4",)])
            p.comp("vector", lambda: v.tensor_copy(thb, c.ps[4][:, 0:8]),
                   reads=[("p4",)], writes=mk(224, 232))
            for e in range(8):
                seg = c.f32b[:, e * 256:(e + 1) * 256]
                cm = c.f32c[:, e * 256:(e + 1) * 256]
                p.comp("vector", lambda cm=cm, seg=seg, e=e:
                       v.tensor_scalar(cm, seg, thb[:, e:e + 1], None, ALU.is_gt),
                       reads=[("f32b", e // 2)] + mk(224, 232), writes=[("f32c", e // 2)])
                p.comp("vector", lambda cm=cm, e=e: v.reduce_sum(cmpred[:, e:e + 1], cm, AX.X),
                       reads=[("f32c", e // 2)], writes=mk(248, 256))
            p.comp("tensor", lambda: t.matmul(c.ps[4][0:1, 0:8], c.onescol, cmpred, start=True, stop=True),
                   reads=mk(248, 256) + [("consts", 0)], writes=[("p4",)])
            p.comp("vector", lambda: v.tensor_copy(cnt, c.ps[4][0:1, 0:8]),
                   reads=[("p4",)], writes=rkeys)
            p.comp("vector", lambda: v.tensor_single_scalar(tmask, cnt, float(CAP), ALU.is_gt),
                   reads=rkeys, writes=[("i32s",)])
            p.comp("vector", lambda: v.select(lo, tmask, mid, lo), reads=rkeys + [("i32s",)], writes=rkeys)
            p.comp("vector", lambda: v.select(hi, tmask, hi, mid), reads=rkeys + [("i32s",)], writes=rkeys)
        th8 = hi
        thbc = c.moeS[:, 224:232]
        p.comp("tensor", lambda: t.matmul(c.ps[4][:, 0:8], c.onesrow, th8, start=True, stop=True),
               reads=rkeys + [("rows", 3)], writes=[("p4",)])
        p.comp("vector", lambda: v.tensor_copy(thbc, c.ps[4][:, 0:8]),
               reads=[("p4",)], writes=mk(224, 232))
        carry = c.rows[0:1, 48:56]
        p.comp("vector", lambda: v.memset(carry, 0.0), writes=rkeys)
        for tt in range(32):
            o = (tt % 2) * 152
            p1 = c.moeS[:, tt:tt + 1]; e1 = c.moeS[:, 32 + tt:33 + tt]
            fin = c.moeS[:, 64 + tt:65 + tt]; rnk = c.moeS[:, 96 + tt:97 + tt]
            oh = c.moeS[:, 128 + o:136 + o]; th_t = c.moeS[:, 159 + o:160 + o]
            kp = c.moeS[:, 240 + o:241 + o]; sc8 = c.moeS[:, 136 + o:144 + o]
            p.comp("vector", lambda oh=oh, e1=e1: v.tensor_scalar(oh, c.io8, e1[:, 0:1], None, ALU.is_equal),
                   reads=[("consts", 0)] + mk(32 + tt, 33 + tt), writes=mk(128 + o, 136 + o))
            p.comp("vector", lambda oh=oh, sc8=sc8: v.tensor_mul(sc8, oh, thbc),
                   reads=mk(128 + o, 136 + o) + mk(224, 232), writes=mk(136 + o, 144 + o))
            p.comp("vector", lambda th_t=th_t, sc8=sc8: v.reduce_sum(th_t, sc8, AX.X),
                   reads=mk(136 + o, 144 + o), writes=mk(159 + o, 160 + o))
            p.comp("vector", lambda kp=kp, p1=p1, th_t=th_t: v.tensor_tensor(kp, p1, th_t, ALU.is_gt),
                   reads=mk(tt, tt + 1) + mk(159 + o, 160 + o), writes=mk(240 + o, 241 + o))
            p.comp("vector", lambda fin=fin, kp=kp, e1=e1: v.tensor_mul(fin, kp, e1),
                   reads=mk(240 + o, 241 + o) + mk(32 + tt, 33 + tt), writes=mk(64 + tt, 65 + tt))
            p.comp("vector", lambda oh=oh, fin=fin: v.tensor_scalar(oh, c.io8, fin[:, 0:1], None, ALU.is_equal),
                   reads=[("consts", 0)] + mk(64 + tt, 65 + tt), writes=mk(128 + o, 136 + o))
            pb5 = 5 if tt % 2 == 0 else 3
            p.comp("tensor", lambda oh=oh, pb5=pb5: t.matmul(c.ps[pb5][:, 0:8], c.Lst, oh, start=True, stop=False),
                   reads=mk(128 + o, 136 + o) + [("consts", 0)], writes=[(f"p{pb5}",)])
            p.comp("tensor", lambda pb5=pb5: t.matmul(c.ps[pb5][:, 0:8], c.onesrow, carry, start=False, stop=True),
                   reads=rkeys + [("rows", 3)], writes=[(f"p{pb5}",)])
            pref = c.moeS[:, 136 + o:144 + o]
            p.comp("vector", lambda pref=pref, pb5=pb5: v.tensor_copy(pref, c.ps[pb5][:, 0:8]),
                   reads=[(f"p{pb5}",)], writes=mk(136 + o, 144 + o))
            p.comp("vector", lambda pref=pref, oh=oh: v.tensor_mul(pref, pref, oh),
                   reads=mk(136 + o, 144 + o) + mk(128 + o, 136 + o), writes=mk(136 + o, 144 + o))
            p.comp("vector", lambda rnk=rnk, pref=pref: v.reduce_sum(rnk, pref, AX.X),
                   reads=mk(136 + o, 144 + o), writes=mk(96 + tt, 97 + tt))
            pb4 = 4 if tt % 2 == 0 else 2
            p.comp("tensor", lambda oh=oh, pb4=pb4: t.matmul(c.ps[pb4][0:1, 0:8], c.onescol, oh, start=True, stop=False),
                   reads=mk(128 + o, 136 + o) + [("consts", 0)], writes=[(f"p{pb4}",)])
            p.comp("tensor", lambda pb4=pb4: t.matmul(c.ps[pb4][0:1, 0:8], c.onesrow[0:1, 0:1], carry, start=False, stop=True),
                   reads=rkeys + [("rows", 3)], writes=[(f"p{pb4}",)])
            p.comp("vector", lambda pb4=pb4: v.tensor_copy(carry, c.ps[pb4][0:1, 0:8]),
                   reads=[(f"p{pb4}",)], writes=rkeys)
        cnt8 = carry
    else:
        carry = c.rows[0:1, 48:56]
        p.comp("vector", lambda: v.memset(carry, 0.0), writes=rkeys)
        for ch in range(64):
            o = (ch % 2) * 152
            ec = c.moeS[:, 244 + o:245 + o]
            eap = c.entE[ch * 128:(ch + 1) * 128]
            p.dma("sync", lambda ec=ec, eap=eap: nc.sync.dma_start(out=ec, in_=eap.rearrange("(p one) -> p one", one=1)),
                  reads=[("entE", ch // 16)], writes=mk(244 + o, 245 + o))
            oh = c.moeS[:, 128 + o:136 + o]
            p.comp("vector", lambda oh=oh, ec=ec: v.tensor_scalar(oh, c.io8, ec[:, 0:1], None, ALU.is_equal),
                   reads=[("consts", 0)] + mk(244 + o, 245 + o), writes=mk(128 + o, 136 + o))
            pb5 = 5 if ch % 2 == 0 else 3
            p.comp("tensor", lambda oh=oh, pb5=pb5: t.matmul(c.ps[pb5][:, 0:8], c.Lst, oh, start=True, stop=False),
                   reads=mk(128 + o, 136 + o) + [("consts", 0)], writes=[(f"p{pb5}",)])
            p.comp("tensor", lambda pb5=pb5: t.matmul(c.ps[pb5][:, 0:8], c.onesrow, carry, start=False, stop=True),
                   reads=rkeys + [("rows", 3)], writes=[(f"p{pb5}",)])
            pref = c.moeS[:, 136 + o:144 + o]
            p.comp("vector", lambda pref=pref, pb5=pb5: v.tensor_copy(pref, c.ps[pb5][:, 0:8]),
                   reads=[(f"p{pb5}",)], writes=mk(136 + o, 144 + o))
            p.comp("vector", lambda pref=pref, oh=oh: v.tensor_mul(pref, pref, oh),
                   reads=mk(136 + o, 144 + o) + mk(128 + o, 136 + o), writes=mk(136 + o, 144 + o))
            rnk = c.moeS[:, 245 + o:246 + o]
            p.comp("vector", lambda rnk=rnk, pref=pref: v.reduce_sum(rnk, pref, AX.X),
                   reads=mk(136 + o, 144 + o), writes=mk(245 + o, 246 + o))
            rap_ = c.entS[ch * 128:(ch + 1) * 128]
            p.dma("sync", lambda rap_=rap_, rnk=rnk: nc.sync.dma_start(out=rap_, in_=rnk[:, 0]),
                  reads=mk(245 + o, 246 + o), writes=[("entS", ch // 16)])
            pb4 = 4 if ch % 2 == 0 else 2
            p.comp("tensor", lambda oh=oh, pb4=pb4: t.matmul(c.ps[pb4][0:1, 0:8], c.onescol, oh, start=True, stop=False),
                   reads=mk(128 + o, 136 + o) + [("consts", 0)], writes=[(f"p{pb4}",)])
            p.comp("tensor", lambda pb4=pb4: t.matmul(c.ps[pb4][0:1, 0:8], c.onesrow[0:1, 0:1], carry, start=False, stop=True),
                   reads=rkeys + [("rows", 3)], writes=[(f"p{pb4}",)])
            p.comp("vector", lambda pb4=pb4: v.tensor_copy(carry, c.ps[pb4][0:1, 0:8]),
                   reads=[(f"p{pb4}",)], writes=rkeys)
        cnt8 = carry
        p.dma("sync", lambda: nc.sync.dma_start(out=c.cc_cnt_in[0:1, :], in_=cnt8),
              reads=rkeys, writes=[("cntin",)])
        p.coll(lambda: g.collective_compute("AllGather", ALU.bypass, [list(range(NCORES))],
                                            [c.cc_cnt_in[:, :].opt()], [c.cc_cnt_out[:, :].opt()]),
               reads=[("cntin",)], writes=[("cntout",)])
        cg = c.moeS[0:8, 256:264]
        pmk = c.moeS[0:8, 264:265]
        p.dma("sync", lambda: nc.sync.dma_start(out=cg, in_=c.cc_cnt_out[:, :]),
              reads=[("cntout",)], writes=mk(256, 264))
        p.dma("sync", lambda: nc.sync.dma_start(out=pmk, in_=c.iw["prevmask"].rearrange("(p one) -> p one", one=1)),
              writes=mk(264, 265))
        p.comp("tensor", lambda: t.matmul(c.ps[4][0:8, 0:1], cg, pmk, start=True, stop=True),
               reads=mk(256, 265), writes=[("p4",)])
        offc = c.moeS[0:8, 265:266]
        p.comp("vector", lambda: v.tensor_copy(offc, c.ps[4][0:8, 0:1]),
               reads=[("p4",)], writes=mk(265, 266))
        p.comp("tensor", lambda: t.transpose(c.ps[4][0:1, 0:8], offc, c.idF[0:8, 0:8]),
               reads=mk(265, 266) + [("consts", 0)], writes=[("p4",)])
        off8 = c.rows[0:1, 56:64]
        p.comp("vector", lambda: v.tensor_copy(off8, c.ps[4][0:1, 0:8]),
               reads=[("p4",)], writes=rkeys)

    nmac = c.rows[0:1, 64:72]; cin = c.rows[0:1, 72:80]; cex = c.rows[0:1, 80:88]
    base8 = c.rows[0:1, 88:96]; ztmp = c.rows[0:1, 96:104]
    p.comp("vector", lambda: v.tensor_scalar(nmac, cnt8, 1.0 / 512, -(1.0 / 512 + 0.499), ALU.mult, ALU.add),
           reads=rkeys, writes=rkeys)
    p.comp("vector", lambda: v.tensor_copy(c.i32s[0:1, 0:8], nmac), reads=rkeys, writes=[("i32s",)])
    p.comp("vector", lambda: v.tensor_copy(nmac, c.i32s[0:1, 0:8]), reads=[("i32s",)], writes=rkeys)
    p.comp("vector", lambda: v.tensor_single_scalar(nmac, nmac, 1.0, ALU.add), reads=rkeys, writes=rkeys)
    p.comp("vector", lambda: v.memset(ztmp, 0.0), writes=rkeys)
    p.comp("vector", lambda: v.tensor_tensor_scan(cin, nmac, ztmp, 0.0, ALU.add, ALU.add),
           reads=rkeys, writes=rkeys)
    p.comp("vector", lambda: v.tensor_sub(cex, cin, nmac), reads=rkeys, writes=rkeys)
    p.comp("vector", lambda: v.tensor_single_scalar(base8, cex, 512.0, ALU.mult), reads=rkeys, writes=rkeys)
    cinC = c.moeS[0:8, 266:267]
    p.comp("tensor", lambda: t.transpose(c.ps[4][0:8, 0:1], cin, c.idF[0:1, 0:1]),
           reads=rkeys + [("consts", 0)], writes=[("p4",)])
    p.comp("vector", lambda: v.tensor_copy(cinC, c.ps[4][0:8, 0:1]),
           reads=[("p4",)], writes=mk(266, 267))
    iom = c.moeS[0:8, 267:267 + NMAC]
    p.comp("gpsimd", lambda: g.iota(iom, [[1, NMAC]], base=0, channel_multiplier=0,
                                    allow_small_or_imprecise_dtypes=True), writes=mk(267, 267 + NMAC))
    cmp_ = c.moeS[0:8, 290:290 + NMAC]
    p.comp("vector", lambda: v.tensor_scalar(cmp_, iom, cinC[0:8, 0:1], None, ALU.is_ge),
           reads=mk(267, 267 + NMAC) + mk(266, 267), writes=mk(290, 290 + NMAC))
    p.comp("tensor", lambda: t.matmul(c.ps[4][0:1, 0:NMAC], c.onescol[0:8, 0:1], cmp_, start=True, stop=True),
           reads=mk(290, 290 + NMAC) + [("consts", 0)], writes=[("p4",)])
    eidr = c.rows[0:1, 104:104 + NMAC]
    p.comp("vector", lambda: v.tensor_copy(eidr, c.ps[4][0:1, 0:NMAC]), reads=[("p4",)], writes=rkeys)
    p.comp("vector", lambda: v.tensor_single_scalar(eidr, eidr, 7.0, ALU.min), reads=rkeys, writes=rkeys)
    eidsb = c.moeS[:, 160:160 + NMAC]
    p.comp("tensor", lambda: t.matmul(c.ps[4][:, 0:NMAC], c.onesrow, eidr, start=True, stop=True),
           reads=rkeys + [("rows", 3)], writes=[("p4",)])
    p.comp("vector", lambda: v.tensor_copy(eidsb, c.ps[4][:, 0:NMAC]),
           reads=[("p4",)], writes=mk(160, 160 + NMAC))
    baseb = c.moeS[:, 224:232]
    p.comp("tensor", lambda: t.matmul(c.ps[4][:, 0:8], c.onesrow, base8, start=True, stop=True),
           reads=rkeys + [("rows", 3)], writes=[("p4",)])
    p.comp("vector", lambda: v.tensor_copy(baseb, c.ps[4][:, 0:8]),
           reads=[("p4",)], writes=mk(224, 232))
    if layer == 1:
        offb = c.moeS[:, 232:240]
        p.comp("tensor", lambda: t.matmul(c.ps[4][:, 0:8], c.onesrow, off8, start=True, stop=True),
               reads=rkeys + [("rows", 3)], writes=[("p4",)])
        p.comp("vector", lambda: v.tensor_copy(offb, c.ps[4][:, 0:8]),
               reads=[("p4",)], writes=mk(232, 240))

    if MOE_STOP < 3:
        return
    p.comp("vector", lambda: v.memset(c.idxs[:, 0:46], T), writes=[("idxs",)])
    p.dma("sync", lambda: nc.sync.dma_start(out=c.table[0:5888], in_=c.idxs[:, 0:46]),
          reads=[("idxs",)], writes=[("table", 0)])
    n2 = (NSLOT - 5888) // 128
    p.dma("sync", lambda n2=n2: nc.sync.dma_start(out=c.table[5888:NSLOT], in_=c.idxs[:, 0:n2]),
          reads=[("idxs",)], writes=[("table", 1)])
    if layer == 0:
        for tt in range(32):
            fin = c.moeS[:, 64 + tt:65 + tt]; rnk = c.moeS[:, 96 + tt:97 + tt]
            oh = c.moeS[:, 128:136]
            bt = c.moeS[:, 136:144]; slot = c.moeS[:, 244:245]
            p.comp("vector", lambda oh=oh, fin=fin: v.tensor_scalar(oh, c.io8, fin[:, 0:1], None, ALU.is_equal),
                   reads=[("consts", 0)] + mk(64 + tt, 65 + tt), writes=mk(128, 136))
            p.comp("vector", lambda bt=bt, oh=oh: v.tensor_mul(bt, oh, baseb),
                   reads=mk(128, 136) + mk(224, 232), writes=mk(136, 144))
            p.comp("vector", lambda slot=slot, bt=bt: v.reduce_sum(slot, bt, AX.X),
                   reads=mk(136, 144), writes=mk(244, 245))
            p.comp("vector", lambda slot=slot, rnk=rnk: v.tensor_add(slot, slot, rnk),
                   reads=mk(244, 245) + mk(96 + tt, 97 + tt), writes=mk(244, 245))
            sap = c.entS[tt * 128:(tt + 1) * 128]
            p.dma("sync", lambda sap=sap, slot=slot: nc.sync.dma_start(out=sap, in_=slot[:, 0]),
                  reads=mk(244, 245), writes=[("entS", tt // 8)])
            sidx = c.idxs[:, 48:49]; tok = c.idxs[:, 49:50]
            p.comp("vector", lambda sidx=sidx, slot=slot: v.tensor_copy(sidx, slot),
                   reads=mk(244, 245), writes=[("idxs",)])
            p.comp("vector", lambda tok=tok, tt=tt: v.tensor_scalar(tok, c.iotaP, 1.0, float(tt * 128), ALU.mult, ALU.add),
                   reads=[("consts", 0)], writes=[("idxs",)])
            p.dma("gpsimd", lambda sidx=sidx, tok=tok:
                  g.indirect_dma_start(c.table[0:NSLOT].rearrange("(n one) -> n one", one=1),
                                       bass.IndirectOffsetOnAxis(ap=sidx, axis=0),
                                       tok, None, oob_is_err=False),
                  reads=[("idxs",)], writes=[("table", 0), ("table", 1)])
    else:
        for ch in range(64):
            o = (ch % 2) * 152
            ec = c.moeS[:, 244 + o:245 + o]; rnk = c.moeS[:, 245 + o:246 + o]; gt = c.moeS[:, 246 + o:247 + o]
            eap = c.entE[ch * 128:(ch + 1) * 128]
            gap = c.entG[ch * 128:(ch + 1) * 128]
            rap_ = c.entS[ch * 128:(ch + 1) * 128]
            p.dma("sync", lambda ec=ec, eap=eap: nc.sync.dma_start(out=ec, in_=eap.rearrange("(p one) -> p one", one=1)),
                  reads=[("entE", ch // 16)], writes=mk(244, 245))
            p.dma("sync", lambda gt=gt, gap=gap: nc.sync.dma_start(out=gt, in_=gap.rearrange("(p one) -> p one", one=1)),
                  reads=[("entG", ch // 16)], writes=mk(246, 247))
            p.dma("sync", lambda rnk=rnk, rap_=rap_: nc.sync.dma_start(out=rnk, in_=rap_.rearrange("(p one) -> p one", one=1)),
                  reads=[("entS", ch // 16)], writes=mk(245, 246))
            oh = c.moeS[:, 128 + o:136 + o]; tmp8 = c.moeS[:, 136 + o:144 + o]
            ot = c.moeS[:, 247 + o:248 + o]; bt = c.moeS[:, 248 + o:249 + o]
            kp = c.moeS[:, 249 + o:250 + o]; w = c.moeS[:, 250 + o:251 + o]; slot = c.moeS[:, 251 + o:252 + o]
            p.comp("vector", lambda oh=oh, ec=ec: v.tensor_scalar(oh, c.io8, ec[:, 0:1], None, ALU.is_equal),
                   reads=[("consts", 0)] + mk(244 + o, 245 + o), writes=mk(128 + o, 136 + o))
            p.comp("vector", lambda tmp8=tmp8, oh=oh: v.tensor_mul(tmp8, oh, c.moeS[:, 232:240]),
                   reads=mk(128 + o, 136 + o) + mk(232, 240), writes=mk(136 + o, 144 + o))
            p.comp("vector", lambda ot=ot, tmp8=tmp8: v.reduce_sum(ot, tmp8, AX.X),
                   reads=mk(136 + o, 144 + o), writes=mk(247 + o, 248 + o))
            p.comp("vector", lambda tmp8=tmp8, oh=oh: v.tensor_mul(tmp8, oh, baseb),
                   reads=mk(128 + o, 136 + o) + mk(224, 232), writes=mk(136 + o, 144 + o))
            p.comp("vector", lambda bt=bt, tmp8=tmp8: v.reduce_sum(bt, tmp8, AX.X),
                   reads=mk(136 + o, 144 + o), writes=mk(248 + o, 249 + o))
            p.comp("vector", lambda ot=ot, rnk=rnk: v.tensor_add(ot, ot, rnk),
                   reads=mk(247 + o, 248 + o) + mk(245 + o, 246 + o), writes=mk(247 + o, 248 + o))
            p.comp("vector", lambda kp=kp, ot=ot: v.tensor_single_scalar(kp, ot, float(CAP), ALU.is_lt),
                   reads=mk(247 + o, 248 + o), writes=mk(249 + o, 250 + o))
            p.comp("vector", lambda w=w, kp=kp, gt=gt: v.tensor_mul(w, kp, gt),
                   reads=mk(249 + o, 250 + o) + mk(246 + o, 247 + o), writes=mk(250 + o, 251 + o))
            kap = c.entK[ch * 128:(ch + 1) * 128]
            p.dma("sync", lambda kap=kap, w=w: nc.sync.dma_start(out=kap, in_=w[:, 0]),
                  reads=mk(250 + o, 251 + o), writes=[("entK", ch // 16)])
            p.comp("vector", lambda slot=slot, bt=bt, rnk=rnk: v.tensor_add(slot, bt, rnk),
                   reads=mk(248 + o, 249 + o) + mk(245 + o, 246 + o), writes=mk(251 + o, 252 + o))
            p.dma("sync", lambda rap_=rap_, slot=slot: nc.sync.dma_start(out=rap_, in_=slot[:, 0]),
                  reads=mk(251 + o, 252 + o), writes=[("entS", ch // 16)])
            sidx = c.idxs[:, 48 + ch % 2:49 + ch % 2]; tok = c.idxs[:, 53 + ch % 2:54 + ch % 2]
            p.comp("vector", lambda sidx=sidx, slot=slot: v.tensor_copy(sidx, slot),
                   reads=mk(251 + o, 252 + o), writes=[("idxs", 48 + ch % 2)])
            p.comp("vector", lambda tok=tok, ch=ch: v.tensor_scalar(tok, c.iotaP, 0.5, float(ch * 64) - 0.25, ALU.mult, ALU.add),
                   reads=[("consts", 0)], writes=[("idxs", 53 + ch % 2)])
            p.dma("gpsimd", lambda sidx=sidx, tok=tok:
                  g.indirect_dma_start(c.table[0:NSLOT].rearrange("(n one) -> n one", one=1),
                                       bass.IndirectOffsetOnAxis(ap=sidx, axis=0),
                                       tok, None, oob_is_err=False),
                  reads=[("idxs", 48 + ch % 2), ("idxs", 53 + ch % 2)], writes=[("table", 0), ("table", 1)])

    if MOE_STOP < 4:
        return
    emit_moe_experts(c, layer, NMAC, KF)
    if MOE_STOP < 5:
        return
    emit_moe_gatherback(c, layer)


def emit_moe_experts(c, layer, NMAC, KF):
    nc, p = c.nc, c.p
    v, s, t, g = nc.vector, nc.scalar, nc.tensor, nc.gpsimd
    w1b = c.w1b[layer]; w2b = c.w2b[layer]
    w1v = w1b
    w2rows = 4096 if layer == 0 else 2048
    w2v = w2b
    NSLOT = NMAC * 512
    b1v = c.iw["f0_b1"].rearrange("e (ft p) -> (e ft) p", p=128) if layer == 0 else None
    xk = lambda col0, n: keys("xnb", col0, col0 + n, gran=512)
    qk = lambda col0, n: keys("qkv", col0, col0 + n, gran=512)
    for m in range(NMAC):
        eidcol = c.moeS[:, 160 + m:161 + m]
        ek = mk(160 + m, 161 + m)
        tw1 = c.moeS[:, 252:253]; tw2 = c.moeS[:, 253:254]
        p.comp("vector", lambda tw1=tw1, eidcol=eidcol:
               v.scalar_tensor_tensor(tw1, eidcol, 1024.0, c.iotaP, ALU.mult, ALU.add),
               reads=ek + [("consts", 0)], writes=mk(252, 253))
        p.comp("vector", lambda tw2=tw2, eidcol=eidcol:
               v.scalar_tensor_tensor(tw2, eidcol, float(w2rows), c.iotaP, ALU.mult, ALU.add),
               reads=ek + [("consts", 0)], writes=mk(253, 254))
        for k in range(8):
            ic = c.idxs[:, k:k + 1]
            p.comp("vector", lambda ic=ic, tw1=tw1, k=k: v.tensor_single_scalar(ic, tw1, float(128 * k), ALU.add),
                   reads=mk(252, 253), writes=[("idxs", k)])
        for k in range(KF):
            ic = c.idxs[:, 8 + k:9 + k]
            p.comp("vector", lambda ic=ic, tw2=tw2, k=k: v.tensor_single_scalar(ic, tw2, float(128 * k), ALU.add),
                   reads=mk(253, 254), writes=[("idxs", 8 + k)])
        if layer == 0:
            bi = c.idxs[0:32, 52:53]
            p.comp("vector", lambda bi=bi, eidcol=eidcol:
                   v.scalar_tensor_tensor(bi, eidcol[0:32, 0:1], 32.0, c.iotaP[0:32, 0:1], ALU.mult, ALU.add),
                   reads=ek + [("consts", 0)], writes=[("idxs", 52)])
            bg = c.f32c[0:32, 0:128]
            p.dma("gpsimd", lambda bg=bg, bi=bi:
                  g.indirect_dma_start(bg, None, b1v, bass.IndirectOffsetOnAxis(ap=bi, axis=0), oob_is_err=False),
                  reads=[("idxs", 52)], writes=[("f32c", 0)])
            p.comp("tensor", lambda bg=bg: t.transpose(c.ps[4][:, 0:32], bg, c.idF[0:32, 0:32]),
                   reads=[("f32c", 0), ("consts", 0)], writes=[("p4",)])
            p.comp("vector", lambda: v.tensor_copy(c.moeS[:, 192:224], c.ps[4][:, 0:32]),
                   reads=[("p4",)], writes=mk(192, 224))
        mp = m % 2
        bbuf = c.qkv if mp == 0 else c.ework
        bbk = (lambda c0, n: qk(c0, n)) if mp == 0 else (lambda c0, n: keys("ework", c0, c0 + n))
        for st in range(4):
            q = m * 4 + st
            ti = c.idxs[:, 50 if st % 2 == 0 else 51:51 if st % 2 == 0 else 52]
            tap_ = c.table[q * 128:(q + 1) * 128]
            p.dma("sync", lambda ti=ti, tap_=tap_: nc.sync.dma_start(out=ti, in_=tap_.rearrange("(p one) -> p one", one=1)),
                  reads=[("table", 0), ("table", 1)], writes=[("idxs", 50 + st % 2)])
            grow = c.att[:, (st % 2) * 1024:(st % 2) * 1024 + 1024]
            gwk = keys("att", (st % 2) * 1024, (st % 2) * 1024 + 1024)
            p.dma("gpsimd", lambda grow=grow, ti=ti:
                  g.indirect_dma_start(grow, None, c.xtok[:, :], bass.IndirectOffsetOnAxis(ap=ti, axis=0), oob_is_err=False),
                  reads=[("idxs", 50 + st % 2), ("xtok", 0), ("xtok", 1), ("xtok", 2), ("xtok", 3),
                         ("xtok", 4), ("xtok", 5), ("xtok", 6), ("xtok", 7), ("xtokz",)],
                  writes=gwk)
            qb = st % 2
            for kt in range(NKT):
                p.comp("tensor", lambda grow=grow, kt=kt, qb=qb:
                       t.transpose(c.qs[qb][:, kt * 128:(kt + 1) * 128], grow[:, kt * 128:(kt + 1) * 128], c.idB),
                       reads=gwk + [("cbf", 0)], writes=[(f"q{qb}",)])
            for kt in range(NKT):
                dst = bbuf[:, kt * 512 + st * 128: kt * 512 + st * 128 + 128]
                p.comp("vector", lambda dst=dst, kt=kt, qb=qb: v.tensor_copy(dst, c.qs[qb][:, kt * 128:(kt + 1) * 128]),
                       reads=[(f"q{qb}",)], writes=bbk(kt * 512, 512))
        for fh in range(4):
            a0 = 16384 + (fh % 2) * 8192
            for k in range(8):
                wdst = c.xnb[:, a0 + k * 1024: a0 + (k + 1) * 1024]
                ic = c.idxs[:, k:k + 1]
                p.dma("gpsimd", lambda wdst=wdst, ic=ic, fh=fh:
                      g.indirect_dma_start(wdst, None, w1v, bass.IndirectOffsetOnAxis(ap=ic, axis=0),
                                           element_offset=fh * 1024, oob_is_err=False),
                      reads=[("idxs", k)] + c.wb_keys[layer], writes=xk(a0 + k * 1024, 1024))
            for f in range(8):
                fi = fh * 8 + f
                pb = f % 2
                for k in range(8):
                    lhs = c.xnb[:, a0 + k * 1024 + f * 128: a0 + k * 1024 + f * 128 + 128]
                    rhs = bbuf[:, k * 512:(k + 1) * 512]
                    p.comp("tensor", lambda lhs=lhs, rhs=rhs, pb=pb, k=k:
                           t.matmul(c.ps[pb], lhs, rhs, start=(k == 0), stop=(k == 7)),
                           reads=xk(a0 + k * 1024, 1024) + bbk(k * 512, 512), writes=[(f"p{pb}",)])
                h1 = c.xnb[:, fi * 512:(fi + 1) * 512]
                if layer == 0:
                    bcol = c.moeS[:, 192 + fi:193 + fi]
                    p.comp("scalar", lambda h1=h1, pb=pb, bcol=bcol:
                           s.activation(h1, c.ps[pb], AF.Relu, bias=bcol, scale=1.0),
                           reads=[(f"p{pb}",)] + mk(192, 224), writes=xk(fi * 512, 512))
                elif fh < 2:
                    p.comp("scalar", lambda h1=h1, pb=pb: s.activation(h1, c.ps[pb], AF.Gelu, bias=0.0, scale=1.0),
                           reads=[(f"p{pb}",)], writes=xk(fi * 512, 512))
                else:
                    gf = c.xnb[:, (fi - 16) * 512:(fi - 16 + 1) * 512]
                    p.comp("vector", lambda gf=gf, pb=pb:
                           v.scalar_tensor_tensor(gf, gf, 1.0, c.ps[pb], ALU.mult, ALU.mult),
                           reads=[(f"p{pb}",)] + xk((fi - 16) * 512, 512), writes=xk((fi - 16) * 512, 512))
        for dm in range(2):
            for k in range(KF):
                wslot = 4096 + ((k % 4) + 0) * 512 if False else 4096 + (k % 4) * 512
                wslot = 4096 + (k % 4) * 512 if (k % 4) < 4 else wslot
                w2sb = c.qkv[:, wslot:wslot + 512]
                ic = c.idxs[:, 8 + k:9 + k]
                p.dma("gpsimd", lambda w2sb=w2sb, ic=ic, dm=dm:
                      g.indirect_dma_start(w2sb, None, w2v, bass.IndirectOffsetOnAxis(ap=ic, axis=0),
                                           element_offset=dm * 512, oob_is_err=False),
                      reads=[("idxs", 8 + k)] + c.wb_keys[layer], writes=qk(wslot, 512))
                for st in range(4):
                    lhs = c.xnb[:, k * 512 + st * 128: k * 512 + st * 128 + 128]
                    p.comp("tensor", lambda lhs=lhs, w2sb=w2sb, st=st, k=k, KF=KF:
                           t.matmul(c.ps[st + 2], lhs, w2sb, start=(k == 0), stop=(k == KF - 1)),
                           reads=xk(k * 512, 512) + qk(wslot, 512), writes=[(f"p{st + 2}",)])
            for st in range(4):
                yt = c.f32b[:, st * 1024 + dm * 512: st * 1024 + dm * 512 + 512]
                p.comp("scalar", lambda yt=yt, st=st: s.copy(yt, c.ps[st + 2]),
                       reads=[(f"p{st + 2}",)], writes=[("f32b", (st * 1024 + dm * 512) // 512)])
        for st in range(4):
            q = m * 4 + st
            ap_ = c.expout[q * 128:(q + 1) * 128, :]
            ysb = c.f32b[:, st * 1024:(st + 1) * 1024]
            p.dma("scalar", lambda ap_=ap_, ysb=ysb: nc.scalar.dma_start(out=ap_, in_=ysb),
                  reads=[("f32b", st * 2), ("f32b", st * 2 + 1)], writes=[("expout", q // 8)])


def emit_moe_gatherback(c, layer):
    nc, p = c.nc, c.p
    v, s, t, g = nc.vector, nc.scalar, nc.tensor, nc.gpsimd
    ek = [("expout", i) for i in range(12)]
    for tm in range(NTM):
        for j in range(4):
            tt = tm * 4 + j
            yt = c.f32b[:, j * 1024:(j + 1) * 1024]
            ytk = [("f32b", j * 2), ("f32b", j * 2 + 1)]
            if layer == 0:
                si = c.idxs[:, 53:54]; fi_ = c.idxs[:, 54:55]
                sl = c.moeS[:, 244:245]
                sap = c.entS[tt * 128:(tt + 1) * 128]
                p.dma("sync", lambda sl=sl, sap=sap: nc.sync.dma_start(out=sl, in_=sap.rearrange("(p one) -> p one", one=1)),
                      reads=[("entS", tt // 8)], writes=mk(244, 245))
                p.comp("vector", lambda si=si, sl=sl: v.tensor_copy(si, sl),
                       reads=mk(244, 245), writes=[("idxs",)])
                fin = c.moeS[:, 64 + tt:65 + tt]
                p.comp("vector", lambda fi_=fi_, fin=fin: v.tensor_copy(fi_, fin),
                       reads=mk(64 + tt, 65 + tt), writes=[("idxs",)])
                p.dma("gpsimd", lambda yt=yt, si=si:
                      g.indirect_dma_start(yt, None, c.expout[:, :], bass.IndirectOffsetOnAxis(ap=si, axis=0), oob_is_err=False),
                      reads=[("idxs",)] + ek, writes=ytk)
                b2g = c.f32c[:, j * 1024:(j + 1) * 1024]
                b2k = [("f32c", j * 2), ("f32c", j * 2 + 1)]
                p.dma("gpsimd", lambda b2g=b2g, fi_=fi_:
                      g.indirect_dma_start(b2g, None, c.iw["f0_b2"][:, :], bass.IndirectOffsetOnAxis(ap=fi_, axis=0), oob_is_err=False),
                      reads=[("idxs",)], writes=b2k)
                p.comp("vector", lambda yt=yt, b2g=b2g: v.tensor_add(yt, yt, b2g),
                       reads=ytk + b2k, writes=ytk)
            else:
                si0 = c.idxs[:, 53:54]; si1 = c.idxs[:, 54:55]
                s0 = c.moeS[:, 244:245]; s1 = c.moeS[:, 245:246]
                w0 = c.moeS[:, 246:247]; w1_ = c.moeS[:, 247:248]
                base = tt * 256
                sap0 = bass.AP(c.entS.tensor, base, [[2, 128], [1, 1]])
                sap1 = bass.AP(c.entS.tensor, base + 1, [[2, 128], [1, 1]])
                kap0 = bass.AP(c.entK.tensor, base, [[2, 128], [1, 1]])
                kap1 = bass.AP(c.entK.tensor, base + 1, [[2, 128], [1, 1]])
                p.dma("sync", lambda s0=s0, sap0=sap0: nc.sync.dma_start(out=s0, in_=sap0, allow_slow_non_contiguous=True),
                      reads=[("entS", tt // 8)], writes=mk(244, 245))
                p.dma("sync", lambda s1=s1, sap1=sap1: nc.sync.dma_start(out=s1, in_=sap1, allow_slow_non_contiguous=True),
                      reads=[("entS", tt // 8)], writes=mk(245, 246))
                p.dma("sync", lambda w0=w0, kap0=kap0: nc.sync.dma_start(out=w0, in_=kap0, allow_slow_non_contiguous=True),
                      reads=[("entK", tt // 8)], writes=mk(246, 247))
                p.dma("sync", lambda w1_=w1_, kap1=kap1: nc.sync.dma_start(out=w1_, in_=kap1, allow_slow_non_contiguous=True),
                      reads=[("entK", tt // 8)], writes=mk(247, 248))
                p.comp("vector", lambda si0=si0, s0=s0: v.tensor_copy(si0, s0), reads=mk(244, 245), writes=[("idxs",)])
                p.comp("vector", lambda si1=si1, s1=s1: v.tensor_copy(si1, s1), reads=mk(245, 246), writes=[("idxs",)])
                g0 = c.f32c[:, j * 1024:(j + 1) * 1024]
                g0k = [("f32c", j * 2), ("f32c", j * 2 + 1)]
                p.dma("gpsimd", lambda g0=g0, si0=si0:
                      g.indirect_dma_start(g0, None, c.expout[:, :], bass.IndirectOffsetOnAxis(ap=si0, axis=0), oob_is_err=False),
                      reads=[("idxs",)] + ek, writes=g0k)
                g1 = c.f32a[:, (j % 2) * 1024:(j % 2) * 1024 + 1024]
                g1k = [("f32a", (j % 2) * 2), ("f32a", (j % 2) * 2 + 1)]
                p.dma("gpsimd", lambda g1=g1, si1=si1:
                      g.indirect_dma_start(g1, None, c.expout[:, :], bass.IndirectOffsetOnAxis(ap=si1, axis=0), oob_is_err=False),
                      reads=[("idxs",)] + ek, writes=g1k)
                p.comp("vector", lambda yt=yt, g0=g0, w0=w0: v.tensor_scalar(yt, g0, w0[:, 0:1], None, ALU.mult),
                       reads=g0k + mk(246, 247), writes=ytk)
                p.comp("vector", lambda yt=yt, g1=g1, w1_=w1_:
                       v.scalar_tensor_tensor(yt, g1, w1_[:, 0:1], yt, ALU.mult, ALU.add),
                       reads=g1k + mk(247, 248) + ytk, writes=ytk)
        for kt in range(NKT):
            pb = kt % 2
            for j in range(4):
                yt = c.f32b[:, j * 1024:(j + 1) * 1024]
                ytk = [("f32b", j * 2), ("f32b", j * 2 + 1)]
                p.comp("tensor", lambda yt=yt, kt=kt, j=j, pb=pb:
                       t.transpose(c.ps[pb][:, j * 128:(j + 1) * 128], yt[:, kt * 128:(kt + 1) * 128], c.idF),
                       reads=ytk + [("consts", 0)], writes=[(f"p{pb}",)])
            xt_ = c.f32a[:, 2048 + (kt % 2) * 512: 2048 + (kt % 2) * 512 + 512]
            xtk = [("f32a", 4 + (kt % 2))]
            Xap = c.X[kt * 128:(kt + 1) * 128, tm * 512:(tm + 1) * 512]
            p.dma("sync", lambda xt_=xt_, Xap=Xap: nc.sync.dma_start(out=xt_, in_=Xap),
                  reads=[("X", kt * 8 + tm)], writes=xtk)
            p.comp("vector", lambda xt_=xt_, pb=pb: v.tensor_add(xt_, xt_, c.ps[pb]),
                   reads=xtk + [(f"p{pb}",)], writes=xtk)
            p.dma("sync", lambda xt_=xt_, Xap=Xap: nc.sync.dma_start(out=Xap, in_=xt_),
                  reads=xtk, writes=[("X", kt * 8 + tm)])


def emit_l0_attn_b(c):
    nc, p = c.nc, c.p
    v, s, t, g = nc.vector, nc.scalar, nc.tensor, nc.gpsimd
    iw = c.iw
    ak = lambda c0, c1: keys("att", c0, c1)
    ew = lambda c0, c1: keys("ework", c0, c1)
    qv = lambda c0, c1: keys("qkv", c0, c1)
    R = 256.0
    emit_ln(c, c.X, "X", dst_sbuf=True)
    xm = c.cols[:, 512:520]
    for kt in range(NKT):
        xr = c.xnb[:, kt * T:(kt + 1) * T]
        p.comp("vector", lambda xr=xr, kt=kt: v.reduce_sum(xm[:, kt:kt + 1], xr, AX.X),
               reads=keys("xnb", kt * T, (kt + 1) * T), writes=[("cols", 1)])
    wsp = c.f32c[:, 0:128]
    wap = iw["b0_wspanT"].rearrange("(kt p) h -> p kt h", p=128)
    p.dma("sync", lambda wsp=wsp, wap=wap: nc.sync.dma_start(out=wsp, in_=wap), writes=[("f32c", 0)])
    for kt in range(NKT):
        p.comp("tensor", lambda kt=kt: t.matmul(c.ps[4][0:16, 0:1], wsp[:, kt * 16:(kt + 1) * 16],
                                                xm[:, kt:kt + 1], start=(kt == 0), stop=(kt == NKT - 1)),
               reads=[("f32c", 0), ("cols", 1)], writes=[("p4",)])
    bsp = c.f32c[0:16, 512:513]
    p.dma("sync", lambda: nc.sync.dma_start(out=bsp, in_=iw["b0_bspan"].rearrange("(h one) -> h one", one=1)),
          writes=[("f32c", 1)])
    zc = c.f32c[0:16, 513:514]
    p.comp("scalar", lambda: s.activation(zc, c.ps[4][0:16, 0:1], AF.Sigmoid, bias=bsp, scale=1.0 / T),
           reads=[("p4",), ("f32c", 1)], writes=[("f32c", 1)])
    p.comp("vector", lambda: v.tensor_single_scalar(zc, zc, float(T) / R, ALU.mult),
           reads=[("f32c", 1)], writes=[("f32c", 1)])
    fak = [("f32a", i) for i in range(3)]
    fbk = [("f32b", i) for i in range(3)]
    for ch in range(3):
        w0 = ch * 1536
        ib = c.f32a[0:16, 0:1536]
        p.comp("gpsimd", lambda ib=ib, w0=w0: g.iota(ib, [[1, 1536]], base=w0, channel_multiplier=0,
                                                     allow_small_or_imprecise_dtypes=True),
               writes=fak)
        mm_ = c.f32b[0:16, 0:1536]
        p.comp("vector", lambda mm_=mm_, ib=ib: v.tensor_scalar(mm_, ib, 1.0 / R, 1.0 - 4095.0 / R, ALU.mult, ALU.add),
               reads=fak, writes=fbk)
        p.comp("vector", lambda mm_=mm_: v.tensor_scalar(mm_, mm_, zc[:, 0:1], None, ALU.add),
               reads=fbk + [("f32c", 1)], writes=fbk)
        p.comp("vector", lambda mm_=mm_: v.tensor_scalar(mm_, mm_, 0.0, 1.0, ALU.max, ALU.min),
               reads=fbk, writes=fbk)
        cz = c.f32a[0:16, 1536:3072]
        czk = [("f32a", i) for i in range(3, 6)]
        p.comp("vector", lambda cz=cz, ib=ib: v.tensor_single_scalar(cz, ib, 4095.0, ALU.is_le),
               reads=fak, writes=czk)
        p.comp("vector", lambda mm_=mm_, cz=cz: v.tensor_mul(mm_, mm_, cz),
               reads=fbk + czk, writes=fbk)
        mb = c.att[0:16, 0:1536]
        p.comp("scalar", lambda mb=mb, mm_=mm_: s.copy(mb, mm_),
               reads=fbk, writes=ak(0, 1536))
        sap = c.strips[:, w0:w0 + 1536]
        p.dma("sync", lambda sap=sap, mb=mb: nc.sync.dma_start(out=sap, in_=mb),
              reads=ak(0, 1536), writes=[("strips", ch)])
    negc = c.cols[:, 520:536]
    for hp in range(8):
        for wi, nm in enumerate(("b0_wqT", "b0_wkT", "b0_wvT")):
            wap = iw[nm][:, hp * 128:(hp + 1) * 128].rearrange("(kt p) m -> p kt m", p=128)
            wsb = c.wst[:, wi * 1024:(wi + 1) * 1024]
            p.dma("gpsimd", lambda wsb=wsb, wap=wap: nc.gpsimd.dma_start(out=wsb, in_=wap),
                  writes=keys("wst", wi * 1024, (wi + 1) * 1024))
        for wi, dst, dkey in ((0, c.qkv, 0), (1, c.qkv, 4096), (2, c.ework, 0)):
            for tm in range(NTM):
                pb = tm % 2
                for kt in range(NKT):
                    lhs = c.wst[:, wi * 1024 + kt * 128:(wi * 1024) + (kt + 1) * 128]
                    rhs = c.xnb[:, kt * T + tm * 512: kt * T + (tm + 1) * 512]
                    p.comp("tensor", lambda lhs=lhs, rhs=rhs, pb=pb, kt=kt:
                           t.matmul(c.ps[pb], lhs, rhs, start=(kt == 0), stop=(kt == NKT - 1)),
                           reads=keys("wst", wi * 1024, (wi + 1) * 1024) +
                                 keys("xnb", (kt * 8 + tm) * 512, (kt * 8 + tm) * 512 + 512),
                           writes=[(f"p{pb}",)])
                osb = dst[:, dkey + tm * 512: dkey + (tm + 1) * 512]
                okey = (qv if dst is c.qkv else ew)(dkey + tm * 512, dkey + (tm + 1) * 512)
                p.comp("scalar", lambda osb=osb, pb=pb: s.copy(osb, c.ps[pb]),
                       reads=[(f"p{pb}",)], writes=okey)
        for h2 in range(2):
            for tt in range(32):
                src_ = c.ework[h2 * 64:(h2 + 1) * 64, tt * 128:(tt + 1) * 128]
                qb = tt % 2
                idsl = c.idB[h2 * 64:(h2 + 1) * 64, h2 * 64:(h2 + 1) * 64]
                p.comp("tensor", lambda src_=src_, qb=qb, tt=tt, idsl=idsl:
                       t.transpose(c.qs[qb][:, (tt % 8) * 64:(tt % 8) * 64 + 64], src_, idsl),
                       reads=ew(tt * 128, (tt + 1) * 128) + [("cbf", 0)], writes=[(f"q{qb}",)])
                v0 = 3072 + (h2 * 32 + tt) * 65
                dst_ = c.ework[:, v0:v0 + 64]
                p.comp("vector", lambda dst_=dst_, qb=qb, tt=tt:
                       v.tensor_copy(dst_, c.qs[qb][:, (tt % 8) * 64:(tt % 8) * 64 + 64]),
                       reads=[(f"q{qb}",)], writes=ew(v0, v0 + 65))
                one_ = c.ework[:, v0 + 64:v0 + 65]
                p.comp("vector", lambda one_=one_: v.memset(one_, 1.0),
                       writes=ew(v0, v0 + 65))
        for h2 in range(2):
            hh = hp * 2 + h2
            for which, src0, skey in (("q", 0, qv), ("k", 4096, qv)):
                rmax = c.rows[0:1, 128 + (0 if which == "q" else 1):129 + (0 if which == "q" else 1)]
                for tm in range(NTM):
                    sq = c.f32c[h2 * 64:(h2 + 1) * 64, 0:512]
                    sap_ = c.qkv[h2 * 64:(h2 + 1) * 64, src0 + tm * 512: src0 + (tm + 1) * 512]
                    p.comp("scalar", lambda sq=sq, sap_=sap_: s.activation(sq, sap_, AF.Square, bias=0.0, scale=1.0),
                           reads=skey(src0 + tm * 512, src0 + (tm + 1) * 512), writes=[("f32c", 0)])
                    ocol = c.onescol[h2 * 64:(h2 + 1) * 64, 0:1]
                    p.comp("tensor", lambda sq=sq, ocol=ocol: t.matmul(c.ps[4][0:1, :], ocol, sq, start=True, stop=True),
                           reads=[("f32c", 0), ("consts", 0)], writes=[("p4",)])
                    cur = c.rows[0:1, 130:131]
                    p.comp("vector", lambda cur=cur: v.reduce_max(cur, c.ps[4][0:1, :], AX.X),
                           reads=[("p4",)], writes=[("rows", 0)])
                    if tm == 0:
                        p.comp("vector", lambda rmax=rmax, cur=cur: v.tensor_copy(rmax, cur),
                               reads=[("rows", 0)], writes=[("rows", 0)])
                    else:
                        p.comp("vector", lambda rmax=rmax, cur=cur: v.tensor_max(rmax, rmax, cur),
                               reads=[("rows", 0)], writes=[("rows", 0)])
            qm = c.rows[0:1, 128:129]; km = c.rows[0:1, 129:130]; cc_ = c.rows[0:1, 131:132]
            p.comp("vector", lambda: v.tensor_mul(cc_, qm, km), reads=[("rows", 0)], writes=[("rows", 0)])
            p.comp("scalar", lambda: s.activation(cc_, cc_, AF.Sqrt, bias=0.0, scale=1.0),
                   reads=[("rows", 0)], writes=[("rows", 0)])
            p.comp("vector", lambda: v.tensor_single_scalar(cc_, cc_, -1.0 / 8.0, ALU.mult),
                   reads=[("rows", 0)], writes=[("rows", 0)])
            p.comp("tensor", lambda hh=hh: t.matmul(c.ps[4][:, 0:1], c.onesrow, cc_, start=True, stop=True),
                   reads=[("rows", 0), ("rows", 3)], writes=[("p4",)])
            p.comp("vector", lambda hh=hh: v.tensor_copy(negc[:, hh:hh + 1], c.ps[4][:, 0:1]),
                   reads=[("p4",)], writes=[("cols", 1)])
        ecnt = [0]
        for tm in range(NTM):
            t0 = tm * 512
            nst = 4 * tm + 4
            def m_load(st):
                for h2 in range(2):
                    hh = hp * 2 + h2
                    mslot = (2 * (st % 2) + h2) * 512
                    m_ = c.att[:, mslot:mslot + 512]
                    moff = hh * 4608 + 3584 - t0 + st * 128
                    map_ = bass.AP(c.strips.tensor, moff, [[1, 128], [1, 512]])
                    p.dma("sync", lambda m_=m_, map_=map_: nc.sync.dma_start(out=m_, in_=map_),
                          reads=[("strips", 0), ("strips", 1), ("strips", 2)], writes=ak(mslot, mslot + 512))
            def flush(pend):
                for (h2, pst, e_, em_, ek_, emk_) in pend:
                    v0 = 3072 + (h2 * 32 + pst) * 65
                    vt = c.ework[:, v0:v0 + 65]
                    p.comp("tensor", lambda vt=vt, em_=em_, pst=pst, nst=nst, h2=h2:
                           t.matmul(c.ps[2 + h2][0:65, :], vt, em_, start=(pst == 0), stop=(pst == nst - 1)),
                           reads=ew(v0, v0 + 65) + emk_, writes=[(f"p{2 + h2}",)])
                    p.comp("tensor", lambda e_=e_, pst=pst, nst=nst, h2=h2:
                           t.matmul(c.ps[4 + h2][0:1, :], c.onescolB, e_, start=(pst == 0), stop=(pst == nst - 1)),
                           reads=ek_ + [("cbf", 0)], writes=[(f"p{4 + h2}",)])
            pend = []
            m_load(0)
            for st in range(nst):
                if st + 1 < nst:
                    m_load(st + 1)
                for h2 in range(2):
                    pb = h2
                    s0 = st * 128
                    lhs = c.qkv[h2 * 64:(h2 + 1) * 64, 4096 + s0:4096 + s0 + 128]
                    rhs = c.qkv[h2 * 64:(h2 + 1) * 64, t0:t0 + 512]
                    p.comp("tensor", lambda lhs=lhs, rhs=rhs, pb=pb:
                           t.matmul(c.ps[pb], lhs, rhs, start=True, stop=True),
                           reads=qv(4096 + s0, 4096 + s0 + 128) + qv(t0, t0 + 512), writes=[(f"p{pb}",)])
                flush(pend)
                pend = []
                for h2 in range(2):
                    hh = hp * 2 + h2
                    pb = h2
                    sl = ecnt[0] % 3
                    ecnt[0] += 1
                    e_ = c.ework[:, sl * 512:sl * 512 + 512]
                    em_ = c.ework[:, 1536 + sl * 512:1536 + sl * 512 + 512]
                    ek_ = ew(sl * 512, sl * 512 + 512)
                    emk_ = ew(1536 + sl * 512, 1536 + sl * 512 + 512)
                    mslot = (2 * (st % 2) + h2) * 512
                    m_ = c.att[:, mslot:mslot + 512]
                    p.comp("scalar", lambda e_=e_, pb=pb, hh=hh:
                           s.activation(e_, c.ps[pb], AF.Exp, bias=negc[:, hh:hh + 1], scale=0.125),
                           reads=[(f"p{pb}",), ("cols", 1)], writes=ek_)
                    p.comp("vector", lambda em_=em_, e_=e_, m_=m_: v.tensor_mul(em_, e_, m_[:, ::-1]),
                           reads=ek_ + ak(mslot, mslot + 512), writes=emk_)
                    pend.append((h2, st, e_, em_, ek_, emk_))
            flush(pend)
            for h2 in range(2):
                hh = hp * 2 + h2
                se_ = c.rows[0:1, 132 + h2 * 512:644 + h2 * 512]
                sek = [("rows", h2), ("rows", h2 + 1)]
                p.comp("vector", lambda se_=se_, h2=h2: v.tensor_copy(se_, c.ps[4 + h2][0:1, :]),
                       reads=[(f"p{4 + h2}",)], writes=sek)
                p.comp("vector", lambda se_=se_, h2=h2:
                       v.scalar_tensor_tensor(se_, se_, 1e-8, c.ps[2 + h2][64:65, :], ALU.mult, ALU.add),
                       reads=sek + [(f"p{2 + h2}",)], writes=sek)
                p.comp("scalar", lambda se_=se_: s.activation(se_, se_, AF.Ln, bias=0.0, scale=1.0),
                       reads=sek, writes=sek)
                p.comp("scalar", lambda se_=se_: s.activation(se_, se_, AF.Exp, bias=0.0, scale=-1.0),
                       reads=sek, writes=sek)
                cx = c.f32c[:, h2 * 512:h2 * 512 + 512]
                p.comp("vector", lambda cx=cx, h2=h2: v.tensor_copy(cx[0:64, :], c.ps[2 + h2][0:64, :]),
                       reads=[(f"p{2 + h2}",)], writes=[("f32c", h2)])
                p.comp("tensor", lambda se_=se_, h2=h2:
                       t.matmul(c.ps[2 + h2][0:64, :], c.onesrow[0:1, 0:64], se_, start=True, stop=True),
                       reads=sek + [("rows", 3), ("f32c", h2)], writes=[(f"p{2 + h2}",)])
                p.comp("vector", lambda cx=cx, h2=h2: v.tensor_mul(cx[0:64, :], cx[0:64, :], c.ps[2 + h2][0:64, :]),
                       reads=[("f32c", h2), (f"p{2 + h2}",)], writes=[("f32c", h2)])
                ob = c.att[0:64, 2048:2560] if h2 == 0 else c.att[64:128, 2048:2560]
                p.comp("scalar", lambda ob=ob, cx=cx: s.copy(ob, cx[0:64, :]),
                       reads=[("f32c", h2)], writes=ak(2048, 2560))
                cap_ = c.ctxT[hh * 64:(hh + 1) * 64, t0:t0 + 512]
                p.dma("sync", lambda cap_=cap_, ob=ob: nc.sync.dma_start(out=cap_, in_=ob),
                      reads=ak(2048, 2560), writes=[("ctxT", (hh // 2) * 8 + tm)])
    emit_proj_residual(c, c.iw["b0_woT"], D, bias_dram=c.iw["b0_bo"])


def emit_l1_attn_a(c):
    nc, p = c.nc, c.p
    v, s, t, g = nc.vector, nc.scalar, nc.tensor, nc.gpsimd
    iw = c.iw
    ak = lambda c0, c1: keys("att", c0, c1)
    ew = lambda c0, c1: keys("ework", c0, c1)
    qv = lambda c0, c1: keys("qkv", c0, c1)
    wk = lambda c0, c1: keys("wst", c0, c1)
    emit_ln(c, c.X, "X", dst_sbuf=True)
    halfpi = constcell(c, PI / 2)
    EPS2 = 1e-6 * 128.0
    for hp in range(8):
        for wi, nm in enumerate(("a1_wqT", "a1_wkT", "a1_wvT")):
            wap = iw[nm][:, hp * 128:(hp + 1) * 128].rearrange("(kt p) m -> p kt m", p=128)
            wsb = c.wst[:, wi * 1024:(wi + 1) * 1024]
            p.dma("gpsimd", lambda wsb=wsb, wap=wap: nc.gpsimd.dma_start(out=wsb, in_=wap),
                  writes=wk(wi * 1024, (wi + 1) * 1024))
        for h2 in range(2):
            hh = hp * 2 + h2
            wrap = iw["a1_wr"][hh * 64:(hh + 1) * 64, :]
            wrsb = c.wst[h2 * 64:(h2 + 1) * 64, 3072 + h2 * 128:3072 + h2 * 128 + 128]
            p.dma("gpsimd", lambda wrsb=wrsb, wrap=wrap: nc.gpsimd.dma_start(out=wrsb, in_=wrap),
                  writes=wk(3072, 3584))
        for wi, dst0 in ((0, 4096), (1, 8192), (2, 12288)):
            for tm in range(NTM):
                pb = tm % 2
                for kt in range(NKT):
                    lhs = c.wst[:, wi * 1024 + kt * 128: wi * 1024 + (kt + 1) * 128]
                    rhs = c.xnb[:, kt * T + tm * 512: kt * T + (tm + 1) * 512]
                    p.comp("tensor", lambda lhs=lhs, rhs=rhs, pb=pb, kt=kt:
                           t.matmul(c.ps[pb], lhs, rhs, start=(kt == 0), stop=(kt == NKT - 1)),
                           reads=wk(wi * 1024, (wi + 1) * 1024) +
                                 keys("xnb", (kt * 8 + tm) * 512, (kt * 8 + tm) * 512 + 512),
                           writes=[(f"p{pb}",)])
                osb = c.wst[:, dst0 + tm * 512: dst0 + (tm + 1) * 512]
                p.comp("scalar", lambda osb=osb, pb=pb: s.copy(osb, c.ps[pb]),
                       reads=[(f"p{pb}",)], writes=wk(dst0 + tm * 512, dst0 + (tm + 1) * 512))
        for h2 in range(2):
            hh = hp * 2 + h2
            b0 = h2 * 64
            wrsb = c.wst[b0:b0 + 64, 3072 + h2 * 128:3072 + h2 * 128 + 128]
            rbr = c.rows[0:1, 160:288]
            p.dma("sync", lambda rbr=rbr, hh=hh: nc.sync.dma_start(out=rbr, in_=iw["a1_br"][hh, :]),
                  writes=[("rows", 0)])
            brc = c.cols[:, 536:537]
            p.comp("tensor", lambda rbr=rbr: t.transpose(c.ps[4][0:128, 0:1], rbr, c.idF[0:1, 0:1]),
                   reads=[("rows", 0), ("consts", 0)], writes=[("p4",)])
            p.comp("vector", lambda brc=brc: v.tensor_copy(brc, c.ps[4][:, 0:1]),
                   reads=[("p4",)], writes=[("cols", 1)])
            for tm in range(NTM):
                pb = tm % 2
                lhs = wrsb
                rhs = c.wst[b0:b0 + 64, 4096 + tm * 512:4096 + (tm + 1) * 512]
                p.comp("tensor", lambda lhs=lhs, rhs=rhs, pb=pb:
                       t.matmul(c.ps[pb], lhs, rhs, start=True, stop=True),
                       reads=wk(3072, 3584) + wk(4096 + tm * 512, 4096 + (tm + 1) * 512),
                       writes=[(f"p{pb}",)])
                prj = c.f32a[:, (tm % 2) * 512:(tm % 2) * 512 + 512]
                pjk = [("f32a", tm % 2)]
                p.comp("scalar", lambda prj=prj, pb=pb, brc=brc:
                       s.activation(prj, c.ps[pb], AF.Identity, bias=brc, scale=1.0),
                       reads=[(f"p{pb}",), ("cols", 1)], writes=pjk)
                emit_sincos(c, prj, pjk, c.qkv[:, tm * 512:(tm + 1) * 512],
                            c.qkv[:, 4096 + tm * 512:4096 + (tm + 1) * 512],
                            qv(tm * 512, (tm + 1) * 512), qv(4096 + tm * 512, 4096 + (tm + 1) * 512),
                            halfpi, tm % 2)
            for ttg in range(8):
                for j in range(4):
                    tt = ttg * 4 + j
                    lhs = c.wst[b0:b0 + 64, 8192 + tt * 128:8192 + (tt + 1) * 128]
                    p.comp("tensor", lambda lhs=lhs, wrsb=wrsb, j=j:
                           t.matmul(c.ps[j][:, 0:128], lhs, wrsb, start=True, stop=False),
                           reads=wk(8192 + tt * 128, 8192 + (tt + 1) * 128) + wk(3072, 3584),
                           writes=[(f"p{j}",)])
                    p.comp("tensor", lambda rbr=rbr, j=j:
                           t.matmul(c.ps[j][:, 0:128], c.onesrow, rbr, start=False, stop=True),
                           reads=[("rows", 0), ("rows", 3)], writes=[(f"p{j}",)])
                prj = c.f32a[:, 1024 + (ttg % 2) * 512:1024 + (ttg % 2) * 512 + 512]
                pjk = [("f32a", 2 + (ttg % 2))]
                for j in range(4):
                    p.comp("scalar", lambda prj=prj, j=j: s.copy(prj[:, j * 128:(j + 1) * 128], c.ps[j][:, 0:128]),
                           reads=[(f"p{j}",)], writes=pjk)
                e3 = c.ework[:, ttg * 1024:(ttg + 1) * 1024].rearrange("p (a b) -> p a b", b=256)
                emit_sincos(c, prj, pjk, e3[:, :, 0:128], e3[:, :, 128:256],
                            ew(ttg * 1024, (ttg + 1) * 1024), ew(ttg * 1024, (ttg + 1) * 1024),
                            halfpi, 2 + ttg % 2)
            if h2 == 0:
                for tt in range(32):
                    src_ = c.wst[:, 12288 + tt * 128:12288 + (tt + 1) * 128]
                    qb = (tt // 8) % 2
                    p.comp("tensor", lambda src_=src_, qb=qb, tt=tt:
                           t.transpose(c.qs[qb][:, (tt % 8) * 128:(tt % 8) * 128 + 128], src_, c.idB),
                           reads=wk(12288 + tt * 128, 12288 + (tt + 1) * 128) + [("cbf", 0)], writes=[(f"q{qb}",)])
                    if tt % 8 == 7:
                        g0 = tt - 7
                        dst_ = c.wst[:, 12288 + g0 * 128:12288 + g0 * 128 + 1024]
                        p.comp("vector", lambda dst_=dst_, qb=qb:
                               v.tensor_copy(dst_, c.qs[qb][:, 0:1024]),
                               reads=[(f"q{qb}",)], writes=wk(12288 + g0 * 128, 12288 + g0 * 128 + 1024))
            for fp in range(2):
                for tt in range(32):
                    lhs = c.ework[:, tt * 256 + fp * 128:tt * 256 + fp * 128 + 128]
                    vt = c.wst[:, 12288 + tt * 128 + h2 * 64:12288 + tt * 128 + h2 * 64 + 64]
                    p.comp("tensor", lambda lhs=lhs, vt=vt, fp=fp, tt=tt:
                           t.matmul(c.ps[2 + fp][:, 0:64], lhs, vt, start=(tt == 0), stop=(tt == 31)),
                           reads=ew(tt * 256 + fp * 128, tt * 256 + fp * 128 + 128) +
                                 wk(12288 + tt * 128, 12288 + (tt + 1) * 128),
                           writes=[(f"p{2 + fp}",)])
                    p.comp("tensor", lambda lhs=lhs, fp=fp, tt=tt:
                           t.matmul(c.ps[4 + fp][:, 0:1], lhs, c.onescolB, start=(tt == 0), stop=(tt == 31)),
                           reads=ew(tt * 256 + fp * 128, tt * 256 + fp * 128 + 128) + [("cbf", 0)],
                           writes=[(f"p{4 + fp}",)])
            for fp in range(2):
                kv = c.att[:, 65 * fp:65 * fp + 65]
                p.comp("vector", lambda kv=kv, fp=fp: v.tensor_copy(kv[:, 0:64], c.ps[2 + fp][:, 0:64]),
                       reads=[(f"p{2 + fp}",)], writes=ak(65 * fp, 65 * fp + 65))
                p.comp("vector", lambda kv=kv, fp=fp: v.tensor_copy(kv[:, 64:65], c.ps[4 + fp][:, 0:1]),
                       reads=[(f"p{4 + fp}",)], writes=ak(65 * fp, 65 * fp + 65))
            for tm in range(NTM):
                for fp in range(2):
                    kv = c.att[:, 65 * fp:65 * fp + 65]
                    rhs = c.qkv[:, fp * 4096 + tm * 512: fp * 4096 + (tm + 1) * 512]
                    p.comp("tensor", lambda kv=kv, rhs=rhs, fp=fp:
                           t.matmul(c.ps[0][0:65, :], kv, rhs, start=(fp == 0), stop=(fp == 1)),
                           reads=ak(65 * fp, 65 * fp + 65) + qv(fp * 4096 + tm * 512, fp * 4096 + (tm + 1) * 512),
                           writes=[("p0",)])
                dn = c.rows[0:1, 644:1156]
                p.comp("vector", lambda dn=dn: v.tensor_scalar(dn, c.ps[0][64:65, :], EPS2, None, ALU.add),
                       reads=[("p0",)], writes=[("rows", 1), ("rows", 2)])
                p.comp("scalar", lambda dn=dn: s.activation(dn, dn, AF.Ln, bias=0.0, scale=1.0),
                       reads=[("rows", 1), ("rows", 2)], writes=[("rows", 1), ("rows", 2)])
                p.comp("scalar", lambda dn=dn: s.activation(dn, dn, AF.Exp, bias=0.0, scale=-1.0),
                       reads=[("rows", 1), ("rows", 2)], writes=[("rows", 1), ("rows", 2)])
                p.comp("tensor", lambda dn=dn: t.matmul(c.ps[4][0:64, :], c.onesrow[0:1, 0:64], dn, start=True, stop=True),
                       reads=[("rows", 1), ("rows", 2), ("rows", 3)], writes=[("p4",)])
                cx = c.f32c[:, 0:512]
                p.comp("vector", lambda cx=cx: v.tensor_copy(cx[0:64, :], c.ps[0][0:64, :]),
                       reads=[("p0",)], writes=[("f32c", 0)])
                p.comp("vector", lambda cx=cx: v.tensor_mul(cx[0:64, :], cx[0:64, :], c.ps[4][0:64, :]),
                       reads=[("f32c", 0), ("p4",)], writes=[("f32c", 0)])
                ob = c.att[0:64, 1024:1536]
                p.comp("scalar", lambda ob=ob, cx=cx: s.copy(ob, cx[0:64, :]),
                       reads=[("f32c", 0)], writes=ak(1024, 1536))
                cap_ = c.ctxT[hh * 64:(hh + 1) * 64, tm * 512:(tm + 1) * 512]
                p.dma("sync", lambda cap_=cap_, ob=ob: nc.sync.dma_start(out=cap_, in_=ob),
                      reads=ak(1024, 1536), writes=[("ctxT", (hh // 2) * 8 + tm)])
    emit_proj_residual(c, c.iw["a1_wpT"], D, bias_dram=c.iw["a1_bp"])


def emit_sincos(c, prj, pjk, cosdst, sindst, cosk, sink, halfpi, scratch_id):
    nc, p = c.nc, c.p
    v, s = nc.vector, nc.scalar
    N = prj.shape[1]
    isl = (scratch_id % 4) * 128 if N <= 128 else (scratch_id % 2) * 512
    k32 = c.i512[:, isl:isl + N]
    kf = c.f32b[:, scratch_id * 512:scratch_id * 512 + N]
    kfk = [("f32b", scratch_id)]
    zz = c.f32c[:, scratch_id * 512:scratch_id * 512 + N]
    zzk = [("f32c", scratch_id)]
    aa = c.f32b[:, (scratch_id + 4) * 512:(scratch_id + 4) * 512 + N]
    aak = [("f32b", scratch_id + 4)]
    ik = [("i512", isl // 128), ("i512", isl // 128 + (1 if N > 128 else 0))]
    p.comp("vector", lambda: v.tensor_single_scalar(k32, prj, 1.0 / (2 * PI), ALU.mult),
           reads=pjk, writes=ik)
    p.comp("vector", lambda: v.tensor_copy(kf, k32), reads=ik, writes=kfk)
    p.comp("vector", lambda: v.scalar_tensor_tensor(zz, kf, 2 * PI, prj, ALU.mult, ALU.subtract),
           reads=kfk + pjk, writes=zzk)
    p.comp("scalar", lambda: s.activation(sindst, zz, AF.Sin, bias=0.0, scale=-1.0),
           reads=zzk, writes=sink)
    p.comp("scalar", lambda: s.activation(aa, zz, AF.Abs, bias=0.0, scale=1.0),
           reads=zzk, writes=aak)
    p.comp("scalar", lambda: s.activation(cosdst, aa, AF.Sin, bias=halfpi, scale=-1.0),
           reads=aak + [("cols", 1)], writes=cosk)


def emit_l1_attn_b(c):
    nc, p = c.nc, c.p
    v, s, t, g = nc.vector, nc.scalar, nc.tensor, nc.gpsimd
    iw = c.iw
    ak = lambda c0, c1: keys("att", c0, c1)
    ew = lambda c0, c1: keys("ework", c0, c1)
    wk = lambda c0, c1: keys("wst", c0, c1)
    emit_ln(c, c.X, "X", dst_sbuf=True, dst32=c.tT, dst32key="tT")
    emit_bias_cols(c, iw["b1_bq"], 8, 40)
    emit_bias_cols(c, iw["b1_bk"], 8, 48)
    emit_bias_cols(c, iw["b1_bv"], 8, 56)
    for hp in range(8):
        for wi, nm in enumerate(("b1_wqT", "b1_wkT", "b1_wvT")):
            wap = iw[nm][:, hp * 128:(hp + 1) * 128].rearrange("(kt p) m -> p kt m", p=128)
            wsb = c.wst[:, wi * 1024:(wi + 1) * 1024]
            p.dma("gpsimd", lambda wsb=wsb, wap=wap: nc.gpsimd.dma_start(out=wsb, in_=wap),
                  writes=wk(wi * 1024, (wi + 1) * 1024))
        for wi, dst0, bcol0, act in ((0, 4096, 40, AF.Relu), (1, 8192, 48, AF.Relu), (2, 12288, 56, AF.Identity)):
            for tm in range(NTM):
                pb = tm % 2
                for kt in range(NKT):
                    lhs = c.wst[:, wi * 1024 + kt * 128: wi * 1024 + (kt + 1) * 128]
                    rhs = c.xnb[:, kt * T + tm * 512: kt * T + (tm + 1) * 512]
                    p.comp("tensor", lambda lhs=lhs, rhs=rhs, pb=pb, kt=kt:
                           t.matmul(c.ps[pb], lhs, rhs, start=(kt == 0), stop=(kt == NKT - 1)),
                           reads=wk(wi * 1024, (wi + 1) * 1024) +
                                 keys("xnb", (kt * 8 + tm) * 512, (kt * 8 + tm) * 512 + 512),
                           writes=[(f"p{pb}",)])
                osb = c.wst[:, dst0 + tm * 512: dst0 + (tm + 1) * 512]
                bcol = c.cols[:, bcol0 + hp:bcol0 + hp + 1]
                p.comp("scalar", lambda osb=osb, pb=pb, bcol=bcol, act=act:
                       s.activation(osb, c.ps[pb], act, bias=bcol, scale=1.0),
                       reads=[(f"p{pb}",), ("cols", 0)], writes=wk(dst0 + tm * 512, dst0 + (tm + 1) * 512))
        for which, src0, dst0 in (("k", 8192, 0), ("v", 12288, 4096)):
            for tt in range(32):
                src_ = c.wst[:, src0 + tt * 128:src0 + (tt + 1) * 128]
                qb = (tt // 8) % 2
                p.comp("tensor", lambda src_=src_, qb=qb, tt=tt:
                       t.transpose(c.qs[qb][:, (tt % 8) * 128:(tt % 8) * 128 + 128], src_, c.idB),
                       reads=wk(src0 + tt * 128, src0 + (tt + 1) * 128) + [("cbf", 0)], writes=[(f"q{qb}",)])
                if tt % 8 == 7:
                    g0 = tt - 7
                    dst_ = c.ework[:, dst0 + g0 * 128:dst0 + g0 * 128 + 1024]
                    p.comp("vector", lambda dst_=dst_, qb=qb:
                           v.tensor_copy(dst_, c.qs[qb][:, 0:1024]),
                           reads=[(f"q{qb}",)], writes=ew(dst0 + g0 * 128, dst0 + g0 * 128 + 1024))
        for h2 in range(2):
            hh = hp * 2 + h2
            b0 = h2 * 64
            for tt in range(32):
                kt_ = c.ework[:, tt * 128 + b0:tt * 128 + b0 + 64]
                vt_ = c.ework[:, 4096 + tt * 128 + b0:4096 + tt * 128 + b0 + 64]
                p.comp("tensor", lambda kt_=kt_, vt_=vt_, tt=tt, h2=h2:
                       t.matmul(c.ps[2 + h2][0:64, 0:64], kt_, vt_, start=(tt == 0), stop=(tt == 31)),
                       reads=ew(tt * 128, (tt + 1) * 128) + ew(4096 + tt * 128, 4096 + (tt + 1) * 128),
                       writes=[(f"p{2 + h2}",)])
                p.comp("tensor", lambda kt_=kt_, tt=tt, h2=h2:
                       t.matmul(c.ps[4 + h2][0:64, 0:1], kt_, c.onescolB, start=(tt == 0), stop=(tt == 31)),
                       reads=ew(tt * 128, (tt + 1) * 128) + [("cbf", 0)], writes=[(f"p{4 + h2}",)])
        for h2 in range(2):
            b0 = h2 * 64
            kv = c.att[b0:b0 + 64, 2048 - 65 - h2 * 65:2048 - h2 * 65]
            p.comp("vector", lambda kv=kv, h2=h2: v.tensor_copy(kv[:, 0:64], c.ps[2 + h2][0:64, 0:64]),
                   reads=[(f"p{2 + h2}",)], writes=ak(1918, 2048))
            p.comp("vector", lambda kv=kv, h2=h2: v.tensor_copy(kv[:, 64:65], c.ps[4 + h2][0:64, 0:1]),
                   reads=[(f"p{4 + h2}",)], writes=ak(1918, 2048))
        for h2 in range(2):
            hh = hp * 2 + h2
            b0 = h2 * 64
            kv = c.att[b0:b0 + 64, 2048 - 65 - h2 * 65:2048 - h2 * 65]
            for tm in range(NTM):
                rhs = c.wst[b0:b0 + 64, 4096 + tm * 512:4096 + (tm + 1) * 512]
                p.comp("tensor", lambda kv=kv, rhs=rhs:
                       t.matmul(c.ps[0][0:65, :], kv, rhs, start=True, stop=True),
                       reads=ak(1918, 2048) + wk(4096 + tm * 512, 4096 + (tm + 1) * 512),
                       writes=[("p0",)])
                dn = c.rows[0:1, 644:1156]
                p.comp("vector", lambda dn=dn: v.tensor_scalar(dn, c.ps[0][64:65, :], 1e-6, None, ALU.max),
                       reads=[("p0",)], writes=[("rows", 1), ("rows", 2)])
                p.comp("scalar", lambda dn=dn: s.activation(dn, dn, AF.Ln, bias=0.0, scale=1.0),
                       reads=[("rows", 1), ("rows", 2)], writes=[("rows", 1), ("rows", 2)])
                p.comp("scalar", lambda dn=dn: s.activation(dn, dn, AF.Exp, bias=0.0, scale=-1.0),
                       reads=[("rows", 1), ("rows", 2)], writes=[("rows", 1), ("rows", 2)])
                p.comp("tensor", lambda dn=dn: t.matmul(c.ps[4][0:64, :], c.onesrow[0:1, 0:64], dn, start=True, stop=True),
                       reads=[("rows", 1), ("rows", 2), ("rows", 3)], writes=[("p4",)])
                cx = c.f32c[:, 0:512]
                p.comp("vector", lambda cx=cx: v.tensor_copy(cx[0:64, :], c.ps[0][0:64, :]),
                       reads=[("p0",)], writes=[("f32c", 0)])
                p.comp("vector", lambda cx=cx: v.tensor_mul(cx[0:64, :], cx[0:64, :], c.ps[4][0:64, :]),
                       reads=[("f32c", 0), ("p4",)], writes=[("f32c", 0)])
                tt_ = c.f32c[:, 512:1024]
                tap_ = c.tT[hh * 64:(hh + 1) * 64, tm * 512:(tm + 1) * 512]
                p.dma("scalar", lambda tt_=tt_, tap_=tap_: nc.scalar.dma_start(out=tt_[0:64, :], in_=tap_),
                      reads=[("tT", (hh // 2) * 8 + tm)], writes=[("f32c", 1)])
                p.comp("vector", lambda cx=cx, tt_=tt_: v.tensor_add(cx[0:64, :], cx[0:64, :], tt_[0:64, :]),
                       reads=[("f32c", 0), ("f32c", 1)], writes=[("f32c", 0)])
                yap = c.yT[hh * 64:(hh + 1) * 64, tm * 512:(tm + 1) * 512]
                p.dma("sync", lambda yap=yap, cx=cx: nc.sync.dma_start(out=yap, in_=cx[0:64, :]),
                      reads=[("f32c", 0)], writes=[("yT", (hh // 2) * 8 + tm)])
    emit_bias_cols(c, iw["b1_gamma"], 8, 8)
    emit_bias_cols(c, iw["b1_beta"], 8, 16)
    emit_ln_add_to_X(c)


def emit_ln_add_to_X(c):
    nc, p = c.nc, c.p
    v, s, t = nc.vector, nc.scalar, nc.tensor
    for m in range(NTM):
        for kt in range(NKT):
            buf = c.f32a[:, kt * 512:(kt + 1) * 512]
            srcap = c.yT[kt * 128:(kt + 1) * 128, m * 512:(m + 1) * 512]
            p.dma("sync", lambda b=buf, a=srcap: nc.sync.dma_start(out=b, in_=a),
                  reads=[("yT", kt * 8 + m)], writes=[("f32a", kt)])
        for kt in range(NKT):
            buf = c.f32a[:, kt * 512:(kt + 1) * 512]
            sq = c.f32c[:, kt * 512:(kt + 1) * 512]
            p.comp("scalar", lambda sq=sq, b=buf: s.activation(sq, b, AF.Square, bias=0.0, scale=1.0),
                   reads=[("f32a", kt)], writes=[("f32c", kt)])
            p.comp("tensor", lambda b=buf, kt=kt: t.matmul(c.ps[4][0:1, :], c.onescol, b,
                                                           start=(kt == 0), stop=(kt == NKT - 1)),
                   reads=[("f32a", kt), ("consts", 0)], writes=[("p4",)])
            p.comp("tensor", lambda sq=sq, kt=kt: t.matmul(c.ps[5][0:1, :], c.onescol, sq,
                                                           start=(kt == 0), stop=(kt == NKT - 1)),
                   reads=[("f32c", kt), ("consts", 0)], writes=[("p5",)])
        r0 = c.rows[0:1, 0:512]; r1 = c.rows[0:1, 512:1024]; r2 = c.rows[0:1, 1024:1536]
        p.comp("vector", lambda: v.tensor_single_scalar(r0, c.ps[4][0:1, :], 1.0 / D, ALU.mult),
               reads=[("p4",)], writes=[("rows", 0)])
        p.comp("vector", lambda: v.tensor_single_scalar(r1, c.ps[5][0:1, :], 1.0 / D, ALU.mult),
               reads=[("p5",)], writes=[("rows", 1)])
        p.comp("vector", lambda: v.scalar_tensor_tensor(r2, r0, 1.0, r0, ALU.mult, ALU.mult),
               reads=[("rows", 0)], writes=[("rows", 2)])
        p.comp("vector", lambda: v.tensor_sub(r1, r1, r2),
               reads=[("rows", 1), ("rows", 2)], writes=[("rows", 1)])
        epsr = constrow(c, 1e-5)
        p.comp("scalar", lambda epsr=epsr: s.activation(r2, r1, AF.Ln, bias=epsr, scale=1.0),
               reads=[("rows", 1), ("cols", 1)], writes=[("rows", 2)])
        p.comp("scalar", lambda: s.activation(r1, r2, AF.Exp, bias=0.0, scale=-0.5),
               reads=[("rows", 2)], writes=[("rows", 1)])
        p.comp("vector", lambda: v.tensor_single_scalar(r0, r0, -1.0, ALU.mult),
               reads=[("rows", 0)], writes=[("rows", 0)])
        p.comp("tensor", lambda: t.matmul(c.ps[4], c.onesrow, r0, start=True, stop=True),
               reads=[("rows", 0), ("rows", 3)], writes=[("p4",)])
        p.comp("tensor", lambda: t.matmul(c.ps[5], c.onesrow, r1, start=True, stop=True),
               reads=[("rows", 1), ("rows", 3)], writes=[("p5",)])
        for kt in range(NKT):
            buf = c.f32a[:, kt * 512:(kt + 1) * 512]
            xb = c.f32b[:, kt * 512:(kt + 1) * 512]
            p.comp("vector", lambda xb=xb, b=buf: v.tensor_add(xb, b, c.ps[4]),
                   reads=[("f32a", kt), ("p4",)], writes=[("f32b", kt)])
            p.comp("vector", lambda xb=xb: v.tensor_mul(xb, xb, c.ps[5]),
                   reads=[("f32b", kt), ("p5",)], writes=[("f32b", kt)])
            gcol = c.cols[:, 8 + kt:9 + kt]; bcol = c.cols[:, 16 + kt:17 + kt]
            p.comp("vector", lambda xb=xb, gcol=gcol, bcol=bcol:
                   v.tensor_scalar(xb, xb, gcol, bcol, ALU.mult, ALU.add),
                   reads=[("f32b", kt), ("cols", 0)], writes=[("f32b", kt)])
            xt_ = c.f32c[:, kt * 512:(kt + 1) * 512]
            Xap = c.X[kt * 128:(kt + 1) * 128, m * 512:(m + 1) * 512]
            p.dma("scalar", lambda xt_=xt_, Xap=Xap: nc.scalar.dma_start(out=xt_, in_=Xap),
                  reads=[("X", kt * 8 + m)], writes=[("f32c", kt)])
            p.comp("vector", lambda xb=xb, xt_=xt_: v.tensor_add(xb, xb, xt_),
                   reads=[("f32b", kt), ("f32c", kt)], writes=[("f32b", kt)])
            p.dma("sync", lambda Xap=Xap, xb=xb: nc.sync.dma_start(out=Xap, in_=xb),
                  reads=[("f32b", kt)], writes=[("X", kt * 8 + m)])


def emit_finish(c):
    nc, p = c.nc, c.p
    for kt in range(NKT):
        for m in range(NTM):
            src = c.X[kt * 128:(kt + 1) * 128, m * 512:(m + 1) * 512]
            dst = c.out_T[kt * 128:(kt + 1) * 128, m * 512:(m + 1) * 512]
            buf = c.f32a[:, m * 512:(m + 1) * 512]
            p.dma("sync", lambda src=src, buf=buf: nc.sync.dma_start(out=buf, in_=src),
                  reads=[("X", kt * 8 + m)], writes=[("f32a", m)])
            p.dma("sync", lambda dst=dst, buf=buf: nc.sync.dma_start(out=dst, in_=buf),
                  reads=[("f32a", m)], writes=[("out", kt * 8 + m)])
    p.comp("vector", lambda: nc.vector.engine_nop(),
           reads=[("out", i) for i in range(NKT * 8)])


def make_inputs(inputs, core):
    tr = lambda a: np.ascontiguousarray(np.asarray(a, dtype=np.float32).T)
    d = {}
    d["x_T"] = tr(inputs["L0_attn_a_x"][core])
    d["a0_wxT"] = tr(inputs["L0_attn_a_w_xproj"])
    d["a0_bx"] = np.asarray(inputs["L0_attn_a_b_xproj"], np.float32)
    d["a0_A"] = np.asarray(inputs["L0_attn_a_A"], np.float32)
    d["a0_D"] = np.asarray(inputs["L0_attn_a_D_param"], np.float32)
    d["a0_woT"] = tr(inputs["L0_attn_a_w_out"])
    d["a0_bo"] = np.asarray(inputs["L0_attn_a_b_out"], np.float32)
    d["a0_gamma"] = np.asarray(inputs["L0_attn_a_gamma"], np.float32)
    d["a0_beta"] = np.asarray(inputs["L0_attn_a_beta"], np.float32)
    d["b0_wqT"] = tr(inputs["L0_attn_b_Wq"])
    d["b0_wkT"] = tr(inputs["L0_attn_b_Wk"])
    d["b0_wvT"] = tr(inputs["L0_attn_b_Wv"])
    d["b0_woT"] = tr(inputs["L0_attn_b_Wo"])
    d["b0_bo"] = np.asarray(inputs["L0_attn_b_bo"], np.float32)
    d["b0_wspanT"] = tr(inputs["L0_attn_b_Wspan"])
    d["b0_bspan"] = np.asarray(inputs["L0_attn_b_bspan"], np.float32)
    d["f0_rwT"] = tr(inputs["L0_ffn_router_w"])
    d["f0_w1"] = np.asarray(inputs["L0_ffn_W1"], np.float32).reshape(8 * 1024, 4096)
    d["f0_b1"] = np.asarray(inputs["L0_ffn_b1"], np.float32)
    d["f0_w2"] = np.asarray(inputs["L0_ffn_W2"], np.float32).reshape(8 * 4096, 1024)
    d["f0_b2"] = np.asarray(inputs["L0_ffn_b2"], np.float32)
    d["a1_wqT"] = tr(inputs["L1_attn_a_Wq"])
    d["a1_wkT"] = tr(inputs["L1_attn_a_Wk"])
    d["a1_wvT"] = tr(inputs["L1_attn_a_Wv"])
    d["a1_wpT"] = tr(inputs["L1_attn_a_Wp"])
    d["a1_bp"] = np.asarray(inputs["L1_attn_a_bp"], np.float32)
    d["a1_wr"] = np.asarray(inputs["L1_attn_a_Wr"], np.float32).reshape(16 * 64, 128)
    d["a1_br"] = np.asarray(inputs["L1_attn_a_br"], np.float32)
    d["b1_wqT"] = tr(inputs["L1_attn_b_Wq"])
    d["b1_wkT"] = tr(inputs["L1_attn_b_Wk"])
    d["b1_wvT"] = tr(inputs["L1_attn_b_Wv"])
    d["b1_bq"] = np.asarray(inputs["L1_attn_b_bq"], np.float32)
    d["b1_bk"] = np.asarray(inputs["L1_attn_b_bk"], np.float32)
    d["b1_bv"] = np.asarray(inputs["L1_attn_b_bv"], np.float32)
    d["b1_gamma"] = np.asarray(inputs["L1_attn_b_gamma"], np.float32)
    d["b1_beta"] = np.asarray(inputs["L1_attn_b_beta"], np.float32)
    d["f1_wgT"] = tr(inputs["L1_ffn_Wg"])
    d["f1_w1"] = np.asarray(inputs["L1_ffn_W1"], np.float32).reshape(8 * 1024, 4096)
    d["f1_w2"] = np.asarray(inputs["L1_ffn_W2"], np.float32).reshape(8 * 2048, 1024)
    pm = np.zeros(8, np.float32); pm[:core] = 1.0
    d["prevmask"] = pm
    return d


def kernel(**inputs):
    nc = build_program(stop_after=8)
    in_maps = [make_inputs(inputs, core) for core in range(NCORES)]
    res = run_bass_kernel_spmd(nc, in_maps, core_ids=list(range(NCORES)))
    out = np.stack([np.ascontiguousarray(res.results[i]["out_T"].T) for i in range(NCORES)])
    return out.astype(np.float32)


# version 15
# speedup vs baseline: 1.2713x; 1.2086x over previous
import math
import numpy as np
import concourse.bass as bass
import concourse.mybir as mybir
from concourse.bass_utils import run_bass_kernel_spmd

F32 = mybir.dt.float32
BF16 = mybir.dt.bfloat16
U32 = mybir.dt.uint32
I32 = mybir.dt.int32
AF = mybir.ActivationFunctionType
ALU = mybir.AluOpType
AX = mybir.AxisListType

B, T, D = 8, 4096, 1024
NCORES = 8
NKT = D // 128
NTM = T // 512
PI = math.pi

ENGINES = ("tensor", "vector", "scalar", "gpsimd", "sync")

class Op:
    __slots__ = ("engine", "kind", "build", "deps", "idx", "inc", "token", "tag", "pre_wait")
    def __init__(self, engine, kind, build, deps, idx, tag):
        self.engine = engine; self.kind = kind; self.build = build
        self.deps = deps; self.idx = idx; self.inc = None; self.token = None
        self.tag = tag; self.pre_wait = None

class Prog:
    SERIALIZE = False
    SER_ENGINE = False
    NDMA = 48
    PE_SKIP = True
    def __init__(self, nc):
        self.nc = nc
        self.ops = []
        self.last_w = {}
        self.readers = {}
        self.tag = ""
        self.bg = False
        self.bg_ids = set()

    def _record(self, engine, kind, build, reads, writes):
        i = len(self.ops)
        deps = set()
        for k in reads:
            w = self.last_w.get(k)
            if w is not None:
                deps.add(w)
        for k in writes:
            w = self.last_w.get(k)
            if w is not None:
                deps.add(w)
            rs = self.readers.get(k)
            if rs:
                deps.update(rs)
        deps.discard(i)
        if Prog.SERIALIZE and i > 0:
            deps.add(i - 1)
        if Prog.SER_ENGINE:
            for j in range(i - 1, -1, -1):
                if self.ops[j].engine == engine:
                    deps.add(j)
                    break
        if engine == "tensor" and Prog.PE_SKIP and not Prog.SERIALIZE:
            deps = {d for d in deps if self.ops[d].engine != "tensor"}
        op = Op(engine, kind, build, deps, i, self.tag)
        self.ops.append(op)
        if self.bg:
            self.bg_ids.add(i)
        for k in reads:
            self.readers.setdefault(k, []).append(i)
        for k in writes:
            self.last_w[k] = i
            self.readers[k] = []
        return i

    def comp(self, engine, build, reads=(), writes=()):
        return self._record(engine, "comp", build, reads, writes)

    def dma(self, engine, build, reads=(), writes=()):
        return self._record(engine, "dma", build, reads, writes)

    def coll(self, build, reads=(), writes=()):
        return self._record("gpsimd", "coll", build, reads, writes)

    def emit(self, stack):
        nc = self.nc
        need_inc = set()
        for op in self.ops:
            need_inc |= op.deps
        if self.bg_ids:
            bg = [op for op in self.ops if op.idx in self.bg_ids]
            main = [op for op in self.ops if op.idx not in self.bg_ids]
            b0 = [j for j, op in enumerate(main) if op.tag == "blk0"]
            lo, hi = b0[0], b0[-1]
            n = len(bg)
            slots = [lo + int((k + 1) * (hi - lo) / (n + 1)) for k in range(n)]
            order = []
            bi = 0
            for j, op in enumerate(main):
                order.append(op)
                while bi < n and slots[bi] <= j:
                    order.append(bg[bi]); bi += 1
            order.extend(bg[bi:])
        else:
            order = self.ops
        esem = {e: stack.enter_context(nc.semaphore(f"se_{e}")) for e in ENGINES}
        pools = {
            "sync": [stack.enter_context(nc.semaphore(f"sdy_{i}")) for i in range(24)],
            "scalar": [stack.enter_context(nc.semaphore(f"sdc_{i}")) for i in range(16)],
            "gpsimd": [stack.enter_context(nc.semaphore(f"sdg_{i}")) for i in range(24)],
        }
        csem = stack.enter_context(nc.semaphore("scoll"))
        duse = {q: [0] * len(pool) for q, pool in pools.items()}
        dnext = {q: 0 for q in pools}
        ccount = 0
        ecount = {e: 0 for e in ENGINES}
        for op in order:
            if op.kind == "comp" and op.idx not in need_inc:
                continue
            if op.kind == "comp":
                ecount[op.engine] += 1
                op.token = (esem[op.engine], ecount[op.engine])
                op.inc = (esem[op.engine], 1)
            elif op.kind == "coll":
                ccount += 1
                if ccount > 1:
                    op.pre_wait = (csem, ccount - 1)
                op.token = (csem, ccount)
                op.inc = (csem, 1)
            else:
                q = op.engine
                pool = pools[q]
                si = dnext[q] % len(pool); dnext[q] += 1
                duse[q][si] += 1
                if duse[q][si] > 1:
                    op.pre_wait = (pool[si], 16 * (duse[q][si] - 1))
                op.token = (pool[si], 16 * duse[q][si])
                op.inc = (pool[si], 16)
        guaranteed = {e: {} for e in ENGINES}
        engobj = {e: getattr(nc, e) for e in ENGINES}
        for op in order:
            g = guaranteed[op.engine]
            waits = {}
            if op.pre_wait is not None:
                s, v = op.pre_wait
                if g.get(s.name, 0) < v:
                    waits[s] = v
            for d in op.deps:
                tk = self.ops[d].token
                assert tk is not None, f"dep {d} has no token"
                s, v = tk
                if g.get(s.name, 0) < v:
                    if waits.get(s, 0) < v:
                        waits[s] = v
            for s, v in waits.items():
                engobj[op.engine].wait_ge(s, v)
                g[s.name] = v
            inst = op.build()
            if op.inc is not None:
                assert inst is not None, f"op {op.idx} ({op.tag}) needs inc but returned no instruction"
                inst.then_inc(*op.inc)
        return esem, pools


def keys(name, c0, c1, gran=512):
    return [(name, g) for g in range(c0 // gran, (c1 + gran - 1) // gran)]


class Ctx:
    pass


def build_program(stop_after=8, debug_dump=()):
    nc = bass.Bass("TRN2", target_bir_lowering=False, debug=False)
    import contextlib
    stack = contextlib.ExitStack()
    c = Ctx()
    c.nc = nc
    c.stack = stack
    p = Prog(nc)
    c.p = p

    def din(name, shape, dtype=F32):
        return nc.dram_tensor(name, shape, dtype, kind="ExternalInput").ap()

    c.x_T = din("x_T", [D, T])
    c.out_T = nc.dram_tensor("out_T", [D, T], F32, kind="ExternalOutput").ap()
    iw = {}
    c.iw = iw
    iw["a0_wxT"] = din("a0_wxT", [D, 2 * D])
    iw["a0_bx"] = din("a0_bx", [2 * D])
    iw["a0_A"] = din("a0_A", [D, 16])
    iw["a0_D"] = din("a0_D", [D])
    iw["a0_woT"] = din("a0_woT", [D, D])
    iw["a0_bo"] = din("a0_bo", [D])
    iw["a0_gamma"] = din("a0_gamma", [D])
    iw["a0_beta"] = din("a0_beta", [D])
    for nm in ("b0_wqT", "b0_wkT", "b0_wvT", "b0_woT"):
        iw[nm] = din(nm, [D, D])
    iw["b0_bo"] = din("b0_bo", [D])
    iw["b0_wspanT"] = din("b0_wspanT", [D, 16])
    iw["b0_bspan"] = din("b0_bspan", [16])
    iw["f0_rwT"] = din("f0_rwT", [D, 8])
    iw["f0_w1"] = din("f0_w1", [8 * D, 4096])
    iw["f0_b1"] = din("f0_b1", [8, 4096])
    iw["f0_w2"] = din("f0_w2", [8 * 4096, D])
    iw["f0_b2"] = din("f0_b2", [8, D])
    for nm in ("a1_wqT", "a1_wkT", "a1_wvT", "a1_wpT"):
        iw[nm] = din(nm, [D, D])
    iw["a1_bp"] = din("a1_bp", [D])
    iw["a1_wr"] = din("a1_wr", [16 * 64, 128])
    iw["a1_br"] = din("a1_br", [16, 128])
    for nm in ("b1_wqT", "b1_wkT", "b1_wvT"):
        iw[nm] = din(nm, [D, D])
    iw["b1_bq"] = din("b1_bq", [D])
    iw["b1_bk"] = din("b1_bk", [D])
    iw["b1_bv"] = din("b1_bv", [D])
    iw["b1_gamma"] = din("b1_gamma", [D])
    iw["b1_beta"] = din("b1_beta", [D])
    iw["f1_wgT"] = din("f1_wgT", [D, 8])
    iw["f1_w1"] = din("f1_w1", [8 * D, 4096])
    iw["f1_w2"] = din("f1_w2", [8 * 2048, D])
    iw["prevmask"] = din("prevmask", [8])

    c.X = nc.dram_tensor("Xs", [D, T], F32).ap()
    c.tT = nc.dram_tensor("tT", [D, T], F32).ap()
    c.xn32 = nc.dram_tensor("xn32", [D, T], F32).ap()
    c.ctxT = nc.dram_tensor("ctxT", [2 * D, T], BF16).ap()
    c.xtok = nc.dram_tensor("xtok", [T + 1, D], BF16).ap()
    c.yT = nc.dram_tensor("yT", [D, T], F32).ap()
    c.strips = nc.dram_tensor("strips", [16, 4608], BF16).ap()
    c.table = nc.dram_tensor("mtable", [11776], U32).ap()
    c.expout = nc.dram_tensor("expout", [11776, D], F32).ap()
    c.entE = nc.dram_tensor("entE", [8192], F32).ap()
    c.entG = nc.dram_tensor("entG", [8192], F32).ap()
    c.entS = nc.dram_tensor("entS", [8192], F32).ap()
    c.entK = nc.dram_tensor("entK", [8192], F32).ap()
    c.cc_p_in = nc.dram_tensor("cc_p_in", [32, 128], F32).ap()
    c.cc_e_in = nc.dram_tensor("cc_e_in", [32, 128], F32).ap()
    c.cc_p_out = nc.dram_tensor("cc_p_out", [NCORES * 32, 128], F32).ap()
    c.cc_e_out = nc.dram_tensor("cc_e_out", [NCORES * 32, 128], F32).ap()
    c.cc_cnt_in = nc.dram_tensor("cc_cnt_in", [1, 8], F32).ap()
    c.cc_cnt_out = nc.dram_tensor("cc_cnt_out", [NCORES, 8], F32).ap()
    c.w1b = [nc.dram_tensor("w1b0", [8 * D, 4096], BF16).ap(),
             nc.dram_tensor("w1b1", [8 * D, 4096], BF16).ap()]
    c.w2b = [nc.dram_tensor("w2b0", [8 * 4096, D], BF16).ap(),
             nc.dram_tensor("w2b1", [8 * 2048, D], BF16).ap()]
    c.moe_wconv_done = {0: False, 1: False}

    sb = lambda name, shape, dt: stack.enter_context(nc.sbuf_tensor(name, shape, dt)).ap()
    c.xnb = sb("xnb", [128, NKT * T], BF16)
    c.wst = sb("wst", [128, 16384], BF16)
    c.qkv = sb("qkv", [128, 8192], BF16)
    c.ework = sb("ework", [128, 8192], BF16)
    c.f32a = sb("f32a", [128, 4096], F32)
    c.f32b = sb("f32b", [128, 4096], F32)
    c.f32c = sb("f32c", [128, 4096], F32)
    c.cols = sb("cols", [128, 1024], F32)
    c.rows = sb("rows", [1, 1664], F32)
    c.idxs = sb("idxs", [128, 64], U32)
    c.moeS = sb("moeS", [128, 512], F32)
    c.att = sb("att", [128, 2560], BF16)
    c.i512 = sb("i512", [128, 1024], I32)
    c.i32s = sb("i32s", [128, 64], I32)
    c.consts = sb("consts", [128, 416], F32)
    c.cbf = sb("cbf", [128, 512], BF16)

    c.ps = [stack.enter_context(nc.psum_tensor(f"p{i}", [128, 512], F32)).ap() for i in range(6)]
    c.qs = [stack.enter_context(nc.psum_tensor(f"q{i}", [128, 1024], BF16)).ap() for i in range(2)]

    emit_consts(c)
    emit_xtok_zero_row(c)
    emit_init_X(c)

    c.wb_keys = {}
    p.bg = True
    emit_moe_wconv(c, 0)
    emit_moe_wconv(c, 1)
    p.bg = False
    c.moe_wconv_done = {0: True, 1: True}

    blocks = [
        emit_l0_attn_a,
        lambda c: emit_moe(c, layer=0),
        emit_l0_attn_b,
        lambda c: emit_moe(c, layer=0),
        emit_l1_attn_a,
        lambda c: emit_moe(c, layer=1),
        emit_l1_attn_b,
        lambda c: emit_moe(c, layer=1),
    ]
    for bi in range(stop_after):
        c.p.tag = f"blk{bi}"
        c.blk = bi
        blocks[bi](c)

    emit_finish(c)

    ndbg = 0
    for nm in debug_dump:
        ap = getattr(c, nm)
        h, w = ap.shape
        dt = ap.dtype
        dbg = nc.dram_tensor(f"dbg_{nm}", [h, w], dt, kind="ExternalOutput").ap()
        rkeys = [(nm, i) for i in range(200)]
        nrow = (h + 127) // 128
        for r in range(nrow):
            r0, r1 = r * 128, min(h, (r + 1) * 128)
            for mcol in range(0, w, 4096):
                m1 = min(w, mcol + 4096)
                src_ = ap[r0:r1, mcol:m1]
                dst_ = dbg[r0:r1, mcol:m1]
                p.dma("gpsimd", lambda src_=src_, dst_=dst_: nc.gpsimd.dma_start(out=dst_, in_=src_),
                      reads=rkeys, writes=[("dbgout", ndbg, r, mcol)])
                ndbg += 1
    if debug_dump:
        p.comp("vector", lambda: nc.vector.engine_nop(),
               reads=[k for k in p.last_w if k[0] == "dbgout"])

    p.emit(stack)
    stack.close()
    return nc


def constcell(c, val):
    if not hasattr(c, "_cells"):
        c._cells = {}
    if val not in c._cells:
        i = 960 + len(c._cells)
        assert i < 1024, "out of const cells"
        ap = c.cols[:, i:i + 1]
        g = c.nc.gpsimd
        c.p.comp("gpsimd", lambda ap=ap, val=val: g.memset(ap, float(val)), writes=[("cols", 1)])
        c._cells[val] = ap
    return c._cells[val]


def constrow(c, val):
    return constcell(c, val)[0:1, :]


def emit_consts(c):
    nc, p = c.nc, c.p
    CON = c.consts
    c.idF = CON[:, 0:128]
    c.Lst = CON[:, 128:256]
    c.Lin = CON[:, 256:384]
    c.onescol = CON[:, 384:385]
    c.iotaP = CON[:, 385:386]
    c.zcol = CON[:, 386:387]
    c.zeros512 = CON[:, 387:899] if False else None
    c.idB = c.cbf[:, 0:128]
    c.onescolB = c.cbf[:, 128:129]
    g = nc.gpsimd
    p.comp("gpsimd", lambda: g.memset(CON[:, 0:387], 0.0), writes=[("consts", 0)])
    p.comp("gpsimd", lambda: g.affine_select(out=c.idF, in_=c.idF, compare_op=ALU.not_equal,
                                             fill=1.0, base=0, pattern=[[-1, 128]], channel_multiplier=1),
           reads=[("consts", 0)], writes=[("consts", 0)])
    p.comp("gpsimd", lambda: g.memset(c.Lst, 0.0), writes=[("consts", 0)])
    p.comp("gpsimd", lambda: g.affine_select(out=c.Lst, in_=c.Lst, compare_op=ALU.is_ge,
                                             fill=1.0, base=0, pattern=[[-1, 128]], channel_multiplier=1),
           reads=[("consts", 0)], writes=[("consts", 0)])
    p.comp("gpsimd", lambda: g.memset(c.Lin, 0.0), writes=[("consts", 0)])
    p.comp("gpsimd", lambda: g.affine_select(out=c.Lin, in_=c.Lin, compare_op=ALU.is_ge,
                                             fill=1.0, base=-1, pattern=[[-1, 128]], channel_multiplier=1),
           reads=[("consts", 0)], writes=[("consts", 0)])
    p.comp("gpsimd", lambda: g.memset(c.onescol, 1.0), writes=[("consts", 0)])
    p.comp("gpsimd", lambda: g.iota(c.iotaP, [[0, 1]], base=0, channel_multiplier=1,
                                    allow_small_or_imprecise_dtypes=True), writes=[("consts", 0)])
    p.comp("gpsimd", lambda: g.memset(c.cbf[:, 0:512], 0.0), writes=[("cbf", 0)])
    p.comp("gpsimd", lambda: g.affine_select(out=c.idB, in_=c.idB, compare_op=ALU.not_equal,
                                             fill=1.0, base=0, pattern=[[-1, 128]], channel_multiplier=1),
           reads=[("cbf", 0)], writes=[("cbf", 0)])
    p.comp("gpsimd", lambda: g.memset(c.onescolB, 1.0), writes=[("cbf", 0)])
    c.io8 = CON[:, 387:395]
    p.comp("gpsimd", lambda: g.iota(c.io8, [[1, 8]], base=0, channel_multiplier=0,
                                    allow_small_or_imprecise_dtypes=True), writes=[("consts", 0)])
    c.onesrow = c.rows[0:1, 1536:1664]
    p.comp("gpsimd", lambda: g.memset(c.onesrow, 1.0), writes=[("rows", 3)])


def emit_xtok_zero_row(c):
    nc, p = c.nc, c.p
    z = c.ework[0:1, 0:1024]
    p.comp("vector", lambda: nc.vector.memset(z, 0.0), writes=keys("ework", 0, 1024))
    p.dma("sync", lambda: nc.sync.dma_start(out=c.xtok[T:T + 1, :], in_=z),
          reads=keys("ework", 0, 1024), writes=[("xtokz",)])


def emit_init_X(c):
    nc, p = c.nc, c.p
    for kt in range(NKT):
        for m in range(NTM):
            src = c.x_T[kt * 128:(kt + 1) * 128, m * 512:(m + 1) * 512]
            dst = c.X[kt * 128:(kt + 1) * 128, m * 512:(m + 1) * 512]
            buf = c.f32a[:, m * 512:(m + 1) * 512]
            p.dma("sync", lambda src=src, buf=buf: nc.sync.dma_start(out=buf, in_=src),
                  writes=[("f32a", m)])
            p.dma("sync", lambda dst=dst, buf=buf: nc.sync.dma_start(out=dst, in_=buf),
                  reads=[("f32a", m)], writes=[("X", kt * 8 + m)])


def emit_ln(c, src, srckey, gamma=None, beta=None, dst_sbuf=True, dst32=None, dst32key=None,
            src_is_sbuf=False):
    nc, p = c.nc, c.p
    v, s, t = nc.vector, nc.scalar, nc.tensor
    for m in range(NTM):
        for kt in range(NKT):
            buf = c.f32a[:, kt * 512:(kt + 1) * 512]
            if src_is_sbuf:
                srcap = src(kt, m)
                p.comp("vector", lambda b=buf, a=srcap: v.tensor_copy(b, a),
                       reads=keys(srckey, (kt * 8 + m) * 512, (kt * 8 + m) * 512 + 512),
                       writes=[("f32a", kt)])
            else:
                srcap = src[kt * 128:(kt + 1) * 128, m * 512:(m + 1) * 512]
                p.dma("sync", lambda b=buf, a=srcap: nc.sync.dma_start(out=b, in_=a),
                      reads=[(srckey, kt * 8 + m)], writes=[("f32a", kt)])
        for kt in range(NKT):
            buf = c.f32a[:, kt * 512:(kt + 1) * 512]
            sq = c.f32c[:, kt * 512:(kt + 1) * 512]
            p.comp("scalar", lambda sq=sq, b=buf: s.activation(sq, b, AF.Square, bias=0.0, scale=1.0),
                   reads=[("f32a", kt)], writes=[("f32c", kt)])
            p.comp("tensor", lambda b=buf, kt=kt: t.matmul(c.ps[4][0:1, :], c.onescol, b,
                                                           start=(kt == 0), stop=(kt == NKT - 1)),
                   reads=[("f32a", kt), ("consts", 0)], writes=[("p4",)])
            p.comp("tensor", lambda sq=sq, kt=kt: t.matmul(c.ps[5][0:1, :], c.onescol, sq,
                                                           start=(kt == 0), stop=(kt == NKT - 1)),
                   reads=[("f32c", kt), ("consts", 0)], writes=[("p5",)])
        r0 = c.rows[0:1, 0:512]; r1 = c.rows[0:1, 512:1024]; r2 = c.rows[0:1, 1024:1536]
        p.comp("vector", lambda: v.tensor_single_scalar(r0, c.ps[4][0:1, :], 1.0 / D, ALU.mult),
               reads=[("p4",)], writes=[("rows", 0)])
        p.comp("vector", lambda: v.tensor_single_scalar(r1, c.ps[5][0:1, :], 1.0 / D, ALU.mult),
               reads=[("p5",)], writes=[("rows", 1)])
        p.comp("vector", lambda: v.scalar_tensor_tensor(r2, r0, 1.0, r0, ALU.mult, ALU.mult),
               reads=[("rows", 0)], writes=[("rows", 2)])
        p.comp("vector", lambda: v.tensor_sub(r1, r1, r2),
               reads=[("rows", 1), ("rows", 2)], writes=[("rows", 1)])
        epsr = constrow(c, 1e-5)
        p.comp("scalar", lambda epsr=epsr: s.activation(r2, r1, AF.Ln, bias=epsr, scale=1.0),
               reads=[("rows", 1), ("cols", 1)], writes=[("rows", 2)])
        p.comp("scalar", lambda: s.activation(r1, r2, AF.Exp, bias=0.0, scale=-0.5),
               reads=[("rows", 2)], writes=[("rows", 1)])
        p.comp("vector", lambda: v.tensor_single_scalar(r0, r0, -1.0, ALU.mult),
               reads=[("rows", 0)], writes=[("rows", 0)])
        p.comp("tensor", lambda: t.matmul(c.ps[4], c.onesrow, r0, start=True, stop=True),
               reads=[("rows", 0), ("rows", 3)], writes=[("p4",)])
        p.comp("tensor", lambda: t.matmul(c.ps[5], c.onesrow, r1, start=True, stop=True),
               reads=[("rows", 1), ("rows", 3)], writes=[("p5",)])
        for kt in range(NKT):
            buf = c.f32a[:, kt * 512:(kt + 1) * 512]
            xb = c.f32b[:, kt * 512:(kt + 1) * 512]
            p.comp("vector", lambda xb=xb, b=buf: v.tensor_add(xb, b, c.ps[4]),
                   reads=[("f32a", kt), ("p4",)], writes=[("f32b", kt)])
            p.comp("vector", lambda xb=xb: v.tensor_mul(xb, xb, c.ps[5]),
                   reads=[("f32b", kt), ("p5",)], writes=[("f32b", kt)])
            if gamma is not None:
                gcol = gamma(kt); bcol = beta(kt)
                p.comp("vector", lambda xb=xb, gcol=gcol, bcol=bcol:
                       v.tensor_scalar(xb, xb, gcol, bcol, ALU.mult, ALU.add),
                       reads=[("f32b", kt), ("cols", 0)], writes=[("f32b", kt)])
            if dst_sbuf:
                xnb = c.xnb[:, kt * T + m * 512: kt * T + (m + 1) * 512]
                p.comp("scalar", lambda xnb=xnb, xb=xb: s.copy(xnb, xb),
                       reads=[("f32b", kt)], writes=keys("xnb", (kt * 8 + m) * 512, (kt * 8 + m) * 512 + 512))
            if dst32 is not None:
                dap = dst32[kt * 128:(kt + 1) * 128, m * 512:(m + 1) * 512]
                p.dma("sync", lambda dap=dap, xb=xb: nc.sync.dma_start(out=dap, in_=xb),
                      reads=[("f32b", kt)], writes=[(dst32key, kt * 8 + m)])


def emit_proj_residual(c, w_dram, Kdim, bias_dram=None, extra_dram=None, extra_key=None,
                       dst=None, dstkey=None, ctx_dram=None, ctx_key="ctxT", scale_out=None):
    nc, p = c.nc, c.p
    v, s, t = nc.vector, nc.scalar, nc.tensor
    if dst is None:
        dst = c.X; dstkey = "X"
    if ctx_dram is None:
        ctx_dram = c.ctxT
    KT = Kdim // 128
    for kt in range(KT):
        wap = w_dram[kt * 128:(kt + 1) * 128, :]
        wsb = c.wst[:, kt * 1024:(kt + 1) * 1024]
        p.dma("gpsimd", lambda wsb=wsb, wap=wap: nc.gpsimd.dma_start(out=wsb, in_=wap),
              writes=keys("wst", kt * 1024, (kt + 1) * 1024))
    if bias_dram is not None:
        emit_bias_cols(c, bias_dram, 8, 0)
    for m in range(NTM):
        for kt in range(KT):
            cap_ = ctx_dram[kt * 128:(kt + 1) * 128, m * 512:(m + 1) * 512]
            esb = c.ework[:, kt * 512:(kt + 1) * 512]
            p.dma("sync", lambda esb=esb, cap_=cap_: nc.sync.dma_start(out=esb, in_=cap_),
                  reads=[(ctx_key, kt * 8 + m)], writes=[("ework", kt)])
        for o in range(8):
            pb = o % 4
            pskey = (f"p{pb}",)
            for kt in range(KT):
                lhs = c.wst[:, kt * 1024 + o * 128: kt * 1024 + o * 128 + 128]
                rhs = c.ework[:, kt * 512:(kt + 1) * 512]
                p.comp("tensor", lambda lhs=lhs, rhs=rhs, pb=pb, kt=kt, KT=KT:
                       t.matmul(c.ps[pb], lhs, rhs, start=(kt == 0), stop=(kt == KT - 1)),
                       reads=keys("wst", kt * 1024, (kt + 1) * 1024) + [("ework", kt)],
                       writes=[pskey])
            xtile = c.f32a[:, o * 512:(o + 1) * 512]
            Xap = dst[o * 128:(o + 1) * 128, m * 512:(m + 1) * 512]
            Xsrc = c.X[o * 128:(o + 1) * 128, m * 512:(m + 1) * 512]
            acc = c.f32c[:, o * 512:(o + 1) * 512]
            p.dma("scalar", lambda xtile=xtile, Xsrc=Xsrc: nc.scalar.dma_start(out=xtile, in_=Xsrc),
                  reads=[("X", o * 8 + m)], writes=[("f32a", o)])
            if bias_dram is not None:
                bcol = c.cols[:, o:o + 1]
                p.comp("scalar", lambda acc=acc, pb=pb, bcol=bcol:
                       s.activation(acc, c.ps[pb], AF.Identity, bias=bcol, scale=1.0),
                       reads=[pskey, ("cols", 0)], writes=[("f32c", o)])
            else:
                p.comp("scalar", lambda acc=acc, pb=pb: s.copy(acc, c.ps[pb]),
                       reads=[pskey], writes=[("f32c", o)])
            if scale_out is not None:
                p.comp("vector", lambda acc=acc: v.tensor_single_scalar(acc, acc, scale_out, ALU.mult),
                       reads=[("f32c", o)], writes=[("f32c", o)])
            p.comp("vector", lambda acc=acc, xtile=xtile: v.tensor_add(acc, acc, xtile),
                   reads=[("f32c", o), ("f32a", o)], writes=[("f32c", o)])
            if extra_dram is not None:
                eap = extra_dram[o * 128:(o + 1) * 128, m * 512:(m + 1) * 512]
                etile = c.f32b[:, o * 512:(o + 1) * 512]
                p.dma("scalar", lambda etile=etile, eap=eap: nc.scalar.dma_start(out=etile, in_=eap),
                      reads=[(extra_key, o * 8 + m)], writes=[("f32b", o)])
                p.comp("vector", lambda acc=acc, etile=etile: v.tensor_add(acc, acc, etile),
                       reads=[("f32c", o), ("f32b", o)], writes=[("f32c", o)])
            p.dma("sync", lambda Xap=Xap, acc=acc: nc.sync.dma_start(out=Xap, in_=acc),
                  reads=[("f32c", o)], writes=[(dstkey, o * 8 + m)])


def emit_bias_cols(c, bias_dram, ntiles, col0):
    nc, p = c.nc, c.p
    v, t = nc.vector, nc.tensor
    bap = bias_dram.rearrange("(n p) -> n p", p=128)
    tmp = c.f32b[0:ntiles, 0:128]
    p.dma("sync", lambda: nc.sync.dma_start(out=tmp, in_=bap), writes=[("f32b", 0)])
    p.comp("tensor", lambda: t.transpose(c.ps[5][:, 0:ntiles], tmp, c.idF[0:ntiles, 0:ntiles]),
           reads=[("f32b", 0), ("consts", 0)], writes=[("p5",)])
    p.comp("vector", lambda: v.tensor_copy(c.cols[:, col0:col0 + ntiles], c.ps[5][:, 0:ntiles]),
           reads=[("p5",)], writes=[("cols", 0)])


def emit_l0_attn_a(c):
    nc, p = c.nc, c.p
    v, s, t = nc.vector, nc.scalar, nc.tensor
    iw = c.iw
    emit_bias_cols(c, iw["a0_gamma"], 8, 8)
    emit_bias_cols(c, iw["a0_beta"], 8, 16)
    emit_bias_cols(c, iw["a0_D"], 8, 24)
    Aap = iw["a0_A"].rearrange("(kt p) s -> p kt s", p=128)
    p.dma("sync", lambda: nc.sync.dma_start(out=c.cols[:, 32:160], in_=Aap, allow_slow_non_contiguous=True),
          writes=[("cols", 0)])
    emit_ln(c, c.X, "X", dst_sbuf=False, dst32=c.tT, dst32key="tT")
    emit_ln(c, c.tT, "tT", gamma=lambda kt: c.cols[:, 8 + kt:9 + kt],
            beta=lambda kt: c.cols[:, 16 + kt:17 + kt], dst_sbuf=True, dst32=c.xn32, dst32key="xn32")
    for kt in range(NKT):
        wap = iw["a0_wxT"][kt * 128:(kt + 1) * 128, :]
        wsb = c.wst[:, kt * 2048:(kt + 1) * 2048]
        p.dma("gpsimd", lambda wsb=wsb, wap=wap: nc.gpsimd.dma_start(out=wsb, in_=wap),
              writes=keys("wst", kt * 2048, (kt + 1) * 2048))
    emit_bias_cols16(c, iw["a0_bx"], 16, 160)

    for i in range(8):
        for m in range(NTM):
            pb = m % 2
            for kt in range(NKT):
                lhs = c.wst[:, kt * 2048 + i * 128: kt * 2048 + i * 128 + 128]
                rhs = c.xnb[:, kt * T + m * 512: kt * T + (m + 1) * 512]
                p.comp("tensor", lambda lhs=lhs, rhs=rhs, pb=pb, kt=kt:
                       t.matmul(c.ps[pb], lhs, rhs, start=(kt == 0), stop=(kt == NKT - 1)),
                       reads=keys("wst", kt * 2048, (kt + 1) * 2048) +
                             keys("xnb", (kt * 8 + m) * 512, (kt * 8 + m) * 512 + 512),
                       writes=[(f"p{pb}",)])
            dtile = c.f32a[:, m * 512:(m + 1) * 512]
            u = c.f32c[:, m * 512:(m + 1) * 512]
            bxcol = c.cols[:, 160 + i:161 + i]
            p.comp("scalar", lambda dtile=dtile, pb=pb, bxcol=bxcol:
                   s.activation(dtile, c.ps[pb], AF.Identity, bias=bxcol, scale=1.0),
                   reads=[(f"p{pb}",), ("cols", 0)], writes=[("f32a", m)])
            p.comp("scalar", lambda u=u, dtile=dtile: s.activation(u, dtile, AF.Abs, bias=0.0, scale=1.0),
                   reads=[("f32a", m)], writes=[("f32c", m)])
            p.comp("scalar", lambda u=u: s.activation(u, u, AF.Exp, bias=0.0, scale=-1.0),
                   reads=[("f32c", m)], writes=[("f32c", m)])
            onec = constcell(c, 1.0)
            p.comp("scalar", lambda u=u, onec=onec: s.activation(u, u, AF.Ln, bias=onec, scale=1.0),
                   reads=[("f32c", m), ("cols", 1)], writes=[("f32c", m)])
            p.comp("vector", lambda dtile=dtile: v.tensor_single_scalar(dtile, dtile, 0.0, ALU.max),
                   reads=[("f32a", m)], writes=[("f32a", m)])
            p.comp("vector", lambda dtile=dtile, u=u: v.tensor_add(dtile, dtile, u),
                   reads=[("f32a", m), ("f32c", m)], writes=[("f32a", m)])
        for si in range(16):
            acol = c.cols[:, 32 + i * 16 + si: 33 + i * 16 + si]
            for hf in range(2):
                e = c.f32c[:, hf * 2048:(hf + 1) * 2048]
                src_ = c.f32a[:, hf * 2048:(hf + 1) * 2048]
                acc = c.f32b[:, hf * 2048:(hf + 1) * 2048]
                rk = [("f32a", mm) for mm in range(hf * 4, hf * 4 + 4)]
                ek = [("f32c", mm) for mm in range(hf * 4, hf * 4 + 4)]
                ak = [("f32b", mm) for mm in range(hf * 4, hf * 4 + 4)]
                p.comp("scalar", lambda e=e, src_=src_, acol=acol:
                       s.activation(e, src_, AF.Exp, bias=0.0, scale=acol),
                       reads=rk + [("cols", 0)], writes=ek)
                if si == 0:
                    p.comp("vector", lambda acc=acc, e=e: v.tensor_copy(acc, e),
                           reads=ek, writes=ak)
                else:
                    p.comp("vector", lambda acc=acc, e=e: v.tensor_add(acc, acc, e),
                           reads=ek + ak, writes=ak)
        for m in range(NTM):
            pb = 2 + (m % 2)
            for kt in range(NKT):
                lhs = c.wst[:, kt * 2048 + (8 + i) * 128: kt * 2048 + (8 + i) * 128 + 128]
                rhs = c.xnb[:, kt * T + m * 512: kt * T + (m + 1) * 512]
                p.comp("tensor", lambda lhs=lhs, rhs=rhs, pb=pb, kt=kt:
                       t.matmul(c.ps[pb], lhs, rhs, start=(kt == 0), stop=(kt == NKT - 1)),
                       reads=keys("wst", kt * 2048, (kt + 1) * 2048) +
                             keys("xnb", (kt * 8 + m) * 512, (kt * 8 + m) * 512 + 512),
                       writes=[(f"p{pb}",)])
            bg = c.f32c[:, m * 512:(m + 1) * 512]
            bxcol = c.cols[:, 160 + 8 + i:161 + 8 + i]
            p.comp("scalar", lambda bg=bg, pb=pb, bxcol=bxcol:
                   s.activation(bg, c.ps[pb], AF.Identity, bias=bxcol, scale=1.0),
                   reads=[(f"p{pb}",), ("cols", 0)], writes=[("f32c", m)])
            xnap = c.xn32[i * 128:(i + 1) * 128, m * 512:(m + 1) * 512]
            xt = c.f32a[:, m * 512:(m + 1) * 512]
            p.dma("scalar", lambda xt=xt, xnap=xnap: nc.scalar.dma_start(out=xt, in_=xnap),
                  reads=[("xn32", i * 8 + m)], writes=[("f32a", m)])
            p.comp("vector", lambda bg=bg, xt=xt: v.tensor_mul(bg, bg, xt),
                   reads=[("f32c", m), ("f32a", m)], writes=[("f32c", m)])
        hrow = c.f32c[:, 0:4096]
        p.comp("vector", lambda: v.tensor_tensor_scan(hrow, hrow, hrow, 0.0, ALU.add, ALU.bypass),
               reads=[("f32c", mm) for mm in range(8)], writes=[("f32c", mm) for mm in range(8)])
        p.comp("vector", lambda: v.tensor_mul(hrow, hrow, c.f32b[:, 0:4096]),
               reads=[("f32c", mm) for mm in range(8)] + [("f32b", mm) for mm in range(8)],
               writes=[("f32c", mm) for mm in range(8)])
        dcol = c.cols[:, 24 + i:25 + i]
        p.comp("vector", lambda dcol=dcol: v.scalar_tensor_tensor(hrow, c.f32a[:, 0:4096], dcol, hrow,
                                                                  ALU.mult, ALU.add),
               reads=[("f32a", mm) for mm in range(8)] + [("f32c", mm) for mm in range(8)] + [("cols", 0)],
               writes=[("f32c", mm) for mm in range(8)])
        for m in range(NTM):
            src = c.f32c[:, m * 512:(m + 1) * 512]
            eb = c.ework[:, m * 512:(m + 1) * 512]
            dap = c.ctxT[i * 128:(i + 1) * 128, m * 512:(m + 1) * 512]
            p.comp("scalar", lambda eb=eb, src=src: s.copy(eb, src),
                   reads=[("f32c", m)], writes=[("ework", m)])
            p.dma("sync", lambda dap=dap, eb=eb: nc.sync.dma_start(out=dap, in_=eb),
                  reads=[("ework", m)], writes=[("ctxT", i * 8 + m)])
    emit_proj_residual(c, c.iw["a0_woT"], D, bias_dram=c.iw["a0_bo"],
                       extra_dram=c.tT, extra_key="tT")


def emit_bias_cols16(c, bias_dram, ntiles, col0):
    nc, p = c.nc, c.p
    v, t = nc.vector, nc.tensor
    done = 0
    while done < ntiles:
        n = min(8, ntiles - done)
        bap = bias_dram.rearrange("(n p) -> n p", p=128)[done:done + n, :]
        tmp = c.f32b[0:n, 0:128]
        p.dma("sync", lambda bap=bap, tmp=tmp: nc.sync.dma_start(out=tmp, in_=bap), writes=[("f32b", 0)])
        p.comp("tensor", lambda tmp=tmp, n=n: t.transpose(c.ps[5][:, 0:n], tmp, c.idF[0:n, 0:n]),
               reads=[("f32b", 0), ("consts", 0)], writes=[("p5",)])
        p.comp("vector", lambda n=n, done=done: v.tensor_copy(c.cols[:, col0 + done:col0 + done + n],
                                                              c.ps[5][:, 0:n]),
               reads=[("p5",)], writes=[("cols", 0)])
        done += n


def mk(c0, c1):
    return keys("moeS", c0, c1, gran=32)


def emit_moe_wconv(c, layer):
    nc, p = c.nc, c.p
    w1 = c.iw[f"f{layer}_w1"]; w2 = c.iw[f"f{layer}_w2"]
    kl = []
    for (wsrc, wdst, rows) in ((w1, c.w1b[layer], 8 * D),
                               (w2, c.w2b[layer], (32768 if layer == 0 else 16384))):
        rt = rows // 16
        for r in range(16):
            sap = wsrc[r * rt:(r + 1) * rt, :]
            dap = wdst[r * rt:(r + 1) * rt, :]
            k = ("wb", layer, len(kl))
            kl.append(k)
            p.dma("gpsimd", lambda dap=dap, sap=sap: nc.gpsimd.dma_start(out=dap, in_=sap),
                  writes=[k])
    c.wb_keys[layer] = kl


def emit_xtok(c):
    nc, p = c.nc, c.p
    v, t = nc.vector, nc.tensor
    for tt in range(T // 128):
        qb = tt % 2
        qk = (f"q{qb}",)
        stage = c.ework[:, (tt % 2) * 1024:(tt % 2) * 1024 + 1024]
        sk = keys("ework", (tt % 2) * 1024, (tt % 2) * 1024 + 1024)
        for kt in range(NKT):
            xs = c.xnb[:, kt * T + tt * 128: kt * T + tt * 128 + 128]
            xk = keys("xnb", kt * T + tt * 128, kt * T + tt * 128 + 128, gran=512)
            p.comp("tensor", lambda xs=xs, qb=qb, kt=kt: t.transpose(c.qs[qb][:, kt * 128:(kt + 1) * 128], xs, c.idB),
                   reads=xk + [("cbf", 0)], writes=[qk])
        p.comp("vector", lambda stage=stage, qb=qb: v.tensor_copy(stage, c.qs[qb][:, 0:1024]),
               reads=[qk], writes=sk)
        dap = c.xtok[tt * 128:(tt + 1) * 128, :]
        p.dma("sync", lambda dap=dap, stage=stage: nc.sync.dma_start(out=dap, in_=stage),
              reads=sk, writes=[("xtok", tt // 4)])


MOE_STOP = 99

def emit_moe(c, layer):
    nc, p = c.nc, c.p
    v, s, t, g = nc.vector, nc.scalar, nc.tensor, nc.gpsimd
    NMAC = 15 if layer == 0 else 23
    NSLOT = NMAC * 512
    F = 4096
    KF = 32 if layer == 0 else 16
    CAP = 5120 if layer == 0 else 10240
    for ly in (0, 1):
        if not c.moe_wconv_done[ly]:
            emit_moe_wconv(c, ly)
            c.moe_wconv_done[ly] = True
    emit_ln(c, c.X, "X", dst_sbuf=True)
    emit_xtok(c)
    rw = c.iw["f0_rwT" if layer == 0 else "f1_wgT"]
    rwsb = c.cbf[:, 256:320]
    rap = rw.rearrange("(kt p) e -> p kt e", p=128)
    p.dma("gpsimd", lambda: g.dma_start(out=rwsb, in_=rap), writes=[("cbf", 0)])

    for tt in range(32):
        pbank = 4 + tt % 2
        pk = (f"p{pbank}",)
        for kt in range(NKT):
            lhs = c.xnb[:, kt * T + tt * 128: kt * T + tt * 128 + 128]
            xk = keys("xnb", kt * T + tt * 128, kt * T + tt * 128 + 128, gran=512)
            rhs = rwsb[:, kt * 8:(kt + 1) * 8]
            p.comp("tensor", lambda lhs=lhs, rhs=rhs, kt=kt, pbank=pbank:
                   t.matmul(c.ps[pbank][:, 0:8], lhs, rhs, start=(kt == 0), stop=(kt == NKT - 1)),
                   reads=xk + [("cbf", 0)], writes=[pk])
        o = (tt % 2) * 152
        lg = c.moeS[:, 144 + o:152 + o]
        p.comp("vector", lambda lg=lg, pbank=pbank: v.tensor_copy(lg, c.ps[pbank][:, 0:8]),
               reads=[pk], writes=mk(144 + o, 152 + o))
        mw = c.moeS[:, 128 + o:136 + o]
        miu = c.idxs[:, 32 + (tt % 2) * 8:40 + (tt % 2) * 8]
        mik = ("idxs", 32 + (tt % 2) * 8)
        p.comp("vector", lambda lg=lg, mw=mw: v.max(mw, lg),
               reads=mk(144 + o, 152 + o), writes=mk(128 + o, 136 + o))
        p.comp("vector", lambda lg=lg, mw=mw, miu=miu: v.max_index(miu, mw, lg),
               reads=mk(144 + o, 152 + o) + mk(128 + o, 136 + o), writes=[mik])
        if layer == 0:
            nm1 = c.moeS[:, 159 + o:160 + o]
            p.comp("vector", lambda nm1=nm1, mw=mw: v.tensor_single_scalar(nm1, mw[:, 0:1], -1.0, ALU.mult),
                   reads=mk(128 + o, 136 + o), writes=mk(159 + o, 160 + o))
            ex = c.moeS[:, 136 + o:144 + o]; se = c.moeS[:, 240 + o:241 + o]
            p.comp("scalar", lambda ex=ex, lg=lg, nm1=nm1, se=se:
                   s.activation(ex, lg, AF.Exp, bias=nm1, scale=1.0, accum_out=se),
                   reads=mk(144 + o, 152 + o) + mk(159 + o, 160 + o), writes=mk(136 + o, 144 + o) + mk(240 + o, 241 + o))
            p1 = c.moeS[:, tt:tt + 1]
            p.comp("vector", lambda p1=p1, se=se: v.reciprocal(p1, se),
                   reads=mk(240 + o, 241 + o), writes=mk(tt, tt + 1))
            e1 = c.moeS[:, 32 + tt:33 + tt]
            p.comp("vector", lambda e1=e1, miu=miu: v.tensor_copy(e1, miu[:, 0:1]),
                   reads=[mik], writes=mk(32 + tt, 33 + tt))
            pap = c.cc_p_in[tt, :]
            eap = c.cc_e_in[tt, :]
            p.dma("sync", lambda pap=pap, p1=p1: nc.sync.dma_start(out=pap, in_=p1[:, 0]),
                  reads=mk(tt, tt + 1), writes=[("ccpin", tt // 8)])
            p.dma("sync", lambda eap=eap, e1=e1: nc.sync.dma_start(out=eap, in_=e1[:, 0]),
                  reads=mk(32 + tt, 33 + tt), writes=[("ccein", tt // 8)])
        else:
            dd = c.moeS[:, 154 + o:155 + o]; g1 = c.moeS[:, 155 + o:156 + o]; g2 = c.moeS[:, 156 + o:157 + o]
            p.comp("vector", lambda dd=dd, mw=mw: v.tensor_sub(dd, mw[:, 0:1], mw[:, 1:2]),
                   reads=mk(128 + o, 136 + o), writes=mk(154 + o, 155 + o))
            p.comp("scalar", lambda g1=g1, dd=dd: s.activation(g1, dd, AF.Sigmoid, bias=0.0, scale=1.0),
                   reads=mk(154 + o, 155 + o), writes=mk(155 + o, 156 + o))
            p.comp("scalar", lambda g2=g2, dd=dd: s.activation(g2, dd, AF.Sigmoid, bias=0.0, scale=-1.0),
                   reads=mk(154 + o, 155 + o), writes=mk(156 + o, 157 + o))
            p.comp("vector", lambda miu=miu, tt=tt: v.tensor_copy(c.f32a[:, 832 + tt:833 + tt], miu[:, 0:1]),
                   reads=[mik], writes=keys("f32a", 832, 896))
            p.comp("vector", lambda miu=miu, tt=tt: v.tensor_copy(c.f32a[:, 864 + tt:865 + tt], miu[:, 1:2]),
                   reads=[mik], writes=keys("f32a", 832, 896))
            p.comp("vector", lambda g1=g1, tt=tt: v.tensor_copy(c.f32a[:, 768 + tt:769 + tt], g1),
                   reads=mk(155 + o, 156 + o), writes=keys("f32a", 768, 832))
            p.comp("vector", lambda g2=g2, tt=tt: v.tensor_copy(c.f32a[:, 800 + tt:801 + tt], g2),
                   reads=mk(156 + o, 157 + o), writes=keys("f32a", 768, 832))

    if MOE_STOP < 2:
        return
    rkeys = [("rows", 0)]
    if layer == 0:
        p.coll(lambda: g.collective_compute("AllGather", ALU.bypass, [list(range(NCORES))],
                                            [c.cc_p_in[:, :].opt()], [c.cc_p_out[:, :].opt()]),
               reads=[("ccpin", i) for i in range(4)], writes=[("ccpout",)])
        p.coll(lambda: g.collective_compute("AllGather", ALU.bypass, [list(range(NCORES))],
                                            [c.cc_e_in[:, :].opt()], [c.cc_e_out[:, :].opt()]),
               reads=[("ccein", i) for i in range(4)], writes=[("cceout",)])
        PT = c.f32a[:, 0:256]; ET = c.f32a[:, 256:512]
        p.dma("sync", lambda: nc.sync.dma_start(out=PT, in_=c.cc_p_out.rearrange("(p a) j -> p (a j)", p=128)),
              reads=[("ccpout",)], writes=[("f32a", 0)])
        p.dma("sync", lambda: nc.sync.dma_start(out=ET, in_=c.cc_e_out.rearrange("(p a) j -> p (a j)", p=128)),
              reads=[("cceout",)], writes=[("f32a", 0)])
        for e in range(8):
            seg = c.f32b[:, e * 256:(e + 1) * 256]
            msk = c.f32c[:, 0:256]
            p.comp("vector", lambda msk=msk, e=e: v.tensor_single_scalar(msk, ET, float(e), ALU.is_equal),
                   reads=[("f32a", 0)], writes=[("f32c", 0)])
            p.comp("vector", lambda seg=seg: v.tensor_single_scalar(seg, PT, 1.0, ALU.add),
                   reads=[("f32a", 0)], writes=[("f32b", e // 2)])
            p.comp("vector", lambda seg=seg, msk=msk: v.tensor_mul(seg, seg, msk),
                   reads=[("f32b", e // 2), ("f32c", 0)], writes=[("f32b", e // 2)])
            p.comp("vector", lambda seg=seg: v.tensor_single_scalar(seg, seg, 1.0, ALU.subtract),
                   reads=[("f32b", e // 2)], writes=[("f32b", e // 2)])
        lo = c.rows[0:1, 0:8]; hi = c.rows[0:1, 8:16]; mid = c.rows[0:1, 16:24]
        cnt = c.rows[0:1, 24:32]; tmask = c.i32s[0:1, 8:16]
        p.comp("vector", lambda: v.memset(lo, 0.0), writes=rkeys)
        p.comp("vector", lambda: v.memset(hi, 1.0), writes=rkeys)
        thb = c.moeS[:, 224:232]
        cmpred = c.moeS[:, 248:256]
        for it in range(28):
            p.comp("vector", lambda: v.tensor_add(mid, lo, hi), reads=rkeys, writes=rkeys)
            p.comp("vector", lambda: v.tensor_single_scalar(mid, mid, 0.5, ALU.mult), reads=rkeys, writes=rkeys)
            p.comp("tensor", lambda: t.matmul(c.ps[4][:, 0:8], c.onesrow, mid, start=True, stop=True),
                   reads=rkeys + [("rows", 3)], writes=[("p4",)])
            p.comp("vector", lambda: v.tensor_copy(thb, c.ps[4][:, 0:8]),
                   reads=[("p4",)], writes=mk(224, 232))
            for e in range(8):
                seg = c.f32b[:, e * 256:(e + 1) * 256]
                cm = c.f32c[:, e * 256:(e + 1) * 256]
                p.comp("vector", lambda cm=cm, seg=seg, e=e:
                       v.tensor_scalar(cm, seg, thb[:, e:e + 1], None, ALU.is_gt),
                       reads=[("f32b", e // 2)] + mk(224, 232), writes=[("f32c", e // 2)])
                p.comp("vector", lambda cm=cm, e=e: v.reduce_sum(cmpred[:, e:e + 1], cm, AX.X),
                       reads=[("f32c", e // 2)], writes=mk(248, 256))
            p.comp("tensor", lambda: t.matmul(c.ps[4][0:1, 0:8], c.onescol, cmpred, start=True, stop=True),
                   reads=mk(248, 256) + [("consts", 0)], writes=[("p4",)])
            p.comp("vector", lambda: v.tensor_copy(cnt, c.ps[4][0:1, 0:8]),
                   reads=[("p4",)], writes=rkeys)
            p.comp("vector", lambda: v.tensor_single_scalar(tmask, cnt, float(CAP), ALU.is_gt),
                   reads=rkeys, writes=[("i32s",)])
            p.comp("vector", lambda: v.select(lo, tmask, mid, lo), reads=rkeys + [("i32s",)], writes=rkeys)
            p.comp("vector", lambda: v.select(hi, tmask, hi, mid), reads=rkeys + [("i32s",)], writes=rkeys)
        th8 = hi
        thbc = c.moeS[:, 224:232]
        p.comp("tensor", lambda: t.matmul(c.ps[4][:, 0:8], c.onesrow, th8, start=True, stop=True),
               reads=rkeys + [("rows", 3)], writes=[("p4",)])
        p.comp("vector", lambda: v.tensor_copy(thbc, c.ps[4][:, 0:8]),
               reads=[("p4",)], writes=mk(224, 232))
        carry = c.rows[0:1, 48:56]
        p.comp("vector", lambda: v.memset(carry, 0.0), writes=rkeys)
        for tt in range(32):
            o = (tt % 2) * 152
            p1 = c.moeS[:, tt:tt + 1]; e1 = c.moeS[:, 32 + tt:33 + tt]
            fin = c.moeS[:, 64 + tt:65 + tt]; rnk = c.moeS[:, 96 + tt:97 + tt]
            oh = c.moeS[:, 128 + o:136 + o]; th_t = c.moeS[:, 159 + o:160 + o]
            kp = c.moeS[:, 240 + o:241 + o]; sc8 = c.moeS[:, 136 + o:144 + o]
            p.comp("vector", lambda oh=oh, e1=e1: v.tensor_scalar(oh, c.io8, e1[:, 0:1], None, ALU.is_equal),
                   reads=[("consts", 0)] + mk(32 + tt, 33 + tt), writes=mk(128 + o, 136 + o))
            p.comp("vector", lambda oh=oh, sc8=sc8: v.tensor_mul(sc8, oh, thbc),
                   reads=mk(128 + o, 136 + o) + mk(224, 232), writes=mk(136 + o, 144 + o))
            p.comp("vector", lambda th_t=th_t, sc8=sc8: v.reduce_sum(th_t, sc8, AX.X),
                   reads=mk(136 + o, 144 + o), writes=mk(159 + o, 160 + o))
            p.comp("vector", lambda kp=kp, p1=p1, th_t=th_t: v.tensor_tensor(kp, p1, th_t, ALU.is_gt),
                   reads=mk(tt, tt + 1) + mk(159 + o, 160 + o), writes=mk(240 + o, 241 + o))
            p.comp("vector", lambda fin=fin, kp=kp, e1=e1: v.tensor_mul(fin, kp, e1),
                   reads=mk(240 + o, 241 + o) + mk(32 + tt, 33 + tt), writes=mk(64 + tt, 65 + tt))
            p.comp("vector", lambda oh=oh, fin=fin: v.tensor_scalar(oh, c.io8, fin[:, 0:1], None, ALU.is_equal),
                   reads=[("consts", 0)] + mk(64 + tt, 65 + tt), writes=mk(128 + o, 136 + o))
            pb5 = 5 if tt % 2 == 0 else 3
            p.comp("tensor", lambda oh=oh, pb5=pb5: t.matmul(c.ps[pb5][:, 0:8], c.Lst, oh, start=True, stop=False),
                   reads=mk(128 + o, 136 + o) + [("consts", 0)], writes=[(f"p{pb5}",)])
            p.comp("tensor", lambda pb5=pb5: t.matmul(c.ps[pb5][:, 0:8], c.onesrow, carry, start=False, stop=True),
                   reads=rkeys + [("rows", 3)], writes=[(f"p{pb5}",)])
            pref = c.moeS[:, 136 + o:144 + o]
            p.comp("vector", lambda pref=pref, pb5=pb5: v.tensor_copy(pref, c.ps[pb5][:, 0:8]),
                   reads=[(f"p{pb5}",)], writes=mk(136 + o, 144 + o))
            p.comp("vector", lambda pref=pref, oh=oh: v.tensor_mul(pref, pref, oh),
                   reads=mk(136 + o, 144 + o) + mk(128 + o, 136 + o), writes=mk(136 + o, 144 + o))
            p.comp("vector", lambda rnk=rnk, pref=pref: v.reduce_sum(rnk, pref, AX.X),
                   reads=mk(136 + o, 144 + o), writes=mk(96 + tt, 97 + tt))
            pb4 = 4 if tt % 2 == 0 else 2
            p.comp("tensor", lambda oh=oh, pb4=pb4: t.matmul(c.ps[pb4][0:1, 0:8], c.onescol, oh, start=True, stop=False),
                   reads=mk(128 + o, 136 + o) + [("consts", 0)], writes=[(f"p{pb4}",)])
            p.comp("tensor", lambda pb4=pb4: t.matmul(c.ps[pb4][0:1, 0:8], c.onesrow[0:1, 0:1], carry, start=False, stop=True),
                   reads=rkeys + [("rows", 3)], writes=[(f"p{pb4}",)])
            p.comp("vector", lambda pb4=pb4: v.tensor_copy(carry, c.ps[pb4][0:1, 0:8]),
                   reads=[(f"p{pb4}",)], writes=rkeys)
        cnt8 = carry
    else:
        fa = lambda a, b: keys("f32a", a, b)
        OH1 = c.f32a[:, 0:256]; OH2 = c.f32a[:, 256:512]; OHS = c.f32a[:, 512:768]
        for tt in range(32):
            o1 = OH1[:, tt * 8:(tt + 1) * 8]; e1c = c.f32a[:, 832 + tt:833 + tt]
            p.comp("vector", lambda o1=o1, e1c=e1c: v.tensor_scalar(o1, c.io8, e1c[:, 0:1], None, ALU.is_equal),
                   reads=[("consts", 0)] + fa(832, 896), writes=fa(0, 256))
            o2 = OH2[:, tt * 8:(tt + 1) * 8]; e2c = c.f32a[:, 864 + tt:865 + tt]
            p.comp("vector", lambda o2=o2, e2c=e2c: v.tensor_scalar(o2, c.io8, e2c[:, 0:1], None, ALU.is_equal),
                   reads=[("consts", 0)] + fa(832, 896), writes=fa(256, 512))
        p.comp("vector", lambda: v.tensor_add(OHS, OH1, OH2), reads=fa(0, 512), writes=fa(512, 768))
        p.comp("tensor", lambda: t.matmul(c.ps[2][0:1, 0:256], c.onescol, OHS, start=True, stop=True),
               reads=fa(512, 768) + [("consts", 0)], writes=[("p2",)])
        r0 = c.rows[0:1, 512:768]; r1 = c.rows[0:1, 768:1024]
        p.comp("vector", lambda: v.tensor_copy(r0, c.ps[2][0:1, 0:256]), reads=[("p2",)], writes=rkeys)
        cur, nxt = r0, r1
        for sdo in (8, 16, 32, 64, 128):
            p.comp("vector", lambda cur=cur, nxt=nxt, sdo=sdo: v.tensor_copy(nxt[0:1, 0:sdo], cur[0:1, 0:sdo]),
                   reads=rkeys, writes=rkeys)
            p.comp("vector", lambda cur=cur, nxt=nxt, sdo=sdo:
                   v.tensor_add(nxt[0:1, sdo:256], cur[0:1, sdo:256], cur[0:1, 0:256 - sdo]),
                   reads=rkeys, writes=rkeys)
            cur, nxt = nxt, cur
        incl = cur
        carry = c.rows[0:1, 48:56]
        p.comp("vector", lambda incl=incl: v.tensor_copy(carry, incl[0:1, 248:256]), reads=rkeys, writes=rkeys)
        cnt8 = carry
        p.dma("sync", lambda: nc.sync.dma_start(out=c.cc_cnt_in[0:1, :], in_=cnt8),
              reads=rkeys, writes=[("cntin",)])
        p.coll(lambda: g.collective_compute("AllGather", ALU.bypass, [list(range(NCORES))],
                                            [c.cc_cnt_in[:, :].opt()], [c.cc_cnt_out[:, :].opt()]),
               reads=[("cntin",)], writes=[("cntout",)])
        cg = c.moeS[0:8, 256:264]
        pmk = c.moeS[0:8, 264:265]
        p.dma("sync", lambda: nc.sync.dma_start(out=cg, in_=c.cc_cnt_out[:, :]),
              reads=[("cntout",)], writes=mk(256, 264))
        p.dma("sync", lambda: nc.sync.dma_start(out=pmk, in_=c.iw["prevmask"].rearrange("(p one) -> p one", one=1)),
              writes=mk(264, 265))
        p.comp("tensor", lambda: t.matmul(c.ps[4][0:8, 0:1], cg, pmk, start=True, stop=True),
               reads=mk(256, 265), writes=[("p4",)])
        offc = c.moeS[0:8, 265:266]
        p.comp("vector", lambda: v.tensor_copy(offc, c.ps[4][0:8, 0:1]),
               reads=[("p4",)], writes=mk(265, 266))
        p.comp("tensor", lambda: t.transpose(c.ps[4][0:1, 0:8], offc, c.idF[0:8, 0:8]),
               reads=mk(265, 266) + [("consts", 0)], writes=[("p4",)])
        off8 = c.rows[0:1, 56:64]
        p.comp("vector", lambda: v.tensor_copy(off8, c.ps[4][0:1, 0:8]),
               reads=[("p4",)], writes=rkeys)

    nmac = c.rows[0:1, 64:72]; cin = c.rows[0:1, 72:80]; cex = c.rows[0:1, 80:88]
    base8 = c.rows[0:1, 88:96]; ztmp = c.rows[0:1, 96:104]
    p.comp("vector", lambda: v.tensor_scalar(nmac, cnt8, 1.0 / 512, -(1.0 / 512 + 0.499), ALU.mult, ALU.add),
           reads=rkeys, writes=rkeys)
    p.comp("vector", lambda: v.tensor_copy(c.i32s[0:1, 0:8], nmac), reads=rkeys, writes=[("i32s",)])
    p.comp("vector", lambda: v.tensor_copy(nmac, c.i32s[0:1, 0:8]), reads=[("i32s",)], writes=rkeys)
    p.comp("vector", lambda: v.tensor_single_scalar(nmac, nmac, 1.0, ALU.add), reads=rkeys, writes=rkeys)
    p.comp("vector", lambda: v.memset(ztmp, 0.0), writes=rkeys)
    p.comp("vector", lambda: v.tensor_tensor_scan(cin, nmac, ztmp, 0.0, ALU.add, ALU.add),
           reads=rkeys, writes=rkeys)
    p.comp("vector", lambda: v.tensor_sub(cex, cin, nmac), reads=rkeys, writes=rkeys)
    p.comp("vector", lambda: v.tensor_single_scalar(base8, cex, 512.0, ALU.mult), reads=rkeys, writes=rkeys)
    cinC = c.moeS[0:8, 266:267]
    p.comp("tensor", lambda: t.transpose(c.ps[4][0:8, 0:1], cin, c.idF[0:1, 0:1]),
           reads=rkeys + [("consts", 0)], writes=[("p4",)])
    p.comp("vector", lambda: v.tensor_copy(cinC, c.ps[4][0:8, 0:1]),
           reads=[("p4",)], writes=mk(266, 267))
    iom = c.moeS[0:8, 267:267 + NMAC]
    p.comp("gpsimd", lambda: g.iota(iom, [[1, NMAC]], base=0, channel_multiplier=0,
                                    allow_small_or_imprecise_dtypes=True), writes=mk(267, 267 + NMAC))
    cmp_ = c.moeS[0:8, 290:290 + NMAC]
    p.comp("vector", lambda: v.tensor_scalar(cmp_, iom, cinC[0:8, 0:1], None, ALU.is_ge),
           reads=mk(267, 267 + NMAC) + mk(266, 267), writes=mk(290, 290 + NMAC))
    p.comp("tensor", lambda: t.matmul(c.ps[4][0:1, 0:NMAC], c.onescol[0:8, 0:1], cmp_, start=True, stop=True),
           reads=mk(290, 290 + NMAC) + [("consts", 0)], writes=[("p4",)])
    eidr = c.rows[0:1, 104:104 + NMAC]
    p.comp("vector", lambda: v.tensor_copy(eidr, c.ps[4][0:1, 0:NMAC]), reads=[("p4",)], writes=rkeys)
    p.comp("vector", lambda: v.tensor_single_scalar(eidr, eidr, 7.0, ALU.min), reads=rkeys, writes=rkeys)
    eidsb = c.moeS[:, 160:160 + NMAC]
    p.comp("tensor", lambda: t.matmul(c.ps[4][:, 0:NMAC], c.onesrow, eidr, start=True, stop=True),
           reads=rkeys + [("rows", 3)], writes=[("p4",)])
    p.comp("vector", lambda: v.tensor_copy(eidsb, c.ps[4][:, 0:NMAC]),
           reads=[("p4",)], writes=mk(160, 160 + NMAC))
    baseb = c.moeS[:, 224:232]
    p.comp("tensor", lambda: t.matmul(c.ps[4][:, 0:8], c.onesrow, base8, start=True, stop=True),
           reads=rkeys + [("rows", 3)], writes=[("p4",)])
    p.comp("vector", lambda: v.tensor_copy(baseb, c.ps[4][:, 0:8]),
           reads=[("p4",)], writes=mk(224, 232))
    if layer == 1:
        offb = c.moeS[:, 232:240]
        p.comp("tensor", lambda: t.matmul(c.ps[4][:, 0:8], c.onesrow, off8, start=True, stop=True),
               reads=rkeys + [("rows", 3)], writes=[("p4",)])
        p.comp("vector", lambda: v.tensor_copy(offb, c.ps[4][:, 0:8]),
               reads=[("p4",)], writes=mk(232, 240))

    if MOE_STOP < 3:
        return
    c.table_keys = [("table", i) for i in range(2 + (32 if layer == 0 else 64))]
    p.comp("vector", lambda: v.memset(c.idxs[:, 0:46], T), writes=[("idxs",)])
    p.dma("sync", lambda: nc.sync.dma_start(out=c.table[0:5888], in_=c.idxs[:, 0:46]),
          reads=[("idxs",)], writes=[("table", 0)])
    n2 = (NSLOT - 5888) // 128
    p.dma("sync", lambda n2=n2: nc.sync.dma_start(out=c.table[5888:NSLOT], in_=c.idxs[:, 0:n2]),
          reads=[("idxs",)], writes=[("table", 1)])
    if layer == 0:
        for tt in range(32):
            fin = c.moeS[:, 64 + tt:65 + tt]; rnk = c.moeS[:, 96 + tt:97 + tt]
            oh = c.moeS[:, 128:136]
            bt = c.moeS[:, 136:144]; slot = c.moeS[:, 244:245]
            p.comp("vector", lambda oh=oh, fin=fin: v.tensor_scalar(oh, c.io8, fin[:, 0:1], None, ALU.is_equal),
                   reads=[("consts", 0)] + mk(64 + tt, 65 + tt), writes=mk(128, 136))
            p.comp("vector", lambda bt=bt, oh=oh: v.tensor_mul(bt, oh, baseb),
                   reads=mk(128, 136) + mk(224, 232), writes=mk(136, 144))
            p.comp("vector", lambda slot=slot, bt=bt: v.reduce_sum(slot, bt, AX.X),
                   reads=mk(136, 144), writes=mk(244, 245))
            p.comp("vector", lambda slot=slot, rnk=rnk: v.tensor_add(slot, slot, rnk),
                   reads=mk(244, 245) + mk(96 + tt, 97 + tt), writes=mk(244, 245))
            sap = c.entS[tt * 128:(tt + 1) * 128]
            p.dma("sync", lambda sap=sap, slot=slot: nc.sync.dma_start(out=sap, in_=slot[:, 0]),
                  reads=mk(244, 245), writes=[("entS", tt // 8)])
            sc = 53 + tt % 6; tc = 59 + tt % 4
            sidx = c.idxs[:, sc:sc + 1]; tok = c.idxs[:, tc:tc + 1]
            p.comp("vector", lambda sidx=sidx, slot=slot: v.tensor_copy(sidx, slot),
                   reads=mk(244, 245), writes=[("idxs", sc)])
            p.comp("vector", lambda tok=tok, tt=tt: v.tensor_scalar(tok, c.iotaP, 1.0, float(tt * 128), ALU.mult, ALU.add),
                   reads=[("consts", 0)], writes=[("idxs", tc)])
            p.dma("gpsimd", lambda sidx=sidx, tok=tok:
                  g.indirect_dma_start(c.table[0:NSLOT].rearrange("(n one) -> n one", one=1),
                                       bass.IndirectOffsetOnAxis(ap=sidx, axis=0),
                                       tok, None, oob_is_err=False),
                  reads=[("idxs", sc), ("idxs", tc), ("table", 0), ("table", 1)],
                  writes=[("table", 2 + tt)])
    else:
        fa = lambda a, b: keys("f32a", a, b)
        fb = lambda a, b: keys("f32b", a, b)
        fc = lambda a, b: keys("f32c", a, b)
        OH1 = c.f32a[:, 0:256]; OH2 = c.f32a[:, 256:512]; OHS = c.f32a[:, 512:768]
        orow = c.f32c[0:1, 3072:3328]; brow = c.f32c[0:1, 3328:3584]
        p.comp("vector", lambda: v.tensor_copy(orow[0:1, 0:8], off8), reads=rkeys, writes=fc(3072, 3328))
        p.comp("vector", lambda: v.tensor_copy(brow[0:1, 0:8], base8), reads=rkeys, writes=fc(3328, 3584))
        for sdo in (8, 16, 32, 64, 128):
            p.comp("vector", lambda sdo=sdo: v.tensor_copy(orow[0:1, sdo:2 * sdo], orow[0:1, 0:sdo]),
                   reads=fc(3072, 3328), writes=fc(3072, 3328))
            p.comp("vector", lambda sdo=sdo: v.tensor_copy(brow[0:1, sdo:2 * sdo], brow[0:1, 0:sdo]),
                   reads=fc(3328, 3584), writes=fc(3328, 3584))
        crow = c.rows[0:1, 1024:1280]
        p.comp("vector", lambda: v.memset(crow[0:1, 0:8], 0.0), writes=rkeys)
        p.comp("vector", lambda incl=incl: v.tensor_copy(crow[0:1, 8:256], incl[0:1, 0:248]),
               reads=rkeys, writes=rkeys)
        for bank, rrow, rk in ((0, orow, fc(3072, 3328)), (1, brow, fc(3328, 3584))):
            p.comp("tensor", lambda bank=bank: t.matmul(c.ps[bank][:, 0:256], c.Lst, OHS, start=True, stop=False),
                   reads=fa(512, 768) + [("consts", 0)], writes=[(f"p{bank}",)])
            p.comp("tensor", lambda bank=bank: t.matmul(c.ps[bank][:, 0:256], c.onesrow, crow, start=False, stop=False),
                   reads=rkeys + [("rows", 3)], writes=[(f"p{bank}",)])
            p.comp("tensor", lambda bank=bank, rrow=rrow: t.matmul(c.ps[bank][:, 0:256], c.onesrow, rrow, start=False, stop=True),
                   reads=rk + [("rows", 3)], writes=[(f"p{bank}",)])
        SLOT0 = c.moeS[:, 320:352]; SLOT1 = c.moeS[:, 352:384]
        W0 = c.moeS[:, 384:416]; W1c = c.moeS[:, 416:448]
        G1a = c.f32a[:, 768:800]; G2a = c.f32a[:, 800:832]
        sel = c.f32b[:, 0:256]
        pos = c.f32b[:, 512:544]; kpb = c.f32b[:, 544:576]
        def select8(dst, OHx, ohk, bank, dkeys):
            p.comp("vector", lambda OHx=OHx, bank=bank: v.tensor_mul(sel, OHx, c.ps[bank][:, 0:256]),
                   reads=ohk + [(f"p{bank}",)], writes=fb(0, 256))
            p.comp("vector", lambda dst=dst: v.tensor_copy(dst, sel[:, 0:256:8]),
                   reads=fb(0, 256), writes=dkeys)
            for e in range(1, 8):
                p.comp("vector", lambda dst=dst, e=e: v.tensor_add(dst, dst, sel[:, e:256:8]),
                       reads=fb(0, 256) + dkeys, writes=dkeys)
        for OHx, ohk, Ga, gk, Wd, wk in ((OH1, fa(0, 256), G1a, fa(768, 832), W0, mk(384, 416)),
                                         (OH2, fa(256, 512), G2a, fa(768, 832), W1c, mk(416, 448))):
            select8(pos, OHx, ohk, 0, fb(512, 544))
            p.comp("vector", lambda: v.tensor_single_scalar(kpb, pos, float(CAP), ALU.is_lt),
                   reads=fb(512, 544), writes=fb(544, 576))
            p.comp("vector", lambda Ga=Ga, Wd=Wd: v.tensor_mul(Wd, kpb, Ga),
                   reads=fb(544, 576) + gk, writes=wk)
        select8(SLOT0, OH1, fa(0, 256), 1, mk(320, 352))
        select8(SLOT1, OH2, fa(256, 512), 1, mk(352, 384))
        for tt in range(32):
            tc = 59 + tt % 4
            tok = c.idxs[:, tc:tc + 1]
            p.comp("vector", lambda tok=tok, tt=tt: v.tensor_scalar(tok, c.iotaP, 1.0, float(tt * 128), ALU.mult, ALU.add),
                   reads=[("consts", 0)], writes=[("idxs", tc)])
            for j in (0, 1):
                sc = 53 + (2 * tt + j) % 6
                sidx = c.idxs[:, sc:sc + 1]
                slotcol = (SLOT0 if j == 0 else SLOT1)[:, tt:tt + 1]
                skey = mk(320 + j * 32 + tt, 321 + j * 32 + tt)
                p.comp("vector", lambda sidx=sidx, slotcol=slotcol: v.tensor_copy(sidx, slotcol),
                       reads=skey, writes=[("idxs", sc)])
                p.dma("gpsimd", lambda sidx=sidx, tok=tok:
                      g.indirect_dma_start(c.table[0:NSLOT].rearrange("(n one) -> n one", one=1),
                                           bass.IndirectOffsetOnAxis(ap=sidx, axis=0),
                                           tok, None, oob_is_err=False),
                      reads=[("idxs", sc), ("idxs", tc), ("table", 0), ("table", 1)],
                      writes=[("table", 2 + 2 * tt + j)])

    if MOE_STOP < 4:
        return
    emit_moe_experts(c, layer, NMAC, KF)
    if MOE_STOP < 5:
        return
    emit_moe_gatherback(c, layer)


def emit_moe_experts(c, layer, NMAC, KF):
    nc, p = c.nc, c.p
    v, s, t, g = nc.vector, nc.scalar, nc.tensor, nc.gpsimd
    w1b = c.w1b[layer]; w2b = c.w2b[layer]
    w1v = w1b
    w2rows = 4096 if layer == 0 else 2048
    w2v = w2b
    NSLOT = NMAC * 512
    b1v = c.iw["f0_b1"].rearrange("e (ft p) -> (e ft) p", p=128) if layer == 0 else None
    xk = lambda col0, n: keys("xnb", col0, col0 + n, gran=512)
    qk = lambda col0, n: keys("qkv", col0, col0 + n, gran=512)
    for m in range(NMAC):
        eidcol = c.moeS[:, 160 + m:161 + m]
        ek = mk(160 + m, 161 + m)
        tw1 = c.moeS[:, 252:253]; tw2 = c.moeS[:, 253:254]
        p.comp("vector", lambda tw1=tw1, eidcol=eidcol:
               v.scalar_tensor_tensor(tw1, eidcol, 1024.0, c.iotaP, ALU.mult, ALU.add),
               reads=ek + [("consts", 0)], writes=mk(252, 253))
        p.comp("vector", lambda tw2=tw2, eidcol=eidcol:
               v.scalar_tensor_tensor(tw2, eidcol, float(w2rows), c.iotaP, ALU.mult, ALU.add),
               reads=ek + [("consts", 0)], writes=mk(253, 254))
        for k in range(8):
            ic = c.idxs[:, k:k + 1]
            p.comp("vector", lambda ic=ic, tw1=tw1, k=k: v.tensor_single_scalar(ic, tw1, float(128 * k), ALU.add),
                   reads=mk(252, 253), writes=[("idxs", k)])
        for k in range(KF):
            ic = c.idxs[:, 8 + k:9 + k]
            p.comp("vector", lambda ic=ic, tw2=tw2, k=k: v.tensor_single_scalar(ic, tw2, float(128 * k), ALU.add),
                   reads=mk(253, 254), writes=[("idxs", 8 + k)])
        if layer == 0:
            bi = c.idxs[0:32, 52:53]
            p.comp("vector", lambda bi=bi, eidcol=eidcol:
                   v.scalar_tensor_tensor(bi, eidcol[0:32, 0:1], 32.0, c.iotaP[0:32, 0:1], ALU.mult, ALU.add),
                   reads=ek + [("consts", 0)], writes=[("idxs", 52)])
            bg = c.f32c[0:32, 0:128]
            p.dma("gpsimd", lambda bg=bg, bi=bi:
                  g.indirect_dma_start(bg, None, b1v, bass.IndirectOffsetOnAxis(ap=bi, axis=0), oob_is_err=False),
                  reads=[("idxs", 52)], writes=[("f32c", 0)])
            p.comp("tensor", lambda bg=bg: t.transpose(c.ps[4][:, 0:32], bg, c.idF[0:32, 0:32]),
                   reads=[("f32c", 0), ("consts", 0)], writes=[("p4",)])
            p.comp("vector", lambda: v.tensor_copy(c.moeS[:, 192:224], c.ps[4][:, 0:32]),
                   reads=[("p4",)], writes=mk(192, 224))
        mp = m % 2
        bbuf = c.qkv if mp == 0 else c.ework
        bbk = (lambda c0, n: qk(c0, n)) if mp == 0 else (lambda c0, n: keys("ework", c0, c0 + n))
        for st in range(4):
            q = m * 4 + st
            ti = c.idxs[:, 50 if st % 2 == 0 else 51:51 if st % 2 == 0 else 52]
            tap_ = c.table[q * 128:(q + 1) * 128]
            p.dma("sync", lambda ti=ti, tap_=tap_: nc.sync.dma_start(out=ti, in_=tap_.rearrange("(p one) -> p one", one=1)),
                  reads=c.table_keys, writes=[("idxs", 50 + st % 2)])
            grow = c.att[:, (st % 2) * 1024:(st % 2) * 1024 + 1024]
            gwk = keys("att", (st % 2) * 1024, (st % 2) * 1024 + 1024)
            p.dma("gpsimd", lambda grow=grow, ti=ti:
                  g.indirect_dma_start(grow, None, c.xtok[:, :], bass.IndirectOffsetOnAxis(ap=ti, axis=0), oob_is_err=False),
                  reads=[("idxs", 50 + st % 2), ("xtok", 0), ("xtok", 1), ("xtok", 2), ("xtok", 3),
                         ("xtok", 4), ("xtok", 5), ("xtok", 6), ("xtok", 7), ("xtokz",)],
                  writes=gwk)
            qb = st % 2
            for kt in range(NKT):
                p.comp("tensor", lambda grow=grow, kt=kt, qb=qb:
                       t.transpose(c.qs[qb][:, kt * 128:(kt + 1) * 128], grow[:, kt * 128:(kt + 1) * 128], c.idB),
                       reads=gwk + [("cbf", 0)], writes=[(f"q{qb}",)])
            for kt in range(NKT):
                dst = bbuf[:, kt * 512 + st * 128: kt * 512 + st * 128 + 128]
                p.comp("vector", lambda dst=dst, kt=kt, qb=qb: v.tensor_copy(dst, c.qs[qb][:, kt * 128:(kt + 1) * 128]),
                       reads=[(f"q{qb}",)], writes=bbk(kt * 512, 512))
        for fh in range(4):
            a0 = 16384 + (fh % 2) * 8192
            for k in range(8):
                wdst = c.xnb[:, a0 + k * 1024: a0 + (k + 1) * 1024]
                ic = c.idxs[:, k:k + 1]
                p.dma("gpsimd", lambda wdst=wdst, ic=ic, fh=fh:
                      g.indirect_dma_start(wdst, None, w1v, bass.IndirectOffsetOnAxis(ap=ic, axis=0),
                                           element_offset=fh * 1024, oob_is_err=False),
                      reads=[("idxs", k)] + c.wb_keys[layer], writes=xk(a0 + k * 1024, 1024))
            for f in range(8):
                fi = fh * 8 + f
                pb = f % 2
                for k in range(8):
                    lhs = c.xnb[:, a0 + k * 1024 + f * 128: a0 + k * 1024 + f * 128 + 128]
                    rhs = bbuf[:, k * 512:(k + 1) * 512]
                    p.comp("tensor", lambda lhs=lhs, rhs=rhs, pb=pb, k=k:
                           t.matmul(c.ps[pb], lhs, rhs, start=(k == 0), stop=(k == 7)),
                           reads=xk(a0 + k * 1024, 1024) + bbk(k * 512, 512), writes=[(f"p{pb}",)])
                h1 = c.xnb[:, fi * 512:(fi + 1) * 512]
                if layer == 0:
                    bcol = c.moeS[:, 192 + fi:193 + fi]
                    p.comp("scalar", lambda h1=h1, pb=pb, bcol=bcol:
                           s.activation(h1, c.ps[pb], AF.Relu, bias=bcol, scale=1.0),
                           reads=[(f"p{pb}",)] + mk(192, 224), writes=xk(fi * 512, 512))
                elif fh < 2:
                    p.comp("scalar", lambda h1=h1, pb=pb: s.activation(h1, c.ps[pb], AF.Gelu, bias=0.0, scale=1.0),
                           reads=[(f"p{pb}",)], writes=xk(fi * 512, 512))
                else:
                    gf = c.xnb[:, (fi - 16) * 512:(fi - 16 + 1) * 512]
                    p.comp("vector", lambda gf=gf, pb=pb:
                           v.scalar_tensor_tensor(gf, gf, 1.0, c.ps[pb], ALU.mult, ALU.mult),
                           reads=[(f"p{pb}",)] + xk((fi - 16) * 512, 512), writes=xk((fi - 16) * 512, 512))
        for dm in range(2):
            for k in range(KF):
                wslot = 4096 + ((k % 4) + 0) * 512 if False else 4096 + (k % 4) * 512
                wslot = 4096 + (k % 4) * 512 if (k % 4) < 4 else wslot
                w2sb = c.qkv[:, wslot:wslot + 512]
                ic = c.idxs[:, 8 + k:9 + k]
                p.dma("gpsimd", lambda w2sb=w2sb, ic=ic, dm=dm:
                      g.indirect_dma_start(w2sb, None, w2v, bass.IndirectOffsetOnAxis(ap=ic, axis=0),
                                           element_offset=dm * 512, oob_is_err=False),
                      reads=[("idxs", 8 + k)] + c.wb_keys[layer], writes=qk(wslot, 512))
                for st in range(4):
                    lhs = c.xnb[:, k * 512 + st * 128: k * 512 + st * 128 + 128]
                    p.comp("tensor", lambda lhs=lhs, w2sb=w2sb, st=st, k=k, KF=KF:
                           t.matmul(c.ps[st + 2], lhs, w2sb, start=(k == 0), stop=(k == KF - 1)),
                           reads=xk(k * 512, 512) + qk(wslot, 512), writes=[(f"p{st + 2}",)])
            for st in range(4):
                yt = c.f32b[:, st * 1024 + dm * 512: st * 1024 + dm * 512 + 512]
                p.comp("scalar", lambda yt=yt, st=st: s.copy(yt, c.ps[st + 2]),
                       reads=[(f"p{st + 2}",)], writes=[("f32b", (st * 1024 + dm * 512) // 512)])
        for st in range(4):
            q = m * 4 + st
            ap_ = c.expout[q * 128:(q + 1) * 128, :]
            ysb = c.f32b[:, st * 1024:(st + 1) * 1024]
            p.dma("scalar", lambda ap_=ap_, ysb=ysb: nc.scalar.dma_start(out=ap_, in_=ysb),
                  reads=[("f32b", st * 2), ("f32b", st * 2 + 1)], writes=[("expout", q // 8)])


def emit_moe_gatherback(c, layer):
    nc, p = c.nc, c.p
    v, s, t, g = nc.vector, nc.scalar, nc.tensor, nc.gpsimd
    ek = [("expout", i) for i in range(12)]
    for tm in range(NTM):
        for j in range(4):
            tt = tm * 4 + j
            yt = c.f32b[:, j * 1024:(j + 1) * 1024]
            ytk = [("f32b", j * 2), ("f32b", j * 2 + 1)]
            if layer == 0:
                si = c.idxs[:, 53:54]; fi_ = c.idxs[:, 54:55]
                sl = c.moeS[:, 244:245]
                sap = c.entS[tt * 128:(tt + 1) * 128]
                p.dma("sync", lambda sl=sl, sap=sap: nc.sync.dma_start(out=sl, in_=sap.rearrange("(p one) -> p one", one=1)),
                      reads=[("entS", tt // 8)], writes=mk(244, 245))
                p.comp("vector", lambda si=si, sl=sl: v.tensor_copy(si, sl),
                       reads=mk(244, 245), writes=[("idxs",)])
                fin = c.moeS[:, 64 + tt:65 + tt]
                p.comp("vector", lambda fi_=fi_, fin=fin: v.tensor_copy(fi_, fin),
                       reads=mk(64 + tt, 65 + tt), writes=[("idxs",)])
                p.dma("gpsimd", lambda yt=yt, si=si:
                      g.indirect_dma_start(yt, None, c.expout[:, :], bass.IndirectOffsetOnAxis(ap=si, axis=0), oob_is_err=False),
                      reads=[("idxs",)] + ek, writes=ytk)
                b2g = c.f32c[:, j * 1024:(j + 1) * 1024]
                b2k = [("f32c", j * 2), ("f32c", j * 2 + 1)]
                p.dma("gpsimd", lambda b2g=b2g, fi_=fi_:
                      g.indirect_dma_start(b2g, None, c.iw["f0_b2"][:, :], bass.IndirectOffsetOnAxis(ap=fi_, axis=0), oob_is_err=False),
                      reads=[("idxs",)], writes=b2k)
                p.comp("vector", lambda yt=yt, b2g=b2g: v.tensor_add(yt, yt, b2g),
                       reads=ytk + b2k, writes=ytk)
            else:
                si0 = c.idxs[:, 53 + tt % 3:54 + tt % 3]; si1 = c.idxs[:, 56 + tt % 3:57 + tt % 3]
                s0c = c.moeS[:, 320 + tt:321 + tt]; s1c = c.moeS[:, 352 + tt:353 + tt]
                w0c = c.moeS[:, 384 + tt:385 + tt]; w1c_ = c.moeS[:, 416 + tt:417 + tt]
                p.comp("vector", lambda si0=si0, s0c=s0c: v.tensor_copy(si0, s0c),
                       reads=mk(320 + tt, 321 + tt), writes=[("idxs", 53 + tt % 3)])
                p.comp("vector", lambda si1=si1, s1c=s1c: v.tensor_copy(si1, s1c),
                       reads=mk(352 + tt, 353 + tt), writes=[("idxs", 56 + tt % 3)])
                g0 = c.f32c[:, j * 1024:(j + 1) * 1024]
                g0k = [("f32c", j * 2), ("f32c", j * 2 + 1)]
                p.dma("gpsimd", lambda g0=g0, si0=si0:
                      g.indirect_dma_start(g0, None, c.expout[:, :], bass.IndirectOffsetOnAxis(ap=si0, axis=0), oob_is_err=False),
                      reads=[("idxs", 53 + tt % 3)] + ek, writes=g0k)
                g1 = c.f32a[:, (j % 2) * 1024:(j % 2) * 1024 + 1024]
                g1k = [("f32a", (j % 2) * 2), ("f32a", (j % 2) * 2 + 1)]
                p.dma("gpsimd", lambda g1=g1, si1=si1:
                      g.indirect_dma_start(g1, None, c.expout[:, :], bass.IndirectOffsetOnAxis(ap=si1, axis=0), oob_is_err=False),
                      reads=[("idxs", 56 + tt % 3)] + ek, writes=g1k)
                p.comp("vector", lambda yt=yt, g0=g0, w0c=w0c: v.tensor_scalar(yt, g0, w0c[:, 0:1], None, ALU.mult),
                       reads=g0k + mk(384 + tt, 385 + tt), writes=ytk)
                p.comp("vector", lambda yt=yt, g1=g1, w1c_=w1c_:
                       v.scalar_tensor_tensor(yt, g1, w1c_[:, 0:1], yt, ALU.mult, ALU.add),
                       reads=g1k + mk(416 + tt, 417 + tt) + ytk, writes=ytk)
        for kt in range(NKT):
            pb = kt % 2
            for j in range(4):
                yt = c.f32b[:, j * 1024:(j + 1) * 1024]
                ytk = [("f32b", j * 2), ("f32b", j * 2 + 1)]
                p.comp("tensor", lambda yt=yt, kt=kt, j=j, pb=pb:
                       t.transpose(c.ps[pb][:, j * 128:(j + 1) * 128], yt[:, kt * 128:(kt + 1) * 128], c.idF),
                       reads=ytk + [("consts", 0)], writes=[(f"p{pb}",)])
            xt_ = c.f32a[:, 2048 + (kt % 2) * 512: 2048 + (kt % 2) * 512 + 512]
            xtk = [("f32a", 4 + (kt % 2))]
            Xap = c.X[kt * 128:(kt + 1) * 128, tm * 512:(tm + 1) * 512]
            p.dma("sync", lambda xt_=xt_, Xap=Xap: nc.sync.dma_start(out=xt_, in_=Xap),
                  reads=[("X", kt * 8 + tm)], writes=xtk)
            p.comp("vector", lambda xt_=xt_, pb=pb: v.tensor_add(xt_, xt_, c.ps[pb]),
                   reads=xtk + [(f"p{pb}",)], writes=xtk)
            p.dma("sync", lambda xt_=xt_, Xap=Xap: nc.sync.dma_start(out=Xap, in_=xt_),
                  reads=xtk, writes=[("X", kt * 8 + tm)])


def emit_l0_attn_b(c):
    nc, p = c.nc, c.p
    v, s, t, g = nc.vector, nc.scalar, nc.tensor, nc.gpsimd
    iw = c.iw
    ak = lambda c0, c1: keys("att", c0, c1)
    ew = lambda c0, c1: keys("ework", c0, c1)
    qv = lambda c0, c1: keys("qkv", c0, c1)
    R = 256.0
    emit_ln(c, c.X, "X", dst_sbuf=True)
    xm = c.cols[:, 512:520]
    for kt in range(NKT):
        xr = c.xnb[:, kt * T:(kt + 1) * T]
        p.comp("vector", lambda xr=xr, kt=kt: v.reduce_sum(xm[:, kt:kt + 1], xr, AX.X),
               reads=keys("xnb", kt * T, (kt + 1) * T), writes=[("cols", 1)])
    wsp = c.f32c[:, 0:128]
    wap = iw["b0_wspanT"].rearrange("(kt p) h -> p kt h", p=128)
    p.dma("sync", lambda wsp=wsp, wap=wap: nc.sync.dma_start(out=wsp, in_=wap), writes=[("f32c", 0)])
    for kt in range(NKT):
        p.comp("tensor", lambda kt=kt: t.matmul(c.ps[4][0:16, 0:1], wsp[:, kt * 16:(kt + 1) * 16],
                                                xm[:, kt:kt + 1], start=(kt == 0), stop=(kt == NKT - 1)),
               reads=[("f32c", 0), ("cols", 1)], writes=[("p4",)])
    bsp = c.f32c[0:16, 512:513]
    p.dma("sync", lambda: nc.sync.dma_start(out=bsp, in_=iw["b0_bspan"].rearrange("(h one) -> h one", one=1)),
          writes=[("f32c", 1)])
    zc = c.f32c[0:16, 513:514]
    p.comp("scalar", lambda: s.activation(zc, c.ps[4][0:16, 0:1], AF.Sigmoid, bias=bsp, scale=1.0 / T),
           reads=[("p4",), ("f32c", 1)], writes=[("f32c", 1)])
    p.comp("vector", lambda: v.tensor_single_scalar(zc, zc, float(T) / R, ALU.mult),
           reads=[("f32c", 1)], writes=[("f32c", 1)])
    fak = [("f32a", i) for i in range(3)]
    fbk = [("f32b", i) for i in range(3)]
    for ch in range(3):
        w0 = ch * 1536
        ib = c.f32a[0:16, 0:1536]
        p.comp("gpsimd", lambda ib=ib, w0=w0: g.iota(ib, [[1, 1536]], base=w0, channel_multiplier=0,
                                                     allow_small_or_imprecise_dtypes=True),
               writes=fak)
        mm_ = c.f32b[0:16, 0:1536]
        p.comp("vector", lambda mm_=mm_, ib=ib: v.tensor_scalar(mm_, ib, 1.0 / R, 1.0 - 4095.0 / R, ALU.mult, ALU.add),
               reads=fak, writes=fbk)
        p.comp("vector", lambda mm_=mm_: v.tensor_scalar(mm_, mm_, zc[:, 0:1], None, ALU.add),
               reads=fbk + [("f32c", 1)], writes=fbk)
        p.comp("vector", lambda mm_=mm_: v.tensor_scalar(mm_, mm_, 0.0, 1.0, ALU.max, ALU.min),
               reads=fbk, writes=fbk)
        cz = c.f32a[0:16, 1536:3072]
        czk = [("f32a", i) for i in range(3, 6)]
        p.comp("vector", lambda cz=cz, ib=ib: v.tensor_single_scalar(cz, ib, 4095.0, ALU.is_le),
               reads=fak, writes=czk)
        p.comp("vector", lambda mm_=mm_, cz=cz: v.tensor_mul(mm_, mm_, cz),
               reads=fbk + czk, writes=fbk)
        mb = c.att[0:16, 0:1536]
        p.comp("scalar", lambda mb=mb, mm_=mm_: s.copy(mb, mm_),
               reads=fbk, writes=ak(0, 1536))
        sap = c.strips[:, w0:w0 + 1536]
        p.dma("sync", lambda sap=sap, mb=mb: nc.sync.dma_start(out=sap, in_=mb),
              reads=ak(0, 1536), writes=[("strips", ch)])
    negc = c.cols[:, 520:536]
    for hp in range(8):
        for wi, nm in enumerate(("b0_wqT", "b0_wkT", "b0_wvT")):
            wap = iw[nm][:, hp * 128:(hp + 1) * 128].rearrange("(kt p) m -> p kt m", p=128)
            wsb = c.wst[:, wi * 1024:(wi + 1) * 1024]
            p.dma("gpsimd", lambda wsb=wsb, wap=wap: nc.gpsimd.dma_start(out=wsb, in_=wap),
                  writes=keys("wst", wi * 1024, (wi + 1) * 1024))
        for wi, dst, dkey in ((0, c.qkv, 0), (1, c.qkv, 4096), (2, c.ework, 0)):
            for tm in range(NTM):
                pb = tm % 2
                for kt in range(NKT):
                    lhs = c.wst[:, wi * 1024 + kt * 128:(wi * 1024) + (kt + 1) * 128]
                    rhs = c.xnb[:, kt * T + tm * 512: kt * T + (tm + 1) * 512]
                    p.comp("tensor", lambda lhs=lhs, rhs=rhs, pb=pb, kt=kt:
                           t.matmul(c.ps[pb], lhs, rhs, start=(kt == 0), stop=(kt == NKT - 1)),
                           reads=keys("wst", wi * 1024, (wi + 1) * 1024) +
                                 keys("xnb", (kt * 8 + tm) * 512, (kt * 8 + tm) * 512 + 512),
                           writes=[(f"p{pb}",)])
                osb = dst[:, dkey + tm * 512: dkey + (tm + 1) * 512]
                okey = (qv if dst is c.qkv else ew)(dkey + tm * 512, dkey + (tm + 1) * 512)
                p.comp("scalar", lambda osb=osb, pb=pb: s.copy(osb, c.ps[pb]),
                       reads=[(f"p{pb}",)], writes=okey)
        for h2 in range(2):
            for tt in range(32):
                src_ = c.ework[h2 * 64:(h2 + 1) * 64, tt * 128:(tt + 1) * 128]
                qb = tt % 2
                idsl = c.idB[h2 * 64:(h2 + 1) * 64, h2 * 64:(h2 + 1) * 64]
                p.comp("tensor", lambda src_=src_, qb=qb, tt=tt, idsl=idsl:
                       t.transpose(c.qs[qb][:, (tt % 8) * 64:(tt % 8) * 64 + 64], src_, idsl),
                       reads=ew(tt * 128, (tt + 1) * 128) + [("cbf", 0)], writes=[(f"q{qb}",)])
                v0 = 3072 + (h2 * 32 + tt) * 65
                dst_ = c.ework[:, v0:v0 + 64]
                p.comp("vector", lambda dst_=dst_, qb=qb, tt=tt:
                       v.tensor_copy(dst_, c.qs[qb][:, (tt % 8) * 64:(tt % 8) * 64 + 64]),
                       reads=[(f"q{qb}",)], writes=ew(v0, v0 + 65))
                one_ = c.ework[:, v0 + 64:v0 + 65]
                p.comp("vector", lambda one_=one_: v.memset(one_, 1.0),
                       writes=ew(v0, v0 + 65))
        for h2 in range(2):
            hh = hp * 2 + h2
            for which, src0, skey in (("q", 0, qv), ("k", 4096, qv)):
                rmax = c.rows[0:1, 128 + (0 if which == "q" else 1):129 + (0 if which == "q" else 1)]
                for tm in range(NTM):
                    sq = c.f32c[h2 * 64:(h2 + 1) * 64, 0:512]
                    sap_ = c.qkv[h2 * 64:(h2 + 1) * 64, src0 + tm * 512: src0 + (tm + 1) * 512]
                    p.comp("scalar", lambda sq=sq, sap_=sap_: s.activation(sq, sap_, AF.Square, bias=0.0, scale=1.0),
                           reads=skey(src0 + tm * 512, src0 + (tm + 1) * 512), writes=[("f32c", 0)])
                    ocol = c.onescol[h2 * 64:(h2 + 1) * 64, 0:1]
                    p.comp("tensor", lambda sq=sq, ocol=ocol: t.matmul(c.ps[4][0:1, :], ocol, sq, start=True, stop=True),
                           reads=[("f32c", 0), ("consts", 0)], writes=[("p4",)])
                    cur = c.rows[0:1, 130:131]
                    p.comp("vector", lambda cur=cur: v.reduce_max(cur, c.ps[4][0:1, :], AX.X),
                           reads=[("p4",)], writes=[("rows", 0)])
                    if tm == 0:
                        p.comp("vector", lambda rmax=rmax, cur=cur: v.tensor_copy(rmax, cur),
                               reads=[("rows", 0)], writes=[("rows", 0)])
                    else:
                        p.comp("vector", lambda rmax=rmax, cur=cur: v.tensor_max(rmax, rmax, cur),
                               reads=[("rows", 0)], writes=[("rows", 0)])
            qm = c.rows[0:1, 128:129]; km = c.rows[0:1, 129:130]; cc_ = c.rows[0:1, 131:132]
            p.comp("vector", lambda: v.tensor_mul(cc_, qm, km), reads=[("rows", 0)], writes=[("rows", 0)])
            p.comp("scalar", lambda: s.activation(cc_, cc_, AF.Sqrt, bias=0.0, scale=1.0),
                   reads=[("rows", 0)], writes=[("rows", 0)])
            p.comp("vector", lambda: v.tensor_single_scalar(cc_, cc_, -1.0 / 8.0, ALU.mult),
                   reads=[("rows", 0)], writes=[("rows", 0)])
            p.comp("tensor", lambda hh=hh: t.matmul(c.ps[4][:, 0:1], c.onesrow, cc_, start=True, stop=True),
                   reads=[("rows", 0), ("rows", 3)], writes=[("p4",)])
            p.comp("vector", lambda hh=hh: v.tensor_copy(negc[:, hh:hh + 1], c.ps[4][:, 0:1]),
                   reads=[("p4",)], writes=[("cols", 1)])
        ecnt = [0]
        for tm in range(NTM):
            t0 = tm * 512
            nst = 4 * tm + 4
            def m_load(st):
                for h2 in range(2):
                    hh = hp * 2 + h2
                    mslot = (2 * (st % 2) + h2) * 512
                    m_ = c.att[:, mslot:mslot + 512]
                    moff = hh * 4608 + 3584 - t0 + st * 128
                    map_ = bass.AP(c.strips.tensor, moff, [[1, 128], [1, 512]])
                    p.dma("sync", lambda m_=m_, map_=map_: nc.sync.dma_start(out=m_, in_=map_),
                          reads=[("strips", 0), ("strips", 1), ("strips", 2)], writes=ak(mslot, mslot + 512))
            def flush(pend):
                for (h2, pst, e_, em_, ek_, emk_) in pend:
                    v0 = 3072 + (h2 * 32 + pst) * 65
                    vt = c.ework[:, v0:v0 + 65]
                    p.comp("tensor", lambda vt=vt, em_=em_, pst=pst, nst=nst, h2=h2:
                           t.matmul(c.ps[2 + h2][0:65, :], vt, em_, start=(pst == 0), stop=(pst == nst - 1)),
                           reads=ew(v0, v0 + 65) + emk_, writes=[(f"p{2 + h2}",)])
                    p.comp("tensor", lambda e_=e_, pst=pst, nst=nst, h2=h2:
                           t.matmul(c.ps[4 + h2][0:1, :], c.onescolB, e_, start=(pst == 0), stop=(pst == nst - 1)),
                           reads=ek_ + [("cbf", 0)], writes=[(f"p{4 + h2}",)])
            pend = []
            m_load(0)
            for st in range(nst):
                if st + 1 < nst:
                    m_load(st + 1)
                for h2 in range(2):
                    pb = h2
                    s0 = st * 128
                    lhs = c.qkv[h2 * 64:(h2 + 1) * 64, 4096 + s0:4096 + s0 + 128]
                    rhs = c.qkv[h2 * 64:(h2 + 1) * 64, t0:t0 + 512]
                    p.comp("tensor", lambda lhs=lhs, rhs=rhs, pb=pb:
                           t.matmul(c.ps[pb], lhs, rhs, start=True, stop=True),
                           reads=qv(4096 + s0, 4096 + s0 + 128) + qv(t0, t0 + 512), writes=[(f"p{pb}",)])
                flush(pend)
                pend = []
                for h2 in range(2):
                    hh = hp * 2 + h2
                    pb = h2
                    sl = ecnt[0] % 3
                    ecnt[0] += 1
                    e_ = c.ework[:, sl * 512:sl * 512 + 512]
                    em_ = c.ework[:, 1536 + sl * 512:1536 + sl * 512 + 512]
                    ek_ = ew(sl * 512, sl * 512 + 512)
                    emk_ = ew(1536 + sl * 512, 1536 + sl * 512 + 512)
                    mslot = (2 * (st % 2) + h2) * 512
                    m_ = c.att[:, mslot:mslot + 512]
                    p.comp("scalar", lambda e_=e_, pb=pb, hh=hh:
                           s.activation(e_, c.ps[pb], AF.Exp, bias=negc[:, hh:hh + 1], scale=0.125),
                           reads=[(f"p{pb}",), ("cols", 1)], writes=ek_)
                    p.comp("vector", lambda em_=em_, e_=e_, m_=m_: v.tensor_mul(em_, e_, m_[:, ::-1]),
                           reads=ek_ + ak(mslot, mslot + 512), writes=emk_)
                    pend.append((h2, st, e_, em_, ek_, emk_))
            flush(pend)
            for h2 in range(2):
                hh = hp * 2 + h2
                se_ = c.rows[0:1, 132 + h2 * 512:644 + h2 * 512]
                sek = [("rows", h2), ("rows", h2 + 1)]
                p.comp("vector", lambda se_=se_, h2=h2: v.tensor_copy(se_, c.ps[4 + h2][0:1, :]),
                       reads=[(f"p{4 + h2}",)], writes=sek)
                p.comp("vector", lambda se_=se_, h2=h2:
                       v.scalar_tensor_tensor(se_, se_, 1e-8, c.ps[2 + h2][64:65, :], ALU.mult, ALU.add),
                       reads=sek + [(f"p{2 + h2}",)], writes=sek)
                p.comp("scalar", lambda se_=se_: s.activation(se_, se_, AF.Ln, bias=0.0, scale=1.0),
                       reads=sek, writes=sek)
                p.comp("scalar", lambda se_=se_: s.activation(se_, se_, AF.Exp, bias=0.0, scale=-1.0),
                       reads=sek, writes=sek)
                cx = c.f32c[:, h2 * 512:h2 * 512 + 512]
                p.comp("vector", lambda cx=cx, h2=h2: v.tensor_copy(cx[0:64, :], c.ps[2 + h2][0:64, :]),
                       reads=[(f"p{2 + h2}",)], writes=[("f32c", h2)])
                p.comp("tensor", lambda se_=se_, h2=h2:
                       t.matmul(c.ps[2 + h2][0:64, :], c.onesrow[0:1, 0:64], se_, start=True, stop=True),
                       reads=sek + [("rows", 3), ("f32c", h2)], writes=[(f"p{2 + h2}",)])
                p.comp("vector", lambda cx=cx, h2=h2: v.tensor_mul(cx[0:64, :], cx[0:64, :], c.ps[2 + h2][0:64, :]),
                       reads=[("f32c", h2), (f"p{2 + h2}",)], writes=[("f32c", h2)])
                ob = c.att[0:64, 2048:2560] if h2 == 0 else c.att[64:128, 2048:2560]
                p.comp("scalar", lambda ob=ob, cx=cx: s.copy(ob, cx[0:64, :]),
                       reads=[("f32c", h2)], writes=ak(2048, 2560))
                cap_ = c.ctxT[hh * 64:(hh + 1) * 64, t0:t0 + 512]
                p.dma("sync", lambda cap_=cap_, ob=ob: nc.sync.dma_start(out=cap_, in_=ob),
                      reads=ak(2048, 2560), writes=[("ctxT", (hh // 2) * 8 + tm)])
    emit_proj_residual(c, c.iw["b0_woT"], D, bias_dram=c.iw["b0_bo"])


def emit_l1_attn_a(c):
    nc, p = c.nc, c.p
    v, s, t, g = nc.vector, nc.scalar, nc.tensor, nc.gpsimd
    iw = c.iw
    ak = lambda c0, c1: keys("att", c0, c1)
    ew = lambda c0, c1: keys("ework", c0, c1)
    qv = lambda c0, c1: keys("qkv", c0, c1)
    wk = lambda c0, c1: keys("wst", c0, c1)
    emit_ln(c, c.X, "X", dst_sbuf=True)
    halfpi = constcell(c, PI / 2)
    EPS2 = 1e-6 * 128.0
    for hp in range(8):
        for wi, nm in enumerate(("a1_wqT", "a1_wkT", "a1_wvT")):
            wap = iw[nm][:, hp * 128:(hp + 1) * 128].rearrange("(kt p) m -> p kt m", p=128)
            wsb = c.wst[:, wi * 1024:(wi + 1) * 1024]
            p.dma("gpsimd", lambda wsb=wsb, wap=wap: nc.gpsimd.dma_start(out=wsb, in_=wap),
                  writes=wk(wi * 1024, (wi + 1) * 1024))
        for h2 in range(2):
            hh = hp * 2 + h2
            wrap = iw["a1_wr"][hh * 64:(hh + 1) * 64, :]
            wrsb = c.wst[h2 * 64:(h2 + 1) * 64, 3072 + h2 * 128:3072 + h2 * 128 + 128]
            p.dma("gpsimd", lambda wrsb=wrsb, wrap=wrap: nc.gpsimd.dma_start(out=wrsb, in_=wrap),
                  writes=wk(3072, 3584))
        for wi, dst0 in ((0, 4096), (1, 8192), (2, 12288)):
            for tm in range(NTM):
                pb = tm % 2
                for kt in range(NKT):
                    lhs = c.wst[:, wi * 1024 + kt * 128: wi * 1024 + (kt + 1) * 128]
                    rhs = c.xnb[:, kt * T + tm * 512: kt * T + (tm + 1) * 512]
                    p.comp("tensor", lambda lhs=lhs, rhs=rhs, pb=pb, kt=kt:
                           t.matmul(c.ps[pb], lhs, rhs, start=(kt == 0), stop=(kt == NKT - 1)),
                           reads=wk(wi * 1024, (wi + 1) * 1024) +
                                 keys("xnb", (kt * 8 + tm) * 512, (kt * 8 + tm) * 512 + 512),
                           writes=[(f"p{pb}",)])
                osb = c.wst[:, dst0 + tm * 512: dst0 + (tm + 1) * 512]
                p.comp("scalar", lambda osb=osb, pb=pb: s.copy(osb, c.ps[pb]),
                       reads=[(f"p{pb}",)], writes=wk(dst0 + tm * 512, dst0 + (tm + 1) * 512))
        for h2 in range(2):
            hh = hp * 2 + h2
            b0 = h2 * 64
            wrsb = c.wst[b0:b0 + 64, 3072 + h2 * 128:3072 + h2 * 128 + 128]
            rbr = c.rows[0:1, 160:288]
            p.dma("sync", lambda rbr=rbr, hh=hh: nc.sync.dma_start(out=rbr, in_=iw["a1_br"][hh, :]),
                  writes=[("rows", 0)])
            brc = c.cols[:, 536:537]
            p.comp("tensor", lambda rbr=rbr: t.transpose(c.ps[4][0:128, 0:1], rbr, c.idF[0:1, 0:1]),
                   reads=[("rows", 0), ("consts", 0)], writes=[("p4",)])
            p.comp("vector", lambda brc=brc: v.tensor_copy(brc, c.ps[4][:, 0:1]),
                   reads=[("p4",)], writes=[("cols", 1)])
            for tm in range(NTM):
                pb = tm % 2
                lhs = wrsb
                rhs = c.wst[b0:b0 + 64, 4096 + tm * 512:4096 + (tm + 1) * 512]
                p.comp("tensor", lambda lhs=lhs, rhs=rhs, pb=pb:
                       t.matmul(c.ps[pb], lhs, rhs, start=True, stop=True),
                       reads=wk(3072, 3584) + wk(4096 + tm * 512, 4096 + (tm + 1) * 512),
                       writes=[(f"p{pb}",)])
                prj = c.f32a[:, (tm % 2) * 512:(tm % 2) * 512 + 512]
                pjk = [("f32a", tm % 2)]
                p.comp("scalar", lambda prj=prj, pb=pb, brc=brc:
                       s.activation(prj, c.ps[pb], AF.Identity, bias=brc, scale=1.0),
                       reads=[(f"p{pb}",), ("cols", 1)], writes=pjk)
                emit_sincos(c, prj, pjk, c.qkv[:, tm * 512:(tm + 1) * 512],
                            c.qkv[:, 4096 + tm * 512:4096 + (tm + 1) * 512],
                            qv(tm * 512, (tm + 1) * 512), qv(4096 + tm * 512, 4096 + (tm + 1) * 512),
                            halfpi, tm % 2)
            for ttg in range(8):
                for j in range(4):
                    tt = ttg * 4 + j
                    lhs = c.wst[b0:b0 + 64, 8192 + tt * 128:8192 + (tt + 1) * 128]
                    p.comp("tensor", lambda lhs=lhs, wrsb=wrsb, j=j:
                           t.matmul(c.ps[j][:, 0:128], lhs, wrsb, start=True, stop=False),
                           reads=wk(8192 + tt * 128, 8192 + (tt + 1) * 128) + wk(3072, 3584),
                           writes=[(f"p{j}",)])
                    p.comp("tensor", lambda rbr=rbr, j=j:
                           t.matmul(c.ps[j][:, 0:128], c.onesrow, rbr, start=False, stop=True),
                           reads=[("rows", 0), ("rows", 3)], writes=[(f"p{j}",)])
                prj = c.f32a[:, 1024 + (ttg % 2) * 512:1024 + (ttg % 2) * 512 + 512]
                pjk = [("f32a", 2 + (ttg % 2))]
                for j in range(4):
                    p.comp("scalar", lambda prj=prj, j=j: s.copy(prj[:, j * 128:(j + 1) * 128], c.ps[j][:, 0:128]),
                           reads=[(f"p{j}",)], writes=pjk)
                e3 = c.ework[:, ttg * 1024:(ttg + 1) * 1024].rearrange("p (a b) -> p a b", b=256)
                emit_sincos(c, prj, pjk, e3[:, :, 0:128], e3[:, :, 128:256],
                            ew(ttg * 1024, (ttg + 1) * 1024), ew(ttg * 1024, (ttg + 1) * 1024),
                            halfpi, 2 + ttg % 2)
            if h2 == 0:
                for tt in range(32):
                    src_ = c.wst[:, 12288 + tt * 128:12288 + (tt + 1) * 128]
                    qb = (tt // 8) % 2
                    p.comp("tensor", lambda src_=src_, qb=qb, tt=tt:
                           t.transpose(c.qs[qb][:, (tt % 8) * 128:(tt % 8) * 128 + 128], src_, c.idB),
                           reads=wk(12288 + tt * 128, 12288 + (tt + 1) * 128) + [("cbf", 0)], writes=[(f"q{qb}",)])
                    if tt % 8 == 7:
                        g0 = tt - 7
                        dst_ = c.wst[:, 12288 + g0 * 128:12288 + g0 * 128 + 1024]
                        p.comp("vector", lambda dst_=dst_, qb=qb:
                               v.tensor_copy(dst_, c.qs[qb][:, 0:1024]),
                               reads=[(f"q{qb}",)], writes=wk(12288 + g0 * 128, 12288 + g0 * 128 + 1024))
            for fp in range(2):
                for tt in range(32):
                    lhs = c.ework[:, tt * 256 + fp * 128:tt * 256 + fp * 128 + 128]
                    vt = c.wst[:, 12288 + tt * 128 + h2 * 64:12288 + tt * 128 + h2 * 64 + 64]
                    p.comp("tensor", lambda lhs=lhs, vt=vt, fp=fp, tt=tt:
                           t.matmul(c.ps[2 + fp][:, 0:64], lhs, vt, start=(tt == 0), stop=(tt == 31)),
                           reads=ew(tt * 256 + fp * 128, tt * 256 + fp * 128 + 128) +
                                 wk(12288 + tt * 128, 12288 + (tt + 1) * 128),
                           writes=[(f"p{2 + fp}",)])
                    p.comp("tensor", lambda lhs=lhs, fp=fp, tt=tt:
                           t.matmul(c.ps[4 + fp][:, 0:1], lhs, c.onescolB, start=(tt == 0), stop=(tt == 31)),
                           reads=ew(tt * 256 + fp * 128, tt * 256 + fp * 128 + 128) + [("cbf", 0)],
                           writes=[(f"p{4 + fp}",)])
            for fp in range(2):
                kv = c.att[:, 65 * fp:65 * fp + 65]
                p.comp("vector", lambda kv=kv, fp=fp: v.tensor_copy(kv[:, 0:64], c.ps[2 + fp][:, 0:64]),
                       reads=[(f"p{2 + fp}",)], writes=ak(65 * fp, 65 * fp + 65))
                p.comp("vector", lambda kv=kv, fp=fp: v.tensor_copy(kv[:, 64:65], c.ps[4 + fp][:, 0:1]),
                       reads=[(f"p{4 + fp}",)], writes=ak(65 * fp, 65 * fp + 65))
            for tm in range(NTM):
                for fp in range(2):
                    kv = c.att[:, 65 * fp:65 * fp + 65]
                    rhs = c.qkv[:, fp * 4096 + tm * 512: fp * 4096 + (tm + 1) * 512]
                    p.comp("tensor", lambda kv=kv, rhs=rhs, fp=fp:
                           t.matmul(c.ps[0][0:65, :], kv, rhs, start=(fp == 0), stop=(fp == 1)),
                           reads=ak(65 * fp, 65 * fp + 65) + qv(fp * 4096 + tm * 512, fp * 4096 + (tm + 1) * 512),
                           writes=[("p0",)])
                dn = c.rows[0:1, 644:1156]
                p.comp("vector", lambda dn=dn: v.tensor_scalar(dn, c.ps[0][64:65, :], EPS2, None, ALU.add),
                       reads=[("p0",)], writes=[("rows", 1), ("rows", 2)])
                p.comp("scalar", lambda dn=dn: s.activation(dn, dn, AF.Ln, bias=0.0, scale=1.0),
                       reads=[("rows", 1), ("rows", 2)], writes=[("rows", 1), ("rows", 2)])
                p.comp("scalar", lambda dn=dn: s.activation(dn, dn, AF.Exp, bias=0.0, scale=-1.0),
                       reads=[("rows", 1), ("rows", 2)], writes=[("rows", 1), ("rows", 2)])
                p.comp("tensor", lambda dn=dn: t.matmul(c.ps[4][0:64, :], c.onesrow[0:1, 0:64], dn, start=True, stop=True),
                       reads=[("rows", 1), ("rows", 2), ("rows", 3)], writes=[("p4",)])
                cx = c.f32c[:, 0:512]
                p.comp("vector", lambda cx=cx: v.tensor_copy(cx[0:64, :], c.ps[0][0:64, :]),
                       reads=[("p0",)], writes=[("f32c", 0)])
                p.comp("vector", lambda cx=cx: v.tensor_mul(cx[0:64, :], cx[0:64, :], c.ps[4][0:64, :]),
                       reads=[("f32c", 0), ("p4",)], writes=[("f32c", 0)])
                ob = c.att[0:64, 1024:1536]
                p.comp("scalar", lambda ob=ob, cx=cx: s.copy(ob, cx[0:64, :]),
                       reads=[("f32c", 0)], writes=ak(1024, 1536))
                cap_ = c.ctxT[hh * 64:(hh + 1) * 64, tm * 512:(tm + 1) * 512]
                p.dma("sync", lambda cap_=cap_, ob=ob: nc.sync.dma_start(out=cap_, in_=ob),
                      reads=ak(1024, 1536), writes=[("ctxT", (hh // 2) * 8 + tm)])
    emit_proj_residual(c, c.iw["a1_wpT"], D, bias_dram=c.iw["a1_bp"])


def emit_sincos(c, prj, pjk, cosdst, sindst, cosk, sink, halfpi, scratch_id):
    nc, p = c.nc, c.p
    v, s = nc.vector, nc.scalar
    N = prj.shape[1]
    isl = (scratch_id % 4) * 128 if N <= 128 else (scratch_id % 2) * 512
    k32 = c.i512[:, isl:isl + N]
    kf = c.f32b[:, scratch_id * 512:scratch_id * 512 + N]
    kfk = [("f32b", scratch_id)]
    zz = c.f32c[:, scratch_id * 512:scratch_id * 512 + N]
    zzk = [("f32c", scratch_id)]
    aa = c.f32b[:, (scratch_id + 4) * 512:(scratch_id + 4) * 512 + N]
    aak = [("f32b", scratch_id + 4)]
    ik = [("i512", isl // 128), ("i512", isl // 128 + (1 if N > 128 else 0))]
    p.comp("vector", lambda: v.tensor_single_scalar(k32, prj, 1.0 / (2 * PI), ALU.mult),
           reads=pjk, writes=ik)
    p.comp("vector", lambda: v.tensor_copy(kf, k32), reads=ik, writes=kfk)
    p.comp("vector", lambda: v.scalar_tensor_tensor(zz, kf, 2 * PI, prj, ALU.mult, ALU.subtract),
           reads=kfk + pjk, writes=zzk)
    p.comp("scalar", lambda: s.activation(sindst, zz, AF.Sin, bias=0.0, scale=-1.0),
           reads=zzk, writes=sink)
    p.comp("scalar", lambda: s.activation(aa, zz, AF.Abs, bias=0.0, scale=1.0),
           reads=zzk, writes=aak)
    p.comp("scalar", lambda: s.activation(cosdst, aa, AF.Sin, bias=halfpi, scale=-1.0),
           reads=aak + [("cols", 1)], writes=cosk)


def emit_l1_attn_b(c):
    nc, p = c.nc, c.p
    v, s, t, g = nc.vector, nc.scalar, nc.tensor, nc.gpsimd
    iw = c.iw
    ak = lambda c0, c1: keys("att", c0, c1)
    ew = lambda c0, c1: keys("ework", c0, c1)
    wk = lambda c0, c1: keys("wst", c0, c1)
    emit_ln(c, c.X, "X", dst_sbuf=True, dst32=c.tT, dst32key="tT")
    emit_bias_cols(c, iw["b1_bq"], 8, 40)
    emit_bias_cols(c, iw["b1_bk"], 8, 48)
    emit_bias_cols(c, iw["b1_bv"], 8, 56)
    for hp in range(8):
        for wi, nm in enumerate(("b1_wqT", "b1_wkT", "b1_wvT")):
            wap = iw[nm][:, hp * 128:(hp + 1) * 128].rearrange("(kt p) m -> p kt m", p=128)
            wsb = c.wst[:, wi * 1024:(wi + 1) * 1024]
            p.dma("gpsimd", lambda wsb=wsb, wap=wap: nc.gpsimd.dma_start(out=wsb, in_=wap),
                  writes=wk(wi * 1024, (wi + 1) * 1024))
        for wi, dst0, bcol0, act in ((0, 4096, 40, AF.Relu), (1, 8192, 48, AF.Relu), (2, 12288, 56, AF.Identity)):
            for tm in range(NTM):
                pb = tm % 2
                for kt in range(NKT):
                    lhs = c.wst[:, wi * 1024 + kt * 128: wi * 1024 + (kt + 1) * 128]
                    rhs = c.xnb[:, kt * T + tm * 512: kt * T + (tm + 1) * 512]
                    p.comp("tensor", lambda lhs=lhs, rhs=rhs, pb=pb, kt=kt:
                           t.matmul(c.ps[pb], lhs, rhs, start=(kt == 0), stop=(kt == NKT - 1)),
                           reads=wk(wi * 1024, (wi + 1) * 1024) +
                                 keys("xnb", (kt * 8 + tm) * 512, (kt * 8 + tm) * 512 + 512),
                           writes=[(f"p{pb}",)])
                osb = c.wst[:, dst0 + tm * 512: dst0 + (tm + 1) * 512]
                bcol = c.cols[:, bcol0 + hp:bcol0 + hp + 1]
                p.comp("scalar", lambda osb=osb, pb=pb, bcol=bcol, act=act:
                       s.activation(osb, c.ps[pb], act, bias=bcol, scale=1.0),
                       reads=[(f"p{pb}",), ("cols", 0)], writes=wk(dst0 + tm * 512, dst0 + (tm + 1) * 512))
        for which, src0, dst0 in (("k", 8192, 0), ("v", 12288, 4096)):
            for tt in range(32):
                src_ = c.wst[:, src0 + tt * 128:src0 + (tt + 1) * 128]
                qb = (tt // 8) % 2
                p.comp("tensor", lambda src_=src_, qb=qb, tt=tt:
                       t.transpose(c.qs[qb][:, (tt % 8) * 128:(tt % 8) * 128 + 128], src_, c.idB),
                       reads=wk(src0 + tt * 128, src0 + (tt + 1) * 128) + [("cbf", 0)], writes=[(f"q{qb}",)])
                if tt % 8 == 7:
                    g0 = tt - 7
                    dst_ = c.ework[:, dst0 + g0 * 128:dst0 + g0 * 128 + 1024]
                    p.comp("vector", lambda dst_=dst_, qb=qb:
                           v.tensor_copy(dst_, c.qs[qb][:, 0:1024]),
                           reads=[(f"q{qb}",)], writes=ew(dst0 + g0 * 128, dst0 + g0 * 128 + 1024))
        for h2 in range(2):
            hh = hp * 2 + h2
            b0 = h2 * 64
            for tt in range(32):
                kt_ = c.ework[:, tt * 128 + b0:tt * 128 + b0 + 64]
                vt_ = c.ework[:, 4096 + tt * 128 + b0:4096 + tt * 128 + b0 + 64]
                p.comp("tensor", lambda kt_=kt_, vt_=vt_, tt=tt, h2=h2:
                       t.matmul(c.ps[2 + h2][0:64, 0:64], kt_, vt_, start=(tt == 0), stop=(tt == 31)),
                       reads=ew(tt * 128, (tt + 1) * 128) + ew(4096 + tt * 128, 4096 + (tt + 1) * 128),
                       writes=[(f"p{2 + h2}",)])
                p.comp("tensor", lambda kt_=kt_, tt=tt, h2=h2:
                       t.matmul(c.ps[4 + h2][0:64, 0:1], kt_, c.onescolB, start=(tt == 0), stop=(tt == 31)),
                       reads=ew(tt * 128, (tt + 1) * 128) + [("cbf", 0)], writes=[(f"p{4 + h2}",)])
        for h2 in range(2):
            b0 = h2 * 64
            kv = c.att[b0:b0 + 64, 2048 - 65 - h2 * 65:2048 - h2 * 65]
            p.comp("vector", lambda kv=kv, h2=h2: v.tensor_copy(kv[:, 0:64], c.ps[2 + h2][0:64, 0:64]),
                   reads=[(f"p{2 + h2}",)], writes=ak(1918, 2048))
            p.comp("vector", lambda kv=kv, h2=h2: v.tensor_copy(kv[:, 64:65], c.ps[4 + h2][0:64, 0:1]),
                   reads=[(f"p{4 + h2}",)], writes=ak(1918, 2048))
        for h2 in range(2):
            hh = hp * 2 + h2
            b0 = h2 * 64
            kv = c.att[b0:b0 + 64, 2048 - 65 - h2 * 65:2048 - h2 * 65]
            for tm in range(NTM):
                rhs = c.wst[b0:b0 + 64, 4096 + tm * 512:4096 + (tm + 1) * 512]
                p.comp("tensor", lambda kv=kv, rhs=rhs:
                       t.matmul(c.ps[0][0:65, :], kv, rhs, start=True, stop=True),
                       reads=ak(1918, 2048) + wk(4096 + tm * 512, 4096 + (tm + 1) * 512),
                       writes=[("p0",)])
                dn = c.rows[0:1, 644:1156]
                p.comp("vector", lambda dn=dn: v.tensor_scalar(dn, c.ps[0][64:65, :], 1e-6, None, ALU.max),
                       reads=[("p0",)], writes=[("rows", 1), ("rows", 2)])
                p.comp("scalar", lambda dn=dn: s.activation(dn, dn, AF.Ln, bias=0.0, scale=1.0),
                       reads=[("rows", 1), ("rows", 2)], writes=[("rows", 1), ("rows", 2)])
                p.comp("scalar", lambda dn=dn: s.activation(dn, dn, AF.Exp, bias=0.0, scale=-1.0),
                       reads=[("rows", 1), ("rows", 2)], writes=[("rows", 1), ("rows", 2)])
                p.comp("tensor", lambda dn=dn: t.matmul(c.ps[4][0:64, :], c.onesrow[0:1, 0:64], dn, start=True, stop=True),
                       reads=[("rows", 1), ("rows", 2), ("rows", 3)], writes=[("p4",)])
                cx = c.f32c[:, 0:512]
                p.comp("vector", lambda cx=cx: v.tensor_copy(cx[0:64, :], c.ps[0][0:64, :]),
                       reads=[("p0",)], writes=[("f32c", 0)])
                p.comp("vector", lambda cx=cx: v.tensor_mul(cx[0:64, :], cx[0:64, :], c.ps[4][0:64, :]),
                       reads=[("f32c", 0), ("p4",)], writes=[("f32c", 0)])
                tt_ = c.f32c[:, 512:1024]
                tap_ = c.tT[hh * 64:(hh + 1) * 64, tm * 512:(tm + 1) * 512]
                p.dma("scalar", lambda tt_=tt_, tap_=tap_: nc.scalar.dma_start(out=tt_[0:64, :], in_=tap_),
                      reads=[("tT", (hh // 2) * 8 + tm)], writes=[("f32c", 1)])
                p.comp("vector", lambda cx=cx, tt_=tt_: v.tensor_add(cx[0:64, :], cx[0:64, :], tt_[0:64, :]),
                       reads=[("f32c", 0), ("f32c", 1)], writes=[("f32c", 0)])
                yap = c.yT[hh * 64:(hh + 1) * 64, tm * 512:(tm + 1) * 512]
                p.dma("sync", lambda yap=yap, cx=cx: nc.sync.dma_start(out=yap, in_=cx[0:64, :]),
                      reads=[("f32c", 0)], writes=[("yT", (hh // 2) * 8 + tm)])
    emit_bias_cols(c, iw["b1_gamma"], 8, 8)
    emit_bias_cols(c, iw["b1_beta"], 8, 16)
    emit_ln_add_to_X(c)


def emit_ln_add_to_X(c):
    nc, p = c.nc, c.p
    v, s, t = nc.vector, nc.scalar, nc.tensor
    for m in range(NTM):
        for kt in range(NKT):
            buf = c.f32a[:, kt * 512:(kt + 1) * 512]
            srcap = c.yT[kt * 128:(kt + 1) * 128, m * 512:(m + 1) * 512]
            p.dma("sync", lambda b=buf, a=srcap: nc.sync.dma_start(out=b, in_=a),
                  reads=[("yT", kt * 8 + m)], writes=[("f32a", kt)])
        for kt in range(NKT):
            buf = c.f32a[:, kt * 512:(kt + 1) * 512]
            sq = c.f32c[:, kt * 512:(kt + 1) * 512]
            p.comp("scalar", lambda sq=sq, b=buf: s.activation(sq, b, AF.Square, bias=0.0, scale=1.0),
                   reads=[("f32a", kt)], writes=[("f32c", kt)])
            p.comp("tensor", lambda b=buf, kt=kt: t.matmul(c.ps[4][0:1, :], c.onescol, b,
                                                           start=(kt == 0), stop=(kt == NKT - 1)),
                   reads=[("f32a", kt), ("consts", 0)], writes=[("p4",)])
            p.comp("tensor", lambda sq=sq, kt=kt: t.matmul(c.ps[5][0:1, :], c.onescol, sq,
                                                           start=(kt == 0), stop=(kt == NKT - 1)),
                   reads=[("f32c", kt), ("consts", 0)], writes=[("p5",)])
        r0 = c.rows[0:1, 0:512]; r1 = c.rows[0:1, 512:1024]; r2 = c.rows[0:1, 1024:1536]
        p.comp("vector", lambda: v.tensor_single_scalar(r0, c.ps[4][0:1, :], 1.0 / D, ALU.mult),
               reads=[("p4",)], writes=[("rows", 0)])
        p.comp("vector", lambda: v.tensor_single_scalar(r1, c.ps[5][0:1, :], 1.0 / D, ALU.mult),
               reads=[("p5",)], writes=[("rows", 1)])
        p.comp("vector", lambda: v.scalar_tensor_tensor(r2, r0, 1.0, r0, ALU.mult, ALU.mult),
               reads=[("rows", 0)], writes=[("rows", 2)])
        p.comp("vector", lambda: v.tensor_sub(r1, r1, r2),
               reads=[("rows", 1), ("rows", 2)], writes=[("rows", 1)])
        epsr = constrow(c, 1e-5)
        p.comp("scalar", lambda epsr=epsr: s.activation(r2, r1, AF.Ln, bias=epsr, scale=1.0),
               reads=[("rows", 1), ("cols", 1)], writes=[("rows", 2)])
        p.comp("scalar", lambda: s.activation(r1, r2, AF.Exp, bias=0.0, scale=-0.5),
               reads=[("rows", 2)], writes=[("rows", 1)])
        p.comp("vector", lambda: v.tensor_single_scalar(r0, r0, -1.0, ALU.mult),
               reads=[("rows", 0)], writes=[("rows", 0)])
        p.comp("tensor", lambda: t.matmul(c.ps[4], c.onesrow, r0, start=True, stop=True),
               reads=[("rows", 0), ("rows", 3)], writes=[("p4",)])
        p.comp("tensor", lambda: t.matmul(c.ps[5], c.onesrow, r1, start=True, stop=True),
               reads=[("rows", 1), ("rows", 3)], writes=[("p5",)])
        for kt in range(NKT):
            buf = c.f32a[:, kt * 512:(kt + 1) * 512]
            xb = c.f32b[:, kt * 512:(kt + 1) * 512]
            p.comp("vector", lambda xb=xb, b=buf: v.tensor_add(xb, b, c.ps[4]),
                   reads=[("f32a", kt), ("p4",)], writes=[("f32b", kt)])
            p.comp("vector", lambda xb=xb: v.tensor_mul(xb, xb, c.ps[5]),
                   reads=[("f32b", kt), ("p5",)], writes=[("f32b", kt)])
            gcol = c.cols[:, 8 + kt:9 + kt]; bcol = c.cols[:, 16 + kt:17 + kt]
            p.comp("vector", lambda xb=xb, gcol=gcol, bcol=bcol:
                   v.tensor_scalar(xb, xb, gcol, bcol, ALU.mult, ALU.add),
                   reads=[("f32b", kt), ("cols", 0)], writes=[("f32b", kt)])
            xt_ = c.f32c[:, kt * 512:(kt + 1) * 512]
            Xap = c.X[kt * 128:(kt + 1) * 128, m * 512:(m + 1) * 512]
            p.dma("scalar", lambda xt_=xt_, Xap=Xap: nc.scalar.dma_start(out=xt_, in_=Xap),
                  reads=[("X", kt * 8 + m)], writes=[("f32c", kt)])
            p.comp("vector", lambda xb=xb, xt_=xt_: v.tensor_add(xb, xb, xt_),
                   reads=[("f32b", kt), ("f32c", kt)], writes=[("f32b", kt)])
            p.dma("sync", lambda Xap=Xap, xb=xb: nc.sync.dma_start(out=Xap, in_=xb),
                  reads=[("f32b", kt)], writes=[("X", kt * 8 + m)])


def emit_finish(c):
    nc, p = c.nc, c.p
    for kt in range(NKT):
        for m in range(NTM):
            src = c.X[kt * 128:(kt + 1) * 128, m * 512:(m + 1) * 512]
            dst = c.out_T[kt * 128:(kt + 1) * 128, m * 512:(m + 1) * 512]
            buf = c.f32a[:, m * 512:(m + 1) * 512]
            p.dma("sync", lambda src=src, buf=buf: nc.sync.dma_start(out=buf, in_=src),
                  reads=[("X", kt * 8 + m)], writes=[("f32a", m)])
            p.dma("sync", lambda dst=dst, buf=buf: nc.sync.dma_start(out=dst, in_=buf),
                  reads=[("f32a", m)], writes=[("out", kt * 8 + m)])
    p.comp("vector", lambda: nc.vector.engine_nop(),
           reads=[("out", i) for i in range(NKT * 8)])


def make_inputs(inputs, core):
    tr = lambda a: np.ascontiguousarray(np.asarray(a, dtype=np.float32).T)
    d = {}
    d["x_T"] = tr(inputs["L0_attn_a_x"][core])
    d["a0_wxT"] = tr(inputs["L0_attn_a_w_xproj"])
    d["a0_bx"] = np.asarray(inputs["L0_attn_a_b_xproj"], np.float32)
    d["a0_A"] = np.asarray(inputs["L0_attn_a_A"], np.float32)
    d["a0_D"] = np.asarray(inputs["L0_attn_a_D_param"], np.float32)
    d["a0_woT"] = tr(inputs["L0_attn_a_w_out"])
    d["a0_bo"] = np.asarray(inputs["L0_attn_a_b_out"], np.float32)
    d["a0_gamma"] = np.asarray(inputs["L0_attn_a_gamma"], np.float32)
    d["a0_beta"] = np.asarray(inputs["L0_attn_a_beta"], np.float32)
    d["b0_wqT"] = tr(inputs["L0_attn_b_Wq"])
    d["b0_wkT"] = tr(inputs["L0_attn_b_Wk"])
    d["b0_wvT"] = tr(inputs["L0_attn_b_Wv"])
    d["b0_woT"] = tr(inputs["L0_attn_b_Wo"])
    d["b0_bo"] = np.asarray(inputs["L0_attn_b_bo"], np.float32)
    d["b0_wspanT"] = tr(inputs["L0_attn_b_Wspan"])
    d["b0_bspan"] = np.asarray(inputs["L0_attn_b_bspan"], np.float32)
    d["f0_rwT"] = tr(inputs["L0_ffn_router_w"])
    d["f0_w1"] = np.asarray(inputs["L0_ffn_W1"], np.float32).reshape(8 * 1024, 4096)
    d["f0_b1"] = np.asarray(inputs["L0_ffn_b1"], np.float32)
    d["f0_w2"] = np.asarray(inputs["L0_ffn_W2"], np.float32).reshape(8 * 4096, 1024)
    d["f0_b2"] = np.asarray(inputs["L0_ffn_b2"], np.float32)
    d["a1_wqT"] = tr(inputs["L1_attn_a_Wq"])
    d["a1_wkT"] = tr(inputs["L1_attn_a_Wk"])
    d["a1_wvT"] = tr(inputs["L1_attn_a_Wv"])
    d["a1_wpT"] = tr(inputs["L1_attn_a_Wp"])
    d["a1_bp"] = np.asarray(inputs["L1_attn_a_bp"], np.float32)
    d["a1_wr"] = np.asarray(inputs["L1_attn_a_Wr"], np.float32).reshape(16 * 64, 128)
    d["a1_br"] = np.asarray(inputs["L1_attn_a_br"], np.float32)
    d["b1_wqT"] = tr(inputs["L1_attn_b_Wq"])
    d["b1_wkT"] = tr(inputs["L1_attn_b_Wk"])
    d["b1_wvT"] = tr(inputs["L1_attn_b_Wv"])
    d["b1_bq"] = np.asarray(inputs["L1_attn_b_bq"], np.float32)
    d["b1_bk"] = np.asarray(inputs["L1_attn_b_bk"], np.float32)
    d["b1_bv"] = np.asarray(inputs["L1_attn_b_bv"], np.float32)
    d["b1_gamma"] = np.asarray(inputs["L1_attn_b_gamma"], np.float32)
    d["b1_beta"] = np.asarray(inputs["L1_attn_b_beta"], np.float32)
    d["f1_wgT"] = tr(inputs["L1_ffn_Wg"])
    d["f1_w1"] = np.asarray(inputs["L1_ffn_W1"], np.float32).reshape(8 * 1024, 4096)
    d["f1_w2"] = np.asarray(inputs["L1_ffn_W2"], np.float32).reshape(8 * 2048, 1024)
    pm = np.zeros(8, np.float32); pm[:core] = 1.0
    d["prevmask"] = pm
    return d


def kernel(**inputs):
    nc = build_program(stop_after=8)
    in_maps = [make_inputs(inputs, core) for core in range(NCORES)]
    res = run_bass_kernel_spmd(nc, in_maps, core_ids=list(range(NCORES)))
    out = np.stack([np.ascontiguousarray(res.results[i]["out_T"].T) for i in range(NCORES)])
    return out.astype(np.float32)
